# version 02
# baseline (speedup 1.0000x reference)
import math
import numpy as np
from contextlib import ExitStack

import concourse.bass as bass
import concourse.mybir as mybir
from concourse.bass_utils import run_bass_kernel_spmd

F32 = mybir.dt.float32
BF16 = mybir.dt.bfloat16
I32 = mybir.dt.int32
AF = mybir.ActivationFunctionType
ALU = mybir.AluOpType
AX = mybir.AxisListType

T = 2048
D = 2048
NCHUNK = 4
TC = 512
NTT = TC // 128
NDK = D // 128

NEG = -1.0e30

L0_NH, L0_NKV, L0_HD, L0_W, L0_CAP = 16, 4, 128, 1024, 50.0
L1_NH, L1_HD = 32, 64
NE, F0, RADJ, NL = 8, 2048, 64, 2
I1 = 8192

GOLD_C = 1.0 / math.e
GOLD_L = 0.5 - math.log(4.0 / 3.0)
GOLD_U = 0.5


def host_constants():
    c = {}
    c["c_identf"] = np.eye(128, dtype=np.float32)
    r0 = np.zeros((128, 128), dtype=np.float32)
    for i in range(64):
        r0[i + 64, i] = -1.0
        r0[i, i + 64] = 1.0
    c["c_rot0"] = r0
    r1 = np.zeros((128, 128), dtype=np.float32)
    for b in (0, 64):
        for i in range(32):
            r1[b + i + 32, b + i] = -1.0
            r1[b + i, b + i + 32] = 1.0
    c["c_rot1"] = r1

    def sine_table(hd, theta):
        frac = np.arange(0, hd, 2, dtype=np.float64) / hd
        invts = theta ** -frac
        tpos = np.arange(T, dtype=np.float64)
        s = np.outer(tpos, invts)
        s = np.concatenate([s, s], axis=1)
        return np.sin(s).T.astype(np.float32), np.cos(s).T.astype(np.float32)

    s0, c0 = sine_table(128, 10000.0)
    c["c_sin0"], c["c_cos0"] = s0, c0
    s1, c1 = sine_table(64, 10000.0)
    c["c_sin1"] = np.concatenate([s1, s1], axis=0)
    c["c_cos1"] = np.concatenate([c1, c1], axis=0)
    i_k = np.arange(128)[:, None]
    i_q = np.arange(128)[None, :]
    c["c_mdiag"] = np.where(i_k <= i_q, 0.0, NEG).astype(np.float32)
    c["c_medge"] = np.where(i_k >= i_q, 0.0, NEG).astype(np.float32)
    rows = np.zeros((128, 8), dtype=np.float32)
    rows[:, 0] = -20.0
    rows[:, 1] = 1e-5
    rows[:, 2] = -GOLD_C
    rows[:, 3] = 1e-8
    rows[:, 4] = 1.0
    c["c_rows"] = rows
    c["c_ones_row"] = np.ones((1, 128), dtype=np.float32)
    c["c_ones_colf"] = np.ones((128, 1), dtype=np.float32)
    return c


ENGINES = ("sync", "tensor", "scalar", "vector", "gpsimd")

_REAL = {
    "dma_start", "matmul", "transpose", "activation", "tensor_tensor",
    "tensor_scalar", "tensor_copy", "tensor_add", "tensor_sub", "tensor_mul",
    "tensor_max", "reduce_sum", "reduce_max", "tensor_reduce", "reciprocal",
    "memset", "memzero", "iota", "copy", "mul", "add", "sqrt", "square",
    "tensor_scalar_add", "tensor_scalar_mul", "tensor_scalar_max",
    "tensor_scalar_min", "scalar_tensor_tensor",
}


_UPDATED = {}


class Sem:
    def __init__(self, handle):
        self.h = handle
        self.n = 0
        self.maxw = 0

    def inc(self, inst, k=1):
        assert id(inst) not in _UPDATED, "instruction already carries a sem update"
        _UPDATED[id(inst)] = inst
        inst.then_inc(self.h, k)
        self.n += k
        return self.n

    def dinc(self, inst):
        return self.inc(inst, 16)


class EW:

    def __init__(self, eng, prog, name):
        self._e = eng
        self._p = prog
        self._n = name

    def __getattr__(self, a):
        f = getattr(self._e, a)
        if a in _REAL:
            def g(*args, **kw):
                inst = f(*args, **kw)
                self._p.last_real[self._n] = inst
                return inst
            return g
        return f

    def wait(self, sem, val):
        sem.maxw = max(sem.maxw, val)
        self._e.wait_ge(sem.h, val)


class Prog:
    def __init__(self, layers=(0, 1), chunks=None, debug=()):
        self.layers = tuple(layers)
        self.chunks = list(range(NCHUNK)) if chunks is None else list(chunks)
        self.debug = set(debug)
        self.nc = bass.Bass("TRN2", target_bir_lowering=False)
        self.ctx = ExitStack()
        self.last_real = {n: None for n in ENGINES}
        self.phase_idx = 0
        self.dbg_tensors = {}
        self._mk_dram()
        self._mk_sbuf()
        self._mk_psum()
        self._mk_sems()
        self.block = self.ctx.enter_context(self.nc.Block())

    def _mk_dram(self):
        nc = self.nc
        def ap(name, shape, dt, kind):
            return nc.dram_tensor(name, shape, dt, kind=kind).ap()
        EI, EO, IN = "ExternalInput", "ExternalOutput", "Internal"
        self.d = d = {}
        d["x"] = ap("x", [T, D], F32, EI)
        d["y"] = ap("y", [T, D], F32, EO)
        d["wq0"] = ap("wq0", [D, 2048], F32, EI)
        d["wk0"] = ap("wk0", [D, 512], F32, EI)
        d["wv0"] = ap("wv0", [D, 512], F32, EI)
        d["wo0"] = ap("wo0", [2048, D], F32, EI)
        d["wg0"] = ap("wg0", [NE, D, F0], F32, EI)
        d["wu0"] = ap("wu0", [NE, D, F0], F32, EI)
        d["wd0"] = ap("wd0", [NE, F0, D], F32, EI)
        d["ga0"] = ap("ga0", [NL, D, RADJ], F32, EI)
        d["gb0"] = ap("gb0", [NL, RADJ, F0], F32, EI)
        d["ua0"] = ap("ua0", [NL, D, RADJ], F32, EI)
        d["ub0"] = ap("ub0", [NL, RADJ, F0], F32, EI)
        d["da0"] = ap("da0", [NL, F0, RADJ], F32, EI)
        d["db0"] = ap("db0", [NL, RADJ, D], F32, EI)
        d["wr0"] = ap("wr0", [D, NE], F32, EI)
        d["temp0"] = ap("temp0", [1, 1], F32, EI)
        d["wqkv1"] = ap("wqkv1", [D, 2176], F32, EI)
        d["wo1"] = ap("wo1", [D, D], F32, EI)
        d["g1q"] = ap("g1q", [I1, D], I32, EI)
        d["g1s"] = ap("g1s", [I1, 1], F32, EI)
        d["u1q"] = ap("u1q", [I1, D], I32, EI)
        d["u1s"] = ap("u1s", [I1, 1], F32, EI)
        d["d1q"] = ap("d1q", [D, I1], I32, EI)
        d["d1s"] = ap("d1s", [D, 1], F32, EI)
        for name, arr in host_constants().items():
            d[name] = ap(name, list(arr.shape), F32, EI)
        d["wg1T"] = ap("wg1T", [I1 // 128, 128, D], BF16, IN)
        d["wu1T"] = ap("wu1T", [I1 // 128, 128, D], BF16, IN)
        d["wd1T"] = ap("wd1T", [D // 128, 128, I1], BF16, IN)

    def row(self, r):
        b = (0, 0, 0, 32, 32, 32, 64, 64, 64)[r]
        col = (r % 3) * TC
        return self.s["rows"][b:b + 1, col:col + TC]

    def dbg(self, name, shape, dt=F32):
        if name not in self.dbg_tensors:
            self.dbg_tensors[name] = self.nc.dram_tensor(
                "dbg_" + name, shape, dt, kind="ExternalOutput").ap()
        return self.dbg_tensors[name]

    def _mk_sbuf(self):
        nc, E = self.nc, self.ctx.enter_context
        def sb(name, shape, dt):
            return E(nc.sbuf_tensor(name, shape, dt))
        self.s = s = {}
        s["identf"] = sb("identf", [128, 128], F32)
        s["identb"] = sb("identb", [128, 128], BF16)
        s["rot0"] = sb("rot0", [128, 128], BF16)
        s["rot1"] = sb("rot1", [128, 128], BF16)
        s["mdiag"] = sb("mdiag", [128, 128], F32)
        s["medge"] = sb("medge", [128, 128], F32)
        s["ones_row"] = sb("ones_row", [1, 128], F32)
        s["ones_col"] = sb("ones_col", [128, 1], BF16)
        s["crows"] = sb("crows", [128, 8], F32)
        s["invtau"] = sb("invtau", [128, 1], F32)
        s["tb"] = sb("tb", [128, 2 * TC], BF16)
        s["xnT"] = sb("xnT", [128, NDK * TC], BF16)
        s["k0s"] = sb("k0s", [128, L0_NKV * T], BF16)
        s["v0s"] = sb("v0s", [128, L0_NKV * T], BF16)
        s["k1s"] = sb("k1s", [128, T], BF16)
        s["v1s"] = sb("v1s", [128, (T // 128) * L1_HD], BF16)
        s["qt"] = sb("qt", [128, TC], BF16)
        s["pst"] = [sb(f"pst{i}", [128, TC], BF16) for i in range(2)]
        s["afm"] = sb("afm", [128, L0_NH * TC], BF16)
        s["accx"] = sb("accx", [128, NDK * TC], F32)
        s["mid"] = sb("mid", [128, 16 * TC], BF16)
        s["ws0"] = sb("ws0", [128, 8192], BF16)
        s["ws1"] = sb("ws1", [128, 8192], BF16)
        s["wl0"] = sb("wl0", [128, 2048], F32)
        s["wl1"] = sb("wl1", [128, 2048], F32)
        s["adsl"] = sb("adsl", [128, 4096], BF16)
        s["xga"] = sb("xga", [64, 4 * TC], BF16)
        s["mda"] = sb("mda", [64, TC], BF16)
        s["lnst"] = sb("lnst", [128, 32], F32)
        s["lnstb"] = sb("lnstb", [128, 32], F32)
        s["rt"] = sb("rt", [128, 4 * 64], F32)
        s["rstat"] = sb("rstat", [128, 48], F32)
        s["rstatb"] = sb("rstatb", [128, 48], F32)
        s["w8f"] = sb("w8f", [8, TC], F32)
        s["w8"] = sb("w8", [128, 4 * 8], F32)
        s["wbc"] = sb("wbc", [128, TC], F32)
        s["rows"] = sb("rows", [128, 3 * TC], F32)
        s["rows8"] = sb("rows8", [8, 128], F32)
        s["scr"] = [sb(f"scr{i}", [128, TC], F32) for i in range(3)]
        s["scrb"] = sb("scrb", [128, TC], BF16)
        s["dmy"] = sb("dmy", [128, 8], F32)

    def _mk_psum(self):
        nc, E = self.nc, self.ctx.enter_context
        self.p = {}
        for name in ["pA", "pB", "pC", "pD", "pE"]:
            self.p[name] = E(nc.psum_tensor(name, [128, 512], F32))
        self.p["pT0"] = E(nc.psum_tensor("pT0", [128, 128], BF16))
        self.p["pT1"] = E(nc.psum_tensor("pT1", [128, 128], BF16))
        self.p["pTf"] = E(nc.psum_tensor("pTf", [128, 128], F32))

    def _mk_sems(self):
        E = self.ctx.enter_context
        self.sems = {}
        for name in ["dW", "dX", "dS", "dO", "dT", "dM",
                     "mT", "mS", "mV", "mG",
                     "bS", "bT", "bC", "bV", "bG",
                     "fV", "fS"]:
            self.sems[name] = Sem(E(self.nc.semaphore(name)))
        self.dL = [Sem(E(self.nc.semaphore(f"dL{i}"))) for i in range(16)]

    def sec(self, engine, fn):
        dec = getattr(self.block, engine)
        def wrapped(raw):
            fn(EW(raw, self, engine))
        dec(wrapped)

    def barrier(self):
        self.phase_idx += 1
        k = self.phase_idx
        dvals = [(self.sems[nm], self.sems[nm].n)
                 for nm in ("dX", "dS", "dW", "dO", "dT", "dM")]
        dvals += [(sem, sem.n) for sem in self.dL]
        def fn_sync(e):
            for sem, val in dvals:
                e.wait_ge(sem.h, val)
            e.nop().then_inc(self.sems["bS"].h, 1)
        self.sec("sync", fn_sync)
        self.sems["bS"].n += 1
        self.last_real["sync"] = None
        bmap = {"tensor": "bT", "scalar": "bC", "vector": "bV", "gpsimd": "bG"}
        for eng in ("tensor", "scalar", "vector", "gpsimd"):
            sem = self.sems[bmap[eng]]
            had_work = self.last_real[eng] is not None
            self.last_real[eng] = None
            def fn(e, sem=sem, eng=eng, had_work=had_work):
                if not had_work:
                    e.nop().then_inc(sem.h, 1)
                    return
                if eng == "tensor":
                    e.wait_ge(self.sems["mV"].h, self.sems["mV"].n)
                    e.wait_ge(self.sems["mS"].h, self.sems["mS"].n)
                    e.wait_ge(self.sems["mT"].h, self.sems["mT"].n)
                    inst = e.matmul(self.p["pE"][0:1, 0:1], self.s["ones_col"][:],
                                    self.s["ones_col"][:], start=True, stop=True)
                elif eng == "scalar":
                    e.wait_ge(self.sems["mS"].h, self.sems["mS"].n)
                    e.wait_ge(self.sems["fS"].h, self.sems["fS"].n)
                    e.wait_ge(self.sems["fV"].h, self.sems["fV"].n)
                    inst = e.activation(self.s["dmy"][:, 0:1], self.s["dmy"][:, 1:2],
                                        AF.Copy)
                elif eng == "vector":
                    e.wait_ge(self.sems["mV"].h, self.sems["mV"].n)
                    e.wait_ge(self.sems["fV"].h, self.sems["fV"].n)
                    inst = e.memset(self.s["dmy"][:, 2:3], 0.0)
                else:
                    e.wait_ge(self.sems["mG"].h, self.sems["mG"].n)
                    inst = e.memset(self.s["dmy"][:, 3:4], 0.0)
                inst.then_inc(sem.h, 1)
            self.sec(eng, fn)
            sem.n += 1
            assert sem.n == k, (eng, sem.n, k)
        for eng in ENGINES:
            def fn(e, k=k):
                for nm in ("bS", "bT", "bC", "bV", "bG"):
                    e.wait_ge(self.sems[nm].h, k)
            self.sec(eng, fn)

    def finalize(self):
        for nm, sem in self.sems.items():
            assert sem.maxw <= sem.n, ("wait beyond final incs", nm, sem.maxw, sem.n)
        self.ctx.close()

    def run(self, in_maps, trace=False):
        self.finalize()
        return run_bass_kernel_spmd(self.nc, in_maps,
                                    core_ids=list(range(len(in_maps))),
                                    trace=trace)

    def sim(self, in_map):
        self.finalize()
        from concourse.bass_interp import CoreSim
        sim = CoreSim(self.nc, require_finite=False, require_nnan=False)
        for name, val in in_map.items():
            view = sim.tensor(name)
            view[:] = val
        sim.simulate()
        outs = {}
        for alloc in self.nc.m.functions[0].allocations:
            import concourse.mybir as _mb
            if isinstance(alloc, _mb.MemoryLocationSet) and alloc.kind == "ExternalOutput":
                nm = alloc.memorylocations[0].name
                outs[nm] = np.array(sim.tensor(nm))
        return outs


def _band0(c):
    out = []
    for kt in range(max(0, 4 * c - 8), 4 * c + 4):
        segs = []
        for tl in range(NTT):
            rel = (4 * c + tl) - kt
            if rel < 0 or rel > 8:
                segs.append("zero")
            elif rel == 0:
                segs.append("diag")
            elif rel == 8:
                segs.append("edge")
            else:
                segs.append("keep")
        out.append((kt, segs))
    return out


def _band1(c):
    out = []
    for kt in range(0, 4 * c + 4):
        segs = []
        for tl in range(NTT):
            rel = (4 * c + tl) - kt
            segs.append("zero" if rel < 0 else ("diag" if rel == 0 else "keep"))
        out.append((kt, segs))
    return out


def ph_setup(P: Prog):
    s, d, p = P.s, P.d, P.p
    dX, dS = P.sems["dX"], P.sems["dS"]
    mV, mT = P.sems["mV"], P.sems["mT"]

    def sync(q):
        for nm_s, nm_d in [("identf", "c_identf"), ("mdiag", "c_mdiag"),
                           ("medge", "c_medge"), ("ones_row", "c_ones_row"),
                           ("crows", "c_rows")]:
            dX.dinc(q.dma_start(out=s[nm_s][:], in_=d[nm_d]))
        dX.dinc(q.dma_start(out=s["scr"][0][:, 0:128], in_=d["c_identf"]))
        dX.dinc(q.dma_start(out=s["scr"][1][:, 0:128], in_=d["c_rot0"]))
        dX.dinc(q.dma_start(out=s["scr"][2][:, 0:128], in_=d["c_rot1"]))
        dX.dinc(q.dma_start(out=s["wl0"][:, 0:1], in_=d["c_ones_colf"]))
        dX.dinc(q.dma_start(out=s["wl1"][0:1, 0:1], in_=d["temp0"]))
        P._t0 = dX.n

    def vector(v):
        P.sems["fV"].inc(v.memset(s["dmy"][:], 0.0))
        v.wait(dX, P._t0)
        v.tensor_copy(s["identb"][:], s["scr"][0][:, 0:128])
        v.tensor_copy(s["rot0"][:], s["scr"][1][:, 0:128])
        v.tensor_copy(s["rot1"][:], s["scr"][2][:, 0:128])
        v.tensor_copy(s["ones_col"][:], s["wl0"][:, 0:1])

    P.sec("sync", sync)
    P.sec("vector", vector)
    P.barrier()

    def tensor2(t):
        mT.inc(t.matmul(p["pTf"][:, 0:1], s["ones_row"][:], s["wl1"][0:1, 0:1],
                        start=True, stop=True))

    def vector2(v):
        v.wait(mT, mT.n)
        mV.inc(v.reciprocal(s["scr"][0][:, 0:1], p["pTf"][:, 0:1]))

    def sync2(q):
        q.wait(mV, mV.n)
        dS.dinc(q.dma_start(out=s["invtau"][:], in_=s["scr"][0][:, 0:1]))

    P.sec("tensor", tensor2)
    P.sec("vector", vector2)
    P.sec("sync", sync2)
    P.barrier()


def ph_ln0(P: Prog, c):
    s, d, p = P.s, P.d, P.p
    xnt = s["mid"]
    dX, dS = P.sems["dX"], P.sems["dS"]
    mV, mS, mT = P.sems["mV"], P.sems["mS"], P.sems["mT"]
    fV = P.sems["fV"]
    wl = [s["wl0"], s["wl1"]]

    v0 = mV.n
    s0 = mS.n
    dL = [P.dL[0], P.dL[4]]
    xdma = {}

    def sync(q):
        for tt in range(NTT):
            if tt >= 2:
                q.wait(mV, v0 + (tt - 1))
                q.wait(mS, s0 + 4 * (tt - 1))
            r0 = c * TC + tt * 128
            xdma[tt] = dL[tt % 2].dinc(q.dma_start(out=wl[tt % 2][:],
                                                   in_=d["x"][r0:r0 + 128, :]))

    def vector(v):
        for tt in range(NTT):
            v.wait(dL[tt % 2], xdma[tt])
            mV.inc(v.reduce_sum(s["lnst"][:, tt:tt + 1], wl[tt % 2][:], axis=AX.X))

    def scalar(sc):
        for tt in range(NTT):
            sc.wait(dL[tt % 2], xdma[tt])
            for j in range(4):
                k = tt * 4 + j
                inst = sc.activation(xnt[:, k * 512:(k + 1) * 512],
                                     wl[tt % 2][:, j * 512:(j + 1) * 512],
                                     AF.Square, accum_out=s["rt"][:, 4 * tt + j:4 * tt + j + 1])
                mS.inc(inst)

    P.sec("sync", sync)
    P.sec("vector", vector)
    P.sec("scalar", scalar)

    v1 = mV.n
    s1 = mS.n

    def vector2(v):
        v.wait(mS, s1)
        v.wait(mV, v0 + 4)
        for tt in range(NTT):
            fV.inc(v.reduce_sum(s["lnst"][:, 4 + tt:5 + tt],
                                s["rt"][:, 4 * tt:4 * tt + 4], axis=AX.X))
        v.wait(fV, fV.n)
        i1 = v.tensor_scalar(s["lnst"][:, 20:24], s["lnst"][:, 0:4], 1.0 / D, None, ALU.mult)
        fV.inc(i1); v.wait(fV, fV.n)
        i2 = v.tensor_scalar(s["lnst"][:, 24:28], s["lnst"][:, 4:8], 1.0 / D, None, ALU.mult)
        fV.inc(i2)
        i3 = v.tensor_tensor(s["lnst"][:, 28:32], s["lnst"][:, 20:24], s["lnst"][:, 20:24], ALU.mult)
        fV.inc(i3); v.wait(fV, fV.n)
        i4 = v.tensor_tensor(s["lnst"][:, 24:28], s["lnst"][:, 24:28], s["lnst"][:, 28:32], ALU.subtract)
        mV.inc(i4)

    def scalar2(sc):
        sc.wait(mV, mV.n)
        inst = sc.activation(s["lnst"][:, 16:20], s["lnst"][:, 24:28], AF.Sqrt,
                             bias=s["crows"][:, 1:2])
        mS.inc(inst)

    def vector3(v):
        v.wait(mS, mS.n)
        i1 = v.reciprocal(s["lnst"][:, 8:12], s["lnst"][:, 16:20])
        fV.inc(i1); v.wait(fV, fV.n)
        i2 = v.tensor_tensor(s["lnst"][:, 12:16], s["lnst"][:, 20:24], s["lnst"][:, 8:12], ALU.mult)
        fV.inc(i2); v.wait(fV, fV.n)
        i3 = v.tensor_scalar(s["lnst"][:, 12:16], s["lnst"][:, 12:16], -1.0, None, ALU.mult)
        mV.inc(i3)

    P.sec("vector", vector2)
    P.sec("scalar", scalar2)
    P.sec("vector", vector3)

    vstat = mV.n

    def sync2(q):
        q.wait(mV, vstat)
        dS.dinc(q.dma_start(out=s["lnstb"][:, 8:16], in_=s["lnst"][:, 8:16]))

    def tensor2(t):
        t.wait(mV, vstat)
        mT.inc(t.transpose(p["pTf"][0:8, :], s["lnst"][:, 16:24], s["identf"][:]))

    def vector4(v):
        v.wait(mT, mT.n)
        mV.inc(v.tensor_copy(s["rows8"][:], p["pTf"][0:8, :]))

    def sync3(q):
        q.wait(mV, mV.n)
        for tt in range(NTT):
            dS.dinc(q.dma_start(out=P.row(0)[:, tt * 128:(tt + 1) * 128],
                                in_=s["rows8"][tt:tt + 1, :]))
            dS.dinc(q.dma_start(out=P.row(1)[:, tt * 128:(tt + 1) * 128],
                                in_=s["rows8"][4 + tt:5 + tt, :]))

    P.sec("sync", sync2)
    P.sec("tensor", tensor2)
    P.sec("vector", vector4)
    P.sec("sync", sync3)
    P.barrier()

    v2 = mV.n
    t2 = mT.n
    xdma2 = {}

    def cast_inc(tt):
        return v2 + tt * 17 + 1

    def evac_inc(tt, dk):
        return v2 + tt * 17 + 1 + dk + 1

    def trans_inc(idx):
        return t2 + idx + 1

    def sync4(q):
        for tt in range(NTT):
            if tt >= 2:
                q.wait(mV, cast_inc(tt - 1))
            r0 = c * TC + tt * 128
            xdma2[tt] = dL[tt % 2].dinc(q.dma_start(out=wl[tt % 2][:],
                                                    in_=d["x"][r0:r0 + 128, :]))

    def vector5(v):
        for tt in range(NTT):
            v.wait(dL[tt % 2], xdma2[tt])
            inst = v.tensor_scalar(xnt[:, 0:D], wl[tt % 2][:],
                                   s["lnstb"][:, 8 + tt:9 + tt],
                                   s["lnstb"][:, 12 + tt:13 + tt],
                                   ALU.mult, ALU.add)
            mV.inc(inst)
            assert mV.n == cast_inc(tt)
            for dk in range(NDK):
                idx = tt * NDK + dk
                v.wait(mT, trans_inc(idx))
                inst = v.tensor_copy(
                    s["xnT"][:, dk * TC + tt * 128: dk * TC + (tt + 1) * 128],
                    p["pT0" if idx % 2 == 0 else "pT1"][:])
                mV.inc(inst)
                assert mV.n == evac_inc(tt, dk)

    def tensor3(t):
        for tt in range(NTT):
            t.wait(mV, cast_inc(tt))
            for dk in range(NDK):
                idx = tt * NDK + dk
                if idx >= 2:
                    ptt, pdk = divmod(idx - 2, NDK)
                    t.wait(mV, evac_inc(ptt, pdk))
                inst = t.transpose(p["pT0" if idx % 2 == 0 else "pT1"][:],
                                   xnt[:, dk * 128:(dk + 1) * 128],
                                   s["identb"][:])
                mT.inc(inst)
                assert mT.n == trans_inc(idx)

    P.sec("sync", sync4)
    P.sec("tensor", tensor3)
    P.sec("vector", vector5)
    P.barrier()


def ph_dump(P: Prog, items):
    dO = P.sems["dO"]

    def sync(q):
        for name, apv, shape in items:
            dO.dinc(q.dma_start(out=P.dbg(name, list(shape), dt=apv.tensor.dtype),
                                in_=apv))
        q.wait(dO, dO.n)

    P.sec("sync", sync)
    P.barrier()


def mk_inmaps(inputs, n_cores=8):
    consts = host_constants()
    base = {
        "wq0": inputs["L0_attn_q_kernel"], "wk0": inputs["L0_attn_k_kernel"],
        "wv0": inputs["L0_attn_v_kernel"], "wo0": inputs["L0_attn_out_kernel"],
        "wg0": inputs["L0_ffn_Wg"], "wu0": inputs["L0_ffn_Wu"], "wd0": inputs["L0_ffn_Wd"],
        "ga0": inputs["L0_ffn_Ga"], "gb0": inputs["L0_ffn_Gb"],
        "ua0": inputs["L0_ffn_Ua"], "ub0": inputs["L0_ffn_Ub"],
        "da0": inputs["L0_ffn_Da"], "db0": inputs["L0_ffn_Db"],
        "wr0": inputs["L0_ffn_Wr"],
        "temp0": np.asarray(inputs["L0_ffn_temperature"], np.float32).reshape(1, 1),
        "wqkv1": inputs["L1_attn_w_qkv"], "wo1": inputs["L1_attn_w_dense"],
        "g1q": np.asarray(inputs["L1_ffn_gate_w"], np.int32),
        "g1s": np.asarray(inputs["L1_ffn_gate_scale"], np.float32).reshape(-1, 1),
        "u1q": np.asarray(inputs["L1_ffn_up_w"], np.int32),
        "u1s": np.asarray(inputs["L1_ffn_up_scale"], np.float32).reshape(-1, 1),
        "d1q": np.asarray(inputs["L1_ffn_down_w"], np.int32),
        "d1s": np.asarray(inputs["L1_ffn_down_scale"], np.float32).reshape(-1, 1),
    }
    base = {k: np.ascontiguousarray(np.asarray(v, dtype=np.int32 if k in
            ("g1q", "u1q", "d1q") else np.float32)) for k, v in base.items()}
    base.update(consts)
    x = np.asarray(inputs["L0_attn_x"], np.float32)
    maps = []
    for b in range(n_cores):
        m = dict(base)
        m["x"] = np.ascontiguousarray(x[b])
        maps.append(m)
    return maps


def ph_prep0(P: Prog, c):
    s, d, p = P.s, P.d, P.p
    dT, dM = P.sems["dT"], P.sems["dM"]
    mV, mS, mT = P.sems["mV"], P.sems["mS"], P.sems["mT"]
    wl = [s["wl0"], s["wl1"]]
    dL = [P.dL[0], P.dL[4]]
    tbc = s["tb"][:, 0:TC]
    tbs = s["tb"][:, TC:2 * TC]

    v0, t0, s0 = mV.n, mT.n, mS.n
    pdma = {}
    tbv = [0]
    wrv = [0]

    def cast(i):
        return v0 + 2 + (2 + i if i <= 5 else (10 if i == 6 else 13))
    qtcopy = lambda m: v0 + 2 + {0: 8, 1: 11, 2: 14, 3: 16}[m]
    krope = lambda m: v0 + 2 + {0: 9, 1: 12, 2: 15, 3: 17}[m]
    vevac = lambda idx: v0 + 2 + 18 + idx
    tbcast = v0 + 2
    v_end = v0 + 35
    _kv_pos = {0: 1, 1: 2, 2: 3, 3: 5, 4: 7, 5: 9, 6: 11, 7: 16}
    kvproj = lambda i: t0 + _kv_pos[i]
    rot = lambda m: t0 + 4 + 2 * m
    tr = lambda idx: t0 + (12 + idx if idx < 4 else 13 + idx)
    kevac = lambda i: s0 + i + 1

    def sync(q):
        dT.dinc(q.dma_start(out=s["scr"][1][:], in_=d["c_cos0"][:, c * TC:(c + 1) * TC]))
        tbv[0] = dT.dinc(q.dma_start(out=s["scr"][2][:], in_=d["c_sin0"][:, c * TC:(c + 1) * TC]))
        wrv[0] = dM.dinc(q.dma_start(
            out=s["scr"][0][:, 0:128].rearrange("p (k e) -> p k e", e=NE),
            in_=d["wr0"].rearrange("(k p) e -> p k e", p=128)))
        for i in range(8):
            m = i % 4
            srcw = d["wk0"] if i < 4 else d["wv0"]
            if i >= 2:
                q.wait(mV, cast(i - 1))
            pdma[i] = dL[i % 2].dinc(q.dma_start(
                out=wl[i % 2][:].rearrange("p (k m) -> p k m", m=128),
                in_=srcw[:, m * 128:(m + 1) * 128].rearrange("(k p) m -> p k m", p=128)))

    def emit_cast(v, i):
        v.wait(dL[i % 2], pdma[i])
        if i >= 4:
            v.wait(mT, kvproj(i - 4))
        mV.inc(v.tensor_copy(s["ws0"][:, (i % 4) * 2048:(i % 4 + 1) * 2048],
                             wl[i % 2][:]))
        assert mV.n == cast(i)

    def emit_qk(v, m):
        tgt = s["scr"][1] if m % 2 == 0 else s["scr"][2]
        v.wait(mS, kevac(m))
        if m >= 1:
            v.wait(mT, rot(m - 1))
            v.wait(mV, krope(m - 1))
        mV.inc(v.tensor_copy(s["qt"][:], tgt[:]))
        assert mV.n == qtcopy(m)
        v.wait(mT, rot(m))
        fV = P.sems["fV"]
        fV.inc(v.tensor_tensor(s["scr"][0][:], tgt[:], tbc, ALU.mult))
        fV.inc(v.tensor_tensor(s["wbc"][:], p["pC"][:], tbs, ALU.mult))
        v.wait(fV, fV.n)
        inst = v.tensor_tensor(s["k0s"][:, m * T + c * TC: m * T + (c + 1) * TC],
                               s["scr"][0][:], s["wbc"][:], ALU.add)
        mV.inc(inst)
        assert mV.n == krope(m)

    def vector_all(v):
        v.wait(dT, tbv[0])
        mV.inc(v.tensor_copy(tbc, s["scr"][1][:]))
        mV.inc(v.tensor_copy(tbs, s["scr"][2][:]))
        v.wait(dM, wrv[0])
        mV.inc(v.tensor_copy(s["adsl"][:, 0:128], s["scr"][0][:, 0:128]))
        for i in range(6):
            emit_cast(v, i)
        emit_qk(v, 0)
        emit_cast(v, 6)
        emit_qk(v, 1)
        emit_cast(v, 7)
        emit_qk(v, 2)
        emit_qk(v, 3)
        for idx in range(16):
            m, j = divmod(idx, NTT)
            v.wait(mT, tr(idx))
            kt = 4 * c + j
            inst = v.tensor_copy(s["v0s"][:, (m * 16 + kt) * 128:(m * 16 + kt + 1) * 128],
                                 p["pT0" if idx % 2 == 0 else "pT1"][:])
            mV.inc(inst)
            assert mV.n == vevac(idx)
        assert mV.n == v_end

    def emit_rot(t, m):
        t.wait(mV, qtcopy(m))
        if m >= 1:
            t.wait(mV, krope(m - 1))
        inst = t.matmul(p["pC"][:], s["rot0"][:], s["qt"][:], start=True, stop=True)
        mT.inc(inst)
        assert mT.n == rot(m)

    def emit_tr(t, idx):
        m, j = divmod(idx, NTT)
        t.wait(mS, kevac(4 + m))
        if idx >= 2:
            t.wait(mV, vevac(idx - 2))
        inst = t.transpose(p["pT0" if idx % 2 == 0 else "pT1"][:],
                           s["scrb"][:, j * 128:(j + 1) * 128],
                           s["identb"][:])
        mT.inc(inst)
        assert mT.n == tr(idx)

    def tensor_kv(t):
        for i in range(8):
            m = i % 4
            t.wait(mV, cast(i))
            if i >= 2:
                t.wait(mS, kevac(i - 2))
            pj = p["pA"] if i % 2 == 0 else p["pB"]
            for dk in range(NDK):
                inst = t.matmul(pj[:], s["ws0"][:, (m * 16 + dk) * 128:(m * 16 + dk + 1) * 128],
                                s["xnT"][:, dk * TC:(dk + 1) * TC],
                                start=(dk == 0), stop=(dk == NDK - 1))
            mT.inc(inst)
            assert mT.n == kvproj(i)
            if 2 <= i <= 5:
                emit_rot(t, i - 2)
            if i == 6:
                for idx in range(4):
                    emit_tr(t, idx)
        for idx in range(4, 16):
            emit_tr(t, idx)

    def scalar_kv(sc):
        for i in range(8):
            m = i % 4
            sc.wait(mT, kvproj(i))
            pj = p["pA"] if i % 2 == 0 else p["pB"]
            if i < 4:
                tgt = s["scr"][1] if m % 2 == 0 else s["scr"][2]
                if i >= 2:
                    sc.wait(mV, krope(i - 2))
                else:
                    sc.wait(mV, tbcast)
                inst = sc.activation(tgt[:], pj[:], AF.Copy)
            else:
                if m >= 1:
                    sc.wait(mT, tr(m * 4 - 1))
                inst = sc.activation(s["scrb"][:], pj[:], AF.Copy)
            mS.inc(inst)
            assert mS.n == kevac(i)

    P.sec("sync", sync)
    P.sec("tensor", tensor_kv)
    P.sec("scalar", scalar_kv)
    P.sec("vector", vector_all)
    P.barrier()


def ph_router0(P: Prog, c):
    s, d, p = P.s, P.d, P.p
    dS = P.sems["dS"]
    mV, mS, mT = P.sems["mV"], P.sems["mS"], P.sems["mT"]
    fV, fS = P.sems["fV"], P.sems["fS"]
    v0, t0, s0, b0 = mV.n, mT.n, mS.n, dS.n

    def R(tt, off, w=8):
        return s["rt"][:, tt * 64 + off: tt * 64 + off + w]

    def tensor_r(t):
        for tt in range(NTT):
            for dk in range(NDK):
                inst = t.matmul(p["pD"][:, tt * 8:(tt + 1) * 8],
                                s["xnT"][:, dk * TC + tt * 128: dk * TC + (tt + 1) * 128],
                                s["adsl"][:, dk * 8:(dk + 1) * 8],
                                start=(dk == 0), stop=(dk == NDK - 1))
            mT.inc(inst)

    def vector_lg(v):
        v.wait(mT, t0 + 4)
        for tt in range(NTT):
            inst = v.tensor_copy(R(tt, 0), p["pD"][:, tt * 8:(tt + 1) * 8])
            mV.inc(inst)

    def scalar_r(sc):
        sc.wait(mV, v0 + 4)
        for tt in range(NTT):
            mS.inc(sc.activation(R(tt, 56), R(tt, 0), AF.Sigmoid,
                                 scale=s["invtau"][:, 0:1]))
        sc.wait(mS, mS.n)
        for tt in range(NTT):
            mS.inc(sc.activation(R(tt, 8), R(tt, 56), AF.Abs, bias=s["crows"][:, 2:3]))
        sc.wait(mS, mS.n)
        for tt in range(NTT):
            mS.inc(sc.activation(R(tt, 24), R(tt, 8), AF.Exp, scale=-10.0))
        for tt in range(NTT):
            mS.inc(sc.activation(R(tt, 32), R(tt, 8), AF.Exp, scale=-(1.0 / 0.3)))

    def vector_r(v):
        v.wait(mS, s0 + 16)
        fV = P.sems["fV"]
        def grp(insts_done):
            fV.inc(insts_done)
            v.wait(fV, fV.n)
        for tt in range(NTT):
            i1 = v.tensor_scalar(R(tt, 16), R(tt, 56), GOLD_L, None, ALU.is_ge)
        grp(i1)
        for tt in range(NTT):
            i1 = v.tensor_scalar(R(tt, 40), R(tt, 56), GOLD_U, None, ALU.is_le)
        grp(i1)
        for tt in range(NTT):
            i1 = v.tensor_tensor(R(tt, 16), R(tt, 16), R(tt, 40), ALU.mult)
        grp(i1)
        for tt in range(NTT):
            i1 = v.tensor_tensor(R(tt, 24), R(tt, 24), R(tt, 16), ALU.mult)
        grp(i1)
        for tt in range(NTT):
            i1 = v.reduce_max(s["rstat"][:, tt:tt + 1], R(tt, 32), axis=AX.X)
        grp(i1)
        inst = v.memset(s["dmy"][:, 5:6], 0.0)
        mV.inc(inst)

    def sync_b1(q):
        q.wait(mV, v0 + 5)
        dS.dinc(q.dma_start(out=s["rstatb"][:, 0:4], in_=s["rstat"][:, 0:4]))

    def vector_r1(v):
        fV = P.sems["fV"]
        def grp(insts_done):
            fV.inc(insts_done)
            v.wait(fV, fV.n)
        v.wait(dS, b0 + 16)
        for tt in range(NTT):
            i1 = v.tensor_scalar(R(tt, 40), R(tt, 32), s["rstatb"][:, tt:tt + 1], None, ALU.is_ge)
        grp(i1)
        for tt in range(NTT):
            i1 = v.tensor_scalar(R(tt, 56), R(tt, 40), -1.0e9, None, ALU.mult)
        grp(i1)
        for tt in range(NTT):
            i1 = v.tensor_tensor(R(tt, 56), R(tt, 56), R(tt, 32), ALU.add)
        grp(i1)
        for tt in range(NTT):
            i1 = v.reduce_max(s["rstat"][:, 4 + tt:5 + tt], R(tt, 56), axis=AX.X)
        grp(i1)
        for tt in range(NTT):
            i1 = v.reduce_sum(s["rstat"][:, 24 + tt:25 + tt], R(tt, 24), axis=AX.X)
        grp(i1)
        i2 = v.tensor_scalar(s["rstat"][:, 8:12], s["rstat"][:, 24:28], 1e-8, None,
                             ALU.max)
        grp(i2)
        inst = v.reciprocal(s["rstat"][:, 28:32], s["rstat"][:, 8:12])
        mV.inc(inst)

    def sync_b2(q):
        q.wait(mV, v0 + 6)
        dS.dinc(q.dma_start(out=s["rstatb"][:, 4:12], in_=s["rstat"][:, 4:12]))
        dS.dinc(q.dma_start(out=s["rstatb"][:, 24:32], in_=s["rstat"][:, 24:32]))

    def vector_r2(v):
        fV = P.sems["fV"]
        def grp(insts_done):
            fV.inc(insts_done)
            v.wait(fV, fV.n)
        v.wait(dS, b0 + 48)
        for tt in range(NTT):
            i1 = v.tensor_scalar(R(tt, 40), R(tt, 32), s["rstatb"][:, 4 + tt:5 + tt],
                                 None, ALU.is_ge)
        grp(i1)
        for tt in range(NTT):
            i1 = v.tensor_tensor(R(tt, 48), R(tt, 32), R(tt, 40), ALU.mult)
        grp(i1)
        for tt in range(NTT):
            i1 = v.reduce_sum(s["rstat"][:, 12 + tt:13 + tt], R(tt, 48), axis=AX.X)
        grp(i1)
        i1 = v.tensor_scalar(s["rstat"][:, 16:20], s["rstatb"][:, 24:28], 1e-8, None, ALU.is_lt)
        i2 = v.tensor_scalar(s["rstat"][:, 20:24], s["rstatb"][:, 24:28], 1e-8, None, ALU.is_ge)
        grp(i2)
        i2 = v.tensor_scalar(s["rstat"][:, 12:16], s["rstat"][:, 12:16], 1e-8, None,
                             ALU.max)
        grp(i2)
        inst = v.reciprocal(s["rstat"][:, 32:36], s["rstat"][:, 12:16])
        mV.inc(inst)

    def sync_b3(q):
        q.wait(mV, v0 + 7)
        dS.dinc(q.dma_start(out=s["rstatb"][:, 12:24], in_=s["rstat"][:, 12:24]))
        dS.dinc(q.dma_start(out=s["rstatb"][:, 32:36], in_=s["rstat"][:, 32:36]))

    def vector_r3(v):
        fV = P.sems["fV"]
        v.wait(dS, b0 + 80)
        for tt in range(NTT):
            i1 = v.tensor_scalar(R(tt, 48), R(tt, 48), s["rstatb"][:, 32 + tt:33 + tt],
                                 s["rstatb"][:, 16 + tt:17 + tt], ALU.mult, ALU.mult)
        fV.inc(i1)
        for tt in range(NTT):
            i1 = v.tensor_scalar(R(tt, 24), R(tt, 24), s["rstatb"][:, 28 + tt:29 + tt],
                                 s["rstatb"][:, 20 + tt:21 + tt], ALU.mult, ALU.mult)
        fV.inc(i1)
        v.wait(fV, fV.n)
        for tt in range(NTT):
            inst = v.tensor_tensor(s["w8"][:, tt * 8:(tt + 1) * 8],
                                   R(tt, 48), R(tt, 24), ALU.add)
            mV.inc(inst)

    def tensor_w8(t):
        for tt in range(NTT):
            t.wait(mV, v0 + 8 + tt)
            if tt >= 1:
                t.wait(mV, v0 + 11 + tt)
            inst = t.transpose(p["pTf"][0:8, :], s["w8"][:, tt * 8:(tt + 1) * 8],
                               s["identf"][:])
            mT.inc(inst)

    def vector_w8(v):
        for tt in range(NTT):
            v.wait(mT, t0 + 4 + tt + 1)
            inst = v.tensor_copy(s["w8f"][0:8, tt * 128:(tt + 1) * 128], p["pTf"][0:8, :])
            mV.inc(inst)

    P.sec("tensor", tensor_r)
    P.sec("vector", vector_lg)
    P.sec("scalar", scalar_r)
    P.sec("vector", vector_r)
    P.sec("sync", sync_b1)
    P.sec("vector", vector_r1)
    P.sec("sync", sync_b2)
    P.sec("vector", vector_r2)
    P.sec("sync", sync_b3)
    P.sec("vector", vector_r3)
    P.sec("tensor", tensor_w8)
    P.sec("vector", vector_w8)
    P.barrier()


def ph_xga0(P: Prog, c):
    s, d, p = P.s, P.d, P.p
    mV, mS, mT = P.sems["mV"], P.sems["mS"], P.sems["mT"]
    v0, t0, s0 = mV.n, mT.n, mS.n
    wl = [s["wl0"], s["wl1"]]
    dL = [P.dL[0], P.dL[4]]
    adma = {}

    def sync(q):
        for i, (e, nm) in enumerate([(0, "ga0"), (0, "ua0"), (1, "ga0"), (1, "ua0")]):
            if i >= 2:
                q.wait(mV, v0 + (i - 1))
            adma[i] = dL[i % 2].dinc(q.dma_start(
                out=wl[i % 2][:, 0:1024].rearrange("p (k r) -> p k r", r=RADJ),
                in_=d[nm][e].rearrange("(k p) r -> p k r", p=128)))

    def vector(v):
        for i in range(4):
            v.wait(dL[i % 2], adma[i])
            if i >= 2:
                v.wait(mT, t0 + (i - 1))
            mV.inc(v.tensor_copy(s["adsl"][:, (i % 2) * 1024:(i % 2 + 1) * 1024],
                                 wl[i % 2][:, 0:1024]))

    def tensor(t):
        for i in range(4):
            t.wait(mV, v0 + i + 1)
            if i >= 2:
                t.wait(mS, s0 + (i - 1))
            pj = p["pA"] if i % 2 == 0 else p["pB"]
            for dk in range(NDK):
                inst = t.matmul(pj[0:64, :],
                                s["adsl"][:, (i % 2) * 1024 + dk * 64:(i % 2) * 1024 + (dk + 1) * 64],
                                s["xnT"][:, dk * TC:(dk + 1) * TC],
                                start=(dk == 0), stop=(dk == NDK - 1))
            mT.inc(inst)

    def scalar(sc):
        for i in range(4):
            sc.wait(mT, t0 + i + 1)
            pj = p["pA"] if i % 2 == 0 else p["pB"]
            e, which = divmod(i, 2)
            inst = sc.activation(s["xga"][0:64, (e * 2 + which) * TC:(e * 2 + which + 1) * TC],
                                 pj[0:64, :], AF.Copy)
            mS.inc(inst)

    P.sec("sync", sync)
    P.sec("vector", vector)
    P.sec("tensor", tensor)
    P.sec("scalar", scalar)
    P.barrier()


def ph_ffn0(P: Prog, c):
    s, d, p = P.s, P.d, P.p
    mV, mS, mT, mG = P.sems["mV"], P.sems["mS"], P.sems["mT"], P.sems["mG"]
    slots = [(s["wl0"], 0), (s["wl0"], 512), (s["wl0"], 1024), (s["wl0"], 1536),
             (s["wl1"], 0), (s["wl1"], 512), (s["wl1"], 1024), (s["wl1"], 1536)]
    nslot = len(slots)

    sync_ops, vec_ops, ten_ops, sca_ops, gps_ops = [], [], [], [], []

    class Ev:
        pass
    ev = Ev()
    ev.v, ev.t, ev.s, ev.g = mV.n, mT.n, mS.n, mG.n

    def S_dma(waits, out_ap, in_ap, slot):
        sem = P.dL[slot]
        sem.n += 16
        myval = sem.n
        def fn(q, out_ap=out_ap, in_ap=in_ap, sem=sem):
            inst = q.dma_start(out=out_ap, in_=in_ap)
            inst.then_inc(sem.h, 16)
            _UPDATED[id(inst)] = inst
        sync_ops.append((list(waits), fn))
        return (slot, myval)

    def V_op(waits, fn):
        ev.v += 1
        def wrap(v, fn=fn):
            mV.inc(fn(v))
        vec_ops.append((list(waits), wrap))
        return ev.v

    def T_group(waits, fn):
        ev.t += 1
        def wrap(t, fn=fn):
            mT.inc(fn(t))
        ten_ops.append((list(waits), wrap))
        return ev.t

    def C_op(waits, fn):
        ev.s += 1
        def wrap(sc, fn=fn):
            mS.inc(fn(sc))
        sca_ops.append((list(waits), wrap))
        return ev.s

    def G_op(waits, fn):
        ev.g += 1
        def wrap(g, fn=fn):
            mG.inc(fn(g))
        gps_ops.append((list(waits), wrap))
        return ev.g

    state = Ev()
    state.dmai = 0
    slot_last = [0] * nslot

    def stream_panel(src_ap, slab_ap, extra_cast_waits):
        width = src_ap.shape[-1]
        if width <= 512:
            sidx = state.dmai % nslot
            wlt, off = slots[sidx]
            w = [("v", slot_last[sidx])] if slot_last[sidx] else []
            dn = S_dma(w, wlt[:, off:off + width], src_ap, sidx)
            cn = V_op([("dl", dn)] + extra_cast_waits,
                      lambda v, wlt=wlt, off=off, width=width, slab_ap=slab_ap:
                      v.tensor_copy(slab_ap, wlt[:, off:off + width]))
            slot_last[sidx] = cn
            state.dmai += 1
            return cn
        assert width == 2048
        half = state.dmai % 2
        base = half * 4
        wlt = s["wl0"] if half == 0 else s["wl1"]
        w = [("v", m) for m in {slot_last[base + k] for k in range(4)} if m]
        dn = S_dma(w, wlt[:], src_ap, base)
        cn = V_op([("dl", dn)] + extra_cast_waits,
                  lambda v, wlt=wlt, slab_ap=slab_ap: v.tensor_copy(slab_ap, wlt[:]))
        for k in range(4):
            slot_last[base + k] = cn
        state.dmai += 1
        return cn

    def clobber_all_slots(val):
        for k in range(nslot):
            slot_last[k] = max(slot_last[k], val)

    last_gate_mm = 0
    last_up_mm = 0
    last_down_mm = 0
    bank_rd = {0: [], 1: []}
    wbc_rd = 0
    prev_wbc_mm = 0
    prev_silu = 0
    prev_mul = 0
    prev_pool = 0
    prev_mda_evac = 0

    for e in range(NE):
        prev_down_last = last_down_mm
        dMsem = P.sems["dM"]
        dMsem.n += 16
        dmval = dMsem.n
        gw = [("t", prev_wbc_mm)] if prev_wbc_mm else []
        def g_dma(q, e=e, dMsem=dMsem):
            inst = q.dma_start(out=P.row(2), in_=s["w8f"][e:e + 1, :])
            inst.then_inc(dMsem.h, 16)
            _UPDATED[id(inst)] = inst
        sync_ops.append((gw, g_dma))
        ww = list(bank_rd[1]) + [("dm", dmval)]
        if wbc_rd:
            ww.append(("g", wbc_rd))
        tw = T_group(ww, lambda t: t.matmul(
            p["pE"][0:1, :] if False else p["pE"][:], s["ones_row"][:],
            P.row(2), start=True, stop=True))
        prev_wbc_mm = tw
        wbc_done = C_op([("t", tw)], lambda sc: sc.activation(
            s["wbc"][:], p["pE"][:], AF.Copy))
        bank_rd[1] = [("s", wbc_done)]

        gbcast = ubcast = None
        if e < NL:
            dn = S_dma([("v", ev.v)], s["wl0"][0:64, :], d["gb0"][e], 0)
            gbcast = V_op([("dl", dn), ("t", last_gate_mm)],
                          lambda v: v.tensor_copy(s["adsl"][0:64, 0:2048],
                                                  s["wl0"][0:64, :]))
            dn2 = S_dma([("v", gbcast)], s["wl1"][0:64, :], d["ub0"][e], 4)
            ubcast = V_op([("dl", dn2), ("t", last_up_mm)],
                          lambda v: v.tensor_copy(s["adsl"][0:64, 2048:4096],
                                                  s["wl1"][0:64, :]))
            clobber_all_slots(ubcast)

        for fq in range(4):
            gcasts, ucasts = [], []
            for dk in range(NDK):
                src = d["wg0"][e, dk * 128:(dk + 1) * 128, fq * 512:(fq + 1) * 512]
                cwl = [("t", last_gate_mm), ("t", last_down_mm)]
                gcasts.append(stream_panel(src, s["ws0"][:, dk * 512:(dk + 1) * 512], cwl))
            for dk in range(NDK):
                src = d["wu0"][e, dk * 128:(dk + 1) * 128, fq * 512:(fq + 1) * 512]
                cwl = [("t", last_up_mm), ("t", last_down_mm)]
                ucasts.append(stream_panel(src, s["ws1"][:, dk * 512:(dk + 1) * 512], cwl))
            for ftl in range(4):
                ft = fq * 4 + ftl

                def g_mm(t, e=e, ft=ft, ftl=ftl):
                    ad = e < NL
                    for dk in range(NDK):
                        inst = t.matmul(
                            p["pA"][:],
                            s["ws0"][:, dk * 512 + ftl * 128: dk * 512 + ftl * 128 + 128],
                            s["xnT"][:, dk * TC:(dk + 1) * TC],
                            start=(dk == 0), stop=(dk == NDK - 1) and not ad)
                    if ad:
                        inst = t.matmul(p["pA"][:],
                                        s["adsl"][0:64, ft * 128:(ft + 1) * 128],
                                        s["xga"][0:64, (e * 2) * TC:(e * 2 + 1) * TC],
                                        start=False, stop=True)
                    return inst

                gwl = [("v", max(gcasts)), ("s", prev_silu)]
                if e < NL:
                    gwl.append(("v", gbcast))
                tg = T_group(gwl, g_mm)
                last_gate_mm = tg

                def u_mm(t, e=e, ft=ft, ftl=ftl):
                    ad = e < NL
                    for dk in range(NDK):
                        inst = t.matmul(
                            p["pB"][:],
                            s["ws1"][:, dk * 512 + ftl * 128: dk * 512 + ftl * 128 + 128],
                            s["xnT"][:, dk * TC:(dk + 1) * TC],
                            start=(dk == 0), stop=(dk == NDK - 1) and not ad)
                    if ad:
                        inst = t.matmul(p["pB"][:],
                                        s["adsl"][0:64, 2048 + ft * 128:2048 + (ft + 1) * 128],
                                        s["xga"][0:64, (e * 2 + 1) * TC:(e * 2 + 2) * TC],
                                        start=False, stop=True)
                    return inst

                uwl = [("v", max(ucasts)), ("v", prev_mul)]
                if e < NL:
                    uwl.append(("v", ubcast))
                tu = T_group(uwl, u_mm)
                last_up_mm = tu

                cs = C_op([("t", tg), ("v", prev_mul)],
                          lambda sc: sc.activation(s["scr"][0][:], p["pA"][:], AF.Silu))
                prev_silu = cs
                mv_ = V_op([("s", cs), ("t", tu), ("g", prev_pool)],
                           lambda v: v.tensor_tensor(s["scr"][1][:], s["scr"][0][:],
                                                     p["pB"][:], ALU.mult))
                prev_mul = mv_
                pg = G_op([("v", mv_), ("s", wbc_done), ("t", prev_down_last)],
                          lambda g, ft=ft: g.tensor_tensor(
                              s["mid"][:, ft * TC:(ft + 1) * TC], s["scr"][1][:],
                              s["wbc"][:], ALU.mult))
                prev_pool = pg
        wbc_rd = prev_pool

        mda_done = None
        if e < NL:
            dn = S_dma([("v", ev.v)], s["wl0"][:, 0:1024].rearrange(
                "p (k r) -> p k r", r=RADJ),
                d["da0"][e].rearrange("(k p) r -> p k r", p=128), 0)
            dacast = V_op([("dl", dn), ("t", last_gate_mm)],
                          lambda v: v.tensor_copy(s["adsl"][:, 0:1024],
                                                  s["wl0"][:, 0:1024]))
            dn2 = S_dma([("v", dacast)], s["wl1"][0:64, :], d["db0"][e], 4)
            dbcast = V_op([("dl", dn2), ("t", last_up_mm)],
                          lambda v: v.tensor_copy(s["adsl"][0:64, 2048:4096],
                                                  s["wl1"][0:64, :]))
            clobber_all_slots(dbcast)

            def mda_mm(t):
                for fk in range(NDK):
                    inst = t.matmul(p["pD"][0:64, :],
                                    s["adsl"][:, fk * 64:(fk + 1) * 64],
                                    s["mid"][:, fk * TC:(fk + 1) * TC],
                                    start=(fk == 0), stop=(fk == NDK - 1))
                return inst
            mdw = [("v", dacast), ("g", prev_pool)]
            if prev_mda_evac:
                mdw.append(("s", prev_mda_evac))
            tmda = T_group(mdw, mda_mm)
            mda_done = C_op([("t", tmda)], lambda sc: sc.activation(
                s["mda"][0:64, :], p["pD"][0:64, :], AF.Copy))
            prev_mda_evac = mda_done

        for half in range(2):
            dcasts = []
            for fkl in range(8):
                fk = half * 8 + fkl
                src = d["wd0"][e, fk * 128:(fk + 1) * 128, :]
                tgt = s["ws0"] if fkl < 4 else s["ws1"]
                cwl = [("t", last_down_mm), ("t", last_gate_mm), ("t", last_up_mm)]
                dcasts.append(stream_panel(
                    src, tgt[:, (fkl % 4) * 2048:(fkl % 4 + 1) * 2048], cwl))
            for dt in range(NDK):
                bank = dt % 2
                pb = p["pC"] if bank == 0 else p["pE"]

                def d_mm(t, e=e, half=half, dt=dt, pb=pb):
                    ad = (half == 0 and e < NL)
                    for fkl in range(8):
                        fk = half * 8 + fkl
                        slab = s["ws0"] if fkl < 4 else s["ws1"]
                        inst = t.matmul(
                            pb[:],
                            slab[:, (fkl % 4) * 2048 + dt * 128:(fkl % 4) * 2048 + dt * 128 + 128],
                            s["mid"][:, fk * TC:(fk + 1) * TC],
                            start=(fkl == 0), stop=(fkl == 7) and not ad)
                    if ad:
                        inst = t.matmul(pb[:],
                                        s["adsl"][0:64, 2048 + dt * 128:2048 + (dt + 1) * 128],
                                        s["mda"][0:64, :],
                                        start=False, stop=True)
                    return inst

                dwl = [("v", max(dcasts)), ("g", prev_pool)] + list(bank_rd[bank])
                if half == 0 and e < NL:
                    dwl.append(("s", mda_done))
                td = T_group(dwl, d_mm)
                last_down_mm = td

                def acc_fn(v, dt=dt, pb=pb, do_copy=(e == 0 and half == 0)):
                    if do_copy:
                        return v.tensor_copy(s["accx"][:, dt * TC:(dt + 1) * TC], pb[:])
                    return v.tensor_tensor(s["accx"][:, dt * TC:(dt + 1) * TC],
                                           s["accx"][:, dt * TC:(dt + 1) * TC],
                                           pb[:], ALU.add)
                av = V_op([("t", td)], acc_fn)
                bank_rd[bank] = [("v", av)]

    def run_ops(ops):
        def fn(eng):
            for waits, op in ops:
                seen = set()
                for kind, val in waits:
                    if val == 0 or val == (0, 0) or (kind, val) in seen:
                        continue
                    seen.add((kind, val))
                    if kind == "dl":
                        slot, v16 = val
                        eng.wait(P.dL[slot], v16)
                    else:
                        sem = {"v": mV, "t": mT, "s": mS, "g": mG,
                               "dm": P.sems["dM"]}[kind]
                        eng.wait(sem, val)
                op(eng)
        return fn

    P.sec("sync", run_ops(sync_ops))
    P.sec("tensor", run_ops(ten_ops))
    P.sec("scalar", run_ops(sca_ops))
    P.sec("vector", run_ops(vec_ops))
    P.sec("gpsimd", run_ops(gps_ops))
    assert mV.n == ev.v and mT.n == ev.t and mS.n == ev.s and mG.n == ev.g
    P.barrier()


class Sched:
    def __init__(self, P: Prog):
        self.P = P
        self.ops = {e: [] for e in ENGINES}
        self.sems = P.sems
        self.slots = [(P.s["wl0"], 0), (P.s["wl0"], 512), (P.s["wl0"], 1024),
                      (P.s["wl0"], 1536), (P.s["wl1"], 0), (P.s["wl1"], 512),
                      (P.s["wl1"], 1024), (P.s["wl1"], 1536)]
        self.slot_last = [0] * 8
        self.dmai = 0
        self.v = P.sems["mV"].n
        self.t = P.sems["mT"].n
        self.c = P.sems["mS"].n
        self.g = P.sems["mG"].n
        self.f = P.sems["fV"].n

    def _add(self, eng, waits, emit):
        self.ops[eng].append((list(waits), emit))

    def dma(self, waits, out_ap, in_ap, slot):
        sem = self.P.dL[slot]
        sem.n += 16
        val = sem.n
        def emit(q, out_ap=out_ap, in_ap=in_ap, sem=sem):
            inst = q.dma_start(out=out_ap, in_=in_ap)
            inst.then_inc(sem.h, 16)
            _UPDATED[id(inst)] = inst
        self._add("sync", waits, emit)
        return ("dl", (slot, val))

    def dma_plain(self, waits, out_ap, in_ap, semname="dM"):
        sem = self.sems[semname]
        sem.n += 16
        val = sem.n
        def emit(q, out_ap=out_ap, in_ap=in_ap, sem=sem):
            inst = q.dma_start(out=out_ap, in_=in_ap)
            inst.then_inc(sem.h, 16)
            _UPDATED[id(inst)] = inst
        self._add("sync", waits, emit)
        return (semname, val)

    def V(self, waits, fn, flush=False):
        self.v += 1
        def emit(v, fn=fn):
            self.sems["mV"].inc(fn(v))
        self._add("vector", waits, emit)
        tok = ("v", self.v)
        if flush:
            self._add("vector", [tok], lambda v: None)
        return tok

    def Vf(self, waits, fn):
        self.f += 1
        def emit(v, fn=fn):
            self.sems["fV"].inc(fn(v))
        self._add("vector", waits, emit)
        tok = ("f", self.f)
        self._add("vector", [tok], lambda v: None)
        return tok

    def T(self, waits, fn):
        self.t += 1
        def emit(t, fn=fn):
            self.sems["mT"].inc(fn(t))
        self._add("tensor", waits, emit)
        return ("t", self.t)

    def C(self, waits, fn):
        self.c += 1
        def emit(sc, fn=fn):
            self.sems["mS"].inc(fn(sc))
        self._add("scalar", waits, emit)
        return ("s", self.c)

    def G(self, waits, fn):
        self.g += 1
        def emit(g, fn=fn):
            self.sems["mG"].inc(fn(g))
        self._add("gpsimd", waits, emit)
        return ("g", self.g)

    def stream(self, waits, src_ap, slab_ap, extra_cast_waits=()):
        shape = list(src_ap.shape)
        width = 1
        for x in shape[1:]:
            width *= x
        inner = shape[-1]
        def land_view(wlt, off):
            flat = wlt[:, off:off + width]
            if len(shape) == 3:
                return flat.rearrange("p (k m) -> p k m", m=inner)
            return flat
        if width <= 512:
            sidx = self.dmai % 8
            wlt, off = self.slots[sidx]
            w = list(waits) + ([("v", self.slot_last[sidx])] if self.slot_last[sidx] else [])
            dn = self.dma(w, land_view(wlt, off), src_ap, sidx)
            cn = self.V([dn] + list(extra_cast_waits),
                        lambda v, wlt=wlt, off=off, width=width, slab_ap=slab_ap:
                        v.tensor_copy(slab_ap, wlt[:, off:off + width]))
            self.slot_last[sidx] = cn[1]
            self.dmai += 1
            return cn
        assert width in (1024, 2048)
        half = self.dmai % 2
        base = half * 4
        wlt = self.P.s["wl0"] if half == 0 else self.P.s["wl1"]
        w = list(waits) + [("v", m) for m in
                           sorted({self.slot_last[base + k] for k in range(4)}) if m]
        dn = self.dma(w, land_view(wlt, 0), src_ap, base)
        cn = self.V([dn] + list(extra_cast_waits),
                    lambda v, wlt=wlt, width=width, slab_ap=slab_ap:
                    v.tensor_copy(slab_ap, wlt[:, 0:width]))
        for k in range(4):
            self.slot_last[base + k] = cn[1]
        self.dmai += 1
        return cn

    def finish(self, barrier=True):
        P = self.P
        semmap = {"v": P.sems["mV"], "t": P.sems["mT"], "s": P.sems["mS"],
                  "g": P.sems["mG"], "f": P.sems["fV"], "dM": P.sems["dM"],
                  "dT": P.sems["dT"], "dS": P.sems["dS"], "dO": P.sems["dO"],
                  "dW": P.sems["dW"], "dX": P.sems["dX"]}
        for eng in ENGINES:
            ops = self.ops[eng]
            if not ops:
                continue
            def fn(e, ops=ops):
                for waits, emit in ops:
                    seen = set()
                    for kind, val in waits:
                        if (kind, val) in seen:
                            continue
                        seen.add((kind, val))
                        if kind == "dl":
                            slot, v16 = val
                            e.wait(P.dL[slot], v16)
                        else:
                            if val == 0:
                                continue
                            e.wait(semmap[kind], val)
                    emit(e)
            P.sec(eng, fn)
        assert P.sems["mV"].n == self.v and P.sems["mT"].n == self.t
        assert P.sems["mS"].n == self.c and P.sems["mG"].n == self.g
        assert P.sems["fV"].n == self.f
        if barrier:
            P.barrier()


def _seg_kinds(segs):
    return segs


def ph_attn0(P: Prog, c):
    s, d, p = P.s, P.d, P.p
    S = Sched(P)
    band = _band0(c)
    nkt = len(band)
    tbc = s["tb"][:, 0:TC]
    tbs = s["tb"][:, TC:2 * TC]

    prev_qevac_tok = prev_qrope_tok = prev_afm_tok = prev_rot_tok = None
    prev_tb_tok = None
    qproj_tok = {}
    for h in range(L0_NH):
        kh = h // 4
        cast_w = []
        if h >= 4:
            cast_w.append(qproj_tok[h - 4])
        cn = S.stream([], d["wq0"][:, h * 128:(h + 1) * 128].rearrange(
            "(k p) m -> p k m", p=128),
            s["ws1"][:, (h % 4) * 2048:(h % 4 + 1) * 2048], cast_w)
        qw = [cn]
        if prev_qevac_tok:
            qw.append(prev_qevac_tok)
        def qmm(t, h=h):
            for dk in range(NDK):
                inst = t.matmul(p["pA"][:],
                                s["ws1"][:, (h % 4) * 2048 + dk * 128:(h % 4) * 2048 + (dk + 1) * 128],
                                s["xnT"][:, dk * TC:(dk + 1) * TC],
                                start=(dk == 0), stop=(dk == NDK - 1))
            return inst
        tq = S.T(qw, qmm)
        qproj_tok[h] = tq
        ew = [tq]
        if prev_qrope_tok:
            ew.append(prev_qrope_tok)
        ce = S.C(ew, lambda sc: sc.activation(s["scr"][1][:], p["pA"][:], AF.Copy))
        prev_qevac_tok = ce
        bw = [ce]
        if prev_rot_tok:
            bw.append(prev_rot_tok)
        vb = S.V(bw, lambda v: v.tensor_copy(s["qt"][:], s["scr"][1][:]))
        rw = [vb]
        if prev_qrope_tok:
            rw.append(prev_qrope_tok)
        tr_ = S.T(rw, lambda t: t.matmul(p["pC"][:], s["rot0"][:], s["qt"][:],
                                         start=True, stop=True))
        prev_rot_tok = tr_
        f1 = S.Vf([tr_], lambda v: v.tensor_tensor(s["wbc"][:], s["scr"][1][:], tbc, ALU.mult))
        f2 = S.Vf([f1], lambda v: v.tensor_tensor(s["scr"][1][:], p["pC"][:], tbs, ALU.mult))
        vq = S.V([f2], lambda v: v.tensor_tensor(s["qt"][:], s["wbc"][:], s["scr"][1][:],
                                                 ALU.add))
        prev_qrope_tok = vq
        prev_tanh = {}
        prev_exp = {}
        prev_pv = {}
        for i, (kt, segs) in enumerate(band):
            psc = p["pD"] if i % 2 == 0 else p["pA"]
            sw = [vq]
            if i % 2 == 0 and i >= 2:
                sw.append(prev_tanh[i - 2])
            if i % 2 == 1:
                sw.append(ce)
                if i >= 3:
                    sw.append(prev_tanh[i - 2])
            tsc = S.T(sw, lambda t, kt=kt, psc=psc, kh=kh: t.matmul(
                psc[:], s["k0s"][:, kh * T + kt * 128: kh * T + (kt + 1) * 128],
                s["qt"][:], start=True, stop=True))
            tw = [tsc]
            if i >= 1:
                tw.append(prev_exp[i - 1])
            cth = S.C(tw, lambda sc, psc=psc: sc.activation(
                s["scr"][0][:], psc[:], AF.Tanh, scale=1.0 / (L0_CAP * math.sqrt(L0_HD))))
            prev_tanh[i] = cth
            mtoks = []
            for tl, kind in enumerate(segs):
                if kind in ("diag", "edge"):
                    mask = s["mdiag"] if kind == "diag" else s["medge"]
                    mtoks.append(S.V([cth], lambda v, tl=tl, mask=mask: v.tensor_tensor(
                        s["scr"][0][:, tl * 128:(tl + 1) * 128],
                        s["scr"][0][:, tl * 128:(tl + 1) * 128], mask[:], ALU.add)))
            xw = [cth] + mtoks
            if i >= 2:
                xw.append(prev_pv[i - 2])
            cxp = S.C(xw, lambda sc, i=i: sc.activation(
                s["pst"][i % 2][:], s["scr"][0][:], AF.Exp, scale=L0_CAP))
            prev_exp[i] = cxp
            ztoks = []
            for tl, kind in enumerate(segs):
                if kind == "zero":
                    ztoks.append(S.V([cxp], lambda v, tl=tl, i=i: v.memset(
                        s["pst"][i % 2][:, tl * 128:(tl + 1) * 128], 0.0)))
            pw = [cxp] + ztoks
            if i == 0 and prev_afm_tok:
                pw.append(prev_afm_tok)
            t1 = S.T(pw, lambda t, kt=kt, i=i, kh=kh: t.matmul(
                p["pB"][:], s["v0s"][:, (kh * 16 + kt) * 128:(kh * 16 + kt + 1) * 128],
                s["pst"][i % 2][:], start=(i == 0), stop=(i == nkt - 1)))
            t2 = S.T([], lambda t, i=i: t.matmul(
                p["pE"][0:1, :], s["ones_col"][:], s["pst"][i % 2][:],
                start=(i == 0), stop=(i == nkt - 1)))
            prev_pv[i] = t2
        vrw = [prev_pv[nkt - 1]]
        if prev_tb_tok:
            vrw.append(prev_tb_tok)
        vr = S.V(vrw, lambda v: v.reciprocal(P.row(2), p["pE"][0:1, :]))
        bw2 = [vr, prev_tanh[nkt - 1] if (nkt - 1) % 2 == 0 else prev_tanh[nkt - 2]]
        tb_ = S.T(bw2, lambda t: t.matmul(p["pD"][:], s["ones_row"][:],
                                          P.row(2), start=True, stop=True))
        prev_tb_tok = tb_
        bev_w = [tb_]
        if prev_afm_tok:
            bev_w.append(prev_afm_tok)
        cbe = S.C(bev_w, lambda sc: sc.activation(s["scr"][2][:], p["pD"][:], AF.Copy))
        va = S.V([cbe, prev_pv[nkt - 1], t1], lambda v, h=h: v.tensor_tensor(
            s["afm"][:, h * TC:(h + 1) * TC], p["pB"][:], s["scr"][2][:], ALU.mult))
        prev_afm_tok = va
    S.finish()


def ph_comb0(P: Prog, c):
    s, d, p = P.s, P.d, P.p
    S = Sched(P)

    t_s = S.T([], lambda t: t.matmul(p["pD"][:], s["ones_row"][:], P.row(0),
                                     start=True, stop=True))
    t_m = S.T([], lambda t: t.matmul(p["pE"][:], s["ones_row"][:], P.row(1),
                                     start=True, stop=True))
    c_s = S.C([t_s], lambda sc: sc.activation(s["scr"][1][:], p["pD"][:], AF.Copy))
    c_m = S.C([t_m], lambda sc: sc.activation(s["scr"][2][:], p["pE"][:], AF.Copy))

    prev_comb = {}
    prev_silu = {}
    prev_t1 = {}
    prev_dense = {}
    casts = {}
    sq_mm = {}
    sq_op = {}
    for dt in range(NDK):
        cw = []
        if dt >= 4:
            cw.append(prev_dense[dt - 4])
        casts[dt] = S.stream([], d["wo0"][:, dt * 128:(dt + 1) * 128].rearrange(
            "(k p) m -> p k m", p=128), s["ws0"][:, (dt % 4) * 2048:(dt % 4 + 1) * 2048],
            cw)
        pb = p["pA"] if dt % 2 == 0 else p["pB"]
        dwl = [casts[dt]]
        if dt >= 2:
            dwl.append(prev_t1[dt - 2])

        def dmm(t, dt=dt, pb=pb):
            for ht in range(L0_NH):
                inst = t.matmul(pb[:],
                                s["ws0"][:, (dt % 4) * 2048 + ht * 128:(dt % 4) * 2048 + ht * 128 + 128],
                                s["afm"][:, ht * TC:(ht + 1) * TC],
                                start=(ht == 0), stop=(ht == L0_NH - 1))
            return inst
        td = S.T(dwl, dmm)
        prev_dense[dt] = td
        sw = [c_s, c_m]
        if dt >= 1:
            sw.append(prev_t1[dt - 1])
        csl = S.C(sw, lambda sc, dt=dt: sc.activation(
            s["scr"][0][:], s["accx"][:, dt * TC:(dt + 1) * TC], AF.Silu))
        prev_silu[dt] = csl
        t1w = [td, csl]
        t1_ = S.Vf(t1w, lambda v, pb=pb: v.tensor_tensor(
            s["wbc"][:], pb[:], s["scr"][0][:], ALU.mult))
        prev_t1[dt] = t1_
        a1 = S.Vf([t1_, csl], lambda v, dt=dt: v.tensor_tensor(
            s["accx"][:, dt * TC:(dt + 1) * TC],
            s["xnT"][:, dt * TC:(dt + 1) * TC], s["scr"][1][:], ALU.mult))
        a2 = S.Vf([a1], lambda v, dt=dt: v.tensor_tensor(
            s["accx"][:, dt * TC:(dt + 1) * TC],
            s["accx"][:, dt * TC:(dt + 1) * TC], s["scr"][2][:], ALU.add))
        a3 = S.V([a2], lambda v, dt=dt: v.tensor_tensor(
            s["accx"][:, dt * TC:(dt + 1) * TC],
            s["accx"][:, dt * TC:(dt + 1) * TC], s["wbc"][:], ALU.add))
        prev_comb[dt] = a3

    for dt in range(NDK):
        sqw = [prev_comb[dt]]
        if dt >= 1:
            sqw.append(sq_mm[dt - 1])
        else:
            sqw.append(prev_t1[NDK - 1])
        csq = S.C(sqw, lambda sc, dt=dt: sc.activation(
            s["scr"][0][:], s["accx"][:, dt * TC:(dt + 1) * TC], AF.Square))
        sq_op[dt] = csq
        m1w = [prev_comb[dt]]
        if dt == 0:
            m1w += [prev_t1[NDK - 1], prev_t1[NDK - 2]]
        t1_ = S.T(m1w, lambda t, dt=dt: t.matmul(
            p["pC"][0:1, :], s["crows"][:, 4:5], s["accx"][:, dt * TC:(dt + 1) * TC],
            start=(dt == 0), stop=(dt == NDK - 1)))
        t2_ = S.T([csq], lambda t, dt=dt: t.matmul(
            p["pE"][0:1, :], s["crows"][:, 4:5], s["scr"][0][:],
            start=(dt == 0), stop=(dt == NDK - 1)))
        sq_mm[dt] = t2_
    sum_mm = t1_
    ssq_mm = t2_

    M0, M1, M2 = P.row(0), P.row(1), P.row(2)
    f1 = S.Vf([sum_mm], lambda v: v.tensor_scalar(M0, p["pC"][0:1, :], 1.0 / D, None, ALU.mult))
    f2 = S.Vf([ssq_mm, f1], lambda v: v.tensor_scalar(M1, p["pE"][0:1, :], 1.0 / D, None, ALU.mult))
    f3 = S.Vf([f2], lambda v: v.tensor_tensor(M2, M0, M0, ALU.mult))
    f4 = S.V([f3], lambda v: v.tensor_tensor(M1, M1, M2, ALU.subtract))
    c_sq = S.C([f4], lambda sc: sc.activation(M2, M1, AF.Sqrt, bias=s["crows"][0:1, 1:2]))
    f5 = S.Vf([c_sq], lambda v: v.reciprocal(M1, M2))
    f6 = S.Vf([f5], lambda v: v.tensor_tensor(M2, M0, M1, ALU.mult))
    f7 = S.V([f6], lambda v: v.tensor_scalar(M2, M2, -1.0, None, ALU.mult))
    fp1 = S.Vf([f7], lambda v: v.tensor_copy(P.row(3), M0))
    fp2 = S.Vf([fp1], lambda v: v.tensor_copy(P.row(4), M1))

    tb1 = S.T([f5, f7, fp2] + [prev_t1[NDK - 1]], lambda t: t.matmul(
        p["pD"][:], s["ones_row"][:], M1, start=True, stop=True))
    tb2 = S.T([f7], lambda t: t.matmul(
        p["pE"][:], s["ones_row"][:], M2, start=True, stop=True))
    cb1w = [tb1, prev_silu[NDK - 1]]
    cb1 = S.C(cb1w, lambda sc: sc.activation(s["scr"][1][:], p["pD"][:], AF.Copy))
    cb2 = S.C([tb2], lambda sc: sc.activation(s["scr"][2][:], p["pE"][:], AF.Copy))

    prev_add = None
    for dt in range(NDK):
        uw = [cb1, cb2, sq_mm[NDK - 1], sum_mm]
        if prev_add:
            uw.append(prev_add)
        u1 = S.Vf(uw, lambda v, dt=dt: v.tensor_tensor(
            s["wbc"][:], s["accx"][:, dt * TC:(dt + 1) * TC],
            s["scr"][1][:], ALU.mult))
        prev_add = S.V([u1], lambda v, dt=dt: v.tensor_tensor(
            s["xnT"][:, dt * TC:(dt + 1) * TC], s["wbc"][:], s["scr"][2][:], ALU.add))
    S.finish()


def ph_prep1(P: Prog, c):
    s, d, p = P.s, P.d, P.p
    S = Sched(P)
    d1 = S.dma_plain([], s["scr"][1][:], d["c_cos1"][:, c * TC:(c + 1) * TC], "dT")
    d2 = S.dma_plain([], s["scr"][2][:], d["c_sin1"][:, c * TC:(c + 1) * TC], "dT")
    tb1 = S.V([d2], lambda v: v.tensor_copy(s["tb"][:, 0:TC], s["scr"][1][:]))
    tb2 = S.V([tb1], lambda v: v.tensor_copy(s["tb"][:, TC:2 * TC], s["scr"][2][:]))
    tbc = s["tb"][0:64, 0:TC]
    tbs = s["tb"][0:64, TC:2 * TC]
    cn = S.stream([], d["wqkv1"][:, 2048:2176].rearrange("(k p) m -> p k m", p=128),
                  s["ws0"][:, 0:2048])
    def kvmm(t):
        for dk in range(NDK):
            inst = t.matmul(p["pA"][:], s["ws0"][:, dk * 128:(dk + 1) * 128],
                            s["xnT"][:, dk * TC:(dk + 1) * TC],
                            start=(dk == 0), stop=(dk == NDK - 1))
        return inst
    tkv = S.T([cn], kvmm)
    ce = S.C([tkv, tb2], lambda sc: sc.activation(s["scr"][1][0:64, :], p["pA"][0:64, :],
                                                  AF.Copy))
    vb = S.V([ce], lambda v: v.tensor_copy(s["qt"][0:64, :], s["scr"][1][0:64, :]))
    tr_ = S.T([vb], lambda t: t.matmul(p["pC"][0:64, :], s["rot1"][0:64, 0:64],
                                       s["qt"][0:64, :], start=True, stop=True))
    f1 = S.Vf([tr_], lambda v: v.tensor_tensor(s["wbc"][0:64, :], s["scr"][1][0:64, :],
                                               tbc, ALU.mult))
    f2 = S.Vf([f1], lambda v: v.tensor_tensor(s["scr"][1][0:64, :], p["pC"][0:64, :],
                                              tbs, ALU.mult))
    k1w = S.V([f2], lambda v: v.tensor_tensor(
        s["k1s"][0:64, c * TC:(c + 1) * TC], s["wbc"][0:64, :], s["scr"][1][0:64, :],
        ALU.add))
    S.V([k1w], lambda v: v.tensor_copy(s["k1s"][64:128, c * TC:(c + 1) * TC],
                                       s["k1s"][0:64, c * TC:(c + 1) * TC]))
    cv = S.C([tkv], lambda sc: sc.activation(s["scrb"][0:64, :], p["pA"][64:128, :],
                                             AF.Copy))
    prev_ev = None
    for j in range(NTT):
        tw = [cv]
        if prev_ev is not None and j >= 2:
            tw.append(evs[j - 2])
        tt_ = S.T(tw, lambda t, j=j: t.transpose(
            p["pT0" if j % 2 == 0 else "pT1"][:, 0:64],
            s["scrb"][0:64, j * 128:(j + 1) * 128], s["identb"][0:64, 0:64]))
        kt = 4 * c + j
        ev = S.V([tt_], lambda v, j=j, kt=kt: v.tensor_copy(
            s["v1s"][:, kt * 64:(kt + 1) * 64],
            p["pT0" if j % 2 == 0 else "pT1"][:, 0:64]))
        if j == 0:
            evs = {}
        evs[j] = ev
        prev_ev = ev
    S.finish()


def ph_attn1(P: Prog, c):
    s, d, p = P.s, P.d, P.p
    S = Sched(P)
    band = _band1(c)
    nkt = len(band)
    tbcF = s["tb"][:, 0:TC]
    tbsF = s["tb"][:, TC:2 * TC]

    prev_qevac = prev_qrope = prev_afm = prev_rot = None
    prev_bc = None
    qproj_tok = {}
    for j in range(L1_NH // 2):
        cast_w = []
        if j >= 4:
            cast_w.append(qproj_tok[j - 4])
        cn = S.stream([], d["wqkv1"][:, j * 128:(j + 1) * 128].rearrange(
            "(k p) m -> p k m", p=128),
            s["ws1"][:, (j % 4) * 2048:(j % 4 + 1) * 2048], cast_w)
        qw = [cn]
        if prev_qevac:
            qw.append(prev_qevac)
        def qmm(t, j=j):
            for dk in range(NDK):
                inst = t.matmul(p["pA"][:],
                                s["ws1"][:, (j % 4) * 2048 + dk * 128:(j % 4) * 2048 + (dk + 1) * 128],
                                s["xnT"][:, dk * TC:(dk + 1) * TC],
                                start=(dk == 0), stop=(dk == NDK - 1))
            return inst
        tq = S.T(qw, qmm)
        qproj_tok[j] = tq
        ew = [tq]
        if prev_qrope:
            ew.append(prev_qrope)
        ce = S.C(ew, lambda sc: sc.activation(s["scr"][1][:], p["pA"][:], AF.Copy))
        prev_qevac = ce
        bw = [ce]
        if prev_rot:
            bw.append(prev_rot)
        vb = S.V(bw, lambda v: v.tensor_copy(s["qt"][:], s["scr"][1][:]))
        rw = [vb]
        if prev_qrope:
            rw.append(prev_qrope)
        tro = S.T(rw, lambda t: t.matmul(p["pC"][:], s["rot1"][:], s["qt"][:],
                                         start=True, stop=True))
        prev_rot = tro
        f1 = S.Vf([tro], lambda v: v.tensor_tensor(s["wbc"][:], s["scr"][1][:], tbcF, ALU.mult))
        f2 = S.Vf([f1], lambda v: v.tensor_tensor(s["scr"][1][:], p["pC"][:], tbsF, ALU.mult))
        vq = S.V([f2], lambda v: v.tensor_tensor(s["qt"][:], s["wbc"][:], s["scr"][1][:],
                                                 ALU.add))
        prev_qrope = vq

        ctx_evac = {}
        recips = {}
        for par in range(2):
            b0 = par * 64
            prev_exp = {}
            prev_pv = {}
            prev_diag = None
            for i, (kt, segs) in enumerate(band):
                psc = p["pD"] if i % 2 == 0 else p["pC"]
                sw = [vq]
                if i >= 2:
                    sw.append(prev_exp[i - 2])
                if i < 2 and par == 1:
                    sw.append(ctx_evac[0])
                if i < 2 and par == 0 and prev_bc:
                    sw.append(prev_bc)
                tsc = S.T(sw, lambda t, kt=kt, psc=psc, b0=b0: t.matmul(
                    psc[:], s["k1s"][b0:b0 + 64, kt * 128:(kt + 1) * 128],
                    s["qt"][b0:b0 + 64, :], start=True, stop=True))
                etoks = []
                for tl, kind in enumerate(segs):
                    sl = slice(tl * 128, (tl + 1) * 128)
                    if kind == "keep":
                        w = [tsc]
                        if i >= 2:
                            w.append(prev_pv[i - 2])
                        etoks.append(S.C(w, lambda sc, psc=psc, sl=sl, i=i:
                                         sc.activation(s["pst"][i % 2][:, sl], psc[:, sl],
                                                       AF.Exp, scale=0.125,
                                                       bias=s["crows"][:, 0:1])))
                    elif kind == "diag":
                        mw = [tsc]
                        if prev_diag:
                            mw.append(prev_diag)
                        vm = S.Vf(mw, lambda v, psc=psc, sl=sl: v.tensor_tensor(
                            s["scr"][0][:, 0:128], psc[:, sl], s["mdiag"][:], ALU.add))
                        w = [vm]
                        if i >= 2:
                            w.append(prev_pv[i - 2])
                        tok = S.C(w, lambda sc, sl=sl, i=i: sc.activation(
                            s["pst"][i % 2][:, sl], s["scr"][0][:, 0:128],
                            AF.Exp, scale=0.125, bias=s["crows"][:, 0:1]))
                        etoks.append(tok)
                        prev_diag = tok
                    else:
                        w = [tsc]
                        if i >= 2:
                            w.append(prev_pv[i - 2])
                        etoks.append(S.V(w, lambda v, sl=sl, i=i: v.memset(
                            s["pst"][i % 2][:, sl], 0.0)))
                prev_exp[i] = etoks[-1] if etoks else tsc
                pw = list(etoks)
                if i == 0:
                    if par == 1:
                        pw.append(ctx_evac[0])
                        pw.append(recips[0])
                    elif prev_afm:
                        pw.append(prev_afm)
                t1 = S.T(pw, lambda t, kt=kt, i=i: t.matmul(
                    p["pB"][0:64, :], s["v1s"][:, kt * 64:(kt + 1) * 64],
                    s["pst"][i % 2][:], start=(i == 0), stop=(i == nkt - 1)))
                t2 = S.T([], lambda t, i=i: t.matmul(
                    p["pE"][0:1, :], s["ones_col"][:], s["pst"][i % 2][:],
                    start=(i == 0), stop=(i == nkt - 1)))
                prev_pv[i] = t2
            rrow = P.row(2) if par == 0 else P.row(0)
            vrw = [prev_pv[nkt - 1]]
            if par == 0 and prev_bc:
                vrw.append(prev_bc)
            vr = S.V(vrw, lambda v, rrow=rrow: v.reciprocal(rrow, p["pE"][0:1, :]))
            recips[par] = vr
            cw = [prev_pv[nkt - 1], t1]
            if par == 0 and prev_afm:
                cw.append(prev_afm)
            cev = S.C(cw, lambda sc, b0=b0: sc.activation(
                s["scrb"][b0:b0 + 64, :], p["pB"][0:64, :], AF.Copy))
            ctx_evac[par] = cev
        tb1_ = S.T([recips[0], recips[1], ctx_evac[1]], lambda t: t.matmul(
            p["pD"][0:64, :], s["ones_row"][0:1, 0:64], P.row(2),
            start=True, stop=True, tile_position=(0, 0)))
        tb2_ = S.T([], lambda t: t.matmul(
            p["pD"][64:128, :], s["ones_row"][0:1, 0:64], P.row(0),
            start=True, stop=True, tile_position=(0, 64)))
        cw = [tb1_, tb2_]
        if prev_afm:
            cw.append(prev_afm)
        cbe = S.C(cw, lambda sc: sc.activation(s["scr"][2][:], p["pD"][:], AF.Copy))
        prev_bc = cbe
        va = S.V([cbe, ctx_evac[1]], lambda v, j=j: v.tensor_tensor(
            s["afm"][:, j * TC:(j + 1) * TC], s["scrb"][:], s["scr"][2][:], ALU.mult))
        prev_afm = va
    S.finish()


def ph_dequant1(P: Prog):
    s, d, p = P.s, P.d, P.p
    S = Sched(P)
    dO = P.sems["dO"]

    def panel_gateup(which, it, prev_toks):
        src = d["g1q" if which == 0 else "u1q"]
        ssc = d["g1s" if which == 0 else "u1s"]
        tgt = d["wg1T" if which == 0 else "wu1T"]
        half = S.dmai_parity = it % 2
        wlt = s["wl0"] if half == 0 else s["wl1"]
        dsc = S.dma_plain([], s["rstat"][:, 40 + which:41 + which],
                          ssc[it * 128:(it + 1) * 128, :], "dT")
        dw = []
        if prev_toks.get(("cast", half)):
            dw.append(prev_toks[("cast", half)])
        dn = S.dma(dw, wlt[:].bitcast(I32), src[it * 128:(it + 1) * 128, :],
                   half * 4)
        cw = [dn, dsc]
        if prev_toks.get(("mmread", half)):
            cw.append(prev_toks[("mmread", half)])
        ct = S.V(cw, lambda v, wlt=wlt, which=which, half=half: v.tensor_scalar(
            s["ws0"][:, half * 4096:half * 4096 + 2048],
            wlt[:].bitcast(I32), 128.0, s["rstat"][:, 40 + which:41 + which],
            ALU.subtract, ALU.mult))
        prev_toks[("cast", half)] = ct
        evs = []
        for k in range(NDK):
            tw = [ct]
            if len(evs) >= 2:
                tw.append(evs[-2])
            elif prev_toks.get(("ev", half)):
                tw.append(prev_toks[("ev", half)])
            ttk = S.T(tw, lambda t, k=k, half=half: t.transpose(
                p["pT0" if k % 2 == 0 else "pT1"][:],
                s["ws0"][:, half * 4096 + k * 128: half * 4096 + (k + 1) * 128],
                s["identb"][:]))
            ew = [ttk]
            if prev_toks.get(("out", half)):
                ew.append(prev_toks[("out", half)])
            ev = S.V(ew, lambda v, k=k, half=half: v.tensor_copy(
                s["ws1"][:, half * 4096 + k * 128: half * 4096 + (k + 1) * 128],
                p["pT0" if k % 2 == 0 else "pT1"][:]))
            evs.append(ev)
        prev_toks[("ev", half)] = evs[-1]
        prev_toks[("mmread", half)] = evs[-1]
        od = S.dma([evs[-1]], tgt[it], s["ws1"][:, half * 4096:half * 4096 + 2048],
                   12 + half)
        prev_toks[("out", half)] = od
        return od

    toks = {}
    for it in range(I1 // 128):
        panel_gateup(0, it, toks)
        panel_gateup(1, it, toks)

    toks2 = {"out": [toks.get(("out", 0)), toks.get(("out", 1))]}
    toks2["out"] = [x for x in toks2["out"] if x]
    for hp in range(NDK):
        dsc = S.dma_plain([], s["rstat"][:, 42:43], d["d1s"][hp * 128:(hp + 1) * 128, :],
                          "dT")
        evs_all = []
        for sub in range(4):
            half = sub % 2
            wlt = s["wl0"] if half == 0 else s["wl1"]
            dw = []
            if toks2.get(("cast", half)):
                dw.append(toks2[("cast", half)])
            dn = S.dma(dw, wlt[:].bitcast(I32),
                       d["d1q"][hp * 128:(hp + 1) * 128, sub * 2048:(sub + 1) * 2048],
                       half * 4)
            cw = [dn, dsc]
            if toks2.get(("tr", half)):
                cw.append(toks2[("tr", half)])
            ct = S.V(cw, lambda v, wlt=wlt, half=half: v.tensor_scalar(
                s["ws0"][:, half * 4096:half * 4096 + 2048],
                wlt[:].bitcast(I32), 128.0, s["rstat"][:, 42:43],
                ALU.subtract, ALU.mult))
            toks2[("cast", half)] = ct
            evs = []
            for k in range(NDK):
                tw = [ct]
                if len(evs) >= 2:
                    tw.append(evs[-2])
                elif evs_all:
                    tw.append(evs_all[-2] if len(evs_all) >= 2 else evs_all[-1])
                ttk = S.T(tw, lambda t, k=k, half=half: t.transpose(
                    p["pT0" if k % 2 == 0 else "pT1"][:],
                    s["ws0"][:, half * 4096 + k * 128: half * 4096 + (k + 1) * 128],
                    s["identb"][:]))
                toks2[("tr", half)] = ttk
                ew = [ttk] + list(toks2.get("out") or [])
                ev = S.V(ew, lambda v, k=k, sub=sub: v.tensor_copy(
                    s["ws1"][:, sub * 2048 + k * 128: sub * 2048 + (k + 1) * 128],
                    p["pT0" if k % 2 == 0 else "pT1"][:]))
                evs.append(ev)
                evs_all.append(ev)
        od = S.dma([evs_all[-1]], d["wd1T"][hp], s["ws1"][:], 14)
        toks2["out"] = [od]
    S.finish()


def ph_ffn1(P: Prog, c):
    s, d, p = P.s, P.d, P.p
    S = Sched(P)
    prev_silu = prev_mul = None
    prev_gmm = prev_umm = prev_dmm = None
    bank_rd = {0: None, 1: None}
    for q1 in range(4):
        mids = {}
        for itg in range(4):
            itbase = q1 * 16 + itg * 4
            gw = []
            if prev_dmm:
                gw.append(prev_dmm)
            if prev_gmm:
                gw.append(prev_gmm)
            gcast = S.dma(gw, s["ws0"][:].rearrange("p (it c) -> p it c", c=2048),
                          d["wg1T"][itbase:itbase + 4].rearrange("it p c -> p it c"), 8)
            ucast = S.dma([prev_umm] if prev_umm else [],
                          s["ws1"][:].rearrange("p (it c) -> p it c", c=2048),
                          d["wu1T"][itbase:itbase + 4].rearrange("it p c -> p it c"), 12)
            for itl in range(4):
                def gmm(t, itl=itl):
                    for hk in range(NDK):
                        inst = t.matmul(p["pA"][:],
                                        s["ws0"][:, itl * 2048 + hk * 128: itl * 2048 + (hk + 1) * 128],
                                        s["xnT"][:, hk * TC:(hk + 1) * TC],
                                        start=(hk == 0), stop=(hk == NDK - 1))
                    return inst
                gwl = [gcast]
                if prev_silu:
                    gwl.append(prev_silu)
                tg = S.T(gwl, gmm)
                prev_gmm = tg
                def umm(t, itl=itl):
                    for hk in range(NDK):
                        inst = t.matmul(p["pB"][:],
                                        s["ws1"][:, itl * 2048 + hk * 128: itl * 2048 + (hk + 1) * 128],
                                        s["xnT"][:, hk * TC:(hk + 1) * TC],
                                        start=(hk == 0), stop=(hk == NDK - 1))
                    return inst
                uwl = [ucast]
                if prev_mul:
                    uwl.append(prev_mul)
                tu = S.T(uwl, umm)
                prev_umm = tu
                sw = [tg]
                if prev_mul:
                    sw.append(prev_mul)
                cs = S.C(sw, lambda sc: sc.activation(s["scr"][0][:], p["pA"][:], AF.Silu))
                prev_silu = cs
                itg_l = itg * 4 + itl
                mw = [cs, tu]
                if prev_dmm:
                    mw.append(prev_dmm)
                mv_ = S.V(mw, lambda v, itg_l=itg_l: v.tensor_tensor(
                    s["mid"][:, itg_l * TC:(itg_l + 1) * TC], s["scr"][0][:],
                    p["pB"][:], ALU.mult))
                prev_mul = mv_
                mids[itg_l] = mv_
        down_slot_rd = {}
        for dt in range(NDK):
            bank = dt % 2
            pb = p["pC"] if bank == 0 else p["pE"]
            dcw = [prev_gmm]
            if (dt % 4) in down_slot_rd:
                dcw.append(down_slot_rd[dt % 4])
            dcast = S.dma(dcw, s["ws0"][:, (dt % 4) * 2048:(dt % 4 + 1) * 2048],
                          d["wd1T"][dt][:, q1 * 2048:(q1 + 1) * 2048], 8 + dt % 4)
            def dmm(t, dt=dt, pb=pb):
                for ik in range(NDK):
                    inst = t.matmul(pb[:],
                                    s["ws0"][:, (dt % 4) * 2048 + ik * 128:(dt % 4) * 2048 + ik * 128 + 128],
                                    s["mid"][:, ik * TC:(ik + 1) * TC],
                                    start=(ik == 0), stop=(ik == NDK - 1))
                return inst
            dwl = [dcast, mids[15]]
            if bank_rd[bank]:
                dwl.append(bank_rd[bank])
            td = S.T(dwl, dmm)
            prev_dmm = td
            down_slot_rd[dt % 4] = td
            av = S.V([td], lambda v, dt=dt, pb=pb, first=(q1 == 0): (
                v.tensor_copy(s["accx"][:, dt * TC:(dt + 1) * TC], pb[:]) if first
                else v.tensor_tensor(s["accx"][:, dt * TC:(dt + 1) * TC],
                                     s["accx"][:, dt * TC:(dt + 1) * TC], pb[:], ALU.add)))
            bank_rd[bank] = av
    S.finish()


def ph_comb1(P: Prog, c):
    s, d, p = P.s, P.d, P.p
    S = Sched(P)
    dO = P.sems["dO"]
    c1 = S.Vf([], lambda v: v.tensor_copy(P.row(0), P.row(4)))
    c2 = S.Vf([c1], lambda v: v.tensor_copy(P.row(1), P.row(3)))
    c3 = S.V([c2], lambda v: v.reciprocal(P.row(2), P.row(0)))
    t_s = S.T([c3], lambda t: t.matmul(p["pD"][:], s["ones_row"][:], P.row(2),
                                       start=True, stop=True))
    t_m = S.T([c2, c3], lambda t: t.matmul(p["pE"][:], s["ones_row"][:], P.row(1),
                                           start=True, stop=True))
    c_s = S.C([t_s], lambda sc: sc.activation(s["scr"][1][:], p["pD"][:], AF.Copy))
    c_m = S.C([t_m], lambda sc: sc.activation(s["scr"][2][:], p["pE"][:], AF.Copy))

    prev_t1 = {}
    prev_dense = {}
    prev_comb = {}
    for dt in range(NDK):
        cw = []
        if dt >= 4:
            cw.append(prev_dense[dt - 4])
        cn = S.stream([], d["wo1"][:, dt * 128:(dt + 1) * 128].rearrange(
            "(k p) m -> p k m", p=128), s["ws1"][:, (dt % 4) * 2048:(dt % 4 + 1) * 2048],
            cw)
        pb = p["pA"] if dt % 2 == 0 else p["pB"]
        dwl = [cn]
        if dt >= 2:
            dwl.append(prev_t1[dt - 2])
        def dmm(t, dt=dt, pb=pb):
            for ht in range(NDK):
                inst = t.matmul(pb[:],
                                s["ws1"][:, (dt % 4) * 2048 + ht * 128:(dt % 4) * 2048 + ht * 128 + 128],
                                s["afm"][:, ht * TC:(ht + 1) * TC],
                                start=(ht == 0), stop=(ht == NDK - 1))
            return inst
        td = S.T(dwl, dmm)
        prev_dense[dt] = td
        sw = [c_s, c_m]
        if dt >= 1:
            sw.append(prev_t1[dt - 1])
        csl = S.C(sw, lambda sc, dt=dt: sc.activation(
            s["scr"][0][:], s["accx"][:, dt * TC:(dt + 1) * TC], AF.Silu))
        t1_ = S.Vf([td, csl], lambda v, pb=pb: v.tensor_tensor(
            s["wbc"][:], pb[:], s["scr"][0][:], ALU.mult))
        prev_t1[dt] = t1_
        a1 = S.Vf([t1_, csl], lambda v, dt=dt: v.tensor_tensor(
            s["accx"][:, dt * TC:(dt + 1) * TC],
            s["xnT"][:, dt * TC:(dt + 1) * TC], s["scr"][1][:], ALU.mult))
        a2 = S.Vf([a1], lambda v, dt=dt: v.tensor_tensor(
            s["accx"][:, dt * TC:(dt + 1) * TC],
            s["accx"][:, dt * TC:(dt + 1) * TC], s["scr"][2][:], ALU.add))
        a3 = S.V([a2], lambda v, dt=dt: v.tensor_tensor(
            s["accx"][:, dt * TC:(dt + 1) * TC],
            s["accx"][:, dt * TC:(dt + 1) * TC], s["wbc"][:], ALU.add))
        prev_comb[dt] = a3

    prev_out_dma = {0: None, 1: None}
    last_evac = None
    for tt in range(NTT):
        half = tt % 2
        wlt = s["wl0"] if half == 0 else s["wl1"]
        evs = []
        for dt in range(NDK):
            tw = [prev_comb[dt]]
            if last_evac is not None:
                tw.append(last_evac)
            ttk = S.T(tw, lambda t, tt=tt, dt=dt: t.transpose(
                p["pTf"][:], s["accx"][:, dt * TC + tt * 128: dt * TC + (tt + 1) * 128],
                s["identf"][:]))
            ew = [ttk]
            if dt == 0 and prev_out_dma[half]:
                ew.append(prev_out_dma[half])
            ev = S.C(ew, lambda sc, dt=dt, wlt=wlt: sc.activation(
                wlt[:, dt * 128:(dt + 1) * 128], p["pTf"][:], AF.Copy))
            evs.append(ev)
            last_evac = ev
        r0 = c * TC + tt * 128
        od = S.dma([evs[-1]], d["y"][r0:r0 + 128, :], wlt[:], half * 4)
        prev_out_dma[half] = od
    S.finish()


def build_full():
    P = Prog()
    ph_setup(P)
    for c in range(NCHUNK):
        ph_ln0(P, c)
        ph_prep0(P, c)
        ph_router0(P, c)
        ph_xga0(P, c)
        ph_ffn0(P, c)
        ph_attn0(P, c)
        ph_comb0(P, c)
        ph_prep1(P, c)
        if c == 0:
            ph_dequant1(P)
        ph_ffn1(P, c)
        ph_attn1(P, c)
        ph_comb1(P, c)
    return P


def kernel(**inputs):
    P = build_full()
    maps = mk_inmaps(inputs, n_cores=8)
    res = P.run(maps)
    return np.stack([np.asarray(r["y"], dtype=np.float32) for r in res.results], axis=0)


# version 03
# speedup vs baseline: 1.0659x; 1.0659x over previous
import math
import numpy as np
from contextlib import ExitStack

import concourse.bass as bass
import concourse.mybir as mybir
from concourse.bass_utils import run_bass_kernel_spmd

F32 = mybir.dt.float32
BF16 = mybir.dt.bfloat16
I32 = mybir.dt.int32
AF = mybir.ActivationFunctionType
ALU = mybir.AluOpType
AX = mybir.AxisListType

T = 2048
D = 2048
NCHUNK = 4
TC = 512
NTT = TC // 128
NDK = D // 128

NEG = -1.0e30

L0_NH, L0_NKV, L0_HD, L0_W, L0_CAP = 16, 4, 128, 1024, 50.0
L1_NH, L1_HD = 32, 64
NE, F0, RADJ, NL = 8, 2048, 64, 2
I1 = 8192

GOLD_C = 1.0 / math.e
GOLD_L = 0.5 - math.log(4.0 / 3.0)
GOLD_U = 0.5


def host_constants():
    c = {}
    c["c_identf"] = np.eye(128, dtype=np.float32)
    r0 = np.zeros((128, 128), dtype=np.float32)
    for i in range(64):
        r0[i + 64, i] = -1.0
        r0[i, i + 64] = 1.0
    c["c_rot0"] = r0
    r1 = np.zeros((128, 128), dtype=np.float32)
    for b in (0, 64):
        for i in range(32):
            r1[b + i + 32, b + i] = -1.0
            r1[b + i, b + i + 32] = 1.0
    c["c_rot1"] = r1

    def sine_table(hd, theta):
        frac = np.arange(0, hd, 2, dtype=np.float64) / hd
        invts = theta ** -frac
        tpos = np.arange(T, dtype=np.float64)
        s = np.outer(tpos, invts)
        s = np.concatenate([s, s], axis=1)
        return np.sin(s).T.astype(np.float32), np.cos(s).T.astype(np.float32)

    s0, c0 = sine_table(128, 10000.0)
    c["c_sin0"], c["c_cos0"] = s0, c0
    s1, c1 = sine_table(64, 10000.0)
    c["c_sin1"] = np.concatenate([s1, s1], axis=0)
    c["c_cos1"] = np.concatenate([c1, c1], axis=0)
    i_k = np.arange(128)[:, None]
    i_q = np.arange(128)[None, :]
    c["c_mdiag"] = np.where(i_k <= i_q, 0.0, NEG).astype(np.float32)
    c["c_medge"] = np.where(i_k >= i_q, 0.0, NEG).astype(np.float32)
    rows = np.zeros((128, 8), dtype=np.float32)
    rows[:, 0] = -20.0
    rows[:, 1] = 1e-5
    rows[:, 2] = -GOLD_C
    rows[:, 3] = 1e-8
    rows[:, 4] = 1.0
    c["c_rows"] = rows
    c["c_ones_row"] = np.ones((1, 128), dtype=np.float32)
    c["c_ones_colf"] = np.ones((128, 1), dtype=np.float32)
    return c


ENGINES = ("sync", "tensor", "scalar", "vector", "gpsimd")

_REAL = {
    "dma_start", "matmul", "transpose", "activation", "tensor_tensor",
    "tensor_scalar", "tensor_copy", "tensor_add", "tensor_sub", "tensor_mul",
    "tensor_max", "reduce_sum", "reduce_max", "tensor_reduce", "reciprocal",
    "memset", "memzero", "iota", "copy", "mul", "add", "sqrt", "square",
    "tensor_scalar_add", "tensor_scalar_mul", "tensor_scalar_max",
    "tensor_scalar_min", "scalar_tensor_tensor",
}


_UPDATED = {}


class Sem:
    def __init__(self, handle):
        self.h = handle
        self.n = 0
        self.maxw = 0

    def inc(self, inst, k=1):
        assert id(inst) not in _UPDATED, "instruction already carries a sem update"
        _UPDATED[id(inst)] = inst
        inst.then_inc(self.h, k)
        self.n += k
        return self.n

    def dinc(self, inst):
        return self.inc(inst, 16)


class EW:

    def __init__(self, eng, prog, name):
        self._e = eng
        self._p = prog
        self._n = name

    def __getattr__(self, a):
        f = getattr(self._e, a)
        if a in _REAL:
            def g(*args, **kw):
                inst = f(*args, **kw)
                self._p.last_real[self._n] = inst
                return inst
            return g
        return f

    def wait(self, sem, val):
        sem.maxw = max(sem.maxw, val)
        self._e.wait_ge(sem.h, val)


class Prog:
    def __init__(self, layers=(0, 1), chunks=None, debug=()):
        self.layers = tuple(layers)
        self.chunks = list(range(NCHUNK)) if chunks is None else list(chunks)
        self.debug = set(debug)
        self.nc = bass.Bass("TRN2", target_bir_lowering=False)
        self.ctx = ExitStack()
        self.last_real = {n: None for n in ENGINES}
        self.phase_idx = 0
        self.dbg_tensors = {}
        self._mk_dram()
        self._mk_sbuf()
        self._mk_psum()
        self._mk_sems()
        self.block = self.ctx.enter_context(self.nc.Block())

    def _mk_dram(self):
        nc = self.nc
        def ap(name, shape, dt, kind):
            return nc.dram_tensor(name, shape, dt, kind=kind).ap()
        EI, EO, IN = "ExternalInput", "ExternalOutput", "Internal"
        self.d = d = {}
        d["x"] = ap("x", [T, D], F32, EI)
        d["y"] = ap("y", [T, D], F32, EO)
        d["wq0"] = ap("wq0", [D, 2048], F32, EI)
        d["wk0"] = ap("wk0", [D, 512], F32, EI)
        d["wv0"] = ap("wv0", [D, 512], F32, EI)
        d["wo0"] = ap("wo0", [2048, D], F32, EI)
        d["wg0"] = ap("wg0", [NE, D, F0], F32, EI)
        d["wu0"] = ap("wu0", [NE, D, F0], F32, EI)
        d["wd0"] = ap("wd0", [NE, F0, D], F32, EI)
        d["ga0"] = ap("ga0", [NL, D, RADJ], F32, EI)
        d["gb0"] = ap("gb0", [NL, RADJ, F0], F32, EI)
        d["ua0"] = ap("ua0", [NL, D, RADJ], F32, EI)
        d["ub0"] = ap("ub0", [NL, RADJ, F0], F32, EI)
        d["da0"] = ap("da0", [NL, F0, RADJ], F32, EI)
        d["db0"] = ap("db0", [NL, RADJ, D], F32, EI)
        d["wr0"] = ap("wr0", [D, NE], F32, EI)
        d["temp0"] = ap("temp0", [1, 1], F32, EI)
        d["wqkv1"] = ap("wqkv1", [D, 2176], F32, EI)
        d["wo1"] = ap("wo1", [D, D], F32, EI)
        d["g1q"] = ap("g1q", [I1, D], I32, EI)
        d["g1s"] = ap("g1s", [I1, 1], F32, EI)
        d["u1q"] = ap("u1q", [I1, D], I32, EI)
        d["u1s"] = ap("u1s", [I1, 1], F32, EI)
        d["d1q"] = ap("d1q", [D, I1], I32, EI)
        d["d1s"] = ap("d1s", [D, 1], F32, EI)
        for name, arr in host_constants().items():
            d[name] = ap(name, list(arr.shape), F32, EI)
        d["wg1T"] = ap("wg1T", [I1 // 128, 128, D], BF16, IN)
        d["wg0b"] = ap("wg0b", [NE * 4, 128, 8192], BF16, IN)
        d["wu0b"] = ap("wu0b", [NE * 4, 128, 8192], BF16, IN)
        d["wd0b"] = ap("wd0b", [NE * 4, 128, 8192], BF16, IN)
        d["wu1T"] = ap("wu1T", [I1 // 128, 128, D], BF16, IN)
        d["wd1T"] = ap("wd1T", [D // 128, 128, I1], BF16, IN)

    def row(self, r):
        b = (0, 0, 0, 32, 32, 32, 64, 64, 64)[r]
        col = (r % 3) * TC
        return self.s["rows"][b:b + 1, col:col + TC]

    def dbg(self, name, shape, dt=F32):
        if name not in self.dbg_tensors:
            self.dbg_tensors[name] = self.nc.dram_tensor(
                "dbg_" + name, shape, dt, kind="ExternalOutput").ap()
        return self.dbg_tensors[name]

    def _mk_sbuf(self):
        nc, E = self.nc, self.ctx.enter_context
        def sb(name, shape, dt):
            return E(nc.sbuf_tensor(name, shape, dt))
        self.s = s = {}
        s["identf"] = sb("identf", [128, 128], F32)
        s["identb"] = sb("identb", [128, 128], BF16)
        s["rot0"] = sb("rot0", [128, 128], BF16)
        s["rot1"] = sb("rot1", [128, 128], BF16)
        s["mdiag"] = sb("mdiag", [128, 128], F32)
        s["medge"] = sb("medge", [128, 128], F32)
        s["ones_row"] = sb("ones_row", [1, 128], F32)
        s["ones_col"] = sb("ones_col", [128, 1], BF16)
        s["crows"] = sb("crows", [128, 8], F32)
        s["invtau"] = sb("invtau", [128, 1], F32)
        s["tb"] = sb("tb", [128, 2 * TC], BF16)
        s["xnT"] = sb("xnT", [128, NDK * TC], BF16)
        s["k0s"] = sb("k0s", [128, L0_NKV * T], BF16)
        s["v0s"] = sb("v0s", [128, L0_NKV * T], BF16)
        s["k1s"] = sb("k1s", [128, T], BF16)
        s["v1s"] = sb("v1s", [128, (T // 128) * L1_HD], BF16)
        s["qt"] = sb("qt", [128, TC], BF16)
        s["pst"] = [sb(f"pst{i}", [128, TC], BF16) for i in range(2)]
        s["afm"] = sb("afm", [128, L0_NH * TC], BF16)
        s["accx"] = sb("accx", [128, NDK * TC], F32)
        s["mid"] = sb("mid", [128, 16 * TC], BF16)
        s["ws0"] = sb("ws0", [128, 8192], BF16)
        s["ws1"] = sb("ws1", [128, 8192], BF16)
        s["wl0"] = sb("wl0", [128, 2048], F32)
        s["wl1"] = sb("wl1", [128, 2048], F32)
        s["adsl"] = sb("adsl", [128, 4096], BF16)
        s["xga"] = sb("xga", [64, 4 * TC], BF16)
        s["mda"] = sb("mda", [64, TC], BF16)
        s["lnst"] = sb("lnst", [128, 32], F32)
        s["lnstb"] = sb("lnstb", [128, 32], F32)
        s["rt"] = sb("rt", [128, 4 * 64], F32)
        s["rstat"] = sb("rstat", [128, 48], F32)
        s["rstatb"] = sb("rstatb", [128, 48], F32)
        s["w8f"] = sb("w8f", [8, TC], F32)
        s["w8"] = sb("w8", [128, 4 * 8], F32)
        s["wbc"] = sb("wbc", [128, TC], F32)
        s["rows"] = sb("rows", [128, 3 * TC], F32)
        s["rows8"] = sb("rows8", [8, 128], F32)
        s["scr"] = [sb(f"scr{i}", [128, TC], F32) for i in range(3)]
        s["scrb"] = sb("scrb", [128, TC], BF16)
        s["dmy"] = sb("dmy", [128, 8], F32)

    def _mk_psum(self):
        nc, E = self.nc, self.ctx.enter_context
        self.p = {}
        for name in ["pA", "pB", "pC", "pD", "pE"]:
            self.p[name] = E(nc.psum_tensor(name, [128, 512], F32))
        self.p["pT0"] = E(nc.psum_tensor("pT0", [128, 128], BF16))
        self.p["pT1"] = E(nc.psum_tensor("pT1", [128, 128], BF16))
        self.p["pTf"] = E(nc.psum_tensor("pTf", [128, 128], F32))

    def _mk_sems(self):
        E = self.ctx.enter_context
        self.sems = {}
        for name in ["dW", "dX", "dS", "dO", "dT", "dM",
                     "mT", "mS", "mV", "mG",
                     "bS", "bT", "bC", "bV", "bG",
                     "fV", "fS"]:
            self.sems[name] = Sem(E(self.nc.semaphore(name)))
        self.dL = [Sem(E(self.nc.semaphore(f"dL{i}"))) for i in range(16)]

    def sec(self, engine, fn):
        dec = getattr(self.block, engine)
        def wrapped(raw):
            fn(EW(raw, self, engine))
        dec(wrapped)

    def barrier(self):
        self.phase_idx += 1
        k = self.phase_idx
        dvals = [(self.sems[nm], self.sems[nm].n)
                 for nm in ("dX", "dS", "dW", "dO", "dT", "dM")]
        dvals += [(sem, sem.n) for sem in self.dL]
        def fn_sync(e):
            for sem, val in dvals:
                e.wait_ge(sem.h, val)
            e.nop().then_inc(self.sems["bS"].h, 1)
        self.sec("sync", fn_sync)
        self.sems["bS"].n += 1
        self.last_real["sync"] = None
        bmap = {"tensor": "bT", "scalar": "bC", "vector": "bV", "gpsimd": "bG"}
        for eng in ("tensor", "scalar", "vector", "gpsimd"):
            sem = self.sems[bmap[eng]]
            had_work = self.last_real[eng] is not None
            self.last_real[eng] = None
            def fn(e, sem=sem, eng=eng, had_work=had_work):
                if not had_work:
                    e.nop().then_inc(sem.h, 1)
                    return
                if eng == "tensor":
                    e.wait_ge(self.sems["mV"].h, self.sems["mV"].n)
                    e.wait_ge(self.sems["mS"].h, self.sems["mS"].n)
                    e.wait_ge(self.sems["mT"].h, self.sems["mT"].n)
                    inst = e.matmul(self.p["pE"][0:1, 0:1], self.s["ones_col"][:],
                                    self.s["ones_col"][:], start=True, stop=True)
                elif eng == "scalar":
                    e.wait_ge(self.sems["mS"].h, self.sems["mS"].n)
                    e.wait_ge(self.sems["fS"].h, self.sems["fS"].n)
                    e.wait_ge(self.sems["fV"].h, self.sems["fV"].n)
                    inst = e.activation(self.s["dmy"][:, 0:1], self.s["dmy"][:, 1:2],
                                        AF.Copy)
                elif eng == "vector":
                    e.wait_ge(self.sems["mV"].h, self.sems["mV"].n)
                    e.wait_ge(self.sems["fV"].h, self.sems["fV"].n)
                    inst = e.memset(self.s["dmy"][:, 2:3], 0.0)
                else:
                    e.wait_ge(self.sems["mG"].h, self.sems["mG"].n)
                    inst = e.memset(self.s["dmy"][:, 3:4], 0.0)
                inst.then_inc(sem.h, 1)
            self.sec(eng, fn)
            sem.n += 1
            assert sem.n == k, (eng, sem.n, k)
        for eng in ENGINES:
            def fn(e, k=k):
                for nm in ("bS", "bT", "bC", "bV", "bG"):
                    e.wait_ge(self.sems[nm].h, k)
            self.sec(eng, fn)

    def finalize(self):
        for nm, sem in self.sems.items():
            assert sem.maxw <= sem.n, ("wait beyond final incs", nm, sem.maxw, sem.n)
        self.ctx.close()

    def run(self, in_maps, trace=False):
        self.finalize()
        return run_bass_kernel_spmd(self.nc, in_maps,
                                    core_ids=list(range(len(in_maps))),
                                    trace=trace)

    def sim(self, in_map):
        self.finalize()
        from concourse.bass_interp import CoreSim
        sim = CoreSim(self.nc, require_finite=False, require_nnan=False)
        for name, val in in_map.items():
            view = sim.tensor(name)
            view[:] = val
        sim.simulate()
        outs = {}
        for alloc in self.nc.m.functions[0].allocations:
            import concourse.mybir as _mb
            if isinstance(alloc, _mb.MemoryLocationSet) and alloc.kind == "ExternalOutput":
                nm = alloc.memorylocations[0].name
                outs[nm] = np.array(sim.tensor(nm))
        return outs


def _band0(c):
    out = []
    for kt in range(max(0, 4 * c - 8), 4 * c + 4):
        segs = []
        for tl in range(NTT):
            rel = (4 * c + tl) - kt
            if rel < 0 or rel > 8:
                segs.append("zero")
            elif rel == 0:
                segs.append("diag")
            elif rel == 8:
                segs.append("edge")
            else:
                segs.append("keep")
        out.append((kt, segs))
    return out


def _band1(c):
    out = []
    for kt in range(0, 4 * c + 4):
        segs = []
        for tl in range(NTT):
            rel = (4 * c + tl) - kt
            segs.append("zero" if rel < 0 else ("diag" if rel == 0 else "keep"))
        out.append((kt, segs))
    return out


def ph_setup(P: Prog):
    s, d, p = P.s, P.d, P.p
    dX, dS = P.sems["dX"], P.sems["dS"]
    mV, mT = P.sems["mV"], P.sems["mT"]

    def sync(q):
        for nm_s, nm_d in [("identf", "c_identf"), ("mdiag", "c_mdiag"),
                           ("medge", "c_medge"), ("ones_row", "c_ones_row"),
                           ("crows", "c_rows")]:
            dX.dinc(q.dma_start(out=s[nm_s][:], in_=d[nm_d]))
        dX.dinc(q.dma_start(out=s["scr"][0][:, 0:128], in_=d["c_identf"]))
        dX.dinc(q.dma_start(out=s["scr"][1][:, 0:128], in_=d["c_rot0"]))
        dX.dinc(q.dma_start(out=s["scr"][2][:, 0:128], in_=d["c_rot1"]))
        dX.dinc(q.dma_start(out=s["wl0"][:, 0:1], in_=d["c_ones_colf"]))
        dX.dinc(q.dma_start(out=s["wl1"][0:1, 0:1], in_=d["temp0"]))
        P._t0 = dX.n

    def vector(v):
        P.sems["fV"].inc(v.memset(s["dmy"][:], 0.0))
        v.wait(dX, P._t0)
        v.tensor_copy(s["identb"][:], s["scr"][0][:, 0:128])
        v.tensor_copy(s["rot0"][:], s["scr"][1][:, 0:128])
        v.tensor_copy(s["rot1"][:], s["scr"][2][:, 0:128])
        v.tensor_copy(s["ones_col"][:], s["wl0"][:, 0:1])

    P.sec("sync", sync)
    P.sec("vector", vector)
    P.barrier()

    def tensor2(t):
        mT.inc(t.matmul(p["pTf"][:, 0:1], s["ones_row"][:], s["wl1"][0:1, 0:1],
                        start=True, stop=True))

    def vector2(v):
        v.wait(mT, mT.n)
        mV.inc(v.reciprocal(s["scr"][0][:, 0:1], p["pTf"][:, 0:1]))

    def sync2(q):
        q.wait(mV, mV.n)
        dS.dinc(q.dma_start(out=s["invtau"][:], in_=s["scr"][0][:, 0:1]))

    P.sec("tensor", tensor2)
    P.sec("vector", vector2)
    P.sec("sync", sync2)
    P.barrier()


def ph_ln0(P: Prog, c):
    s, d, p = P.s, P.d, P.p
    xnt = s["mid"]
    dX, dS = P.sems["dX"], P.sems["dS"]
    mV, mS, mT = P.sems["mV"], P.sems["mS"], P.sems["mT"]
    fV = P.sems["fV"]
    wl = [s["wl0"], s["wl1"]]

    v0 = mV.n
    s0 = mS.n
    dL = [P.dL[0], P.dL[4]]
    xdma = {}

    def sync(q):
        for tt in range(NTT):
            if tt >= 2:
                q.wait(mV, v0 + (tt - 1))
                q.wait(mS, s0 + 4 * (tt - 1))
            r0 = c * TC + tt * 128
            xdma[tt] = dL[tt % 2].dinc(q.dma_start(out=wl[tt % 2][:],
                                                   in_=d["x"][r0:r0 + 128, :]))

    def vector(v):
        for tt in range(NTT):
            v.wait(dL[tt % 2], xdma[tt])
            mV.inc(v.reduce_sum(s["lnst"][:, tt:tt + 1], wl[tt % 2][:], axis=AX.X))

    def scalar(sc):
        for tt in range(NTT):
            sc.wait(dL[tt % 2], xdma[tt])
            for j in range(4):
                k = tt * 4 + j
                inst = sc.activation(xnt[:, k * 512:(k + 1) * 512],
                                     wl[tt % 2][:, j * 512:(j + 1) * 512],
                                     AF.Square, accum_out=s["rt"][:, 4 * tt + j:4 * tt + j + 1])
                mS.inc(inst)

    P.sec("sync", sync)
    P.sec("vector", vector)
    P.sec("scalar", scalar)

    v1 = mV.n
    s1 = mS.n

    def vector2(v):
        v.wait(mS, s1)
        v.wait(mV, v0 + 4)
        for tt in range(NTT):
            fV.inc(v.reduce_sum(s["lnst"][:, 4 + tt:5 + tt],
                                s["rt"][:, 4 * tt:4 * tt + 4], axis=AX.X))
        v.wait(fV, fV.n)
        i1 = v.tensor_scalar(s["lnst"][:, 20:24], s["lnst"][:, 0:4], 1.0 / D, None, ALU.mult)
        fV.inc(i1); v.wait(fV, fV.n)
        i2 = v.tensor_scalar(s["lnst"][:, 24:28], s["lnst"][:, 4:8], 1.0 / D, None, ALU.mult)
        fV.inc(i2)
        i3 = v.tensor_tensor(s["lnst"][:, 28:32], s["lnst"][:, 20:24], s["lnst"][:, 20:24], ALU.mult)
        fV.inc(i3); v.wait(fV, fV.n)
        i4 = v.tensor_tensor(s["lnst"][:, 24:28], s["lnst"][:, 24:28], s["lnst"][:, 28:32], ALU.subtract)
        mV.inc(i4)

    def scalar2(sc):
        sc.wait(mV, mV.n)
        inst = sc.activation(s["lnst"][:, 16:20], s["lnst"][:, 24:28], AF.Sqrt,
                             bias=s["crows"][:, 1:2])
        mS.inc(inst)

    def vector3(v):
        v.wait(mS, mS.n)
        i1 = v.reciprocal(s["lnst"][:, 8:12], s["lnst"][:, 16:20])
        fV.inc(i1); v.wait(fV, fV.n)
        i2 = v.tensor_tensor(s["lnst"][:, 12:16], s["lnst"][:, 20:24], s["lnst"][:, 8:12], ALU.mult)
        fV.inc(i2); v.wait(fV, fV.n)
        i3 = v.tensor_scalar(s["lnst"][:, 12:16], s["lnst"][:, 12:16], -1.0, None, ALU.mult)
        mV.inc(i3)

    P.sec("vector", vector2)
    P.sec("scalar", scalar2)
    P.sec("vector", vector3)

    vstat = mV.n

    def sync2(q):
        q.wait(mV, vstat)
        dS.dinc(q.dma_start(out=s["lnstb"][:, 8:16], in_=s["lnst"][:, 8:16]))

    def tensor2(t):
        t.wait(mV, vstat)
        mT.inc(t.transpose(p["pTf"][0:8, :], s["lnst"][:, 16:24], s["identf"][:]))

    def vector4(v):
        v.wait(mT, mT.n)
        mV.inc(v.tensor_copy(s["rows8"][:], p["pTf"][0:8, :]))

    def sync3(q):
        q.wait(mV, mV.n)
        for tt in range(NTT):
            dS.dinc(q.dma_start(out=P.row(0)[:, tt * 128:(tt + 1) * 128],
                                in_=s["rows8"][tt:tt + 1, :]))
            dS.dinc(q.dma_start(out=P.row(1)[:, tt * 128:(tt + 1) * 128],
                                in_=s["rows8"][4 + tt:5 + tt, :]))

    P.sec("sync", sync2)
    P.sec("tensor", tensor2)
    P.sec("vector", vector4)
    P.sec("sync", sync3)
    P.barrier()

    v2 = mV.n
    t2 = mT.n
    xdma2 = {}

    def cast_inc(tt):
        return v2 + tt * 17 + 1

    def evac_inc(tt, dk):
        return v2 + tt * 17 + 1 + dk + 1

    def trans_inc(idx):
        return t2 + idx + 1

    def sync4(q):
        for tt in range(NTT):
            if tt >= 2:
                q.wait(mV, cast_inc(tt - 1))
            r0 = c * TC + tt * 128
            xdma2[tt] = dL[tt % 2].dinc(q.dma_start(out=wl[tt % 2][:],
                                                    in_=d["x"][r0:r0 + 128, :]))

    def vector5(v):
        for tt in range(NTT):
            v.wait(dL[tt % 2], xdma2[tt])
            inst = v.tensor_scalar(xnt[:, 0:D], wl[tt % 2][:],
                                   s["lnstb"][:, 8 + tt:9 + tt],
                                   s["lnstb"][:, 12 + tt:13 + tt],
                                   ALU.mult, ALU.add)
            mV.inc(inst)
            assert mV.n == cast_inc(tt)
            for dk in range(NDK):
                idx = tt * NDK + dk
                v.wait(mT, trans_inc(idx))
                inst = v.tensor_copy(
                    s["xnT"][:, dk * TC + tt * 128: dk * TC + (tt + 1) * 128],
                    p["pT0" if idx % 2 == 0 else "pT1"][:])
                mV.inc(inst)
                assert mV.n == evac_inc(tt, dk)

    def tensor3(t):
        for tt in range(NTT):
            t.wait(mV, cast_inc(tt))
            for dk in range(NDK):
                idx = tt * NDK + dk
                if idx >= 2:
                    ptt, pdk = divmod(idx - 2, NDK)
                    t.wait(mV, evac_inc(ptt, pdk))
                inst = t.transpose(p["pT0" if idx % 2 == 0 else "pT1"][:],
                                   xnt[:, dk * 128:(dk + 1) * 128],
                                   s["identb"][:])
                mT.inc(inst)
                assert mT.n == trans_inc(idx)

    P.sec("sync", sync4)
    P.sec("tensor", tensor3)
    P.sec("vector", vector5)
    P.barrier()


def ph_dump(P: Prog, items):
    dO = P.sems["dO"]

    def sync(q):
        for name, apv, shape in items:
            dO.dinc(q.dma_start(out=P.dbg(name, list(shape), dt=apv.tensor.dtype),
                                in_=apv))
        q.wait(dO, dO.n)

    P.sec("sync", sync)
    P.barrier()


def mk_inmaps(inputs, n_cores=8):
    consts = host_constants()
    base = {
        "wq0": inputs["L0_attn_q_kernel"], "wk0": inputs["L0_attn_k_kernel"],
        "wv0": inputs["L0_attn_v_kernel"], "wo0": inputs["L0_attn_out_kernel"],
        "wg0": inputs["L0_ffn_Wg"], "wu0": inputs["L0_ffn_Wu"], "wd0": inputs["L0_ffn_Wd"],
        "ga0": inputs["L0_ffn_Ga"], "gb0": inputs["L0_ffn_Gb"],
        "ua0": inputs["L0_ffn_Ua"], "ub0": inputs["L0_ffn_Ub"],
        "da0": inputs["L0_ffn_Da"], "db0": inputs["L0_ffn_Db"],
        "wr0": inputs["L0_ffn_Wr"],
        "temp0": np.asarray(inputs["L0_ffn_temperature"], np.float32).reshape(1, 1),
        "wqkv1": inputs["L1_attn_w_qkv"], "wo1": inputs["L1_attn_w_dense"],
        "g1q": np.asarray(inputs["L1_ffn_gate_w"], np.int32),
        "g1s": np.asarray(inputs["L1_ffn_gate_scale"], np.float32).reshape(-1, 1),
        "u1q": np.asarray(inputs["L1_ffn_up_w"], np.int32),
        "u1s": np.asarray(inputs["L1_ffn_up_scale"], np.float32).reshape(-1, 1),
        "d1q": np.asarray(inputs["L1_ffn_down_w"], np.int32),
        "d1s": np.asarray(inputs["L1_ffn_down_scale"], np.float32).reshape(-1, 1),
    }
    base = {k: np.ascontiguousarray(np.asarray(v, dtype=np.int32 if k in
            ("g1q", "u1q", "d1q") else np.float32)) for k, v in base.items()}
    base.update(consts)
    x = np.asarray(inputs["L0_attn_x"], np.float32)
    maps = []
    for b in range(n_cores):
        m = dict(base)
        m["x"] = np.ascontiguousarray(x[b])
        maps.append(m)
    return maps


def ph_prep0(P: Prog, c):
    s, d, p = P.s, P.d, P.p
    dT, dM = P.sems["dT"], P.sems["dM"]
    mV, mS, mT = P.sems["mV"], P.sems["mS"], P.sems["mT"]
    wl = [s["wl0"], s["wl1"]]
    dL = [P.dL[0], P.dL[4]]
    tbc = s["tb"][:, 0:TC]
    tbs = s["tb"][:, TC:2 * TC]

    v0, t0, s0 = mV.n, mT.n, mS.n
    pdma = {}
    tbv = [0]
    wrv = [0]

    def cast(i):
        return v0 + 2 + (2 + i if i <= 5 else (10 if i == 6 else 13))
    qtcopy = lambda m: v0 + 2 + {0: 8, 1: 11, 2: 14, 3: 16}[m]
    krope = lambda m: v0 + 2 + {0: 9, 1: 12, 2: 15, 3: 17}[m]
    vevac = lambda idx: v0 + 2 + 18 + idx
    tbcast = v0 + 2
    v_end = v0 + 35
    _kv_pos = {0: 1, 1: 2, 2: 3, 3: 5, 4: 7, 5: 9, 6: 11, 7: 16}
    kvproj = lambda i: t0 + _kv_pos[i]
    rot = lambda m: t0 + 4 + 2 * m
    tr = lambda idx: t0 + (12 + idx if idx < 4 else 13 + idx)
    kevac = lambda i: s0 + i + 1

    def sync(q):
        dT.dinc(q.dma_start(out=s["scr"][1][:], in_=d["c_cos0"][:, c * TC:(c + 1) * TC]))
        tbv[0] = dT.dinc(q.dma_start(out=s["scr"][2][:], in_=d["c_sin0"][:, c * TC:(c + 1) * TC]))
        wrv[0] = dM.dinc(q.dma_start(
            out=s["scr"][0][:, 0:128].rearrange("p (k e) -> p k e", e=NE),
            in_=d["wr0"].rearrange("(k p) e -> p k e", p=128)))
        for i in range(8):
            m = i % 4
            srcw = d["wk0"] if i < 4 else d["wv0"]
            if i >= 2:
                q.wait(mV, cast(i - 1))
            pdma[i] = dL[i % 2].dinc(q.dma_start(
                out=wl[i % 2][:].rearrange("p (k m) -> p k m", m=128),
                in_=srcw[:, m * 128:(m + 1) * 128].rearrange("(k p) m -> p k m", p=128)))

    def emit_cast(v, i):
        v.wait(dL[i % 2], pdma[i])
        if i >= 4:
            v.wait(mT, kvproj(i - 4))
        mV.inc(v.tensor_copy(s["ws0"][:, (i % 4) * 2048:(i % 4 + 1) * 2048],
                             wl[i % 2][:]))
        assert mV.n == cast(i)

    def emit_qk(v, m):
        tgt = s["scr"][1] if m % 2 == 0 else s["scr"][2]
        v.wait(mS, kevac(m))
        if m >= 1:
            v.wait(mT, rot(m - 1))
            v.wait(mV, krope(m - 1))
        mV.inc(v.tensor_copy(s["qt"][:], tgt[:]))
        assert mV.n == qtcopy(m)
        v.wait(mT, rot(m))
        fV = P.sems["fV"]
        fV.inc(v.tensor_tensor(s["scr"][0][:], tgt[:], tbc, ALU.mult))
        fV.inc(v.tensor_tensor(s["wbc"][:], p["pC"][:], tbs, ALU.mult))
        v.wait(fV, fV.n)
        inst = v.tensor_tensor(s["k0s"][:, m * T + c * TC: m * T + (c + 1) * TC],
                               s["scr"][0][:], s["wbc"][:], ALU.add)
        mV.inc(inst)
        assert mV.n == krope(m)

    def vector_all(v):
        v.wait(dT, tbv[0])
        mV.inc(v.tensor_copy(tbc, s["scr"][1][:]))
        mV.inc(v.tensor_copy(tbs, s["scr"][2][:]))
        v.wait(dM, wrv[0])
        mV.inc(v.tensor_copy(s["adsl"][:, 0:128], s["scr"][0][:, 0:128]))
        for i in range(6):
            emit_cast(v, i)
        emit_qk(v, 0)
        emit_cast(v, 6)
        emit_qk(v, 1)
        emit_cast(v, 7)
        emit_qk(v, 2)
        emit_qk(v, 3)
        for idx in range(16):
            m, j = divmod(idx, NTT)
            v.wait(mT, tr(idx))
            kt = 4 * c + j
            inst = v.tensor_copy(s["v0s"][:, (m * 16 + kt) * 128:(m * 16 + kt + 1) * 128],
                                 p["pT0" if idx % 2 == 0 else "pT1"][:])
            mV.inc(inst)
            assert mV.n == vevac(idx)
        assert mV.n == v_end

    def emit_rot(t, m):
        t.wait(mV, qtcopy(m))
        if m >= 1:
            t.wait(mV, krope(m - 1))
        inst = t.matmul(p["pC"][:], s["rot0"][:], s["qt"][:], start=True, stop=True)
        mT.inc(inst)
        assert mT.n == rot(m)

    def emit_tr(t, idx):
        m, j = divmod(idx, NTT)
        t.wait(mS, kevac(4 + m))
        if idx >= 2:
            t.wait(mV, vevac(idx - 2))
        inst = t.transpose(p["pT0" if idx % 2 == 0 else "pT1"][:],
                           s["scrb"][:, j * 128:(j + 1) * 128],
                           s["identb"][:])
        mT.inc(inst)
        assert mT.n == tr(idx)

    def tensor_kv(t):
        for i in range(8):
            m = i % 4
            t.wait(mV, cast(i))
            if i >= 2:
                t.wait(mS, kevac(i - 2))
            pj = p["pA"] if i % 2 == 0 else p["pB"]
            for dk in range(NDK):
                inst = t.matmul(pj[:], s["ws0"][:, (m * 16 + dk) * 128:(m * 16 + dk + 1) * 128],
                                s["xnT"][:, dk * TC:(dk + 1) * TC],
                                start=(dk == 0), stop=(dk == NDK - 1))
            mT.inc(inst)
            assert mT.n == kvproj(i)
            if 2 <= i <= 5:
                emit_rot(t, i - 2)
            if i == 6:
                for idx in range(4):
                    emit_tr(t, idx)
        for idx in range(4, 16):
            emit_tr(t, idx)

    def scalar_kv(sc):
        for i in range(8):
            m = i % 4
            sc.wait(mT, kvproj(i))
            pj = p["pA"] if i % 2 == 0 else p["pB"]
            if i < 4:
                tgt = s["scr"][1] if m % 2 == 0 else s["scr"][2]
                if i >= 2:
                    sc.wait(mV, krope(i - 2))
                else:
                    sc.wait(mV, tbcast)
                inst = sc.activation(tgt[:], pj[:], AF.Copy)
            else:
                if m >= 1:
                    sc.wait(mT, tr(m * 4 - 1))
                inst = sc.activation(s["scrb"][:], pj[:], AF.Copy)
            mS.inc(inst)
            assert mS.n == kevac(i)

    P.sec("sync", sync)
    P.sec("tensor", tensor_kv)
    P.sec("scalar", scalar_kv)
    P.sec("vector", vector_all)
    P.barrier()


def ph_router0(P: Prog, c):
    s, d, p = P.s, P.d, P.p
    dS = P.sems["dS"]
    mV, mS, mT = P.sems["mV"], P.sems["mS"], P.sems["mT"]
    fV, fS = P.sems["fV"], P.sems["fS"]
    v0, t0, s0, b0 = mV.n, mT.n, mS.n, dS.n

    def R(tt, off, w=8):
        return s["rt"][:, tt * 64 + off: tt * 64 + off + w]

    def tensor_r(t):
        for tt in range(NTT):
            for dk in range(NDK):
                inst = t.matmul(p["pD"][:, tt * 8:(tt + 1) * 8],
                                s["xnT"][:, dk * TC + tt * 128: dk * TC + (tt + 1) * 128],
                                s["adsl"][:, dk * 8:(dk + 1) * 8],
                                start=(dk == 0), stop=(dk == NDK - 1))
            mT.inc(inst)

    def vector_lg(v):
        v.wait(mT, t0 + 4)
        for tt in range(NTT):
            inst = v.tensor_copy(R(tt, 0), p["pD"][:, tt * 8:(tt + 1) * 8])
            mV.inc(inst)

    def scalar_r(sc):
        sc.wait(mV, v0 + 4)
        for tt in range(NTT):
            mS.inc(sc.activation(R(tt, 56), R(tt, 0), AF.Sigmoid,
                                 scale=s["invtau"][:, 0:1]))
        sc.wait(mS, mS.n)
        for tt in range(NTT):
            mS.inc(sc.activation(R(tt, 8), R(tt, 56), AF.Abs, bias=s["crows"][:, 2:3]))
        sc.wait(mS, mS.n)
        for tt in range(NTT):
            mS.inc(sc.activation(R(tt, 24), R(tt, 8), AF.Exp, scale=-10.0))
        for tt in range(NTT):
            mS.inc(sc.activation(R(tt, 32), R(tt, 8), AF.Exp, scale=-(1.0 / 0.3)))

    def vector_r(v):
        v.wait(mS, s0 + 16)
        fV = P.sems["fV"]
        def grp(insts_done):
            fV.inc(insts_done)
            v.wait(fV, fV.n)
        for tt in range(NTT):
            i1 = v.tensor_scalar(R(tt, 16), R(tt, 56), GOLD_L, None, ALU.is_ge)
        grp(i1)
        for tt in range(NTT):
            i1 = v.tensor_scalar(R(tt, 40), R(tt, 56), GOLD_U, None, ALU.is_le)
        grp(i1)
        for tt in range(NTT):
            i1 = v.tensor_tensor(R(tt, 16), R(tt, 16), R(tt, 40), ALU.mult)
        grp(i1)
        for tt in range(NTT):
            i1 = v.tensor_tensor(R(tt, 24), R(tt, 24), R(tt, 16), ALU.mult)
        grp(i1)
        for tt in range(NTT):
            i1 = v.reduce_max(s["rstat"][:, tt:tt + 1], R(tt, 32), axis=AX.X)
        grp(i1)
        inst = v.memset(s["dmy"][:, 5:6], 0.0)
        mV.inc(inst)

    def sync_b1(q):
        q.wait(mV, v0 + 5)
        dS.dinc(q.dma_start(out=s["rstatb"][:, 0:4], in_=s["rstat"][:, 0:4]))

    def vector_r1(v):
        fV = P.sems["fV"]
        def grp(insts_done):
            fV.inc(insts_done)
            v.wait(fV, fV.n)
        v.wait(dS, b0 + 16)
        for tt in range(NTT):
            i1 = v.tensor_scalar(R(tt, 40), R(tt, 32), s["rstatb"][:, tt:tt + 1], None, ALU.is_ge)
        grp(i1)
        for tt in range(NTT):
            i1 = v.tensor_scalar(R(tt, 56), R(tt, 40), -1.0e9, None, ALU.mult)
        grp(i1)
        for tt in range(NTT):
            i1 = v.tensor_tensor(R(tt, 56), R(tt, 56), R(tt, 32), ALU.add)
        grp(i1)
        for tt in range(NTT):
            i1 = v.reduce_max(s["rstat"][:, 4 + tt:5 + tt], R(tt, 56), axis=AX.X)
        grp(i1)
        for tt in range(NTT):
            i1 = v.reduce_sum(s["rstat"][:, 24 + tt:25 + tt], R(tt, 24), axis=AX.X)
        grp(i1)
        i2 = v.tensor_scalar(s["rstat"][:, 8:12], s["rstat"][:, 24:28], 1e-8, None,
                             ALU.max)
        grp(i2)
        inst = v.reciprocal(s["rstat"][:, 28:32], s["rstat"][:, 8:12])
        mV.inc(inst)

    def sync_b2(q):
        q.wait(mV, v0 + 6)
        dS.dinc(q.dma_start(out=s["rstatb"][:, 4:12], in_=s["rstat"][:, 4:12]))
        dS.dinc(q.dma_start(out=s["rstatb"][:, 24:32], in_=s["rstat"][:, 24:32]))

    def vector_r2(v):
        fV = P.sems["fV"]
        def grp(insts_done):
            fV.inc(insts_done)
            v.wait(fV, fV.n)
        v.wait(dS, b0 + 48)
        for tt in range(NTT):
            i1 = v.tensor_scalar(R(tt, 40), R(tt, 32), s["rstatb"][:, 4 + tt:5 + tt],
                                 None, ALU.is_ge)
        grp(i1)
        for tt in range(NTT):
            i1 = v.tensor_tensor(R(tt, 48), R(tt, 32), R(tt, 40), ALU.mult)
        grp(i1)
        for tt in range(NTT):
            i1 = v.reduce_sum(s["rstat"][:, 12 + tt:13 + tt], R(tt, 48), axis=AX.X)
        grp(i1)
        i1 = v.tensor_scalar(s["rstat"][:, 16:20], s["rstatb"][:, 24:28], 1e-8, None, ALU.is_lt)
        i2 = v.tensor_scalar(s["rstat"][:, 20:24], s["rstatb"][:, 24:28], 1e-8, None, ALU.is_ge)
        grp(i2)
        i2 = v.tensor_scalar(s["rstat"][:, 12:16], s["rstat"][:, 12:16], 1e-8, None,
                             ALU.max)
        grp(i2)
        inst = v.reciprocal(s["rstat"][:, 32:36], s["rstat"][:, 12:16])
        mV.inc(inst)

    def sync_b3(q):
        q.wait(mV, v0 + 7)
        dS.dinc(q.dma_start(out=s["rstatb"][:, 12:24], in_=s["rstat"][:, 12:24]))
        dS.dinc(q.dma_start(out=s["rstatb"][:, 32:36], in_=s["rstat"][:, 32:36]))

    def vector_r3(v):
        fV = P.sems["fV"]
        v.wait(dS, b0 + 80)
        for tt in range(NTT):
            i1 = v.tensor_scalar(R(tt, 48), R(tt, 48), s["rstatb"][:, 32 + tt:33 + tt],
                                 s["rstatb"][:, 16 + tt:17 + tt], ALU.mult, ALU.mult)
        fV.inc(i1)
        for tt in range(NTT):
            i1 = v.tensor_scalar(R(tt, 24), R(tt, 24), s["rstatb"][:, 28 + tt:29 + tt],
                                 s["rstatb"][:, 20 + tt:21 + tt], ALU.mult, ALU.mult)
        fV.inc(i1)
        v.wait(fV, fV.n)
        for tt in range(NTT):
            inst = v.tensor_tensor(s["w8"][:, tt * 8:(tt + 1) * 8],
                                   R(tt, 48), R(tt, 24), ALU.add)
            mV.inc(inst)

    def tensor_w8(t):
        for tt in range(NTT):
            t.wait(mV, v0 + 8 + tt)
            if tt >= 1:
                t.wait(mV, v0 + 11 + tt)
            inst = t.transpose(p["pTf"][0:8, :], s["w8"][:, tt * 8:(tt + 1) * 8],
                               s["identf"][:])
            mT.inc(inst)

    def vector_w8(v):
        for tt in range(NTT):
            v.wait(mT, t0 + 4 + tt + 1)
            inst = v.tensor_copy(s["w8f"][0:8, tt * 128:(tt + 1) * 128], p["pTf"][0:8, :])
            mV.inc(inst)

    P.sec("tensor", tensor_r)
    P.sec("vector", vector_lg)
    P.sec("scalar", scalar_r)
    P.sec("vector", vector_r)
    P.sec("sync", sync_b1)
    P.sec("vector", vector_r1)
    P.sec("sync", sync_b2)
    P.sec("vector", vector_r2)
    P.sec("sync", sync_b3)
    P.sec("vector", vector_r3)
    P.sec("tensor", tensor_w8)
    P.sec("vector", vector_w8)
    P.barrier()


def ph_xga0(P: Prog, c):
    s, d, p = P.s, P.d, P.p
    mV, mS, mT = P.sems["mV"], P.sems["mS"], P.sems["mT"]
    v0, t0, s0 = mV.n, mT.n, mS.n
    wl = [s["wl0"], s["wl1"]]
    dL = [P.dL[0], P.dL[4]]
    adma = {}

    def sync(q):
        for i, (e, nm) in enumerate([(0, "ga0"), (0, "ua0"), (1, "ga0"), (1, "ua0")]):
            if i >= 2:
                q.wait(mV, v0 + (i - 1))
            adma[i] = dL[i % 2].dinc(q.dma_start(
                out=wl[i % 2][:, 0:1024].rearrange("p (k r) -> p k r", r=RADJ),
                in_=d[nm][e].rearrange("(k p) r -> p k r", p=128)))

    def vector(v):
        for i in range(4):
            v.wait(dL[i % 2], adma[i])
            if i >= 2:
                v.wait(mT, t0 + (i - 1))
            mV.inc(v.tensor_copy(s["adsl"][:, (i % 2) * 1024:(i % 2 + 1) * 1024],
                                 wl[i % 2][:, 0:1024]))

    def tensor(t):
        for i in range(4):
            t.wait(mV, v0 + i + 1)
            if i >= 2:
                t.wait(mS, s0 + (i - 1))
            pj = p["pA"] if i % 2 == 0 else p["pB"]
            for dk in range(NDK):
                inst = t.matmul(pj[0:64, :],
                                s["adsl"][:, (i % 2) * 1024 + dk * 64:(i % 2) * 1024 + (dk + 1) * 64],
                                s["xnT"][:, dk * TC:(dk + 1) * TC],
                                start=(dk == 0), stop=(dk == NDK - 1))
            mT.inc(inst)

    def scalar(sc):
        for i in range(4):
            sc.wait(mT, t0 + i + 1)
            pj = p["pA"] if i % 2 == 0 else p["pB"]
            e, which = divmod(i, 2)
            inst = sc.activation(s["xga"][0:64, (e * 2 + which) * TC:(e * 2 + which + 1) * TC],
                                 pj[0:64, :], AF.Copy)
            mS.inc(inst)

    P.sec("sync", sync)
    P.sec("vector", vector)
    P.sec("tensor", tensor)
    P.sec("scalar", scalar)
    P.barrier()


def ph_ffn0(P: Prog, c):
    s, d, p = P.s, P.d, P.p
    mV, mS, mT, mG = P.sems["mV"], P.sems["mS"], P.sems["mT"], P.sems["mG"]
    slots = [(s["wl0"], 0), (s["wl0"], 512), (s["wl0"], 1024), (s["wl0"], 1536),
             (s["wl1"], 0), (s["wl1"], 512), (s["wl1"], 1024), (s["wl1"], 1536)]
    nslot = len(slots)

    sync_ops, vec_ops, ten_ops, sca_ops, gps_ops = [], [], [], [], []

    class Ev:
        pass
    ev = Ev()
    ev.v, ev.t, ev.s, ev.g = mV.n, mT.n, mS.n, mG.n

    def S_dma(waits, out_ap, in_ap, slot):
        sem = P.dL[slot]
        sem.n += 16
        myval = sem.n
        def fn(q, out_ap=out_ap, in_ap=in_ap, sem=sem):
            inst = q.dma_start(out=out_ap, in_=in_ap)
            inst.then_inc(sem.h, 16)
            _UPDATED[id(inst)] = inst
        sync_ops.append((list(waits), fn))
        return (slot, myval)

    def V_op(waits, fn):
        ev.v += 1
        def wrap(v, fn=fn):
            mV.inc(fn(v))
        vec_ops.append((list(waits), wrap))
        return ev.v

    def T_group(waits, fn):
        ev.t += 1
        def wrap(t, fn=fn):
            mT.inc(fn(t))
        ten_ops.append((list(waits), wrap))
        return ev.t

    def C_op(waits, fn):
        ev.s += 1
        def wrap(sc, fn=fn):
            mS.inc(fn(sc))
        sca_ops.append((list(waits), wrap))
        return ev.s

    def G_op(waits, fn):
        ev.g += 1
        def wrap(g, fn=fn):
            mG.inc(fn(g))
        gps_ops.append((list(waits), wrap))
        return ev.g

    state = Ev()
    state.dmai = 0
    slot_last = [0] * nslot
    wb_guard = {"ws0": [], "ws1": []}

    def S_bf(waits, out_ap, in_ap, slot):
        sem = P.dL[slot]
        sem.n += 16
        myval = sem.n
        def fn(q, out_ap=out_ap, in_ap=in_ap, sem=sem):
            inst = q.dma_start(out=out_ap, in_=in_ap)
            inst.then_inc(sem.h, 16)
            _UPDATED[id(inst)] = inst
        sync_ops.append((list(waits), fn))
        return (slot, myval)

    def S_wb(cast_toks, dst, src_ap, slot):
        sem = P.dL[slot]
        sem.n += 16
        myval = sem.n
        w = [("v", m) for m in sorted({t for t in cast_toks if isinstance(t, int)})]
        for t in cast_toks:
            if isinstance(t, int):
                continue
        w = [("v", t) for t in cast_toks]
        def fn(q, dst=dst, src_ap=src_ap, sem=sem):
            inst = q.dma_start(out=dst.rearrange("p (a c) -> p a c", c=2048) if False
                               else dst, in_=src_ap)
            inst.then_inc(sem.h, 16)
            _UPDATED[id(inst)] = inst
        sync_ops.append((w, fn))
        return ("dl", (slot, myval))

    def stream_panel(src_ap, slab_ap, extra_cast_waits):
        width = src_ap.shape[-1]
        if width <= 512:
            sidx = state.dmai % nslot
            wlt, off = slots[sidx]
            w = [("v", slot_last[sidx])] if slot_last[sidx] else []
            dn = S_dma(w, wlt[:, off:off + width], src_ap, sidx)
            cn = V_op([("dl", dn)] + extra_cast_waits,
                      lambda v, wlt=wlt, off=off, width=width, slab_ap=slab_ap:
                      v.tensor_copy(slab_ap, wlt[:, off:off + width]))
            slot_last[sidx] = cn
            state.dmai += 1
            return cn
        assert width == 2048
        half = state.dmai % 2
        base = half * 4
        wlt = s["wl0"] if half == 0 else s["wl1"]
        w = [("v", m) for m in {slot_last[base + k] for k in range(4)} if m]
        dn = S_dma(w, wlt[:], src_ap, base)
        cn = V_op([("dl", dn)] + extra_cast_waits,
                  lambda v, wlt=wlt, slab_ap=slab_ap: v.tensor_copy(slab_ap, wlt[:]))
        for k in range(4):
            slot_last[base + k] = cn
        state.dmai += 1
        return cn

    def clobber_all_slots(val):
        for k in range(nslot):
            slot_last[k] = max(slot_last[k], val)

    last_gate_mm = 0
    last_up_mm = 0
    last_down_mm = 0
    bank_rd = {0: [], 1: []}
    wbc_rd = 0
    prev_wbc_mm = 0
    prev_silu = 0
    prev_mul = 0
    prev_pool = 0
    prev_mda_evac = 0

    for e in range(NE):
        prev_down_last = last_down_mm
        dMsem = P.sems["dM"]
        dMsem.n += 16
        dmval = dMsem.n
        gw = [("t", prev_wbc_mm)] if prev_wbc_mm else []
        def g_dma(q, e=e, dMsem=dMsem):
            inst = q.dma_start(out=P.row(2), in_=s["w8f"][e:e + 1, :])
            inst.then_inc(dMsem.h, 16)
            _UPDATED[id(inst)] = inst
        sync_ops.append((gw, g_dma))
        ww = list(bank_rd[1]) + [("dm", dmval)]
        if wbc_rd:
            ww.append(("g", wbc_rd))
        tw = T_group(ww, lambda t: t.matmul(
            p["pE"][0:1, :] if False else p["pE"][:], s["ones_row"][:],
            P.row(2), start=True, stop=True))
        prev_wbc_mm = tw
        wbc_done = C_op([("t", tw)], lambda sc: sc.activation(
            s["wbc"][:], p["pE"][:], AF.Copy))
        bank_rd[1] = [("s", wbc_done)]

        gbcast = ubcast = None
        if e < NL:
            dn = S_dma([("v", ev.v)], s["wl0"][0:64, :], d["gb0"][e], 0)
            gbcast = V_op([("dl", dn), ("t", last_gate_mm)],
                          lambda v: v.tensor_copy(s["adsl"][0:64, 0:2048],
                                                  s["wl0"][0:64, :]))
            dn2 = S_dma([("v", gbcast)], s["wl1"][0:64, :], d["ub0"][e], 4)
            ubcast = V_op([("dl", dn2), ("t", last_up_mm)],
                          lambda v: v.tensor_copy(s["adsl"][0:64, 2048:4096],
                                                  s["wl1"][0:64, :]))
            clobber_all_slots(ubcast)

        for fq in range(4):
            gcasts, ucasts = [], []
            if c == 0:
                for dk in range(NDK):
                    src = d["wg0"][e, dk * 128:(dk + 1) * 128, fq * 512:(fq + 1) * 512]
                    cwl = [("t", last_gate_mm), ("t", last_down_mm)] + wb_guard["ws0"]
                    gcasts.append(stream_panel(src, s["ws0"][:, dk * 512:(dk + 1) * 512], cwl))
                for dk in range(NDK):
                    src = d["wu0"][e, dk * 128:(dk + 1) * 128, fq * 512:(fq + 1) * 512]
                    cwl = [("t", last_up_mm), ("t", last_down_mm)] + wb_guard["ws1"]
                    ucasts.append(stream_panel(src, s["ws1"][:, dk * 512:(dk + 1) * 512], cwl))
                wb_guard["ws0"] = [S_wb(gcasts, d["wg0b"][e * 4 + fq], s["ws0"][:], 13)]
                wb_guard["ws1"] = [S_wb(ucasts, d["wu0b"][e * 4 + fq], s["ws1"][:], 14)]
            else:
                g_ld = S_bf(
                    [("t", last_gate_mm), ("t", last_down_mm)],
                    s["ws0"][:].rearrange("p (a c) -> p a c", c=2048),
                    d["wg0b"][e * 4 + fq].rearrange("p (a c) -> p a c", c=2048), 8)
                u_ld = S_bf(
                    [("t", last_up_mm), ("t", last_down_mm)],
                    s["ws1"][:].rearrange("p (a c) -> p a c", c=2048),
                    d["wu0b"][e * 4 + fq].rearrange("p (a c) -> p a c", c=2048), 12)
                gcasts = [("dl", g_ld)]
                ucasts = [("dl", u_ld)]
            for ftl in range(4):
                ft = fq * 4 + ftl

                def g_mm(t, e=e, ft=ft, ftl=ftl):
                    ad = e < NL
                    for dk in range(NDK):
                        inst = t.matmul(
                            p["pA"][:],
                            s["ws0"][:, dk * 512 + ftl * 128: dk * 512 + ftl * 128 + 128],
                            s["xnT"][:, dk * TC:(dk + 1) * TC],
                            start=(dk == 0), stop=(dk == NDK - 1) and not ad)
                    if ad:
                        inst = t.matmul(p["pA"][:],
                                        s["adsl"][0:64, ft * 128:(ft + 1) * 128],
                                        s["xga"][0:64, (e * 2) * TC:(e * 2 + 1) * TC],
                                        start=False, stop=True)
                    return inst

                gwl = [gcasts[0] if c > 0 else ("v", max(gcasts)), ("s", prev_silu)]
                if e < NL:
                    gwl.append(("v", gbcast))
                tg = T_group(gwl, g_mm)
                last_gate_mm = tg

                def u_mm(t, e=e, ft=ft, ftl=ftl):
                    ad = e < NL
                    for dk in range(NDK):
                        inst = t.matmul(
                            p["pB"][:],
                            s["ws1"][:, dk * 512 + ftl * 128: dk * 512 + ftl * 128 + 128],
                            s["xnT"][:, dk * TC:(dk + 1) * TC],
                            start=(dk == 0), stop=(dk == NDK - 1) and not ad)
                    if ad:
                        inst = t.matmul(p["pB"][:],
                                        s["adsl"][0:64, 2048 + ft * 128:2048 + (ft + 1) * 128],
                                        s["xga"][0:64, (e * 2 + 1) * TC:(e * 2 + 2) * TC],
                                        start=False, stop=True)
                    return inst

                uwl = [ucasts[0] if c > 0 else ("v", max(ucasts)), ("v", prev_mul)]
                if e < NL:
                    uwl.append(("v", ubcast))
                tu = T_group(uwl, u_mm)
                last_up_mm = tu

                cs = C_op([("t", tg), ("v", prev_mul)],
                          lambda sc: sc.activation(s["scr"][0][:], p["pA"][:], AF.Silu))
                prev_silu = cs
                mv_ = V_op([("s", cs), ("t", tu), ("g", prev_pool)],
                           lambda v: v.tensor_tensor(s["scr"][1][:], s["scr"][0][:],
                                                     p["pB"][:], ALU.mult))
                prev_mul = mv_
                pg = G_op([("v", mv_), ("s", wbc_done), ("t", prev_down_last)],
                          lambda g, ft=ft: g.tensor_tensor(
                              s["mid"][:, ft * TC:(ft + 1) * TC], s["scr"][1][:],
                              s["wbc"][:], ALU.mult))
                prev_pool = pg
        wbc_rd = prev_pool

        mda_done = None
        if e < NL:
            dn = S_dma([("v", ev.v)], s["wl0"][:, 0:1024].rearrange(
                "p (k r) -> p k r", r=RADJ),
                d["da0"][e].rearrange("(k p) r -> p k r", p=128), 0)
            dacast = V_op([("dl", dn), ("t", last_gate_mm)],
                          lambda v: v.tensor_copy(s["adsl"][:, 0:1024],
                                                  s["wl0"][:, 0:1024]))
            dn2 = S_dma([("v", dacast)], s["wl1"][0:64, :], d["db0"][e], 4)
            dbcast = V_op([("dl", dn2), ("t", last_up_mm)],
                          lambda v: v.tensor_copy(s["adsl"][0:64, 2048:4096],
                                                  s["wl1"][0:64, :]))
            clobber_all_slots(dbcast)

            def mda_mm(t):
                for fk in range(NDK):
                    inst = t.matmul(p["pD"][0:64, :],
                                    s["adsl"][:, fk * 64:(fk + 1) * 64],
                                    s["mid"][:, fk * TC:(fk + 1) * TC],
                                    start=(fk == 0), stop=(fk == NDK - 1))
                return inst
            mdw = [("v", dacast), ("g", prev_pool)]
            if prev_mda_evac:
                mdw.append(("s", prev_mda_evac))
            tmda = T_group(mdw, mda_mm)
            mda_done = C_op([("t", tmda)], lambda sc: sc.activation(
                s["mda"][0:64, :], p["pD"][0:64, :], AF.Copy))
            prev_mda_evac = mda_done

        for half in range(2):
            dcasts = []
            if c == 0:
                for fkl in range(8):
                    fk = half * 8 + fkl
                    src = d["wd0"][e, fk * 128:(fk + 1) * 128, :]
                    tgt = s["ws0"] if fkl < 4 else s["ws1"]
                    cwl = [("t", last_down_mm), ("t", last_gate_mm), ("t", last_up_mm)]
                    cwl += wb_guard["ws0"] if fkl < 4 else wb_guard["ws1"]
                    dcasts.append(stream_panel(
                        src, tgt[:, (fkl % 4) * 2048:(fkl % 4 + 1) * 2048], cwl))
                wb_guard["ws0"] = [S_wb(dcasts[:4], d["wd0b"][e * 4 + half * 2],
                                        s["ws0"][:], 13)]
                wb_guard["ws1"] = [S_wb(dcasts[4:], d["wd0b"][e * 4 + half * 2 + 1],
                                        s["ws1"][:], 14)]
                dtok = [("v", max(dcasts))]
            else:
                d_ld0 = S_bf([("t", last_down_mm), ("t", last_gate_mm), ("t", last_up_mm)],
                             s["ws0"][:].rearrange("p (a c) -> p a c", c=2048),
                             d["wd0b"][e * 4 + half * 2].rearrange("p (a c) -> p a c", c=2048), 8)
                d_ld1 = S_bf([("t", last_down_mm), ("t", last_up_mm)],
                             s["ws1"][:].rearrange("p (a c) -> p a c", c=2048),
                             d["wd0b"][e * 4 + half * 2 + 1].rearrange("p (a c) -> p a c", c=2048), 12)
                dtok = [("dl", d_ld0), ("dl", d_ld1)]
            for dt in range(NDK):
                bank = dt % 2
                pb = p["pC"] if bank == 0 else p["pE"]

                def d_mm(t, e=e, half=half, dt=dt, pb=pb):
                    ad = (half == 0 and e < NL)
                    for fkl in range(8):
                        fk = half * 8 + fkl
                        slab = s["ws0"] if fkl < 4 else s["ws1"]
                        inst = t.matmul(
                            pb[:],
                            slab[:, (fkl % 4) * 2048 + dt * 128:(fkl % 4) * 2048 + dt * 128 + 128],
                            s["mid"][:, fk * TC:(fk + 1) * TC],
                            start=(fkl == 0), stop=(fkl == 7) and not ad)
                    if ad:
                        inst = t.matmul(pb[:],
                                        s["adsl"][0:64, 2048 + dt * 128:2048 + (dt + 1) * 128],
                                        s["mda"][0:64, :],
                                        start=False, stop=True)
                    return inst

                dwl = list(dtok) + [("g", prev_pool)] + list(bank_rd[bank])
                if half == 0 and e < NL:
                    dwl.append(("s", mda_done))
                td = T_group(dwl, d_mm)
                last_down_mm = td

                def acc_fn(v, dt=dt, pb=pb, do_copy=(e == 0 and half == 0)):
                    if do_copy:
                        return v.tensor_copy(s["accx"][:, dt * TC:(dt + 1) * TC], pb[:])
                    return v.tensor_tensor(s["accx"][:, dt * TC:(dt + 1) * TC],
                                           s["accx"][:, dt * TC:(dt + 1) * TC],
                                           pb[:], ALU.add)
                av = V_op([("t", td)], acc_fn)
                bank_rd[bank] = [("v", av)]

    def run_ops(ops):
        def fn(eng):
            for waits, op in ops:
                seen = set()
                for kind, val in waits:
                    if val == 0 or val == (0, 0) or (kind, val) in seen:
                        continue
                    seen.add((kind, val))
                    if kind == "dl":
                        slot, v16 = val
                        eng.wait(P.dL[slot], v16)
                    else:
                        sem = {"v": mV, "t": mT, "s": mS, "g": mG,
                               "dm": P.sems["dM"]}[kind]
                        eng.wait(sem, val)
                op(eng)
        return fn

    P.sec("sync", run_ops(sync_ops))
    P.sec("tensor", run_ops(ten_ops))
    P.sec("scalar", run_ops(sca_ops))
    P.sec("vector", run_ops(vec_ops))
    P.sec("gpsimd", run_ops(gps_ops))
    assert mV.n == ev.v and mT.n == ev.t and mS.n == ev.s and mG.n == ev.g
    P.barrier()


class Sched:
    def __init__(self, P: Prog):
        self.P = P
        self.ops = {e: [] for e in ENGINES}
        self.sems = P.sems
        self.slots = [(P.s["wl0"], 0), (P.s["wl0"], 512), (P.s["wl0"], 1024),
                      (P.s["wl0"], 1536), (P.s["wl1"], 0), (P.s["wl1"], 512),
                      (P.s["wl1"], 1024), (P.s["wl1"], 1536)]
        self.slot_last = [0] * 8
        self.dmai = 0
        self.v = P.sems["mV"].n
        self.t = P.sems["mT"].n
        self.c = P.sems["mS"].n
        self.g = P.sems["mG"].n
        self.f = P.sems["fV"].n

    def _add(self, eng, waits, emit):
        self.ops[eng].append((list(waits), emit))

    def dma(self, waits, out_ap, in_ap, slot):
        sem = self.P.dL[slot]
        sem.n += 16
        val = sem.n
        def emit(q, out_ap=out_ap, in_ap=in_ap, sem=sem):
            inst = q.dma_start(out=out_ap, in_=in_ap)
            inst.then_inc(sem.h, 16)
            _UPDATED[id(inst)] = inst
        self._add("sync", waits, emit)
        return ("dl", (slot, val))

    def dma_plain(self, waits, out_ap, in_ap, semname="dM"):
        sem = self.sems[semname]
        sem.n += 16
        val = sem.n
        def emit(q, out_ap=out_ap, in_ap=in_ap, sem=sem):
            inst = q.dma_start(out=out_ap, in_=in_ap)
            inst.then_inc(sem.h, 16)
            _UPDATED[id(inst)] = inst
        self._add("sync", waits, emit)
        return (semname, val)

    def V(self, waits, fn, flush=False):
        self.v += 1
        def emit(v, fn=fn):
            self.sems["mV"].inc(fn(v))
        self._add("vector", waits, emit)
        tok = ("v", self.v)
        if flush:
            self._add("vector", [tok], lambda v: None)
        return tok

    def Vf(self, waits, fn):
        self.f += 1
        def emit(v, fn=fn):
            self.sems["fV"].inc(fn(v))
        self._add("vector", waits, emit)
        tok = ("f", self.f)
        self._add("vector", [tok], lambda v: None)
        return tok

    def T(self, waits, fn):
        self.t += 1
        def emit(t, fn=fn):
            self.sems["mT"].inc(fn(t))
        self._add("tensor", waits, emit)
        return ("t", self.t)

    def C(self, waits, fn):
        self.c += 1
        def emit(sc, fn=fn):
            self.sems["mS"].inc(fn(sc))
        self._add("scalar", waits, emit)
        return ("s", self.c)

    def G(self, waits, fn):
        self.g += 1
        def emit(g, fn=fn):
            self.sems["mG"].inc(fn(g))
        self._add("gpsimd", waits, emit)
        return ("g", self.g)

    def stream(self, waits, src_ap, slab_ap, extra_cast_waits=()):
        shape = list(src_ap.shape)
        width = 1
        for x in shape[1:]:
            width *= x
        inner = shape[-1]
        def land_view(wlt, off):
            flat = wlt[:, off:off + width]
            if len(shape) == 3:
                return flat.rearrange("p (k m) -> p k m", m=inner)
            return flat
        if width <= 512:
            sidx = self.dmai % 8
            wlt, off = self.slots[sidx]
            w = list(waits) + ([("v", self.slot_last[sidx])] if self.slot_last[sidx] else [])
            dn = self.dma(w, land_view(wlt, off), src_ap, sidx)
            cn = self.V([dn] + list(extra_cast_waits),
                        lambda v, wlt=wlt, off=off, width=width, slab_ap=slab_ap:
                        v.tensor_copy(slab_ap, wlt[:, off:off + width]))
            self.slot_last[sidx] = cn[1]
            self.dmai += 1
            return cn
        assert width in (1024, 2048)
        half = self.dmai % 2
        base = half * 4
        wlt = self.P.s["wl0"] if half == 0 else self.P.s["wl1"]
        w = list(waits) + [("v", m) for m in
                           sorted({self.slot_last[base + k] for k in range(4)}) if m]
        dn = self.dma(w, land_view(wlt, 0), src_ap, base)
        cn = self.V([dn] + list(extra_cast_waits),
                    lambda v, wlt=wlt, width=width, slab_ap=slab_ap:
                    v.tensor_copy(slab_ap, wlt[:, 0:width]))
        for k in range(4):
            self.slot_last[base + k] = cn[1]
        self.dmai += 1
        return cn

    def finish(self, barrier=True):
        P = self.P
        semmap = {"v": P.sems["mV"], "t": P.sems["mT"], "s": P.sems["mS"],
                  "g": P.sems["mG"], "f": P.sems["fV"], "dM": P.sems["dM"],
                  "dT": P.sems["dT"], "dS": P.sems["dS"], "dO": P.sems["dO"],
                  "dW": P.sems["dW"], "dX": P.sems["dX"]}
        for eng in ENGINES:
            ops = self.ops[eng]
            if not ops:
                continue
            def fn(e, ops=ops):
                for waits, emit in ops:
                    seen = set()
                    for kind, val in waits:
                        if (kind, val) in seen:
                            continue
                        seen.add((kind, val))
                        if kind == "dl":
                            slot, v16 = val
                            e.wait(P.dL[slot], v16)
                        else:
                            if val == 0:
                                continue
                            e.wait(semmap[kind], val)
                    emit(e)
            P.sec(eng, fn)
        assert P.sems["mV"].n == self.v and P.sems["mT"].n == self.t
        assert P.sems["mS"].n == self.c and P.sems["mG"].n == self.g
        assert P.sems["fV"].n == self.f
        if barrier:
            P.barrier()


def _seg_kinds(segs):
    return segs


def ph_attn0(P: Prog, c):
    s, d, p = P.s, P.d, P.p
    S = Sched(P)
    band = _band0(c)
    nkt = len(band)
    tbc = s["tb"][:, 0:TC]
    tbs = s["tb"][:, TC:2 * TC]

    prev_qevac_tok = prev_qrope_tok = prev_afm_tok = prev_rot_tok = None
    prev_tb_tok = None
    qproj_tok = {}
    for h in range(L0_NH):
        kh = h // 4
        cast_w = []
        if h >= 4:
            cast_w.append(qproj_tok[h - 4])
        cn = S.stream([], d["wq0"][:, h * 128:(h + 1) * 128].rearrange(
            "(k p) m -> p k m", p=128),
            s["ws1"][:, (h % 4) * 2048:(h % 4 + 1) * 2048], cast_w)
        qw = [cn]
        if prev_qevac_tok:
            qw.append(prev_qevac_tok)
        def qmm(t, h=h):
            for dk in range(NDK):
                inst = t.matmul(p["pA"][:],
                                s["ws1"][:, (h % 4) * 2048 + dk * 128:(h % 4) * 2048 + (dk + 1) * 128],
                                s["xnT"][:, dk * TC:(dk + 1) * TC],
                                start=(dk == 0), stop=(dk == NDK - 1))
            return inst
        tq = S.T(qw, qmm)
        qproj_tok[h] = tq
        ew = [tq]
        if prev_qrope_tok:
            ew.append(prev_qrope_tok)
        ce = S.C(ew, lambda sc: sc.activation(s["scr"][1][:], p["pA"][:], AF.Copy))
        prev_qevac_tok = ce
        bw = [ce]
        if prev_rot_tok:
            bw.append(prev_rot_tok)
        vb = S.V(bw, lambda v: v.tensor_copy(s["qt"][:], s["scr"][1][:]))
        rw = [vb]
        if prev_qrope_tok:
            rw.append(prev_qrope_tok)
        tr_ = S.T(rw, lambda t: t.matmul(p["pC"][:], s["rot0"][:], s["qt"][:],
                                         start=True, stop=True))
        prev_rot_tok = tr_
        f1 = S.Vf([tr_], lambda v: v.tensor_tensor(s["wbc"][:], s["scr"][1][:], tbc, ALU.mult))
        f2 = S.Vf([f1], lambda v: v.tensor_tensor(s["scr"][1][:], p["pC"][:], tbs, ALU.mult))
        vq = S.V([f2], lambda v: v.tensor_tensor(s["qt"][:], s["wbc"][:], s["scr"][1][:],
                                                 ALU.add))
        prev_qrope_tok = vq
        prev_tanh = {}
        prev_exp = {}
        prev_pv = {}
        for i, (kt, segs) in enumerate(band):
            psc = p["pD"] if i % 2 == 0 else p["pA"]
            sw = [vq]
            if i % 2 == 0 and i >= 2:
                sw.append(prev_tanh[i - 2])
            if i % 2 == 1:
                sw.append(ce)
                if i >= 3:
                    sw.append(prev_tanh[i - 2])
            tsc = S.T(sw, lambda t, kt=kt, psc=psc, kh=kh: t.matmul(
                psc[:], s["k0s"][:, kh * T + kt * 128: kh * T + (kt + 1) * 128],
                s["qt"][:], start=True, stop=True))
            tw = [tsc]
            if i >= 1:
                tw.append(prev_exp[i - 1])
            cth = S.C(tw, lambda sc, psc=psc: sc.activation(
                s["scr"][0][:], psc[:], AF.Tanh, scale=1.0 / (L0_CAP * math.sqrt(L0_HD))))
            prev_tanh[i] = cth
            mtoks = []
            for tl, kind in enumerate(segs):
                if kind in ("diag", "edge"):
                    mask = s["mdiag"] if kind == "diag" else s["medge"]
                    mtoks.append(S.V([cth], lambda v, tl=tl, mask=mask: v.tensor_tensor(
                        s["scr"][0][:, tl * 128:(tl + 1) * 128],
                        s["scr"][0][:, tl * 128:(tl + 1) * 128], mask[:], ALU.add)))
            xw = [cth] + mtoks
            if i >= 2:
                xw.append(prev_pv[i - 2])
            cxp = S.C(xw, lambda sc, i=i: sc.activation(
                s["pst"][i % 2][:], s["scr"][0][:], AF.Exp, scale=L0_CAP))
            prev_exp[i] = cxp
            ztoks = []
            for tl, kind in enumerate(segs):
                if kind == "zero":
                    ztoks.append(S.V([cxp], lambda v, tl=tl, i=i: v.memset(
                        s["pst"][i % 2][:, tl * 128:(tl + 1) * 128], 0.0)))
            pw = [cxp] + ztoks
            if i == 0 and prev_afm_tok:
                pw.append(prev_afm_tok)
            t1 = S.T(pw, lambda t, kt=kt, i=i, kh=kh: t.matmul(
                p["pB"][:], s["v0s"][:, (kh * 16 + kt) * 128:(kh * 16 + kt + 1) * 128],
                s["pst"][i % 2][:], start=(i == 0), stop=(i == nkt - 1)))
            t2 = S.T([], lambda t, i=i: t.matmul(
                p["pE"][0:1, :], s["ones_col"][:], s["pst"][i % 2][:],
                start=(i == 0), stop=(i == nkt - 1)))
            prev_pv[i] = t2
        vrw = [prev_pv[nkt - 1]]
        if prev_tb_tok:
            vrw.append(prev_tb_tok)
        vr = S.V(vrw, lambda v: v.reciprocal(P.row(2), p["pE"][0:1, :]))
        bw2 = [vr, prev_tanh[nkt - 1] if (nkt - 1) % 2 == 0 else prev_tanh[nkt - 2]]
        tb_ = S.T(bw2, lambda t: t.matmul(p["pD"][:], s["ones_row"][:],
                                          P.row(2), start=True, stop=True))
        prev_tb_tok = tb_
        bev_w = [tb_]
        if prev_afm_tok:
            bev_w.append(prev_afm_tok)
        cbe = S.C(bev_w, lambda sc: sc.activation(s["scr"][2][:], p["pD"][:], AF.Copy))
        va = S.V([cbe, prev_pv[nkt - 1], t1], lambda v, h=h: v.tensor_tensor(
            s["afm"][:, h * TC:(h + 1) * TC], p["pB"][:], s["scr"][2][:], ALU.mult))
        prev_afm_tok = va
    S.finish()


def ph_comb0(P: Prog, c):
    s, d, p = P.s, P.d, P.p
    S = Sched(P)

    t_s = S.T([], lambda t: t.matmul(p["pD"][:], s["ones_row"][:], P.row(0),
                                     start=True, stop=True))
    t_m = S.T([], lambda t: t.matmul(p["pE"][:], s["ones_row"][:], P.row(1),
                                     start=True, stop=True))
    c_s = S.C([t_s], lambda sc: sc.activation(s["scr"][1][:], p["pD"][:], AF.Copy))
    c_m = S.C([t_m], lambda sc: sc.activation(s["scr"][2][:], p["pE"][:], AF.Copy))

    prev_comb = {}
    prev_silu = {}
    prev_t1 = {}
    prev_dense = {}
    casts = {}
    sq_mm = {}
    sq_op = {}
    for dt in range(NDK):
        cw = []
        if dt >= 4:
            cw.append(prev_dense[dt - 4])
        casts[dt] = S.stream([], d["wo0"][:, dt * 128:(dt + 1) * 128].rearrange(
            "(k p) m -> p k m", p=128), s["ws0"][:, (dt % 4) * 2048:(dt % 4 + 1) * 2048],
            cw)
        pb = p["pA"] if dt % 2 == 0 else p["pB"]
        dwl = [casts[dt]]
        if dt >= 2:
            dwl.append(prev_t1[dt - 2])

        def dmm(t, dt=dt, pb=pb):
            for ht in range(L0_NH):
                inst = t.matmul(pb[:],
                                s["ws0"][:, (dt % 4) * 2048 + ht * 128:(dt % 4) * 2048 + ht * 128 + 128],
                                s["afm"][:, ht * TC:(ht + 1) * TC],
                                start=(ht == 0), stop=(ht == L0_NH - 1))
            return inst
        td = S.T(dwl, dmm)
        prev_dense[dt] = td
        sw = [c_s, c_m]
        if dt >= 1:
            sw.append(prev_t1[dt - 1])
        csl = S.C(sw, lambda sc, dt=dt: sc.activation(
            s["scr"][0][:], s["accx"][:, dt * TC:(dt + 1) * TC], AF.Silu))
        prev_silu[dt] = csl
        t1w = [td, csl]
        t1_ = S.Vf(t1w, lambda v, pb=pb: v.tensor_tensor(
            s["wbc"][:], pb[:], s["scr"][0][:], ALU.mult))
        prev_t1[dt] = t1_
        a1 = S.Vf([t1_, csl], lambda v, dt=dt: v.tensor_tensor(
            s["accx"][:, dt * TC:(dt + 1) * TC],
            s["xnT"][:, dt * TC:(dt + 1) * TC], s["scr"][1][:], ALU.mult))
        a2 = S.Vf([a1], lambda v, dt=dt: v.tensor_tensor(
            s["accx"][:, dt * TC:(dt + 1) * TC],
            s["accx"][:, dt * TC:(dt + 1) * TC], s["scr"][2][:], ALU.add))
        a3 = S.V([a2], lambda v, dt=dt: v.tensor_tensor(
            s["accx"][:, dt * TC:(dt + 1) * TC],
            s["accx"][:, dt * TC:(dt + 1) * TC], s["wbc"][:], ALU.add))
        prev_comb[dt] = a3

    for dt in range(NDK):
        sqw = [prev_comb[dt]]
        if dt >= 1:
            sqw.append(sq_mm[dt - 1])
        else:
            sqw.append(prev_t1[NDK - 1])
        csq = S.C(sqw, lambda sc, dt=dt: sc.activation(
            s["scr"][0][:], s["accx"][:, dt * TC:(dt + 1) * TC], AF.Square))
        sq_op[dt] = csq
        m1w = [prev_comb[dt]]
        if dt == 0:
            m1w += [prev_t1[NDK - 1], prev_t1[NDK - 2]]
        t1_ = S.T(m1w, lambda t, dt=dt: t.matmul(
            p["pC"][0:1, :], s["crows"][:, 4:5], s["accx"][:, dt * TC:(dt + 1) * TC],
            start=(dt == 0), stop=(dt == NDK - 1)))
        t2_ = S.T([csq], lambda t, dt=dt: t.matmul(
            p["pE"][0:1, :], s["crows"][:, 4:5], s["scr"][0][:],
            start=(dt == 0), stop=(dt == NDK - 1)))
        sq_mm[dt] = t2_
    sum_mm = t1_
    ssq_mm = t2_

    M0, M1, M2 = P.row(0), P.row(1), P.row(2)
    f1 = S.Vf([sum_mm], lambda v: v.tensor_scalar(M0, p["pC"][0:1, :], 1.0 / D, None, ALU.mult))
    f2 = S.Vf([ssq_mm, f1], lambda v: v.tensor_scalar(M1, p["pE"][0:1, :], 1.0 / D, None, ALU.mult))
    f3 = S.Vf([f2], lambda v: v.tensor_tensor(M2, M0, M0, ALU.mult))
    f4 = S.V([f3], lambda v: v.tensor_tensor(M1, M1, M2, ALU.subtract))
    c_sq = S.C([f4], lambda sc: sc.activation(M2, M1, AF.Sqrt, bias=s["crows"][0:1, 1:2]))
    f5 = S.Vf([c_sq], lambda v: v.reciprocal(M1, M2))
    f6 = S.Vf([f5], lambda v: v.tensor_tensor(M2, M0, M1, ALU.mult))
    f7 = S.V([f6], lambda v: v.tensor_scalar(M2, M2, -1.0, None, ALU.mult))
    fp1 = S.Vf([f7], lambda v: v.tensor_copy(P.row(3), M0))
    fp2 = S.Vf([fp1], lambda v: v.tensor_copy(P.row(4), M1))

    tb1 = S.T([f5, f7, fp2] + [prev_t1[NDK - 1]], lambda t: t.matmul(
        p["pD"][:], s["ones_row"][:], M1, start=True, stop=True))
    tb2 = S.T([f7], lambda t: t.matmul(
        p["pE"][:], s["ones_row"][:], M2, start=True, stop=True))
    cb1w = [tb1, prev_silu[NDK - 1]]
    cb1 = S.C(cb1w, lambda sc: sc.activation(s["scr"][1][:], p["pD"][:], AF.Copy))
    cb2 = S.C([tb2], lambda sc: sc.activation(s["scr"][2][:], p["pE"][:], AF.Copy))

    prev_add = None
    for dt in range(NDK):
        uw = [cb1, cb2, sq_mm[NDK - 1], sum_mm]
        if prev_add:
            uw.append(prev_add)
        u1 = S.Vf(uw, lambda v, dt=dt: v.tensor_tensor(
            s["wbc"][:], s["accx"][:, dt * TC:(dt + 1) * TC],
            s["scr"][1][:], ALU.mult))
        prev_add = S.V([u1], lambda v, dt=dt: v.tensor_tensor(
            s["xnT"][:, dt * TC:(dt + 1) * TC], s["wbc"][:], s["scr"][2][:], ALU.add))
    S.finish()


def ph_prep1(P: Prog, c):
    s, d, p = P.s, P.d, P.p
    S = Sched(P)
    d1 = S.dma_plain([], s["scr"][1][:], d["c_cos1"][:, c * TC:(c + 1) * TC], "dT")
    d2 = S.dma_plain([], s["scr"][2][:], d["c_sin1"][:, c * TC:(c + 1) * TC], "dT")
    tb1 = S.V([d2], lambda v: v.tensor_copy(s["tb"][:, 0:TC], s["scr"][1][:]))
    tb2 = S.V([tb1], lambda v: v.tensor_copy(s["tb"][:, TC:2 * TC], s["scr"][2][:]))
    tbc = s["tb"][0:64, 0:TC]
    tbs = s["tb"][0:64, TC:2 * TC]
    cn = S.stream([], d["wqkv1"][:, 2048:2176].rearrange("(k p) m -> p k m", p=128),
                  s["ws0"][:, 0:2048])
    def kvmm(t):
        for dk in range(NDK):
            inst = t.matmul(p["pA"][:], s["ws0"][:, dk * 128:(dk + 1) * 128],
                            s["xnT"][:, dk * TC:(dk + 1) * TC],
                            start=(dk == 0), stop=(dk == NDK - 1))
        return inst
    tkv = S.T([cn], kvmm)
    ce = S.C([tkv, tb2], lambda sc: sc.activation(s["scr"][1][0:64, :], p["pA"][0:64, :],
                                                  AF.Copy))
    vb = S.V([ce], lambda v: v.tensor_copy(s["qt"][0:64, :], s["scr"][1][0:64, :]))
    tr_ = S.T([vb], lambda t: t.matmul(p["pC"][0:64, :], s["rot1"][0:64, 0:64],
                                       s["qt"][0:64, :], start=True, stop=True))
    f1 = S.Vf([tr_], lambda v: v.tensor_tensor(s["wbc"][0:64, :], s["scr"][1][0:64, :],
                                               tbc, ALU.mult))
    f2 = S.Vf([f1], lambda v: v.tensor_tensor(s["scr"][1][0:64, :], p["pC"][0:64, :],
                                              tbs, ALU.mult))
    k1w = S.V([f2], lambda v: v.tensor_tensor(
        s["k1s"][0:64, c * TC:(c + 1) * TC], s["wbc"][0:64, :], s["scr"][1][0:64, :],
        ALU.add))
    S.V([k1w], lambda v: v.tensor_copy(s["k1s"][64:128, c * TC:(c + 1) * TC],
                                       s["k1s"][0:64, c * TC:(c + 1) * TC]))
    cv = S.C([tkv], lambda sc: sc.activation(s["scrb"][0:64, :], p["pA"][64:128, :],
                                             AF.Copy))
    prev_ev = None
    for j in range(NTT):
        tw = [cv]
        if prev_ev is not None and j >= 2:
            tw.append(evs[j - 2])
        tt_ = S.T(tw, lambda t, j=j: t.transpose(
            p["pT0" if j % 2 == 0 else "pT1"][:, 0:64],
            s["scrb"][0:64, j * 128:(j + 1) * 128], s["identb"][0:64, 0:64]))
        kt = 4 * c + j
        ev = S.V([tt_], lambda v, j=j, kt=kt: v.tensor_copy(
            s["v1s"][:, kt * 64:(kt + 1) * 64],
            p["pT0" if j % 2 == 0 else "pT1"][:, 0:64]))
        if j == 0:
            evs = {}
        evs[j] = ev
        prev_ev = ev
    S.finish()


def ph_attn1(P: Prog, c):
    s, d, p = P.s, P.d, P.p
    S = Sched(P)
    band = _band1(c)
    nkt = len(band)
    tbcF = s["tb"][:, 0:TC]
    tbsF = s["tb"][:, TC:2 * TC]

    prev_qevac = prev_qrope = prev_afm = prev_rot = None
    prev_bc = None
    qproj_tok = {}
    for j in range(L1_NH // 2):
        cast_w = []
        if j >= 4:
            cast_w.append(qproj_tok[j - 4])
        cn = S.stream([], d["wqkv1"][:, j * 128:(j + 1) * 128].rearrange(
            "(k p) m -> p k m", p=128),
            s["ws1"][:, (j % 4) * 2048:(j % 4 + 1) * 2048], cast_w)
        qw = [cn]
        if prev_qevac:
            qw.append(prev_qevac)
        def qmm(t, j=j):
            for dk in range(NDK):
                inst = t.matmul(p["pA"][:],
                                s["ws1"][:, (j % 4) * 2048 + dk * 128:(j % 4) * 2048 + (dk + 1) * 128],
                                s["xnT"][:, dk * TC:(dk + 1) * TC],
                                start=(dk == 0), stop=(dk == NDK - 1))
            return inst
        tq = S.T(qw, qmm)
        qproj_tok[j] = tq
        ew = [tq]
        if prev_qrope:
            ew.append(prev_qrope)
        ce = S.C(ew, lambda sc: sc.activation(s["scr"][1][:], p["pA"][:], AF.Copy))
        prev_qevac = ce
        bw = [ce]
        if prev_rot:
            bw.append(prev_rot)
        vb = S.V(bw, lambda v: v.tensor_copy(s["qt"][:], s["scr"][1][:]))
        rw = [vb]
        if prev_qrope:
            rw.append(prev_qrope)
        tro = S.T(rw, lambda t: t.matmul(p["pC"][:], s["rot1"][:], s["qt"][:],
                                         start=True, stop=True))
        prev_rot = tro
        f1 = S.Vf([tro], lambda v: v.tensor_tensor(s["wbc"][:], s["scr"][1][:], tbcF, ALU.mult))
        f2 = S.Vf([f1], lambda v: v.tensor_tensor(s["scr"][1][:], p["pC"][:], tbsF, ALU.mult))
        vq = S.V([f2], lambda v: v.tensor_tensor(s["qt"][:], s["wbc"][:], s["scr"][1][:],
                                                 ALU.add))
        prev_qrope = vq

        ctx_evac = {}
        recips = {}
        for par in range(2):
            b0 = par * 64
            prev_exp = {}
            prev_pv = {}
            prev_diag = None
            for i, (kt, segs) in enumerate(band):
                psc = p["pD"] if i % 2 == 0 else p["pC"]
                sw = [vq]
                if i >= 2:
                    sw.append(prev_exp[i - 2])
                if i < 2 and par == 1:
                    sw.append(ctx_evac[0])
                if i < 2 and par == 0 and prev_bc:
                    sw.append(prev_bc)
                tsc = S.T(sw, lambda t, kt=kt, psc=psc, b0=b0: t.matmul(
                    psc[:], s["k1s"][b0:b0 + 64, kt * 128:(kt + 1) * 128],
                    s["qt"][b0:b0 + 64, :], start=True, stop=True))
                etoks = []
                for tl, kind in enumerate(segs):
                    sl = slice(tl * 128, (tl + 1) * 128)
                    if kind == "keep":
                        w = [tsc]
                        if i >= 2:
                            w.append(prev_pv[i - 2])
                        etoks.append(S.C(w, lambda sc, psc=psc, sl=sl, i=i:
                                         sc.activation(s["pst"][i % 2][:, sl], psc[:, sl],
                                                       AF.Exp, scale=0.125,
                                                       bias=s["crows"][:, 0:1])))
                    elif kind == "diag":
                        mw = [tsc]
                        if prev_diag:
                            mw.append(prev_diag)
                        vm = S.Vf(mw, lambda v, psc=psc, sl=sl: v.tensor_tensor(
                            s["scr"][0][:, 0:128], psc[:, sl], s["mdiag"][:], ALU.add))
                        w = [vm]
                        if i >= 2:
                            w.append(prev_pv[i - 2])
                        tok = S.C(w, lambda sc, sl=sl, i=i: sc.activation(
                            s["pst"][i % 2][:, sl], s["scr"][0][:, 0:128],
                            AF.Exp, scale=0.125, bias=s["crows"][:, 0:1]))
                        etoks.append(tok)
                        prev_diag = tok
                    else:
                        w = [tsc]
                        if i >= 2:
                            w.append(prev_pv[i - 2])
                        etoks.append(S.V(w, lambda v, sl=sl, i=i: v.memset(
                            s["pst"][i % 2][:, sl], 0.0)))
                prev_exp[i] = etoks[-1] if etoks else tsc
                pw = list(etoks)
                if i == 0:
                    if par == 1:
                        pw.append(ctx_evac[0])
                        pw.append(recips[0])
                    elif prev_afm:
                        pw.append(prev_afm)
                t1 = S.T(pw, lambda t, kt=kt, i=i: t.matmul(
                    p["pB"][0:64, :], s["v1s"][:, kt * 64:(kt + 1) * 64],
                    s["pst"][i % 2][:], start=(i == 0), stop=(i == nkt - 1)))
                t2 = S.T([], lambda t, i=i: t.matmul(
                    p["pE"][0:1, :], s["ones_col"][:], s["pst"][i % 2][:],
                    start=(i == 0), stop=(i == nkt - 1)))
                prev_pv[i] = t2
            rrow = P.row(2) if par == 0 else P.row(0)
            vrw = [prev_pv[nkt - 1]]
            if par == 0 and prev_bc:
                vrw.append(prev_bc)
            vr = S.V(vrw, lambda v, rrow=rrow: v.reciprocal(rrow, p["pE"][0:1, :]))
            recips[par] = vr
            cw = [prev_pv[nkt - 1], t1]
            if par == 0 and prev_afm:
                cw.append(prev_afm)
            cev = S.C(cw, lambda sc, b0=b0: sc.activation(
                s["scrb"][b0:b0 + 64, :], p["pB"][0:64, :], AF.Copy))
            ctx_evac[par] = cev
        tb1_ = S.T([recips[0], recips[1], ctx_evac[1]], lambda t: t.matmul(
            p["pD"][0:64, :], s["ones_row"][0:1, 0:64], P.row(2),
            start=True, stop=True, tile_position=(0, 0)))
        tb2_ = S.T([], lambda t: t.matmul(
            p["pD"][64:128, :], s["ones_row"][0:1, 0:64], P.row(0),
            start=True, stop=True, tile_position=(0, 64)))
        cw = [tb1_, tb2_]
        if prev_afm:
            cw.append(prev_afm)
        cbe = S.C(cw, lambda sc: sc.activation(s["scr"][2][:], p["pD"][:], AF.Copy))
        prev_bc = cbe
        va = S.V([cbe, ctx_evac[1]], lambda v, j=j: v.tensor_tensor(
            s["afm"][:, j * TC:(j + 1) * TC], s["scrb"][:], s["scr"][2][:], ALU.mult))
        prev_afm = va
    S.finish()


def ph_dequant1(P: Prog):
    s, d, p = P.s, P.d, P.p
    S = Sched(P)
    dO = P.sems["dO"]

    def panel_gateup(which, it, prev_toks):
        src = d["g1q" if which == 0 else "u1q"]
        ssc = d["g1s" if which == 0 else "u1s"]
        tgt = d["wg1T" if which == 0 else "wu1T"]
        half = S.dmai_parity = it % 2
        wlt = s["wl0"] if half == 0 else s["wl1"]
        dsc = S.dma_plain([], s["rstat"][:, 40 + which:41 + which],
                          ssc[it * 128:(it + 1) * 128, :], "dT")
        dw = []
        if prev_toks.get(("cast", half)):
            dw.append(prev_toks[("cast", half)])
        dn = S.dma(dw, wlt[:].bitcast(I32), src[it * 128:(it + 1) * 128, :],
                   half * 4)
        cw = [dn, dsc]
        if prev_toks.get(("mmread", half)):
            cw.append(prev_toks[("mmread", half)])
        ct = S.V(cw, lambda v, wlt=wlt, which=which, half=half: v.tensor_scalar(
            s["ws0"][:, half * 4096:half * 4096 + 2048],
            wlt[:].bitcast(I32), 128.0, s["rstat"][:, 40 + which:41 + which],
            ALU.subtract, ALU.mult))
        prev_toks[("cast", half)] = ct
        evs = []
        for k in range(NDK):
            tw = [ct]
            if len(evs) >= 2:
                tw.append(evs[-2])
            elif prev_toks.get(("ev", half)):
                tw.append(prev_toks[("ev", half)])
            ttk = S.T(tw, lambda t, k=k, half=half: t.transpose(
                p["pT0" if k % 2 == 0 else "pT1"][:],
                s["ws0"][:, half * 4096 + k * 128: half * 4096 + (k + 1) * 128],
                s["identb"][:]))
            ew = [ttk]
            if prev_toks.get(("out", half)):
                ew.append(prev_toks[("out", half)])
            ev = S.V(ew, lambda v, k=k, half=half: v.tensor_copy(
                s["ws1"][:, half * 4096 + k * 128: half * 4096 + (k + 1) * 128],
                p["pT0" if k % 2 == 0 else "pT1"][:]))
            evs.append(ev)
        prev_toks[("ev", half)] = evs[-1]
        prev_toks[("mmread", half)] = evs[-1]
        od = S.dma([evs[-1]], tgt[it], s["ws1"][:, half * 4096:half * 4096 + 2048],
                   12 + half)
        prev_toks[("out", half)] = od
        return od

    toks = {}
    for it in range(I1 // 128):
        panel_gateup(0, it, toks)
        panel_gateup(1, it, toks)

    toks2 = {"out": [toks.get(("out", 0)), toks.get(("out", 1))]}
    toks2["out"] = [x for x in toks2["out"] if x]
    for hp in range(NDK):
        dsc = S.dma_plain([], s["rstat"][:, 42:43], d["d1s"][hp * 128:(hp + 1) * 128, :],
                          "dT")
        evs_all = []
        for sub in range(4):
            half = sub % 2
            wlt = s["wl0"] if half == 0 else s["wl1"]
            dw = []
            if toks2.get(("cast", half)):
                dw.append(toks2[("cast", half)])
            dn = S.dma(dw, wlt[:].bitcast(I32),
                       d["d1q"][hp * 128:(hp + 1) * 128, sub * 2048:(sub + 1) * 2048],
                       half * 4)
            cw = [dn, dsc]
            if toks2.get(("tr", half)):
                cw.append(toks2[("tr", half)])
            ct = S.V(cw, lambda v, wlt=wlt, half=half: v.tensor_scalar(
                s["ws0"][:, half * 4096:half * 4096 + 2048],
                wlt[:].bitcast(I32), 128.0, s["rstat"][:, 42:43],
                ALU.subtract, ALU.mult))
            toks2[("cast", half)] = ct
            evs = []
            for k in range(NDK):
                tw = [ct]
                if len(evs) >= 2:
                    tw.append(evs[-2])
                elif evs_all:
                    tw.append(evs_all[-2] if len(evs_all) >= 2 else evs_all[-1])
                ttk = S.T(tw, lambda t, k=k, half=half: t.transpose(
                    p["pT0" if k % 2 == 0 else "pT1"][:],
                    s["ws0"][:, half * 4096 + k * 128: half * 4096 + (k + 1) * 128],
                    s["identb"][:]))
                toks2[("tr", half)] = ttk
                ew = [ttk] + list(toks2.get("out") or [])
                ev = S.V(ew, lambda v, k=k, sub=sub: v.tensor_copy(
                    s["ws1"][:, sub * 2048 + k * 128: sub * 2048 + (k + 1) * 128],
                    p["pT0" if k % 2 == 0 else "pT1"][:]))
                evs.append(ev)
                evs_all.append(ev)
        od = S.dma([evs_all[-1]], d["wd1T"][hp], s["ws1"][:], 14)
        toks2["out"] = [od]
    S.finish()


def ph_ffn1(P: Prog, c):
    s, d, p = P.s, P.d, P.p
    S = Sched(P)
    prev_silu = prev_mul = None
    prev_gmm = prev_umm = prev_dmm = None
    bank_rd = {0: None, 1: None}
    for q1 in range(4):
        mids = {}
        for itg in range(4):
            itbase = q1 * 16 + itg * 4
            gw = []
            if prev_dmm:
                gw.append(prev_dmm)
            if prev_gmm:
                gw.append(prev_gmm)
            gcast = S.dma(gw, s["ws0"][:].rearrange("p (it c) -> p it c", c=2048),
                          d["wg1T"][itbase:itbase + 4].rearrange("it p c -> p it c"), 8)
            ucast = S.dma([prev_umm] if prev_umm else [],
                          s["ws1"][:].rearrange("p (it c) -> p it c", c=2048),
                          d["wu1T"][itbase:itbase + 4].rearrange("it p c -> p it c"), 12)
            for itl in range(4):
                def gmm(t, itl=itl):
                    for hk in range(NDK):
                        inst = t.matmul(p["pA"][:],
                                        s["ws0"][:, itl * 2048 + hk * 128: itl * 2048 + (hk + 1) * 128],
                                        s["xnT"][:, hk * TC:(hk + 1) * TC],
                                        start=(hk == 0), stop=(hk == NDK - 1))
                    return inst
                gwl = [gcast]
                if prev_silu:
                    gwl.append(prev_silu)
                tg = S.T(gwl, gmm)
                prev_gmm = tg
                def umm(t, itl=itl):
                    for hk in range(NDK):
                        inst = t.matmul(p["pB"][:],
                                        s["ws1"][:, itl * 2048 + hk * 128: itl * 2048 + (hk + 1) * 128],
                                        s["xnT"][:, hk * TC:(hk + 1) * TC],
                                        start=(hk == 0), stop=(hk == NDK - 1))
                    return inst
                uwl = [ucast]
                if prev_mul:
                    uwl.append(prev_mul)
                tu = S.T(uwl, umm)
                prev_umm = tu
                sw = [tg]
                if prev_mul:
                    sw.append(prev_mul)
                cs = S.C(sw, lambda sc: sc.activation(s["scr"][0][:], p["pA"][:], AF.Silu))
                prev_silu = cs
                itg_l = itg * 4 + itl
                mw = [cs, tu]
                if prev_dmm:
                    mw.append(prev_dmm)
                mv_ = S.V(mw, lambda v, itg_l=itg_l: v.tensor_tensor(
                    s["mid"][:, itg_l * TC:(itg_l + 1) * TC], s["scr"][0][:],
                    p["pB"][:], ALU.mult))
                prev_mul = mv_
                mids[itg_l] = mv_
        down_slot_rd = {}
        for dt in range(NDK):
            bank = dt % 2
            pb = p["pC"] if bank == 0 else p["pE"]
            dcw = [prev_gmm]
            if (dt % 4) in down_slot_rd:
                dcw.append(down_slot_rd[dt % 4])
            dcast = S.dma(dcw, s["ws0"][:, (dt % 4) * 2048:(dt % 4 + 1) * 2048],
                          d["wd1T"][dt][:, q1 * 2048:(q1 + 1) * 2048], 8 + dt % 4)
            def dmm(t, dt=dt, pb=pb):
                for ik in range(NDK):
                    inst = t.matmul(pb[:],
                                    s["ws0"][:, (dt % 4) * 2048 + ik * 128:(dt % 4) * 2048 + ik * 128 + 128],
                                    s["mid"][:, ik * TC:(ik + 1) * TC],
                                    start=(ik == 0), stop=(ik == NDK - 1))
                return inst
            dwl = [dcast, mids[15]]
            if bank_rd[bank]:
                dwl.append(bank_rd[bank])
            td = S.T(dwl, dmm)
            prev_dmm = td
            down_slot_rd[dt % 4] = td
            av = S.V([td], lambda v, dt=dt, pb=pb, first=(q1 == 0): (
                v.tensor_copy(s["accx"][:, dt * TC:(dt + 1) * TC], pb[:]) if first
                else v.tensor_tensor(s["accx"][:, dt * TC:(dt + 1) * TC],
                                     s["accx"][:, dt * TC:(dt + 1) * TC], pb[:], ALU.add)))
            bank_rd[bank] = av
    S.finish()


def ph_comb1(P: Prog, c):
    s, d, p = P.s, P.d, P.p
    S = Sched(P)
    dO = P.sems["dO"]
    c1 = S.Vf([], lambda v: v.tensor_copy(P.row(0), P.row(4)))
    c2 = S.Vf([c1], lambda v: v.tensor_copy(P.row(1), P.row(3)))
    c3 = S.V([c2], lambda v: v.reciprocal(P.row(2), P.row(0)))
    t_s = S.T([c3], lambda t: t.matmul(p["pD"][:], s["ones_row"][:], P.row(2),
                                       start=True, stop=True))
    t_m = S.T([c2, c3], lambda t: t.matmul(p["pE"][:], s["ones_row"][:], P.row(1),
                                           start=True, stop=True))
    c_s = S.C([t_s], lambda sc: sc.activation(s["scr"][1][:], p["pD"][:], AF.Copy))
    c_m = S.C([t_m], lambda sc: sc.activation(s["scr"][2][:], p["pE"][:], AF.Copy))

    prev_t1 = {}
    prev_dense = {}
    prev_comb = {}
    for dt in range(NDK):
        cw = []
        if dt >= 4:
            cw.append(prev_dense[dt - 4])
        cn = S.stream([], d["wo1"][:, dt * 128:(dt + 1) * 128].rearrange(
            "(k p) m -> p k m", p=128), s["ws1"][:, (dt % 4) * 2048:(dt % 4 + 1) * 2048],
            cw)
        pb = p["pA"] if dt % 2 == 0 else p["pB"]
        dwl = [cn]
        if dt >= 2:
            dwl.append(prev_t1[dt - 2])
        def dmm(t, dt=dt, pb=pb):
            for ht in range(NDK):
                inst = t.matmul(pb[:],
                                s["ws1"][:, (dt % 4) * 2048 + ht * 128:(dt % 4) * 2048 + ht * 128 + 128],
                                s["afm"][:, ht * TC:(ht + 1) * TC],
                                start=(ht == 0), stop=(ht == NDK - 1))
            return inst
        td = S.T(dwl, dmm)
        prev_dense[dt] = td
        sw = [c_s, c_m]
        if dt >= 1:
            sw.append(prev_t1[dt - 1])
        csl = S.C(sw, lambda sc, dt=dt: sc.activation(
            s["scr"][0][:], s["accx"][:, dt * TC:(dt + 1) * TC], AF.Silu))
        t1_ = S.Vf([td, csl], lambda v, pb=pb: v.tensor_tensor(
            s["wbc"][:], pb[:], s["scr"][0][:], ALU.mult))
        prev_t1[dt] = t1_
        a1 = S.Vf([t1_, csl], lambda v, dt=dt: v.tensor_tensor(
            s["accx"][:, dt * TC:(dt + 1) * TC],
            s["xnT"][:, dt * TC:(dt + 1) * TC], s["scr"][1][:], ALU.mult))
        a2 = S.Vf([a1], lambda v, dt=dt: v.tensor_tensor(
            s["accx"][:, dt * TC:(dt + 1) * TC],
            s["accx"][:, dt * TC:(dt + 1) * TC], s["scr"][2][:], ALU.add))
        a3 = S.V([a2], lambda v, dt=dt: v.tensor_tensor(
            s["accx"][:, dt * TC:(dt + 1) * TC],
            s["accx"][:, dt * TC:(dt + 1) * TC], s["wbc"][:], ALU.add))
        prev_comb[dt] = a3

    prev_out_dma = {0: None, 1: None}
    last_evac = None
    for tt in range(NTT):
        half = tt % 2
        wlt = s["wl0"] if half == 0 else s["wl1"]
        evs = []
        for dt in range(NDK):
            tw = [prev_comb[dt]]
            if last_evac is not None:
                tw.append(last_evac)
            ttk = S.T(tw, lambda t, tt=tt, dt=dt: t.transpose(
                p["pTf"][:], s["accx"][:, dt * TC + tt * 128: dt * TC + (tt + 1) * 128],
                s["identf"][:]))
            ew = [ttk]
            if dt == 0 and prev_out_dma[half]:
                ew.append(prev_out_dma[half])
            ev = S.C(ew, lambda sc, dt=dt, wlt=wlt: sc.activation(
                wlt[:, dt * 128:(dt + 1) * 128], p["pTf"][:], AF.Copy))
            evs.append(ev)
            last_evac = ev
        r0 = c * TC + tt * 128
        od = S.dma([evs[-1]], d["y"][r0:r0 + 128, :], wlt[:], half * 4)
        prev_out_dma[half] = od
    S.finish()


def build_full():
    P = Prog()
    ph_setup(P)
    for c in range(NCHUNK):
        ph_ln0(P, c)
        ph_prep0(P, c)
        ph_router0(P, c)
        ph_xga0(P, c)
        ph_ffn0(P, c)
        ph_attn0(P, c)
        ph_comb0(P, c)
        ph_prep1(P, c)
        if c == 0:
            ph_dequant1(P)
        ph_ffn1(P, c)
        ph_attn1(P, c)
        ph_comb1(P, c)
    return P


def kernel(**inputs):
    P = build_full()
    maps = mk_inmaps(inputs, n_cores=8)
    res = P.run(maps)
    return np.stack([np.asarray(r["y"], dtype=np.float32) for r in res.results], axis=0)


# version 04
# speedup vs baseline: 1.0965x; 1.0287x over previous
import math
import numpy as np
from contextlib import ExitStack

import concourse.bass as bass
import concourse.mybir as mybir
from concourse.bass_utils import run_bass_kernel_spmd

F32 = mybir.dt.float32
BF16 = mybir.dt.bfloat16
I32 = mybir.dt.int32
AF = mybir.ActivationFunctionType
ALU = mybir.AluOpType
AX = mybir.AxisListType

T = 2048
D = 2048
NCHUNK = 4
TC = 512
NTT = TC // 128
NDK = D // 128

NEG = -1.0e30

L0_NH, L0_NKV, L0_HD, L0_W, L0_CAP = 16, 4, 128, 1024, 50.0
L1_NH, L1_HD = 32, 64
NE, F0, RADJ, NL = 8, 2048, 64, 2
I1 = 8192

GOLD_C = 1.0 / math.e
GOLD_L = 0.5 - math.log(4.0 / 3.0)
GOLD_U = 0.5


def host_constants():
    c = {}
    c["c_identf"] = np.eye(128, dtype=np.float32)
    r0 = np.zeros((128, 128), dtype=np.float32)
    for i in range(64):
        r0[i + 64, i] = -1.0
        r0[i, i + 64] = 1.0
    c["c_rot0"] = r0
    r1 = np.zeros((128, 128), dtype=np.float32)
    for b in (0, 64):
        for i in range(32):
            r1[b + i + 32, b + i] = -1.0
            r1[b + i, b + i + 32] = 1.0
    c["c_rot1"] = r1

    def sine_table(hd, theta):
        frac = np.arange(0, hd, 2, dtype=np.float64) / hd
        invts = theta ** -frac
        tpos = np.arange(T, dtype=np.float64)
        s = np.outer(tpos, invts)
        s = np.concatenate([s, s], axis=1)
        return np.sin(s).T.astype(np.float32), np.cos(s).T.astype(np.float32)

    s0, c0 = sine_table(128, 10000.0)
    c["c_sin0"], c["c_cos0"] = s0, c0
    s1, c1 = sine_table(64, 10000.0)
    c["c_sin1"] = np.concatenate([s1, s1], axis=0)
    c["c_cos1"] = np.concatenate([c1, c1], axis=0)
    i_k = np.arange(128)[:, None]
    i_q = np.arange(128)[None, :]
    c["c_mdiag"] = np.where(i_k <= i_q, 0.0, NEG).astype(np.float32)
    c["c_medge"] = np.where(i_k >= i_q, 0.0, NEG).astype(np.float32)
    rows = np.zeros((128, 8), dtype=np.float32)
    rows[:, 0] = -20.0
    rows[:, 1] = 1e-5
    rows[:, 2] = -GOLD_C
    rows[:, 3] = 1e-8
    rows[:, 4] = 1.0
    c["c_rows"] = rows
    c["c_ones_row"] = np.ones((1, 128), dtype=np.float32)
    c["c_ones_colf"] = np.ones((128, 1), dtype=np.float32)
    return c


ENGINES = ("sync", "tensor", "scalar", "vector", "gpsimd")

_REAL = {
    "dma_start", "matmul", "transpose", "activation", "tensor_tensor",
    "tensor_scalar", "tensor_copy", "tensor_add", "tensor_sub", "tensor_mul",
    "tensor_max", "reduce_sum", "reduce_max", "tensor_reduce", "reciprocal",
    "memset", "memzero", "iota", "copy", "mul", "add", "sqrt", "square",
    "tensor_scalar_add", "tensor_scalar_mul", "tensor_scalar_max",
    "tensor_scalar_min", "scalar_tensor_tensor",
}


_UPDATED = {}


class Sem:
    def __init__(self, handle):
        self.h = handle
        self.n = 0
        self.maxw = 0

    def inc(self, inst, k=1):
        assert id(inst) not in _UPDATED, "instruction already carries a sem update"
        _UPDATED[id(inst)] = inst
        inst.then_inc(self.h, k)
        self.n += k
        return self.n

    def dinc(self, inst):
        return self.inc(inst, 16)


class EW:

    def __init__(self, eng, prog, name):
        self._e = eng
        self._p = prog
        self._n = name
        self._seen = prog.wait_hwm.setdefault(name, {})

    def __getattr__(self, a):
        f = getattr(self._e, a)
        if a in _REAL:
            def g(*args, **kw):
                inst = f(*args, **kw)
                self._p.last_real[self._n] = inst
                return inst
            return g
        return f

    def wait(self, sem, val):
        key = id(sem)
        if self._seen.get(key, -1) >= val:
            return
        self._seen[key] = val
        sem.maxw = max(sem.maxw, val)
        self._e.wait_ge(sem.h, val)

    def wait_ge(self, handle, val):
        key = ("h", id(handle))
        if self._seen.get(key, -1) >= val:
            return
        self._seen[key] = val
        self._e.wait_ge(handle, val)


class Prog:
    def __init__(self, layers=(0, 1), chunks=None, debug=()):
        self.layers = tuple(layers)
        self.chunks = list(range(NCHUNK)) if chunks is None else list(chunks)
        self.debug = set(debug)
        self.nc = bass.Bass("TRN2", target_bir_lowering=False)
        self.ctx = ExitStack()
        self.wait_hwm = {}
        self.last_real = {n: None for n in ENGINES}
        self.phase_idx = 0
        self.dbg_tensors = {}
        self._mk_dram()
        self._mk_sbuf()
        self._mk_psum()
        self._mk_sems()
        self.block = self.ctx.enter_context(self.nc.Block())

    def _mk_dram(self):
        nc = self.nc
        def ap(name, shape, dt, kind):
            return nc.dram_tensor(name, shape, dt, kind=kind).ap()
        EI, EO, IN = "ExternalInput", "ExternalOutput", "Internal"
        self.d = d = {}
        d["x"] = ap("x", [T, D], F32, EI)
        d["y"] = ap("y", [T, D], F32, EO)
        d["wq0"] = ap("wq0", [D, 2048], F32, EI)
        d["wk0"] = ap("wk0", [D, 512], F32, EI)
        d["wv0"] = ap("wv0", [D, 512], F32, EI)
        d["wo0"] = ap("wo0", [2048, D], F32, EI)
        d["wg0"] = ap("wg0", [NE, D, F0], F32, EI)
        d["wu0"] = ap("wu0", [NE, D, F0], F32, EI)
        d["wd0"] = ap("wd0", [NE, F0, D], F32, EI)
        d["ga0"] = ap("ga0", [NL, D, RADJ], F32, EI)
        d["gb0"] = ap("gb0", [NL, RADJ, F0], F32, EI)
        d["ua0"] = ap("ua0", [NL, D, RADJ], F32, EI)
        d["ub0"] = ap("ub0", [NL, RADJ, F0], F32, EI)
        d["da0"] = ap("da0", [NL, F0, RADJ], F32, EI)
        d["db0"] = ap("db0", [NL, RADJ, D], F32, EI)
        d["wr0"] = ap("wr0", [D, NE], F32, EI)
        d["temp0"] = ap("temp0", [1, 1], F32, EI)
        d["wqkv1"] = ap("wqkv1", [D, 2176], F32, EI)
        d["wo1"] = ap("wo1", [D, D], F32, EI)
        d["g1q"] = ap("g1q", [I1, D], I32, EI)
        d["g1s"] = ap("g1s", [I1, 1], F32, EI)
        d["u1q"] = ap("u1q", [I1, D], I32, EI)
        d["u1s"] = ap("u1s", [I1, 1], F32, EI)
        d["d1q"] = ap("d1q", [D, I1], I32, EI)
        d["d1s"] = ap("d1s", [D, 1], F32, EI)
        for name, arr in host_constants().items():
            d[name] = ap(name, list(arr.shape), F32, EI)
        d["wg1T"] = ap("wg1T", [I1 // 128, 128, D], BF16, IN)
        d["wg0b"] = ap("wg0b", [NE * 4, 128, 8192], BF16, IN)
        d["wu0b"] = ap("wu0b", [NE * 4, 128, 8192], BF16, IN)
        d["wd0b"] = ap("wd0b", [NE * 4, 128, 8192], BF16, IN)
        d["wu1T"] = ap("wu1T", [I1 // 128, 128, D], BF16, IN)
        d["wd1T"] = ap("wd1T", [D // 128, 128, I1], BF16, IN)

    def row(self, r):
        b = (0, 0, 0, 32, 32, 32, 64, 64, 64)[r]
        col = (r % 3) * TC
        return self.s["rows"][b:b + 1, col:col + TC]

    def dbg(self, name, shape, dt=F32):
        if name not in self.dbg_tensors:
            self.dbg_tensors[name] = self.nc.dram_tensor(
                "dbg_" + name, shape, dt, kind="ExternalOutput").ap()
        return self.dbg_tensors[name]

    def _mk_sbuf(self):
        nc, E = self.nc, self.ctx.enter_context
        def sb(name, shape, dt):
            return E(nc.sbuf_tensor(name, shape, dt))
        self.s = s = {}
        s["identf"] = sb("identf", [128, 128], F32)
        s["identb"] = sb("identb", [128, 128], BF16)
        s["rot0"] = sb("rot0", [128, 128], BF16)
        s["rot1"] = sb("rot1", [128, 128], BF16)
        s["mdiag"] = sb("mdiag", [128, 128], F32)
        s["medge"] = sb("medge", [128, 128], F32)
        s["ones_row"] = sb("ones_row", [1, 128], F32)
        s["ones_col"] = sb("ones_col", [128, 1], BF16)
        s["crows"] = sb("crows", [128, 8], F32)
        s["invtau"] = sb("invtau", [128, 1], F32)
        s["tb"] = sb("tb", [128, 2 * TC], BF16)
        s["xnT"] = sb("xnT", [128, NDK * TC], BF16)
        s["k0s"] = sb("k0s", [128, L0_NKV * T], BF16)
        s["v0s"] = sb("v0s", [128, L0_NKV * T], BF16)
        s["k1s"] = sb("k1s", [128, T], BF16)
        s["v1s"] = sb("v1s", [128, (T // 128) * L1_HD], BF16)
        s["qt"] = sb("qt", [128, TC], BF16)
        s["pst"] = [sb(f"pst{i}", [128, TC], BF16) for i in range(2)]
        s["afm"] = sb("afm", [128, L0_NH * TC], BF16)
        s["accx"] = sb("accx", [128, NDK * TC], F32)
        s["mid"] = sb("mid", [128, 16 * TC], BF16)
        s["ws0"] = sb("ws0", [128, 8192], BF16)
        s["ws1"] = sb("ws1", [128, 8192], BF16)
        s["wl0"] = sb("wl0", [128, 2048], F32)
        s["wl1"] = sb("wl1", [128, 2048], F32)
        s["adsl"] = sb("adsl", [128, 4096], BF16)
        s["xga"] = sb("xga", [64, 4 * TC], BF16)
        s["mda"] = sb("mda", [64, TC], BF16)
        s["lnst"] = sb("lnst", [128, 32], F32)
        s["lnstb"] = sb("lnstb", [128, 32], F32)
        s["rt"] = sb("rt", [128, 4 * 64], F32)
        s["rstat"] = sb("rstat", [128, 48], F32)
        s["rstatb"] = sb("rstatb", [128, 48], F32)
        s["w8f"] = sb("w8f", [8, TC], F32)
        s["w8"] = sb("w8", [128, 4 * 8], F32)
        s["wbc"] = sb("wbc", [128, TC], F32)
        s["rows"] = sb("rows", [128, 3 * TC], F32)
        s["rows8"] = sb("rows8", [8, 128], F32)
        s["scr"] = [sb(f"scr{i}", [128, TC], F32) for i in range(3)]
        s["scrb"] = sb("scrb", [128, TC], BF16)
        s["dmy"] = sb("dmy", [128, 8], F32)

    def _mk_psum(self):
        nc, E = self.nc, self.ctx.enter_context
        self.p = {}
        for name in ["pA", "pB", "pC", "pD", "pE"]:
            self.p[name] = E(nc.psum_tensor(name, [128, 512], F32))
        self.p["pT0"] = E(nc.psum_tensor("pT0", [128, 128], BF16))
        self.p["pT1"] = E(nc.psum_tensor("pT1", [128, 128], BF16))
        self.p["pTf"] = E(nc.psum_tensor("pTf", [128, 128], F32))

    def _mk_sems(self):
        E = self.ctx.enter_context
        self.sems = {}
        for name in ["dW", "dX", "dS", "dO", "dT", "dM",
                     "mT", "mS", "mV", "mG",
                     "bS", "bT", "bC", "bV", "bG",
                     "fV", "fS"]:
            self.sems[name] = Sem(E(self.nc.semaphore(name)))
        self.dL = [Sem(E(self.nc.semaphore(f"dL{i}"))) for i in range(16)]

    def sec(self, engine, fn):
        dec = getattr(self.block, engine)
        def wrapped(raw):
            fn(EW(raw, self, engine))
        dec(wrapped)

    def barrier(self):
        self.phase_idx += 1
        k = self.phase_idx
        dvals = [(self.sems[nm], self.sems[nm].n)
                 for nm in ("dX", "dS", "dW", "dO", "dT", "dM")]
        dvals += [(sem, sem.n) for sem in self.dL]
        def fn_sync(e):
            for sem, val in dvals:
                e.wait_ge(sem.h, val)
            e.nop().then_inc(self.sems["bS"].h, 1)
        self.sec("sync", fn_sync)
        self.sems["bS"].n += 1
        self.last_real["sync"] = None
        bmap = {"tensor": "bT", "scalar": "bC", "vector": "bV", "gpsimd": "bG"}
        for eng in ("tensor", "scalar", "vector", "gpsimd"):
            sem = self.sems[bmap[eng]]
            had_work = self.last_real[eng] is not None
            self.last_real[eng] = None
            def fn(e, sem=sem, eng=eng, had_work=had_work):
                if not had_work:
                    e.nop().then_inc(sem.h, 1)
                    return
                if eng == "tensor":
                    e.wait_ge(self.sems["mV"].h, self.sems["mV"].n)
                    e.wait_ge(self.sems["mS"].h, self.sems["mS"].n)
                    e.wait_ge(self.sems["mT"].h, self.sems["mT"].n)
                    inst = e.matmul(self.p["pE"][0:1, 0:1], self.s["ones_col"][:],
                                    self.s["ones_col"][:], start=True, stop=True)
                elif eng == "scalar":
                    e.wait_ge(self.sems["mS"].h, self.sems["mS"].n)
                    e.wait_ge(self.sems["fS"].h, self.sems["fS"].n)
                    e.wait_ge(self.sems["fV"].h, self.sems["fV"].n)
                    inst = e.activation(self.s["dmy"][:, 0:1], self.s["dmy"][:, 1:2],
                                        AF.Copy)
                elif eng == "vector":
                    e.wait_ge(self.sems["mV"].h, self.sems["mV"].n)
                    e.wait_ge(self.sems["fV"].h, self.sems["fV"].n)
                    inst = e.memset(self.s["dmy"][:, 2:3], 0.0)
                else:
                    e.wait_ge(self.sems["mG"].h, self.sems["mG"].n)
                    inst = e.memset(self.s["dmy"][:, 3:4], 0.0)
                inst.then_inc(sem.h, 1)
            self.sec(eng, fn)
            sem.n += 1
            assert sem.n == k, (eng, sem.n, k)
        for eng in ENGINES:
            def fn(e, k=k):
                for nm in ("bS", "bT", "bC", "bV", "bG"):
                    e.wait_ge(self.sems[nm].h, k)
            self.sec(eng, fn)

    def finalize(self):
        for nm, sem in self.sems.items():
            assert sem.maxw <= sem.n, ("wait beyond final incs", nm, sem.maxw, sem.n)
        self.ctx.close()

    def run(self, in_maps, trace=False):
        self.finalize()
        return run_bass_kernel_spmd(self.nc, in_maps,
                                    core_ids=list(range(len(in_maps))),
                                    trace=trace)

    def sim(self, in_map):
        self.finalize()
        from concourse.bass_interp import CoreSim
        sim = CoreSim(self.nc, require_finite=False, require_nnan=False)
        for name, val in in_map.items():
            view = sim.tensor(name)
            view[:] = val
        sim.simulate()
        outs = {}
        for alloc in self.nc.m.functions[0].allocations:
            import concourse.mybir as _mb
            if isinstance(alloc, _mb.MemoryLocationSet) and alloc.kind == "ExternalOutput":
                nm = alloc.memorylocations[0].name
                outs[nm] = np.array(sim.tensor(nm))
        return outs


def _band0(c):
    out = []
    for kt in range(max(0, 4 * c - 8), 4 * c + 4):
        segs = []
        for tl in range(NTT):
            rel = (4 * c + tl) - kt
            if rel < 0 or rel > 8:
                segs.append("zero")
            elif rel == 0:
                segs.append("diag")
            elif rel == 8:
                segs.append("edge")
            else:
                segs.append("keep")
        out.append((kt, segs))
    return out


def _band1(c):
    out = []
    for kt in range(0, 4 * c + 4):
        segs = []
        for tl in range(NTT):
            rel = (4 * c + tl) - kt
            segs.append("zero" if rel < 0 else ("diag" if rel == 0 else "keep"))
        out.append((kt, segs))
    return out


def ph_setup(P: Prog):
    s, d, p = P.s, P.d, P.p
    dX, dS = P.sems["dX"], P.sems["dS"]
    mV, mT = P.sems["mV"], P.sems["mT"]

    def sync(q):
        for nm_s, nm_d in [("identf", "c_identf"), ("mdiag", "c_mdiag"),
                           ("medge", "c_medge"), ("ones_row", "c_ones_row"),
                           ("crows", "c_rows")]:
            dX.dinc(q.dma_start(out=s[nm_s][:], in_=d[nm_d]))
        dX.dinc(q.dma_start(out=s["scr"][0][:, 0:128], in_=d["c_identf"]))
        dX.dinc(q.dma_start(out=s["scr"][1][:, 0:128], in_=d["c_rot0"]))
        dX.dinc(q.dma_start(out=s["scr"][2][:, 0:128], in_=d["c_rot1"]))
        dX.dinc(q.dma_start(out=s["wl0"][:, 0:1], in_=d["c_ones_colf"]))
        dX.dinc(q.dma_start(out=s["wl1"][0:1, 0:1], in_=d["temp0"]))
        P._t0 = dX.n

    def vector(v):
        P.sems["fV"].inc(v.memset(s["dmy"][:], 0.0))
        v.wait(dX, P._t0)
        v.tensor_copy(s["identb"][:], s["scr"][0][:, 0:128])
        v.tensor_copy(s["rot0"][:], s["scr"][1][:, 0:128])
        v.tensor_copy(s["rot1"][:], s["scr"][2][:, 0:128])
        v.tensor_copy(s["ones_col"][:], s["wl0"][:, 0:1])

    P.sec("sync", sync)
    P.sec("vector", vector)
    P.barrier()

    def tensor2(t):
        mT.inc(t.matmul(p["pTf"][:, 0:1], s["ones_row"][:], s["wl1"][0:1, 0:1],
                        start=True, stop=True))

    def vector2(v):
        v.wait(mT, mT.n)
        mV.inc(v.reciprocal(s["scr"][0][:, 0:1], p["pTf"][:, 0:1]))

    def sync2(q):
        q.wait(mV, mV.n)
        dS.dinc(q.dma_start(out=s["invtau"][:], in_=s["scr"][0][:, 0:1]))

    P.sec("tensor", tensor2)
    P.sec("vector", vector2)
    P.sec("sync", sync2)
    P.barrier()


def ph_ln0(P: Prog, c):
    s, d, p = P.s, P.d, P.p
    xnt = s["mid"]
    dX, dS = P.sems["dX"], P.sems["dS"]
    mV, mS, mT = P.sems["mV"], P.sems["mS"], P.sems["mT"]
    fV = P.sems["fV"]
    wl = [s["wl0"], s["wl1"]]

    v0 = mV.n
    s0 = mS.n
    dL = [P.dL[0], P.dL[4]]
    xdma = {}

    def sync(q):
        for tt in range(NTT):
            if tt >= 2:
                q.wait(mV, v0 + (tt - 1))
                q.wait(mS, s0 + 4 * (tt - 1))
            r0 = c * TC + tt * 128
            xdma[tt] = dL[tt % 2].dinc(q.dma_start(out=wl[tt % 2][:],
                                                   in_=d["x"][r0:r0 + 128, :]))

    def vector(v):
        for tt in range(NTT):
            v.wait(dL[tt % 2], xdma[tt])
            mV.inc(v.reduce_sum(s["lnst"][:, tt:tt + 1], wl[tt % 2][:], axis=AX.X))

    def scalar(sc):
        for tt in range(NTT):
            sc.wait(dL[tt % 2], xdma[tt])
            for j in range(4):
                k = tt * 4 + j
                inst = sc.activation(xnt[:, k * 512:(k + 1) * 512],
                                     wl[tt % 2][:, j * 512:(j + 1) * 512],
                                     AF.Square, accum_out=s["rt"][:, 4 * tt + j:4 * tt + j + 1])
                mS.inc(inst)

    P.sec("sync", sync)
    P.sec("vector", vector)
    P.sec("scalar", scalar)

    v1 = mV.n
    s1 = mS.n

    def vector2(v):
        v.wait(mS, s1)
        v.wait(mV, v0 + 4)
        for tt in range(NTT):
            fV.inc(v.reduce_sum(s["lnst"][:, 4 + tt:5 + tt],
                                s["rt"][:, 4 * tt:4 * tt + 4], axis=AX.X))
        v.wait(fV, fV.n)
        i1 = v.tensor_scalar(s["lnst"][:, 20:24], s["lnst"][:, 0:4], 1.0 / D, None, ALU.mult)
        fV.inc(i1); v.wait(fV, fV.n)
        i2 = v.tensor_scalar(s["lnst"][:, 24:28], s["lnst"][:, 4:8], 1.0 / D, None, ALU.mult)
        fV.inc(i2)
        i3 = v.tensor_tensor(s["lnst"][:, 28:32], s["lnst"][:, 20:24], s["lnst"][:, 20:24], ALU.mult)
        fV.inc(i3); v.wait(fV, fV.n)
        i4 = v.tensor_tensor(s["lnst"][:, 24:28], s["lnst"][:, 24:28], s["lnst"][:, 28:32], ALU.subtract)
        mV.inc(i4)

    def scalar2(sc):
        sc.wait(mV, mV.n)
        inst = sc.activation(s["lnst"][:, 16:20], s["lnst"][:, 24:28], AF.Sqrt,
                             bias=s["crows"][:, 1:2])
        mS.inc(inst)

    def vector3(v):
        v.wait(mS, mS.n)
        i1 = v.reciprocal(s["lnst"][:, 8:12], s["lnst"][:, 16:20])
        fV.inc(i1); v.wait(fV, fV.n)
        i2 = v.tensor_tensor(s["lnst"][:, 12:16], s["lnst"][:, 20:24], s["lnst"][:, 8:12], ALU.mult)
        fV.inc(i2); v.wait(fV, fV.n)
        i3 = v.tensor_scalar(s["lnst"][:, 12:16], s["lnst"][:, 12:16], -1.0, None, ALU.mult)
        mV.inc(i3)

    P.sec("vector", vector2)
    P.sec("scalar", scalar2)
    P.sec("vector", vector3)

    vstat = mV.n

    def sync2(q):
        q.wait(mV, vstat)
        dS.dinc(q.dma_start(out=s["lnstb"][:, 8:16], in_=s["lnst"][:, 8:16]))

    def tensor2(t):
        t.wait(mV, vstat)
        mT.inc(t.transpose(p["pTf"][0:8, :], s["lnst"][:, 16:24], s["identf"][:]))

    def vector4(v):
        v.wait(mT, mT.n)
        mV.inc(v.tensor_copy(s["rows8"][:], p["pTf"][0:8, :]))

    def sync3(q):
        q.wait(mV, mV.n)
        for tt in range(NTT):
            dS.dinc(q.dma_start(out=P.row(0)[:, tt * 128:(tt + 1) * 128],
                                in_=s["rows8"][tt:tt + 1, :]))
            dS.dinc(q.dma_start(out=P.row(1)[:, tt * 128:(tt + 1) * 128],
                                in_=s["rows8"][4 + tt:5 + tt, :]))

    P.sec("sync", sync2)
    P.sec("tensor", tensor2)
    P.sec("vector", vector4)
    P.sec("sync", sync3)
    P.barrier()

    v2 = mV.n
    t2 = mT.n
    xdma2 = {}

    def cast_inc(tt):
        return v2 + tt * 17 + 1

    def evac_inc(tt, dk):
        return v2 + tt * 17 + 1 + dk + 1

    def trans_inc(idx):
        return t2 + idx + 1

    def sync4(q):
        for tt in range(NTT):
            if tt >= 2:
                q.wait(mV, cast_inc(tt - 1))
            r0 = c * TC + tt * 128
            xdma2[tt] = dL[tt % 2].dinc(q.dma_start(out=wl[tt % 2][:],
                                                    in_=d["x"][r0:r0 + 128, :]))

    def vector5(v):
        for tt in range(NTT):
            v.wait(dL[tt % 2], xdma2[tt])
            inst = v.tensor_scalar(xnt[:, 0:D], wl[tt % 2][:],
                                   s["lnstb"][:, 8 + tt:9 + tt],
                                   s["lnstb"][:, 12 + tt:13 + tt],
                                   ALU.mult, ALU.add)
            mV.inc(inst)
            assert mV.n == cast_inc(tt)
            for dk in range(NDK):
                idx = tt * NDK + dk
                v.wait(mT, trans_inc(idx))
                inst = v.tensor_copy(
                    s["xnT"][:, dk * TC + tt * 128: dk * TC + (tt + 1) * 128],
                    p["pT0" if idx % 2 == 0 else "pT1"][:])
                mV.inc(inst)
                assert mV.n == evac_inc(tt, dk)

    def tensor3(t):
        for tt in range(NTT):
            t.wait(mV, cast_inc(tt))
            for dk in range(NDK):
                idx = tt * NDK + dk
                if idx >= 2:
                    ptt, pdk = divmod(idx - 2, NDK)
                    t.wait(mV, evac_inc(ptt, pdk))
                inst = t.transpose(p["pT0" if idx % 2 == 0 else "pT1"][:],
                                   xnt[:, dk * 128:(dk + 1) * 128],
                                   s["identb"][:])
                mT.inc(inst)
                assert mT.n == trans_inc(idx)

    P.sec("sync", sync4)
    P.sec("tensor", tensor3)
    P.sec("vector", vector5)
    P.barrier()


def ph_dump(P: Prog, items):
    dO = P.sems["dO"]

    def sync(q):
        for name, apv, shape in items:
            dO.dinc(q.dma_start(out=P.dbg(name, list(shape), dt=apv.tensor.dtype),
                                in_=apv))
        q.wait(dO, dO.n)

    P.sec("sync", sync)
    P.barrier()


def mk_inmaps(inputs, n_cores=8):
    consts = host_constants()
    base = {
        "wq0": inputs["L0_attn_q_kernel"], "wk0": inputs["L0_attn_k_kernel"],
        "wv0": inputs["L0_attn_v_kernel"], "wo0": inputs["L0_attn_out_kernel"],
        "wg0": inputs["L0_ffn_Wg"], "wu0": inputs["L0_ffn_Wu"], "wd0": inputs["L0_ffn_Wd"],
        "ga0": inputs["L0_ffn_Ga"], "gb0": inputs["L0_ffn_Gb"],
        "ua0": inputs["L0_ffn_Ua"], "ub0": inputs["L0_ffn_Ub"],
        "da0": inputs["L0_ffn_Da"], "db0": inputs["L0_ffn_Db"],
        "wr0": inputs["L0_ffn_Wr"],
        "temp0": np.asarray(inputs["L0_ffn_temperature"], np.float32).reshape(1, 1),
        "wqkv1": inputs["L1_attn_w_qkv"], "wo1": inputs["L1_attn_w_dense"],
        "g1q": np.asarray(inputs["L1_ffn_gate_w"], np.int32),
        "g1s": np.asarray(inputs["L1_ffn_gate_scale"], np.float32).reshape(-1, 1),
        "u1q": np.asarray(inputs["L1_ffn_up_w"], np.int32),
        "u1s": np.asarray(inputs["L1_ffn_up_scale"], np.float32).reshape(-1, 1),
        "d1q": np.asarray(inputs["L1_ffn_down_w"], np.int32),
        "d1s": np.asarray(inputs["L1_ffn_down_scale"], np.float32).reshape(-1, 1),
    }
    base = {k: np.ascontiguousarray(np.asarray(v, dtype=np.int32 if k in
            ("g1q", "u1q", "d1q") else np.float32)) for k, v in base.items()}
    base.update(consts)
    x = np.asarray(inputs["L0_attn_x"], np.float32)
    maps = []
    for b in range(n_cores):
        m = dict(base)
        m["x"] = np.ascontiguousarray(x[b])
        maps.append(m)
    return maps


def ph_prep0(P: Prog, c):
    s, d, p = P.s, P.d, P.p
    dT, dM = P.sems["dT"], P.sems["dM"]
    mV, mS, mT = P.sems["mV"], P.sems["mS"], P.sems["mT"]
    wl = [s["wl0"], s["wl1"]]
    dL = [P.dL[0], P.dL[4]]
    tbc = s["tb"][:, 0:TC]
    tbs = s["tb"][:, TC:2 * TC]

    v0, t0, s0 = mV.n, mT.n, mS.n
    pdma = {}
    tbv = [0]
    wrv = [0]

    def cast(i):
        return v0 + 2 + (2 + i if i <= 5 else (10 if i == 6 else 13))
    qtcopy = lambda m: v0 + 2 + {0: 8, 1: 11, 2: 14, 3: 16}[m]
    krope = lambda m: v0 + 2 + {0: 9, 1: 12, 2: 15, 3: 17}[m]
    vevac = lambda idx: v0 + 2 + 18 + idx
    tbcast = v0 + 2
    v_end = v0 + 35
    _kv_pos = {0: 1, 1: 2, 2: 3, 3: 5, 4: 7, 5: 9, 6: 11, 7: 16}
    kvproj = lambda i: t0 + _kv_pos[i]
    rot = lambda m: t0 + 4 + 2 * m
    tr = lambda idx: t0 + (12 + idx if idx < 4 else 13 + idx)
    kevac = lambda i: s0 + i + 1

    def sync(q):
        dT.dinc(q.dma_start(out=s["scr"][1][:], in_=d["c_cos0"][:, c * TC:(c + 1) * TC]))
        tbv[0] = dT.dinc(q.dma_start(out=s["scr"][2][:], in_=d["c_sin0"][:, c * TC:(c + 1) * TC]))
        wrv[0] = dM.dinc(q.dma_start(
            out=s["scr"][0][:, 0:128].rearrange("p (k e) -> p k e", e=NE),
            in_=d["wr0"].rearrange("(k p) e -> p k e", p=128)))
        for i in range(8):
            m = i % 4
            srcw = d["wk0"] if i < 4 else d["wv0"]
            if i >= 2:
                q.wait(mV, cast(i - 1))
            pdma[i] = dL[i % 2].dinc(q.dma_start(
                out=wl[i % 2][:].rearrange("p (k m) -> p k m", m=128),
                in_=srcw[:, m * 128:(m + 1) * 128].rearrange("(k p) m -> p k m", p=128)))

    def emit_cast(v, i):
        v.wait(dL[i % 2], pdma[i])
        if i >= 4:
            v.wait(mT, kvproj(i - 4))
        mV.inc(v.tensor_copy(s["ws0"][:, (i % 4) * 2048:(i % 4 + 1) * 2048],
                             wl[i % 2][:]))
        assert mV.n == cast(i)

    def emit_qk(v, m):
        tgt = s["scr"][1] if m % 2 == 0 else s["scr"][2]
        v.wait(mS, kevac(m))
        if m >= 1:
            v.wait(mT, rot(m - 1))
            v.wait(mV, krope(m - 1))
        mV.inc(v.tensor_copy(s["qt"][:], tgt[:]))
        assert mV.n == qtcopy(m)
        v.wait(mT, rot(m))
        fV = P.sems["fV"]
        fV.inc(v.tensor_tensor(s["scr"][0][:], tgt[:], tbc, ALU.mult))
        fV.inc(v.tensor_tensor(s["wbc"][:], p["pC"][:], tbs, ALU.mult))
        v.wait(fV, fV.n)
        inst = v.tensor_tensor(s["k0s"][:, m * T + c * TC: m * T + (c + 1) * TC],
                               s["scr"][0][:], s["wbc"][:], ALU.add)
        mV.inc(inst)
        assert mV.n == krope(m)

    def vector_all(v):
        v.wait(dT, tbv[0])
        mV.inc(v.tensor_copy(tbc, s["scr"][1][:]))
        mV.inc(v.tensor_copy(tbs, s["scr"][2][:]))
        v.wait(dM, wrv[0])
        mV.inc(v.tensor_copy(s["adsl"][:, 0:128], s["scr"][0][:, 0:128]))
        for i in range(6):
            emit_cast(v, i)
        emit_qk(v, 0)
        emit_cast(v, 6)
        emit_qk(v, 1)
        emit_cast(v, 7)
        emit_qk(v, 2)
        emit_qk(v, 3)
        for idx in range(16):
            m, j = divmod(idx, NTT)
            v.wait(mT, tr(idx))
            kt = 4 * c + j
            inst = v.tensor_copy(s["v0s"][:, (m * 16 + kt) * 128:(m * 16 + kt + 1) * 128],
                                 p["pT0" if idx % 2 == 0 else "pT1"][:])
            mV.inc(inst)
            assert mV.n == vevac(idx)
        assert mV.n == v_end

    def emit_rot(t, m):
        t.wait(mV, qtcopy(m))
        if m >= 1:
            t.wait(mV, krope(m - 1))
        inst = t.matmul(p["pC"][:], s["rot0"][:], s["qt"][:], start=True, stop=True)
        mT.inc(inst)
        assert mT.n == rot(m)

    def emit_tr(t, idx):
        m, j = divmod(idx, NTT)
        t.wait(mS, kevac(4 + m))
        if idx >= 2:
            t.wait(mV, vevac(idx - 2))
        inst = t.transpose(p["pT0" if idx % 2 == 0 else "pT1"][:],
                           s["scrb"][:, j * 128:(j + 1) * 128],
                           s["identb"][:])
        mT.inc(inst)
        assert mT.n == tr(idx)

    def tensor_kv(t):
        for i in range(8):
            m = i % 4
            t.wait(mV, cast(i))
            if i >= 2:
                t.wait(mS, kevac(i - 2))
            pj = p["pA"] if i % 2 == 0 else p["pB"]
            for dk in range(NDK):
                inst = t.matmul(pj[:], s["ws0"][:, (m * 16 + dk) * 128:(m * 16 + dk + 1) * 128],
                                s["xnT"][:, dk * TC:(dk + 1) * TC],
                                start=(dk == 0), stop=(dk == NDK - 1))
            mT.inc(inst)
            assert mT.n == kvproj(i)
            if 2 <= i <= 5:
                emit_rot(t, i - 2)
            if i == 6:
                for idx in range(4):
                    emit_tr(t, idx)
        for idx in range(4, 16):
            emit_tr(t, idx)

    def scalar_kv(sc):
        for i in range(8):
            m = i % 4
            sc.wait(mT, kvproj(i))
            pj = p["pA"] if i % 2 == 0 else p["pB"]
            if i < 4:
                tgt = s["scr"][1] if m % 2 == 0 else s["scr"][2]
                if i >= 2:
                    sc.wait(mV, krope(i - 2))
                else:
                    sc.wait(mV, tbcast)
                inst = sc.activation(tgt[:], pj[:], AF.Copy)
            else:
                if m >= 1:
                    sc.wait(mT, tr(m * 4 - 1))
                inst = sc.activation(s["scrb"][:], pj[:], AF.Copy)
            mS.inc(inst)
            assert mS.n == kevac(i)

    P.sec("sync", sync)
    P.sec("tensor", tensor_kv)
    P.sec("scalar", scalar_kv)
    P.sec("vector", vector_all)
    P.barrier()


def ph_router0(P: Prog, c):
    s, d, p = P.s, P.d, P.p
    dS = P.sems["dS"]
    mV, mS, mT = P.sems["mV"], P.sems["mS"], P.sems["mT"]
    fV, fS = P.sems["fV"], P.sems["fS"]
    v0, t0, s0, b0 = mV.n, mT.n, mS.n, dS.n

    def R(tt, off, w=8):
        return s["rt"][:, tt * 64 + off: tt * 64 + off + w]

    def tensor_r(t):
        for tt in range(NTT):
            for dk in range(NDK):
                inst = t.matmul(p["pD"][:, tt * 8:(tt + 1) * 8],
                                s["xnT"][:, dk * TC + tt * 128: dk * TC + (tt + 1) * 128],
                                s["adsl"][:, dk * 8:(dk + 1) * 8],
                                start=(dk == 0), stop=(dk == NDK - 1))
            mT.inc(inst)

    def vector_lg(v):
        v.wait(mT, t0 + 4)
        for tt in range(NTT):
            inst = v.tensor_copy(R(tt, 0), p["pD"][:, tt * 8:(tt + 1) * 8])
            mV.inc(inst)

    def scalar_r(sc):
        sc.wait(mV, v0 + 4)
        for tt in range(NTT):
            mS.inc(sc.activation(R(tt, 56), R(tt, 0), AF.Sigmoid,
                                 scale=s["invtau"][:, 0:1]))
        sc.wait(mS, mS.n)
        for tt in range(NTT):
            mS.inc(sc.activation(R(tt, 8), R(tt, 56), AF.Abs, bias=s["crows"][:, 2:3]))
        sc.wait(mS, mS.n)
        for tt in range(NTT):
            mS.inc(sc.activation(R(tt, 24), R(tt, 8), AF.Exp, scale=-10.0))
        for tt in range(NTT):
            mS.inc(sc.activation(R(tt, 32), R(tt, 8), AF.Exp, scale=-(1.0 / 0.3)))

    def vector_r(v):
        v.wait(mS, s0 + 16)
        fV = P.sems["fV"]
        def grp(insts_done):
            fV.inc(insts_done)
            v.wait(fV, fV.n)
        for tt in range(NTT):
            i1 = v.tensor_scalar(R(tt, 16), R(tt, 56), GOLD_L, None, ALU.is_ge)
        grp(i1)
        for tt in range(NTT):
            i1 = v.tensor_scalar(R(tt, 40), R(tt, 56), GOLD_U, None, ALU.is_le)
        grp(i1)
        for tt in range(NTT):
            i1 = v.tensor_tensor(R(tt, 16), R(tt, 16), R(tt, 40), ALU.mult)
        grp(i1)
        for tt in range(NTT):
            i1 = v.tensor_tensor(R(tt, 24), R(tt, 24), R(tt, 16), ALU.mult)
        grp(i1)
        for tt in range(NTT):
            i1 = v.reduce_max(s["rstat"][:, tt:tt + 1], R(tt, 32), axis=AX.X)
        grp(i1)
        inst = v.memset(s["dmy"][:, 5:6], 0.0)
        mV.inc(inst)

    def sync_b1(q):
        q.wait(mV, v0 + 5)
        dS.dinc(q.dma_start(out=s["rstatb"][:, 0:4], in_=s["rstat"][:, 0:4]))

    def vector_r1(v):
        fV = P.sems["fV"]
        def grp(insts_done):
            fV.inc(insts_done)
            v.wait(fV, fV.n)
        v.wait(dS, b0 + 16)
        for tt in range(NTT):
            i1 = v.tensor_scalar(R(tt, 40), R(tt, 32), s["rstatb"][:, tt:tt + 1], None, ALU.is_ge)
        grp(i1)
        for tt in range(NTT):
            i1 = v.tensor_scalar(R(tt, 56), R(tt, 40), -1.0e9, None, ALU.mult)
        grp(i1)
        for tt in range(NTT):
            i1 = v.tensor_tensor(R(tt, 56), R(tt, 56), R(tt, 32), ALU.add)
        grp(i1)
        for tt in range(NTT):
            i1 = v.reduce_max(s["rstat"][:, 4 + tt:5 + tt], R(tt, 56), axis=AX.X)
        grp(i1)
        for tt in range(NTT):
            i1 = v.reduce_sum(s["rstat"][:, 24 + tt:25 + tt], R(tt, 24), axis=AX.X)
        grp(i1)
        i2 = v.tensor_scalar(s["rstat"][:, 8:12], s["rstat"][:, 24:28], 1e-8, None,
                             ALU.max)
        grp(i2)
        inst = v.reciprocal(s["rstat"][:, 28:32], s["rstat"][:, 8:12])
        mV.inc(inst)

    def sync_b2(q):
        q.wait(mV, v0 + 6)
        dS.dinc(q.dma_start(out=s["rstatb"][:, 4:12], in_=s["rstat"][:, 4:12]))
        dS.dinc(q.dma_start(out=s["rstatb"][:, 24:32], in_=s["rstat"][:, 24:32]))

    def vector_r2(v):
        fV = P.sems["fV"]
        def grp(insts_done):
            fV.inc(insts_done)
            v.wait(fV, fV.n)
        v.wait(dS, b0 + 48)
        for tt in range(NTT):
            i1 = v.tensor_scalar(R(tt, 40), R(tt, 32), s["rstatb"][:, 4 + tt:5 + tt],
                                 None, ALU.is_ge)
        grp(i1)
        for tt in range(NTT):
            i1 = v.tensor_tensor(R(tt, 48), R(tt, 32), R(tt, 40), ALU.mult)
        grp(i1)
        for tt in range(NTT):
            i1 = v.reduce_sum(s["rstat"][:, 12 + tt:13 + tt], R(tt, 48), axis=AX.X)
        grp(i1)
        i1 = v.tensor_scalar(s["rstat"][:, 16:20], s["rstatb"][:, 24:28], 1e-8, None, ALU.is_lt)
        i2 = v.tensor_scalar(s["rstat"][:, 20:24], s["rstatb"][:, 24:28], 1e-8, None, ALU.is_ge)
        grp(i2)
        i2 = v.tensor_scalar(s["rstat"][:, 12:16], s["rstat"][:, 12:16], 1e-8, None,
                             ALU.max)
        grp(i2)
        inst = v.reciprocal(s["rstat"][:, 32:36], s["rstat"][:, 12:16])
        mV.inc(inst)

    def sync_b3(q):
        q.wait(mV, v0 + 7)
        dS.dinc(q.dma_start(out=s["rstatb"][:, 12:24], in_=s["rstat"][:, 12:24]))
        dS.dinc(q.dma_start(out=s["rstatb"][:, 32:36], in_=s["rstat"][:, 32:36]))

    def vector_r3(v):
        fV = P.sems["fV"]
        v.wait(dS, b0 + 80)
        for tt in range(NTT):
            i1 = v.tensor_scalar(R(tt, 48), R(tt, 48), s["rstatb"][:, 32 + tt:33 + tt],
                                 s["rstatb"][:, 16 + tt:17 + tt], ALU.mult, ALU.mult)
        fV.inc(i1)
        for tt in range(NTT):
            i1 = v.tensor_scalar(R(tt, 24), R(tt, 24), s["rstatb"][:, 28 + tt:29 + tt],
                                 s["rstatb"][:, 20 + tt:21 + tt], ALU.mult, ALU.mult)
        fV.inc(i1)
        v.wait(fV, fV.n)
        for tt in range(NTT):
            inst = v.tensor_tensor(s["w8"][:, tt * 8:(tt + 1) * 8],
                                   R(tt, 48), R(tt, 24), ALU.add)
            mV.inc(inst)

    def tensor_w8(t):
        for tt in range(NTT):
            t.wait(mV, v0 + 8 + tt)
            if tt >= 1:
                t.wait(mV, v0 + 11 + tt)
            inst = t.transpose(p["pTf"][0:8, :], s["w8"][:, tt * 8:(tt + 1) * 8],
                               s["identf"][:])
            mT.inc(inst)

    def vector_w8(v):
        for tt in range(NTT):
            v.wait(mT, t0 + 4 + tt + 1)
            inst = v.tensor_copy(s["w8f"][0:8, tt * 128:(tt + 1) * 128], p["pTf"][0:8, :])
            mV.inc(inst)

    P.sec("tensor", tensor_r)
    P.sec("vector", vector_lg)
    P.sec("scalar", scalar_r)
    P.sec("vector", vector_r)
    P.sec("sync", sync_b1)
    P.sec("vector", vector_r1)
    P.sec("sync", sync_b2)
    P.sec("vector", vector_r2)
    P.sec("sync", sync_b3)
    P.sec("vector", vector_r3)
    P.sec("tensor", tensor_w8)
    P.sec("vector", vector_w8)
    P.barrier()


def ph_xga0(P: Prog, c):
    s, d, p = P.s, P.d, P.p
    mV, mS, mT = P.sems["mV"], P.sems["mS"], P.sems["mT"]
    v0, t0, s0 = mV.n, mT.n, mS.n
    wl = [s["wl0"], s["wl1"]]
    dL = [P.dL[0], P.dL[4]]
    adma = {}

    def sync(q):
        for i, (e, nm) in enumerate([(0, "ga0"), (0, "ua0"), (1, "ga0"), (1, "ua0")]):
            if i >= 2:
                q.wait(mV, v0 + (i - 1))
            adma[i] = dL[i % 2].dinc(q.dma_start(
                out=wl[i % 2][:, 0:1024].rearrange("p (k r) -> p k r", r=RADJ),
                in_=d[nm][e].rearrange("(k p) r -> p k r", p=128)))

    def vector(v):
        for i in range(4):
            v.wait(dL[i % 2], adma[i])
            if i >= 2:
                v.wait(mT, t0 + (i - 1))
            mV.inc(v.tensor_copy(s["adsl"][:, (i % 2) * 1024:(i % 2 + 1) * 1024],
                                 wl[i % 2][:, 0:1024]))

    def tensor(t):
        for i in range(4):
            t.wait(mV, v0 + i + 1)
            if i >= 2:
                t.wait(mS, s0 + (i - 1))
            pj = p["pA"] if i % 2 == 0 else p["pB"]
            for dk in range(NDK):
                inst = t.matmul(pj[0:64, :],
                                s["adsl"][:, (i % 2) * 1024 + dk * 64:(i % 2) * 1024 + (dk + 1) * 64],
                                s["xnT"][:, dk * TC:(dk + 1) * TC],
                                start=(dk == 0), stop=(dk == NDK - 1))
            mT.inc(inst)

    def scalar(sc):
        for i in range(4):
            sc.wait(mT, t0 + i + 1)
            pj = p["pA"] if i % 2 == 0 else p["pB"]
            e, which = divmod(i, 2)
            inst = sc.activation(s["xga"][0:64, (e * 2 + which) * TC:(e * 2 + which + 1) * TC],
                                 pj[0:64, :], AF.Copy)
            mS.inc(inst)

    P.sec("sync", sync)
    P.sec("vector", vector)
    P.sec("tensor", tensor)
    P.sec("scalar", scalar)
    P.barrier()


def ph_ffn0(P: Prog, c):
    s, d, p = P.s, P.d, P.p
    mV, mS, mT, mG = P.sems["mV"], P.sems["mS"], P.sems["mT"], P.sems["mG"]
    slots = [(s["wl0"], 0), (s["wl0"], 512), (s["wl0"], 1024), (s["wl0"], 1536),
             (s["wl1"], 0), (s["wl1"], 512), (s["wl1"], 1024), (s["wl1"], 1536)]
    nslot = len(slots)

    sync_ops, vec_ops, ten_ops, sca_ops, gps_ops = [], [], [], [], []

    class Ev:
        pass
    ev = Ev()
    ev.v, ev.t, ev.s, ev.g = mV.n, mT.n, mS.n, mG.n

    def S_dma(waits, out_ap, in_ap, slot):
        sem = P.dL[slot]
        sem.n += 16
        myval = sem.n
        def fn(q, out_ap=out_ap, in_ap=in_ap, sem=sem):
            inst = q.dma_start(out=out_ap, in_=in_ap)
            inst.then_inc(sem.h, 16)
            _UPDATED[id(inst)] = inst
        sync_ops.append((list(waits), fn))
        return (slot, myval)

    def V_op(waits, fn):
        ev.v += 1
        def wrap(v, fn=fn):
            mV.inc(fn(v))
        vec_ops.append((list(waits), wrap))
        return ev.v

    def T_group(waits, fn):
        ev.t += 1
        def wrap(t, fn=fn):
            mT.inc(fn(t))
        ten_ops.append((list(waits), wrap))
        return ev.t

    def C_op(waits, fn):
        ev.s += 1
        def wrap(sc, fn=fn):
            mS.inc(fn(sc))
        sca_ops.append((list(waits), wrap))
        return ev.s

    def G_op(waits, fn):
        ev.g += 1
        def wrap(g, fn=fn):
            mG.inc(fn(g))
        gps_ops.append((list(waits), wrap))
        return ev.g

    state = Ev()
    state.dmai = 0
    slot_last = [0] * nslot
    wb_guard = {"ws0": [], "ws1": []}

    def S_bf(waits, out_ap, in_ap, slot):
        sem = P.dL[slot]
        sem.n += 16
        myval = sem.n
        def fn(q, out_ap=out_ap, in_ap=in_ap, sem=sem):
            inst = q.dma_start(out=out_ap, in_=in_ap)
            inst.then_inc(sem.h, 16)
            _UPDATED[id(inst)] = inst
        sync_ops.append((list(waits), fn))
        return (slot, myval)

    def S_wb(cast_toks, dst, src_ap, slot):
        sem = P.dL[slot]
        sem.n += 16
        myval = sem.n
        w = [("v", m) for m in sorted({t for t in cast_toks if isinstance(t, int)})]
        for t in cast_toks:
            if isinstance(t, int):
                continue
        w = [("v", t) for t in cast_toks]
        def fn(q, dst=dst, src_ap=src_ap, sem=sem):
            inst = q.dma_start(out=dst.rearrange("p (a c) -> p a c", c=2048) if False
                               else dst, in_=src_ap)
            inst.then_inc(sem.h, 16)
            _UPDATED[id(inst)] = inst
        sync_ops.append((w, fn))
        return ("dl", (slot, myval))

    def stream_panel(src_ap, slab_ap, extra_cast_waits):
        width = src_ap.shape[-1]
        if width <= 512:
            sidx = state.dmai % nslot
            wlt, off = slots[sidx]
            w = [("v", slot_last[sidx])] if slot_last[sidx] else []
            dn = S_dma(w, wlt[:, off:off + width], src_ap, sidx)
            cn = V_op([("dl", dn)] + extra_cast_waits,
                      lambda v, wlt=wlt, off=off, width=width, slab_ap=slab_ap:
                      v.tensor_copy(slab_ap, wlt[:, off:off + width]))
            slot_last[sidx] = cn
            state.dmai += 1
            return cn
        assert width == 2048
        half = state.dmai % 2
        base = half * 4
        wlt = s["wl0"] if half == 0 else s["wl1"]
        w = [("v", m) for m in {slot_last[base + k] for k in range(4)} if m]
        dn = S_dma(w, wlt[:], src_ap, base)
        cn = V_op([("dl", dn)] + extra_cast_waits,
                  lambda v, wlt=wlt, slab_ap=slab_ap: v.tensor_copy(slab_ap, wlt[:]))
        for k in range(4):
            slot_last[base + k] = cn
        state.dmai += 1
        return cn

    def clobber_all_slots(val):
        for k in range(nslot):
            slot_last[k] = max(slot_last[k], val)

    last_gate_mm = 0
    last_up_mm = 0
    last_down_mm = 0
    bank_rd = {0: [], 1: []}
    wbc_rd = 0
    prev_wbc_mm = 0
    prev_silu = 0
    prev_mul = 0
    prev_pool = 0
    prev_mda_evac = 0

    for e in range(NE):
        prev_down_last = last_down_mm
        dMsem = P.sems["dM"]
        dMsem.n += 16
        dmval = dMsem.n
        gw = [("t", prev_wbc_mm)] if prev_wbc_mm else []
        def g_dma(q, e=e, dMsem=dMsem):
            inst = q.dma_start(out=P.row(2), in_=s["w8f"][e:e + 1, :])
            inst.then_inc(dMsem.h, 16)
            _UPDATED[id(inst)] = inst
        sync_ops.append((gw, g_dma))
        ww = list(bank_rd[1]) + [("dm", dmval)]
        if wbc_rd:
            ww.append(("g", wbc_rd))
        tw = T_group(ww, lambda t: t.matmul(
            p["pE"][0:1, :] if False else p["pE"][:], s["ones_row"][:],
            P.row(2), start=True, stop=True))
        prev_wbc_mm = tw
        wbc_done = C_op([("t", tw)], lambda sc: sc.activation(
            s["wbc"][:], p["pE"][:], AF.Copy))
        bank_rd[1] = [("s", wbc_done)]

        gbcast = ubcast = None
        if e < NL:
            dn = S_dma([("v", ev.v)], s["wl0"][0:64, :], d["gb0"][e], 0)
            gbcast = V_op([("dl", dn), ("t", last_gate_mm)],
                          lambda v: v.tensor_copy(s["adsl"][0:64, 0:2048],
                                                  s["wl0"][0:64, :]))
            dn2 = S_dma([("v", gbcast)], s["wl1"][0:64, :], d["ub0"][e], 4)
            ubcast = V_op([("dl", dn2), ("t", last_up_mm)],
                          lambda v: v.tensor_copy(s["adsl"][0:64, 2048:4096],
                                                  s["wl1"][0:64, :]))
            clobber_all_slots(ubcast)

        for fq in range(4):
            gcasts, ucasts = [], []
            if c == 0:
                for dk in range(NDK):
                    src = d["wg0"][e, dk * 128:(dk + 1) * 128, fq * 512:(fq + 1) * 512]
                    cwl = [("t", last_gate_mm), ("t", last_down_mm)] + wb_guard["ws0"]
                    gcasts.append(stream_panel(src, s["ws0"][:, dk * 512:(dk + 1) * 512], cwl))
                for dk in range(NDK):
                    src = d["wu0"][e, dk * 128:(dk + 1) * 128, fq * 512:(fq + 1) * 512]
                    cwl = [("t", last_up_mm), ("t", last_down_mm)] + wb_guard["ws1"]
                    ucasts.append(stream_panel(src, s["ws1"][:, dk * 512:(dk + 1) * 512], cwl))
                wb_guard["ws0"] = [S_wb(gcasts, d["wg0b"][e * 4 + fq], s["ws0"][:], 13)]
                wb_guard["ws1"] = [S_wb(ucasts, d["wu0b"][e * 4 + fq], s["ws1"][:], 14)]
            else:
                g_ld = S_bf(
                    [("t", last_gate_mm), ("t", last_down_mm)],
                    s["ws0"][:].rearrange("p (a c) -> p a c", c=2048),
                    d["wg0b"][e * 4 + fq].rearrange("p (a c) -> p a c", c=2048), 8)
                u_ld = S_bf(
                    [("t", last_up_mm), ("t", last_down_mm)],
                    s["ws1"][:].rearrange("p (a c) -> p a c", c=2048),
                    d["wu0b"][e * 4 + fq].rearrange("p (a c) -> p a c", c=2048), 12)
                gcasts = [("dl", g_ld)]
                ucasts = [("dl", u_ld)]
            for ftl in range(4):
                ft = fq * 4 + ftl

                def g_mm(t, e=e, ft=ft, ftl=ftl):
                    ad = e < NL
                    for dk in range(NDK):
                        inst = t.matmul(
                            p["pA"][:],
                            s["ws0"][:, dk * 512 + ftl * 128: dk * 512 + ftl * 128 + 128],
                            s["xnT"][:, dk * TC:(dk + 1) * TC],
                            start=(dk == 0), stop=(dk == NDK - 1) and not ad)
                    if ad:
                        inst = t.matmul(p["pA"][:],
                                        s["adsl"][0:64, ft * 128:(ft + 1) * 128],
                                        s["xga"][0:64, (e * 2) * TC:(e * 2 + 1) * TC],
                                        start=False, stop=True)
                    return inst

                gwl = [gcasts[0] if c > 0 else ("v", max(gcasts)), ("s", prev_silu)]
                if e < NL:
                    gwl.append(("v", gbcast))
                tg = T_group(gwl, g_mm)
                last_gate_mm = tg

                def u_mm(t, e=e, ft=ft, ftl=ftl):
                    ad = e < NL
                    for dk in range(NDK):
                        inst = t.matmul(
                            p["pB"][:],
                            s["ws1"][:, dk * 512 + ftl * 128: dk * 512 + ftl * 128 + 128],
                            s["xnT"][:, dk * TC:(dk + 1) * TC],
                            start=(dk == 0), stop=(dk == NDK - 1) and not ad)
                    if ad:
                        inst = t.matmul(p["pB"][:],
                                        s["adsl"][0:64, 2048 + ft * 128:2048 + (ft + 1) * 128],
                                        s["xga"][0:64, (e * 2 + 1) * TC:(e * 2 + 2) * TC],
                                        start=False, stop=True)
                    return inst

                uwl = [ucasts[0] if c > 0 else ("v", max(ucasts)), ("v", prev_mul)]
                if e < NL:
                    uwl.append(("v", ubcast))
                tu = T_group(uwl, u_mm)
                last_up_mm = tu

                cs = C_op([("t", tg), ("v", prev_mul)],
                          lambda sc: sc.activation(s["scr"][0][:], p["pA"][:], AF.Silu))
                prev_silu = cs
                mv_ = V_op([("s", cs), ("t", tu), ("g", prev_pool)],
                           lambda v: v.tensor_tensor(s["scr"][1][:], s["scr"][0][:],
                                                     p["pB"][:], ALU.mult))
                prev_mul = mv_
                pg = G_op([("v", mv_), ("s", wbc_done), ("t", prev_down_last)],
                          lambda g, ft=ft: g.tensor_tensor(
                              s["mid"][:, ft * TC:(ft + 1) * TC], s["scr"][1][:],
                              s["wbc"][:], ALU.mult))
                prev_pool = pg
        wbc_rd = prev_pool

        mda_done = None
        if e < NL:
            dn = S_dma([("v", ev.v)], s["wl0"][:, 0:1024].rearrange(
                "p (k r) -> p k r", r=RADJ),
                d["da0"][e].rearrange("(k p) r -> p k r", p=128), 0)
            dacast = V_op([("dl", dn), ("t", last_gate_mm)],
                          lambda v: v.tensor_copy(s["adsl"][:, 0:1024],
                                                  s["wl0"][:, 0:1024]))
            dn2 = S_dma([("v", dacast)], s["wl1"][0:64, :], d["db0"][e], 4)
            dbcast = V_op([("dl", dn2), ("t", last_up_mm)],
                          lambda v: v.tensor_copy(s["adsl"][0:64, 2048:4096],
                                                  s["wl1"][0:64, :]))
            clobber_all_slots(dbcast)

            def mda_mm(t):
                for fk in range(NDK):
                    inst = t.matmul(p["pD"][0:64, :],
                                    s["adsl"][:, fk * 64:(fk + 1) * 64],
                                    s["mid"][:, fk * TC:(fk + 1) * TC],
                                    start=(fk == 0), stop=(fk == NDK - 1))
                return inst
            mdw = [("v", dacast), ("g", prev_pool)]
            if prev_mda_evac:
                mdw.append(("s", prev_mda_evac))
            tmda = T_group(mdw, mda_mm)
            mda_done = C_op([("t", tmda)], lambda sc: sc.activation(
                s["mda"][0:64, :], p["pD"][0:64, :], AF.Copy))
            prev_mda_evac = mda_done

        for half in range(2):
            dcasts = []
            if c == 0:
                for fkl in range(8):
                    fk = half * 8 + fkl
                    src = d["wd0"][e, fk * 128:(fk + 1) * 128, :]
                    tgt = s["ws0"] if fkl < 4 else s["ws1"]
                    cwl = [("t", last_down_mm), ("t", last_gate_mm), ("t", last_up_mm)]
                    cwl += wb_guard["ws0"] if fkl < 4 else wb_guard["ws1"]
                    dcasts.append(stream_panel(
                        src, tgt[:, (fkl % 4) * 2048:(fkl % 4 + 1) * 2048], cwl))
                wb_guard["ws0"] = [S_wb(dcasts[:4], d["wd0b"][e * 4 + half * 2],
                                        s["ws0"][:], 13)]
                wb_guard["ws1"] = [S_wb(dcasts[4:], d["wd0b"][e * 4 + half * 2 + 1],
                                        s["ws1"][:], 14)]
                dtok = [("v", max(dcasts))]
            else:
                d_ld0 = S_bf([("t", last_down_mm), ("t", last_gate_mm), ("t", last_up_mm)],
                             s["ws0"][:].rearrange("p (a c) -> p a c", c=2048),
                             d["wd0b"][e * 4 + half * 2].rearrange("p (a c) -> p a c", c=2048), 8)
                d_ld1 = S_bf([("t", last_down_mm), ("t", last_up_mm)],
                             s["ws1"][:].rearrange("p (a c) -> p a c", c=2048),
                             d["wd0b"][e * 4 + half * 2 + 1].rearrange("p (a c) -> p a c", c=2048), 12)
                dtok = [("dl", d_ld0), ("dl", d_ld1)]
            for dt in range(NDK):
                bank = dt % 2
                pb = p["pC"] if bank == 0 else p["pE"]

                def d_mm(t, e=e, half=half, dt=dt, pb=pb):
                    ad = (half == 0 and e < NL)
                    for fkl in range(8):
                        fk = half * 8 + fkl
                        slab = s["ws0"] if fkl < 4 else s["ws1"]
                        inst = t.matmul(
                            pb[:],
                            slab[:, (fkl % 4) * 2048 + dt * 128:(fkl % 4) * 2048 + dt * 128 + 128],
                            s["mid"][:, fk * TC:(fk + 1) * TC],
                            start=(fkl == 0), stop=(fkl == 7) and not ad)
                    if ad:
                        inst = t.matmul(pb[:],
                                        s["adsl"][0:64, 2048 + dt * 128:2048 + (dt + 1) * 128],
                                        s["mda"][0:64, :],
                                        start=False, stop=True)
                    return inst

                dwl = list(dtok) + [("g", prev_pool)] + list(bank_rd[bank])
                if half == 0 and e < NL:
                    dwl.append(("s", mda_done))
                td = T_group(dwl, d_mm)
                last_down_mm = td

                def acc_fn(v, dt=dt, pb=pb, do_copy=(e == 0 and half == 0)):
                    if do_copy:
                        return v.tensor_copy(s["accx"][:, dt * TC:(dt + 1) * TC], pb[:])
                    return v.tensor_tensor(s["accx"][:, dt * TC:(dt + 1) * TC],
                                           s["accx"][:, dt * TC:(dt + 1) * TC],
                                           pb[:], ALU.add)
                av = V_op([("t", td)], acc_fn)
                bank_rd[bank] = [("v", av)]

    def run_ops(ops):
        def fn(eng):
            for waits, op in ops:
                seen = set()
                for kind, val in waits:
                    if val == 0 or val == (0, 0) or (kind, val) in seen:
                        continue
                    seen.add((kind, val))
                    if kind == "dl":
                        slot, v16 = val
                        eng.wait(P.dL[slot], v16)
                    else:
                        sem = {"v": mV, "t": mT, "s": mS, "g": mG,
                               "dm": P.sems["dM"]}[kind]
                        eng.wait(sem, val)
                op(eng)
        return fn

    P.sec("sync", run_ops(sync_ops))
    P.sec("tensor", run_ops(ten_ops))
    P.sec("scalar", run_ops(sca_ops))
    P.sec("vector", run_ops(vec_ops))
    P.sec("gpsimd", run_ops(gps_ops))
    assert mV.n == ev.v and mT.n == ev.t and mS.n == ev.s and mG.n == ev.g
    P.barrier()


class Sched:
    def __init__(self, P: Prog):
        self.P = P
        self.ops = {e: [] for e in ENGINES}
        self.sems = P.sems
        self.slots = [(P.s["wl0"], 0), (P.s["wl0"], 512), (P.s["wl0"], 1024),
                      (P.s["wl0"], 1536), (P.s["wl1"], 0), (P.s["wl1"], 512),
                      (P.s["wl1"], 1024), (P.s["wl1"], 1536)]
        self.slot_last = [0] * 8
        self.dmai = 0
        self.v = P.sems["mV"].n
        self.t = P.sems["mT"].n
        self.c = P.sems["mS"].n
        self.g = P.sems["mG"].n
        self.f = P.sems["fV"].n

    def _add(self, eng, waits, emit):
        self.ops[eng].append((list(waits), emit))

    def dma(self, waits, out_ap, in_ap, slot):
        sem = self.P.dL[slot]
        sem.n += 16
        val = sem.n
        def emit(q, out_ap=out_ap, in_ap=in_ap, sem=sem):
            inst = q.dma_start(out=out_ap, in_=in_ap)
            inst.then_inc(sem.h, 16)
            _UPDATED[id(inst)] = inst
        self._add("sync", waits, emit)
        return ("dl", (slot, val))

    def dma_plain(self, waits, out_ap, in_ap, semname="dM"):
        sem = self.sems[semname]
        sem.n += 16
        val = sem.n
        def emit(q, out_ap=out_ap, in_ap=in_ap, sem=sem):
            inst = q.dma_start(out=out_ap, in_=in_ap)
            inst.then_inc(sem.h, 16)
            _UPDATED[id(inst)] = inst
        self._add("sync", waits, emit)
        return (semname, val)

    def V(self, waits, fn, flush=False):
        self.v += 1
        def emit(v, fn=fn):
            self.sems["mV"].inc(fn(v))
        self._add("vector", waits, emit)
        tok = ("v", self.v)
        if flush:
            self._add("vector", [tok], lambda v: None)
        return tok

    def Vf(self, waits, fn):
        self.f += 1
        def emit(v, fn=fn):
            self.sems["fV"].inc(fn(v))
        self._add("vector", waits, emit)
        tok = ("f", self.f)
        self._add("vector", [tok], lambda v: None)
        return tok

    def T(self, waits, fn):
        self.t += 1
        def emit(t, fn=fn):
            self.sems["mT"].inc(fn(t))
        self._add("tensor", waits, emit)
        return ("t", self.t)

    def C(self, waits, fn):
        self.c += 1
        def emit(sc, fn=fn):
            self.sems["mS"].inc(fn(sc))
        self._add("scalar", waits, emit)
        return ("s", self.c)

    def G(self, waits, fn):
        self.g += 1
        def emit(g, fn=fn):
            self.sems["mG"].inc(fn(g))
        self._add("gpsimd", waits, emit)
        return ("g", self.g)

    def stream(self, waits, src_ap, slab_ap, extra_cast_waits=()):
        shape = list(src_ap.shape)
        width = 1
        for x in shape[1:]:
            width *= x
        inner = shape[-1]
        def land_view(wlt, off):
            flat = wlt[:, off:off + width]
            if len(shape) == 3:
                return flat.rearrange("p (k m) -> p k m", m=inner)
            return flat
        if width <= 512:
            sidx = self.dmai % 8
            wlt, off = self.slots[sidx]
            w = list(waits) + ([("v", self.slot_last[sidx])] if self.slot_last[sidx] else [])
            dn = self.dma(w, land_view(wlt, off), src_ap, sidx)
            cn = self.V([dn] + list(extra_cast_waits),
                        lambda v, wlt=wlt, off=off, width=width, slab_ap=slab_ap:
                        v.tensor_copy(slab_ap, wlt[:, off:off + width]))
            self.slot_last[sidx] = cn[1]
            self.dmai += 1
            return cn
        assert width in (1024, 2048)
        half = self.dmai % 2
        base = half * 4
        wlt = self.P.s["wl0"] if half == 0 else self.P.s["wl1"]
        w = list(waits) + [("v", m) for m in
                           sorted({self.slot_last[base + k] for k in range(4)}) if m]
        dn = self.dma(w, land_view(wlt, 0), src_ap, base)
        cn = self.V([dn] + list(extra_cast_waits),
                    lambda v, wlt=wlt, width=width, slab_ap=slab_ap:
                    v.tensor_copy(slab_ap, wlt[:, 0:width]))
        for k in range(4):
            self.slot_last[base + k] = cn[1]
        self.dmai += 1
        return cn

    def finish(self, barrier=True):
        P = self.P
        semmap = {"v": P.sems["mV"], "t": P.sems["mT"], "s": P.sems["mS"],
                  "g": P.sems["mG"], "f": P.sems["fV"], "dM": P.sems["dM"],
                  "dT": P.sems["dT"], "dS": P.sems["dS"], "dO": P.sems["dO"],
                  "dW": P.sems["dW"], "dX": P.sems["dX"]}
        for eng in ENGINES:
            ops = self.ops[eng]
            if not ops:
                continue
            def fn(e, ops=ops):
                for waits, emit in ops:
                    seen = set()
                    for kind, val in waits:
                        if (kind, val) in seen:
                            continue
                        seen.add((kind, val))
                        if kind == "dl":
                            slot, v16 = val
                            e.wait(P.dL[slot], v16)
                        else:
                            if val == 0:
                                continue
                            e.wait(semmap[kind], val)
                    emit(e)
            P.sec(eng, fn)
        assert P.sems["mV"].n == self.v and P.sems["mT"].n == self.t
        assert P.sems["mS"].n == self.c and P.sems["mG"].n == self.g
        assert P.sems["fV"].n == self.f
        if barrier:
            P.barrier()


def _seg_kinds(segs):
    return segs


def ph_attn0(P: Prog, c):
    s, d, p = P.s, P.d, P.p
    S = Sched(P)
    band = _band0(c)
    nkt = len(band)
    tbc = s["tb"][:, 0:TC]
    tbs = s["tb"][:, TC:2 * TC]

    prev_qevac_tok = prev_qrope_tok = prev_afm_tok = prev_rot_tok = None
    prev_tb_tok = None
    qproj_tok = {}
    for h in range(L0_NH):
        kh = h // 4
        cast_w = []
        if h >= 4:
            cast_w.append(qproj_tok[h - 4])
        cn = S.stream([], d["wq0"][:, h * 128:(h + 1) * 128].rearrange(
            "(k p) m -> p k m", p=128),
            s["ws1"][:, (h % 4) * 2048:(h % 4 + 1) * 2048], cast_w)
        qw = [cn]
        if prev_qevac_tok:
            qw.append(prev_qevac_tok)
        def qmm(t, h=h):
            for dk in range(NDK):
                inst = t.matmul(p["pA"][:],
                                s["ws1"][:, (h % 4) * 2048 + dk * 128:(h % 4) * 2048 + (dk + 1) * 128],
                                s["xnT"][:, dk * TC:(dk + 1) * TC],
                                start=(dk == 0), stop=(dk == NDK - 1))
            return inst
        tq = S.T(qw, qmm)
        qproj_tok[h] = tq
        ew = [tq]
        if prev_qrope_tok:
            ew.append(prev_qrope_tok)
        ce = S.C(ew, lambda sc: sc.activation(s["scr"][1][:], p["pA"][:], AF.Copy))
        prev_qevac_tok = ce
        bw = [ce]
        if prev_rot_tok:
            bw.append(prev_rot_tok)
        vb = S.V(bw, lambda v: v.tensor_copy(s["qt"][:], s["scr"][1][:]))
        rw = [vb]
        if prev_qrope_tok:
            rw.append(prev_qrope_tok)
        tr_ = S.T(rw, lambda t: t.matmul(p["pC"][:], s["rot0"][:], s["qt"][:],
                                         start=True, stop=True))
        prev_rot_tok = tr_
        f1 = S.Vf([tr_], lambda v: v.tensor_tensor(s["wbc"][:], s["scr"][1][:], tbc, ALU.mult))
        f2 = S.Vf([f1], lambda v: v.tensor_tensor(s["scr"][1][:], p["pC"][:], tbs, ALU.mult))
        vq = S.V([f2], lambda v: v.tensor_tensor(s["qt"][:], s["wbc"][:], s["scr"][1][:],
                                                 ALU.add))
        prev_qrope_tok = vq
        prev_tanh = {}
        prev_exp = {}
        prev_pv = {}
        for i, (kt, segs) in enumerate(band):
            psc = p["pD"] if i % 2 == 0 else p["pA"]
            sw = [vq]
            if i % 2 == 0 and i >= 2:
                sw.append(prev_tanh[i - 2])
            if i % 2 == 1:
                sw.append(ce)
                if i >= 3:
                    sw.append(prev_tanh[i - 2])
            tsc = S.T(sw, lambda t, kt=kt, psc=psc, kh=kh: t.matmul(
                psc[:], s["k0s"][:, kh * T + kt * 128: kh * T + (kt + 1) * 128],
                s["qt"][:], start=True, stop=True))
            tw = [tsc]
            if i >= 1:
                tw.append(prev_exp[i - 1])
            cth = S.C(tw, lambda sc, psc=psc: sc.activation(
                s["scr"][0][:], psc[:], AF.Tanh, scale=1.0 / (L0_CAP * math.sqrt(L0_HD))))
            prev_tanh[i] = cth
            mtoks = []
            for tl, kind in enumerate(segs):
                if kind in ("diag", "edge"):
                    mask = s["mdiag"] if kind == "diag" else s["medge"]
                    mtoks.append(S.V([cth], lambda v, tl=tl, mask=mask: v.tensor_tensor(
                        s["scr"][0][:, tl * 128:(tl + 1) * 128],
                        s["scr"][0][:, tl * 128:(tl + 1) * 128], mask[:], ALU.add)))
            xw = [cth] + mtoks
            if i >= 2:
                xw.append(prev_pv[i - 2])
            cxp = S.C(xw, lambda sc, i=i: sc.activation(
                s["pst"][i % 2][:], s["scr"][0][:], AF.Exp, scale=L0_CAP))
            prev_exp[i] = cxp
            ztoks = []
            for tl, kind in enumerate(segs):
                if kind == "zero":
                    ztoks.append(S.V([cxp], lambda v, tl=tl, i=i: v.memset(
                        s["pst"][i % 2][:, tl * 128:(tl + 1) * 128], 0.0)))
            pw = [cxp] + ztoks
            if i == 0 and prev_afm_tok:
                pw.append(prev_afm_tok)
            t1 = S.T(pw, lambda t, kt=kt, i=i, kh=kh: t.matmul(
                p["pB"][:], s["v0s"][:, (kh * 16 + kt) * 128:(kh * 16 + kt + 1) * 128],
                s["pst"][i % 2][:], start=(i == 0), stop=(i == nkt - 1)))
            t2 = S.T([], lambda t, i=i: t.matmul(
                p["pE"][0:1, :], s["ones_col"][:], s["pst"][i % 2][:],
                start=(i == 0), stop=(i == nkt - 1)))
            prev_pv[i] = t2
        vrw = [prev_pv[nkt - 1]]
        if prev_tb_tok:
            vrw.append(prev_tb_tok)
        vr = S.V(vrw, lambda v: v.reciprocal(P.row(2), p["pE"][0:1, :]))
        bw2 = [vr, prev_tanh[nkt - 1] if (nkt - 1) % 2 == 0 else prev_tanh[nkt - 2]]
        tb_ = S.T(bw2, lambda t: t.matmul(p["pD"][:], s["ones_row"][:],
                                          P.row(2), start=True, stop=True))
        prev_tb_tok = tb_
        bev_w = [tb_]
        if prev_afm_tok:
            bev_w.append(prev_afm_tok)
        cbe = S.C(bev_w, lambda sc: sc.activation(s["scr"][2][:], p["pD"][:], AF.Copy))
        va = S.V([cbe, prev_pv[nkt - 1], t1], lambda v, h=h: v.tensor_tensor(
            s["afm"][:, h * TC:(h + 1) * TC], p["pB"][:], s["scr"][2][:], ALU.mult))
        prev_afm_tok = va
    S.finish()


def ph_comb0(P: Prog, c):
    s, d, p = P.s, P.d, P.p
    S = Sched(P)

    t_s = S.T([], lambda t: t.matmul(p["pD"][:], s["ones_row"][:], P.row(0),
                                     start=True, stop=True))
    t_m = S.T([], lambda t: t.matmul(p["pE"][:], s["ones_row"][:], P.row(1),
                                     start=True, stop=True))
    c_s = S.C([t_s], lambda sc: sc.activation(s["scr"][1][:], p["pD"][:], AF.Copy))
    c_m = S.C([t_m], lambda sc: sc.activation(s["scr"][2][:], p["pE"][:], AF.Copy))

    prev_comb = {}
    prev_silu = {}
    prev_t1 = {}
    prev_dense = {}
    casts = {}
    sq_mm = {}
    sq_op = {}
    for dt in range(NDK):
        cw = []
        if dt >= 4:
            cw.append(prev_dense[dt - 4])
        casts[dt] = S.stream([], d["wo0"][:, dt * 128:(dt + 1) * 128].rearrange(
            "(k p) m -> p k m", p=128), s["ws0"][:, (dt % 4) * 2048:(dt % 4 + 1) * 2048],
            cw)
        pb = p["pA"] if dt % 2 == 0 else p["pB"]
        dwl = [casts[dt]]
        if dt >= 2:
            dwl.append(prev_t1[dt - 2])

        def dmm(t, dt=dt, pb=pb):
            for ht in range(L0_NH):
                inst = t.matmul(pb[:],
                                s["ws0"][:, (dt % 4) * 2048 + ht * 128:(dt % 4) * 2048 + ht * 128 + 128],
                                s["afm"][:, ht * TC:(ht + 1) * TC],
                                start=(ht == 0), stop=(ht == L0_NH - 1))
            return inst
        td = S.T(dwl, dmm)
        prev_dense[dt] = td
        sw = [c_s, c_m]
        if dt >= 1:
            sw.append(prev_t1[dt - 1])
        csl = S.C(sw, lambda sc, dt=dt: sc.activation(
            s["scr"][0][:], s["accx"][:, dt * TC:(dt + 1) * TC], AF.Silu))
        prev_silu[dt] = csl
        t1w = [td, csl]
        t1_ = S.Vf(t1w, lambda v, pb=pb: v.tensor_tensor(
            s["wbc"][:], pb[:], s["scr"][0][:], ALU.mult))
        prev_t1[dt] = t1_
        a1 = S.Vf([t1_, csl], lambda v, dt=dt: v.tensor_tensor(
            s["accx"][:, dt * TC:(dt + 1) * TC],
            s["xnT"][:, dt * TC:(dt + 1) * TC], s["scr"][1][:], ALU.mult))
        a2 = S.Vf([a1], lambda v, dt=dt: v.tensor_tensor(
            s["accx"][:, dt * TC:(dt + 1) * TC],
            s["accx"][:, dt * TC:(dt + 1) * TC], s["scr"][2][:], ALU.add))
        a3 = S.V([a2], lambda v, dt=dt: v.tensor_tensor(
            s["accx"][:, dt * TC:(dt + 1) * TC],
            s["accx"][:, dt * TC:(dt + 1) * TC], s["wbc"][:], ALU.add))
        prev_comb[dt] = a3

    for dt in range(NDK):
        sqw = [prev_comb[dt]]
        if dt >= 1:
            sqw.append(sq_mm[dt - 1])
        else:
            sqw.append(prev_t1[NDK - 1])
        csq = S.C(sqw, lambda sc, dt=dt: sc.activation(
            s["scr"][0][:], s["accx"][:, dt * TC:(dt + 1) * TC], AF.Square))
        sq_op[dt] = csq
        m1w = [prev_comb[dt]]
        if dt == 0:
            m1w += [prev_t1[NDK - 1], prev_t1[NDK - 2]]
        t1_ = S.T(m1w, lambda t, dt=dt: t.matmul(
            p["pC"][0:1, :], s["crows"][:, 4:5], s["accx"][:, dt * TC:(dt + 1) * TC],
            start=(dt == 0), stop=(dt == NDK - 1)))
        t2_ = S.T([csq], lambda t, dt=dt: t.matmul(
            p["pE"][0:1, :], s["crows"][:, 4:5], s["scr"][0][:],
            start=(dt == 0), stop=(dt == NDK - 1)))
        sq_mm[dt] = t2_
    sum_mm = t1_
    ssq_mm = t2_

    M0, M1, M2 = P.row(0), P.row(1), P.row(2)
    f1 = S.Vf([sum_mm], lambda v: v.tensor_scalar(M0, p["pC"][0:1, :], 1.0 / D, None, ALU.mult))
    f2 = S.Vf([ssq_mm, f1], lambda v: v.tensor_scalar(M1, p["pE"][0:1, :], 1.0 / D, None, ALU.mult))
    f3 = S.Vf([f2], lambda v: v.tensor_tensor(M2, M0, M0, ALU.mult))
    f4 = S.V([f3], lambda v: v.tensor_tensor(M1, M1, M2, ALU.subtract))
    c_sq = S.C([f4], lambda sc: sc.activation(M2, M1, AF.Sqrt, bias=s["crows"][0:1, 1:2]))
    f5 = S.Vf([c_sq], lambda v: v.reciprocal(M1, M2))
    f6 = S.Vf([f5], lambda v: v.tensor_tensor(M2, M0, M1, ALU.mult))
    f7 = S.V([f6], lambda v: v.tensor_scalar(M2, M2, -1.0, None, ALU.mult))
    fp1 = S.Vf([f7], lambda v: v.tensor_copy(P.row(3), M0))
    fp2 = S.Vf([fp1], lambda v: v.tensor_copy(P.row(4), M1))

    tb1 = S.T([f5, f7, fp2] + [prev_t1[NDK - 1]], lambda t: t.matmul(
        p["pD"][:], s["ones_row"][:], M1, start=True, stop=True))
    tb2 = S.T([f7], lambda t: t.matmul(
        p["pE"][:], s["ones_row"][:], M2, start=True, stop=True))
    cb1w = [tb1, prev_silu[NDK - 1]]
    cb1 = S.C(cb1w, lambda sc: sc.activation(s["scr"][1][:], p["pD"][:], AF.Copy))
    cb2 = S.C([tb2], lambda sc: sc.activation(s["scr"][2][:], p["pE"][:], AF.Copy))

    prev_add = None
    for dt in range(NDK):
        uw = [cb1, cb2, sq_mm[NDK - 1], sum_mm]
        if prev_add:
            uw.append(prev_add)
        u1 = S.Vf(uw, lambda v, dt=dt: v.tensor_tensor(
            s["wbc"][:], s["accx"][:, dt * TC:(dt + 1) * TC],
            s["scr"][1][:], ALU.mult))
        prev_add = S.V([u1], lambda v, dt=dt: v.tensor_tensor(
            s["xnT"][:, dt * TC:(dt + 1) * TC], s["wbc"][:], s["scr"][2][:], ALU.add))
    S.finish()


def ph_prep1(P: Prog, c):
    s, d, p = P.s, P.d, P.p
    S = Sched(P)
    d1 = S.dma_plain([], s["scr"][1][:], d["c_cos1"][:, c * TC:(c + 1) * TC], "dT")
    d2 = S.dma_plain([], s["scr"][2][:], d["c_sin1"][:, c * TC:(c + 1) * TC], "dT")
    tb1 = S.V([d2], lambda v: v.tensor_copy(s["tb"][:, 0:TC], s["scr"][1][:]))
    tb2 = S.V([tb1], lambda v: v.tensor_copy(s["tb"][:, TC:2 * TC], s["scr"][2][:]))
    tbc = s["tb"][0:64, 0:TC]
    tbs = s["tb"][0:64, TC:2 * TC]
    cn = S.stream([], d["wqkv1"][:, 2048:2176].rearrange("(k p) m -> p k m", p=128),
                  s["ws0"][:, 0:2048])
    def kvmm(t):
        for dk in range(NDK):
            inst = t.matmul(p["pA"][:], s["ws0"][:, dk * 128:(dk + 1) * 128],
                            s["xnT"][:, dk * TC:(dk + 1) * TC],
                            start=(dk == 0), stop=(dk == NDK - 1))
        return inst
    tkv = S.T([cn], kvmm)
    ce = S.C([tkv, tb2], lambda sc: sc.activation(s["scr"][1][0:64, :], p["pA"][0:64, :],
                                                  AF.Copy))
    vb = S.V([ce], lambda v: v.tensor_copy(s["qt"][0:64, :], s["scr"][1][0:64, :]))
    tr_ = S.T([vb], lambda t: t.matmul(p["pC"][0:64, :], s["rot1"][0:64, 0:64],
                                       s["qt"][0:64, :], start=True, stop=True))
    f1 = S.Vf([tr_], lambda v: v.tensor_tensor(s["wbc"][0:64, :], s["scr"][1][0:64, :],
                                               tbc, ALU.mult))
    f2 = S.Vf([f1], lambda v: v.tensor_tensor(s["scr"][1][0:64, :], p["pC"][0:64, :],
                                              tbs, ALU.mult))
    k1w = S.V([f2], lambda v: v.tensor_tensor(
        s["k1s"][0:64, c * TC:(c + 1) * TC], s["wbc"][0:64, :], s["scr"][1][0:64, :],
        ALU.add))
    S.V([k1w], lambda v: v.tensor_copy(s["k1s"][64:128, c * TC:(c + 1) * TC],
                                       s["k1s"][0:64, c * TC:(c + 1) * TC]))
    cv = S.C([tkv], lambda sc: sc.activation(s["scrb"][0:64, :], p["pA"][64:128, :],
                                             AF.Copy))
    prev_ev = None
    for j in range(NTT):
        tw = [cv]
        if prev_ev is not None and j >= 2:
            tw.append(evs[j - 2])
        tt_ = S.T(tw, lambda t, j=j: t.transpose(
            p["pT0" if j % 2 == 0 else "pT1"][:, 0:64],
            s["scrb"][0:64, j * 128:(j + 1) * 128], s["identb"][0:64, 0:64]))
        kt = 4 * c + j
        ev = S.V([tt_], lambda v, j=j, kt=kt: v.tensor_copy(
            s["v1s"][:, kt * 64:(kt + 1) * 64],
            p["pT0" if j % 2 == 0 else "pT1"][:, 0:64]))
        if j == 0:
            evs = {}
        evs[j] = ev
        prev_ev = ev
    S.finish()


def ph_attn1(P: Prog, c):
    s, d, p = P.s, P.d, P.p
    S = Sched(P)
    band = _band1(c)
    nkt = len(band)
    tbcF = s["tb"][:, 0:TC]
    tbsF = s["tb"][:, TC:2 * TC]

    prev_qevac = prev_qrope = prev_afm = prev_rot = None
    prev_bc = None
    qproj_tok = {}
    for j in range(L1_NH // 2):
        cast_w = []
        if j >= 4:
            cast_w.append(qproj_tok[j - 4])
        cn = S.stream([], d["wqkv1"][:, j * 128:(j + 1) * 128].rearrange(
            "(k p) m -> p k m", p=128),
            s["ws1"][:, (j % 4) * 2048:(j % 4 + 1) * 2048], cast_w)
        qw = [cn]
        if prev_qevac:
            qw.append(prev_qevac)
        def qmm(t, j=j):
            for dk in range(NDK):
                inst = t.matmul(p["pA"][:],
                                s["ws1"][:, (j % 4) * 2048 + dk * 128:(j % 4) * 2048 + (dk + 1) * 128],
                                s["xnT"][:, dk * TC:(dk + 1) * TC],
                                start=(dk == 0), stop=(dk == NDK - 1))
            return inst
        tq = S.T(qw, qmm)
        qproj_tok[j] = tq
        ew = [tq]
        if prev_qrope:
            ew.append(prev_qrope)
        ce = S.C(ew, lambda sc: sc.activation(s["scr"][1][:], p["pA"][:], AF.Copy))
        prev_qevac = ce
        bw = [ce]
        if prev_rot:
            bw.append(prev_rot)
        vb = S.V(bw, lambda v: v.tensor_copy(s["qt"][:], s["scr"][1][:]))
        rw = [vb]
        if prev_qrope:
            rw.append(prev_qrope)
        tro = S.T(rw, lambda t: t.matmul(p["pC"][:], s["rot1"][:], s["qt"][:],
                                         start=True, stop=True))
        prev_rot = tro
        f1 = S.Vf([tro], lambda v: v.tensor_tensor(s["wbc"][:], s["scr"][1][:], tbcF, ALU.mult))
        f2 = S.Vf([f1], lambda v: v.tensor_tensor(s["scr"][1][:], p["pC"][:], tbsF, ALU.mult))
        vq = S.V([f2], lambda v: v.tensor_tensor(s["qt"][:], s["wbc"][:], s["scr"][1][:],
                                                 ALU.add))
        prev_qrope = vq

        ctx_evac = {}
        recips = {}
        for par in range(2):
            b0 = par * 64
            prev_exp = {}
            prev_pv = {}
            prev_diag = None
            for i, (kt, segs) in enumerate(band):
                psc = p["pD"] if i % 2 == 0 else p["pC"]
                sw = [vq]
                if i >= 2:
                    sw.append(prev_exp[i - 2])
                if i < 2 and par == 1:
                    sw.append(ctx_evac[0])
                if i < 2 and par == 0 and prev_bc:
                    sw.append(prev_bc)
                tsc = S.T(sw, lambda t, kt=kt, psc=psc, b0=b0: t.matmul(
                    psc[:], s["k1s"][b0:b0 + 64, kt * 128:(kt + 1) * 128],
                    s["qt"][b0:b0 + 64, :], start=True, stop=True))
                etoks = []
                for tl, kind in enumerate(segs):
                    sl = slice(tl * 128, (tl + 1) * 128)
                    if kind == "keep":
                        w = [tsc]
                        if i >= 2:
                            w.append(prev_pv[i - 2])
                        etoks.append(S.C(w, lambda sc, psc=psc, sl=sl, i=i:
                                         sc.activation(s["pst"][i % 2][:, sl], psc[:, sl],
                                                       AF.Exp, scale=0.125,
                                                       bias=s["crows"][:, 0:1])))
                    elif kind == "diag":
                        mw = [tsc]
                        if prev_diag:
                            mw.append(prev_diag)
                        vm = S.Vf(mw, lambda v, psc=psc, sl=sl: v.tensor_tensor(
                            s["scr"][0][:, 0:128], psc[:, sl], s["mdiag"][:], ALU.add))
                        w = [vm]
                        if i >= 2:
                            w.append(prev_pv[i - 2])
                        tok = S.C(w, lambda sc, sl=sl, i=i: sc.activation(
                            s["pst"][i % 2][:, sl], s["scr"][0][:, 0:128],
                            AF.Exp, scale=0.125, bias=s["crows"][:, 0:1]))
                        etoks.append(tok)
                        prev_diag = tok
                    else:
                        w = [tsc]
                        if i >= 2:
                            w.append(prev_pv[i - 2])
                        etoks.append(S.V(w, lambda v, sl=sl, i=i: v.memset(
                            s["pst"][i % 2][:, sl], 0.0)))
                prev_exp[i] = etoks[-1] if etoks else tsc
                pw = list(etoks)
                if i == 0:
                    if par == 1:
                        pw.append(ctx_evac[0])
                        pw.append(recips[0])
                    elif prev_afm:
                        pw.append(prev_afm)
                t1 = S.T(pw, lambda t, kt=kt, i=i: t.matmul(
                    p["pB"][0:64, :], s["v1s"][:, kt * 64:(kt + 1) * 64],
                    s["pst"][i % 2][:], start=(i == 0), stop=(i == nkt - 1)))
                t2 = S.T([], lambda t, i=i: t.matmul(
                    p["pE"][0:1, :], s["ones_col"][:], s["pst"][i % 2][:],
                    start=(i == 0), stop=(i == nkt - 1)))
                prev_pv[i] = t2
            rrow = P.row(2) if par == 0 else P.row(0)
            vrw = [prev_pv[nkt - 1]]
            if par == 0 and prev_bc:
                vrw.append(prev_bc)
            vr = S.V(vrw, lambda v, rrow=rrow: v.reciprocal(rrow, p["pE"][0:1, :]))
            recips[par] = vr
            cw = [prev_pv[nkt - 1], t1]
            if par == 0 and prev_afm:
                cw.append(prev_afm)
            cev = S.C(cw, lambda sc, b0=b0: sc.activation(
                s["scrb"][b0:b0 + 64, :], p["pB"][0:64, :], AF.Copy))
            ctx_evac[par] = cev
        tb1_ = S.T([recips[0], recips[1], ctx_evac[1]], lambda t: t.matmul(
            p["pD"][0:64, :], s["ones_row"][0:1, 0:64], P.row(2),
            start=True, stop=True, tile_position=(0, 0)))
        tb2_ = S.T([], lambda t: t.matmul(
            p["pD"][64:128, :], s["ones_row"][0:1, 0:64], P.row(0),
            start=True, stop=True, tile_position=(0, 64)))
        cw = [tb1_, tb2_]
        if prev_afm:
            cw.append(prev_afm)
        cbe = S.C(cw, lambda sc: sc.activation(s["scr"][2][:], p["pD"][:], AF.Copy))
        prev_bc = cbe
        va = S.V([cbe, ctx_evac[1]], lambda v, j=j: v.tensor_tensor(
            s["afm"][:, j * TC:(j + 1) * TC], s["scrb"][:], s["scr"][2][:], ALU.mult))
        prev_afm = va
    S.finish()


def ph_dequant1(P: Prog):
    s, d, p = P.s, P.d, P.p
    S = Sched(P)
    dO = P.sems["dO"]

    def panel_gateup(which, it, prev_toks):
        src = d["g1q" if which == 0 else "u1q"]
        ssc = d["g1s" if which == 0 else "u1s"]
        tgt = d["wg1T" if which == 0 else "wu1T"]
        half = S.dmai_parity = it % 2
        wlt = s["wl0"] if half == 0 else s["wl1"]
        dsc = S.dma_plain([], s["rstat"][:, 40 + which:41 + which],
                          ssc[it * 128:(it + 1) * 128, :], "dT")
        dw = []
        if prev_toks.get(("cast", half)):
            dw.append(prev_toks[("cast", half)])
        dn = S.dma(dw, wlt[:].bitcast(I32), src[it * 128:(it + 1) * 128, :],
                   half * 4)
        cw = [dn, dsc]
        if prev_toks.get(("mmread", half)):
            cw.append(prev_toks[("mmread", half)])
        ct = S.V(cw, lambda v, wlt=wlt, which=which, half=half: v.tensor_scalar(
            s["ws0"][:, half * 4096:half * 4096 + 2048],
            wlt[:].bitcast(I32), 128.0, s["rstat"][:, 40 + which:41 + which],
            ALU.subtract, ALU.mult))
        prev_toks[("cast", half)] = ct
        evs = []
        for k in range(NDK):
            tw = [ct]
            if len(evs) >= 2:
                tw.append(evs[-2])
            elif prev_toks.get(("ev", half)):
                tw.append(prev_toks[("ev", half)])
            ttk = S.T(tw, lambda t, k=k, half=half: t.transpose(
                p["pT0" if k % 2 == 0 else "pT1"][:],
                s["ws0"][:, half * 4096 + k * 128: half * 4096 + (k + 1) * 128],
                s["identb"][:]))
            ew = [ttk]
            if prev_toks.get(("out", half)):
                ew.append(prev_toks[("out", half)])
            ev = S.V(ew, lambda v, k=k, half=half: v.tensor_copy(
                s["ws1"][:, half * 4096 + k * 128: half * 4096 + (k + 1) * 128],
                p["pT0" if k % 2 == 0 else "pT1"][:]))
            evs.append(ev)
        prev_toks[("ev", half)] = evs[-1]
        prev_toks[("mmread", half)] = evs[-1]
        od = S.dma([evs[-1]], tgt[it], s["ws1"][:, half * 4096:half * 4096 + 2048],
                   12 + half)
        prev_toks[("out", half)] = od
        return od

    toks = {}
    for it in range(I1 // 128):
        panel_gateup(0, it, toks)
        panel_gateup(1, it, toks)

    toks2 = {"out": [toks.get(("out", 0)), toks.get(("out", 1))]}
    toks2["out"] = [x for x in toks2["out"] if x]
    for hp in range(NDK):
        dsc = S.dma_plain([], s["rstat"][:, 42:43], d["d1s"][hp * 128:(hp + 1) * 128, :],
                          "dT")
        evs_all = []
        for sub in range(4):
            half = sub % 2
            wlt = s["wl0"] if half == 0 else s["wl1"]
            dw = []
            if toks2.get(("cast", half)):
                dw.append(toks2[("cast", half)])
            dn = S.dma(dw, wlt[:].bitcast(I32),
                       d["d1q"][hp * 128:(hp + 1) * 128, sub * 2048:(sub + 1) * 2048],
                       half * 4)
            cw = [dn, dsc]
            if toks2.get(("tr", half)):
                cw.append(toks2[("tr", half)])
            ct = S.V(cw, lambda v, wlt=wlt, half=half: v.tensor_scalar(
                s["ws0"][:, half * 4096:half * 4096 + 2048],
                wlt[:].bitcast(I32), 128.0, s["rstat"][:, 42:43],
                ALU.subtract, ALU.mult))
            toks2[("cast", half)] = ct
            evs = []
            for k in range(NDK):
                tw = [ct]
                if len(evs) >= 2:
                    tw.append(evs[-2])
                elif evs_all:
                    tw.append(evs_all[-2] if len(evs_all) >= 2 else evs_all[-1])
                ttk = S.T(tw, lambda t, k=k, half=half: t.transpose(
                    p["pT0" if k % 2 == 0 else "pT1"][:],
                    s["ws0"][:, half * 4096 + k * 128: half * 4096 + (k + 1) * 128],
                    s["identb"][:]))
                toks2[("tr", half)] = ttk
                ew = [ttk] + list(toks2.get("out") or [])
                ev = S.V(ew, lambda v, k=k, sub=sub: v.tensor_copy(
                    s["ws1"][:, sub * 2048 + k * 128: sub * 2048 + (k + 1) * 128],
                    p["pT0" if k % 2 == 0 else "pT1"][:]))
                evs.append(ev)
                evs_all.append(ev)
        od = S.dma([evs_all[-1]], d["wd1T"][hp], s["ws1"][:], 14)
        toks2["out"] = [od]
    S.finish()


def ph_ffn1(P: Prog, c):
    s, d, p = P.s, P.d, P.p
    S = Sched(P)
    prev_silu = prev_mul = None
    prev_gmm = prev_umm = prev_dmm = None
    bank_rd = {0: None, 1: None}
    for q1 in range(4):
        mids = {}
        for itg in range(4):
            itbase = q1 * 16 + itg * 4
            gw = []
            if prev_dmm:
                gw.append(prev_dmm)
            if prev_gmm:
                gw.append(prev_gmm)
            gcast = S.dma(gw, s["ws0"][:].rearrange("p (it c) -> p it c", c=2048),
                          d["wg1T"][itbase:itbase + 4].rearrange("it p c -> p it c"), 8)
            ucast = S.dma([prev_umm] if prev_umm else [],
                          s["ws1"][:].rearrange("p (it c) -> p it c", c=2048),
                          d["wu1T"][itbase:itbase + 4].rearrange("it p c -> p it c"), 12)
            for itl in range(4):
                def gmm(t, itl=itl):
                    for hk in range(NDK):
                        inst = t.matmul(p["pA"][:],
                                        s["ws0"][:, itl * 2048 + hk * 128: itl * 2048 + (hk + 1) * 128],
                                        s["xnT"][:, hk * TC:(hk + 1) * TC],
                                        start=(hk == 0), stop=(hk == NDK - 1))
                    return inst
                gwl = [gcast]
                if prev_silu:
                    gwl.append(prev_silu)
                tg = S.T(gwl, gmm)
                prev_gmm = tg
                def umm(t, itl=itl):
                    for hk in range(NDK):
                        inst = t.matmul(p["pB"][:],
                                        s["ws1"][:, itl * 2048 + hk * 128: itl * 2048 + (hk + 1) * 128],
                                        s["xnT"][:, hk * TC:(hk + 1) * TC],
                                        start=(hk == 0), stop=(hk == NDK - 1))
                    return inst
                uwl = [ucast]
                if prev_mul:
                    uwl.append(prev_mul)
                tu = S.T(uwl, umm)
                prev_umm = tu
                sw = [tg]
                if prev_mul:
                    sw.append(prev_mul)
                cs = S.C(sw, lambda sc: sc.activation(s["scr"][0][:], p["pA"][:], AF.Silu))
                prev_silu = cs
                itg_l = itg * 4 + itl
                mw = [cs, tu]
                if prev_dmm:
                    mw.append(prev_dmm)
                mv_ = S.V(mw, lambda v, itg_l=itg_l: v.tensor_tensor(
                    s["mid"][:, itg_l * TC:(itg_l + 1) * TC], s["scr"][0][:],
                    p["pB"][:], ALU.mult))
                prev_mul = mv_
                mids[itg_l] = mv_
        down_slot_rd = {}
        for dt in range(NDK):
            bank = dt % 2
            pb = p["pC"] if bank == 0 else p["pE"]
            dcw = [prev_gmm]
            if (dt % 4) in down_slot_rd:
                dcw.append(down_slot_rd[dt % 4])
            dcast = S.dma(dcw, s["ws0"][:, (dt % 4) * 2048:(dt % 4 + 1) * 2048],
                          d["wd1T"][dt][:, q1 * 2048:(q1 + 1) * 2048], 8 + dt % 4)
            def dmm(t, dt=dt, pb=pb):
                for ik in range(NDK):
                    inst = t.matmul(pb[:],
                                    s["ws0"][:, (dt % 4) * 2048 + ik * 128:(dt % 4) * 2048 + ik * 128 + 128],
                                    s["mid"][:, ik * TC:(ik + 1) * TC],
                                    start=(ik == 0), stop=(ik == NDK - 1))
                return inst
            dwl = [dcast, mids[15]]
            if bank_rd[bank]:
                dwl.append(bank_rd[bank])
            td = S.T(dwl, dmm)
            prev_dmm = td
            down_slot_rd[dt % 4] = td
            av = S.V([td], lambda v, dt=dt, pb=pb, first=(q1 == 0): (
                v.tensor_copy(s["accx"][:, dt * TC:(dt + 1) * TC], pb[:]) if first
                else v.tensor_tensor(s["accx"][:, dt * TC:(dt + 1) * TC],
                                     s["accx"][:, dt * TC:(dt + 1) * TC], pb[:], ALU.add)))
            bank_rd[bank] = av
    S.finish()


def ph_comb1(P: Prog, c):
    s, d, p = P.s, P.d, P.p
    S = Sched(P)
    dO = P.sems["dO"]
    c1 = S.Vf([], lambda v: v.tensor_copy(P.row(0), P.row(4)))
    c2 = S.Vf([c1], lambda v: v.tensor_copy(P.row(1), P.row(3)))
    c3 = S.V([c2], lambda v: v.reciprocal(P.row(2), P.row(0)))
    t_s = S.T([c3], lambda t: t.matmul(p["pD"][:], s["ones_row"][:], P.row(2),
                                       start=True, stop=True))
    t_m = S.T([c2, c3], lambda t: t.matmul(p["pE"][:], s["ones_row"][:], P.row(1),
                                           start=True, stop=True))
    c_s = S.C([t_s], lambda sc: sc.activation(s["scr"][1][:], p["pD"][:], AF.Copy))
    c_m = S.C([t_m], lambda sc: sc.activation(s["scr"][2][:], p["pE"][:], AF.Copy))

    prev_t1 = {}
    prev_dense = {}
    prev_comb = {}
    for dt in range(NDK):
        cw = []
        if dt >= 4:
            cw.append(prev_dense[dt - 4])
        cn = S.stream([], d["wo1"][:, dt * 128:(dt + 1) * 128].rearrange(
            "(k p) m -> p k m", p=128), s["ws1"][:, (dt % 4) * 2048:(dt % 4 + 1) * 2048],
            cw)
        pb = p["pA"] if dt % 2 == 0 else p["pB"]
        dwl = [cn]
        if dt >= 2:
            dwl.append(prev_t1[dt - 2])
        def dmm(t, dt=dt, pb=pb):
            for ht in range(NDK):
                inst = t.matmul(pb[:],
                                s["ws1"][:, (dt % 4) * 2048 + ht * 128:(dt % 4) * 2048 + ht * 128 + 128],
                                s["afm"][:, ht * TC:(ht + 1) * TC],
                                start=(ht == 0), stop=(ht == NDK - 1))
            return inst
        td = S.T(dwl, dmm)
        prev_dense[dt] = td
        sw = [c_s, c_m]
        if dt >= 1:
            sw.append(prev_t1[dt - 1])
        csl = S.C(sw, lambda sc, dt=dt: sc.activation(
            s["scr"][0][:], s["accx"][:, dt * TC:(dt + 1) * TC], AF.Silu))
        t1_ = S.Vf([td, csl], lambda v, pb=pb: v.tensor_tensor(
            s["wbc"][:], pb[:], s["scr"][0][:], ALU.mult))
        prev_t1[dt] = t1_
        a1 = S.Vf([t1_, csl], lambda v, dt=dt: v.tensor_tensor(
            s["accx"][:, dt * TC:(dt + 1) * TC],
            s["xnT"][:, dt * TC:(dt + 1) * TC], s["scr"][1][:], ALU.mult))
        a2 = S.Vf([a1], lambda v, dt=dt: v.tensor_tensor(
            s["accx"][:, dt * TC:(dt + 1) * TC],
            s["accx"][:, dt * TC:(dt + 1) * TC], s["scr"][2][:], ALU.add))
        a3 = S.V([a2], lambda v, dt=dt: v.tensor_tensor(
            s["accx"][:, dt * TC:(dt + 1) * TC],
            s["accx"][:, dt * TC:(dt + 1) * TC], s["wbc"][:], ALU.add))
        prev_comb[dt] = a3

    prev_out_dma = {0: None, 1: None}
    last_evac = None
    for tt in range(NTT):
        half = tt % 2
        wlt = s["wl0"] if half == 0 else s["wl1"]
        evs = []
        for dt in range(NDK):
            tw = [prev_comb[dt]]
            if last_evac is not None:
                tw.append(last_evac)
            ttk = S.T(tw, lambda t, tt=tt, dt=dt: t.transpose(
                p["pTf"][:], s["accx"][:, dt * TC + tt * 128: dt * TC + (tt + 1) * 128],
                s["identf"][:]))
            ew = [ttk]
            if dt == 0 and prev_out_dma[half]:
                ew.append(prev_out_dma[half])
            ev = S.C(ew, lambda sc, dt=dt, wlt=wlt: sc.activation(
                wlt[:, dt * 128:(dt + 1) * 128], p["pTf"][:], AF.Copy))
            evs.append(ev)
            last_evac = ev
        r0 = c * TC + tt * 128
        od = S.dma([evs[-1]], d["y"][r0:r0 + 128, :], wlt[:], half * 4)
        prev_out_dma[half] = od
    S.finish()


def build_full():
    P = Prog()
    ph_setup(P)
    for c in range(NCHUNK):
        ph_ln0(P, c)
        ph_prep0(P, c)
        ph_router0(P, c)
        ph_xga0(P, c)
        ph_ffn0(P, c)
        ph_attn0(P, c)
        ph_comb0(P, c)
        ph_prep1(P, c)
        if c == 0:
            ph_dequant1(P)
        ph_ffn1(P, c)
        ph_attn1(P, c)
        ph_comb1(P, c)
    return P


def kernel(**inputs):
    P = build_full()
    maps = mk_inmaps(inputs, n_cores=8)
    res = P.run(maps)
    return np.stack([np.asarray(r["y"], dtype=np.float32) for r in res.results], axis=0)


# version 08
# speedup vs baseline: 1.1134x; 1.0154x over previous
import math
import numpy as np
from contextlib import ExitStack

import concourse.bass as bass
import concourse.mybir as mybir
from concourse.bass_utils import run_bass_kernel_spmd

F32 = mybir.dt.float32
BF16 = mybir.dt.bfloat16
I32 = mybir.dt.int32
AF = mybir.ActivationFunctionType
ALU = mybir.AluOpType
AX = mybir.AxisListType

T = 2048
D = 2048
NCHUNK = 4
TC = 512
NTT = TC // 128
NDK = D // 128

NEG = -1.0e30

L0_NH, L0_NKV, L0_HD, L0_W, L0_CAP = 16, 4, 128, 1024, 50.0
L1_NH, L1_HD = 32, 64
NE, F0, RADJ, NL = 8, 2048, 64, 2
I1 = 8192

GOLD_C = 1.0 / math.e
GOLD_L = 0.5 - math.log(4.0 / 3.0)
GOLD_U = 0.5


def host_constants():
    c = {}
    c["c_identf"] = np.eye(128, dtype=np.float32)
    r0 = np.zeros((128, 128), dtype=np.float32)
    for i in range(64):
        r0[i + 64, i] = -1.0
        r0[i, i + 64] = 1.0
    c["c_rot0"] = r0
    r1 = np.zeros((128, 128), dtype=np.float32)
    for b in (0, 64):
        for i in range(32):
            r1[b + i + 32, b + i] = -1.0
            r1[b + i, b + i + 32] = 1.0
    c["c_rot1"] = r1

    def sine_table(hd, theta):
        frac = np.arange(0, hd, 2, dtype=np.float64) / hd
        invts = theta ** -frac
        tpos = np.arange(T, dtype=np.float64)
        s = np.outer(tpos, invts)
        s = np.concatenate([s, s], axis=1)
        return np.sin(s).T.astype(np.float32), np.cos(s).T.astype(np.float32)

    s0, c0 = sine_table(128, 10000.0)
    c["c_sin0"], c["c_cos0"] = s0, c0
    s1, c1 = sine_table(64, 10000.0)
    c["c_sin1"] = np.concatenate([s1, s1], axis=0)
    c["c_cos1"] = np.concatenate([c1, c1], axis=0)
    i_k = np.arange(128)[:, None]
    i_q = np.arange(128)[None, :]
    c["c_mdiag"] = np.where(i_k <= i_q, 0.0, NEG).astype(np.float32)
    c["c_medge"] = np.where(i_k >= i_q, 0.0, NEG).astype(np.float32)
    rows = np.zeros((128, 8), dtype=np.float32)
    rows[:, 0] = -20.0
    rows[:, 1] = 1e-5
    rows[:, 2] = -GOLD_C
    rows[:, 3] = 1e-8
    rows[:, 4] = 1.0
    c["c_rows"] = rows
    c["c_ones_row"] = np.ones((1, 128), dtype=np.float32)
    c["c_ones_colf"] = np.ones((128, 1), dtype=np.float32)
    return c


ENGINES = ("sync", "tensor", "scalar", "vector", "gpsimd")

_REAL = {
    "dma_start", "matmul", "transpose", "activation", "tensor_tensor",
    "tensor_scalar", "tensor_copy", "tensor_add", "tensor_sub", "tensor_mul",
    "tensor_max", "reduce_sum", "reduce_max", "tensor_reduce", "reciprocal",
    "memset", "memzero", "iota", "copy", "mul", "add", "sqrt", "square",
    "tensor_scalar_add", "tensor_scalar_mul", "tensor_scalar_max",
    "tensor_scalar_min", "scalar_tensor_tensor",
}


_UPDATED = {}


class Sem:
    def __init__(self, handle):
        self.h = handle
        self.n = 0
        self.maxw = 0

    def inc(self, inst, k=1):
        assert id(inst) not in _UPDATED, "instruction already carries a sem update"
        _UPDATED[id(inst)] = inst
        inst.then_inc(self.h, k)
        self.n += k
        return self.n

    def dinc(self, inst):
        return self.inc(inst, 16)


class EW:

    def __init__(self, eng, prog, name):
        self._e = eng
        self._p = prog
        self._n = name
        self._seen = prog.wait_hwm.setdefault(name, {})

    def __getattr__(self, a):
        f = getattr(self._e, a)
        if a in _REAL:
            def g(*args, **kw):
                inst = f(*args, **kw)
                self._p.last_real[self._n] = inst
                return inst
            return g
        return f

    def wait(self, sem, val):
        key = id(sem)
        if self._seen.get(key, -1) >= val:
            return
        self._seen[key] = val
        sem.maxw = max(sem.maxw, val)
        self._e.wait_ge(sem.h, val)

    def wait_ge(self, handle, val):
        key = ("h", id(handle))
        if self._seen.get(key, -1) >= val:
            return
        self._seen[key] = val
        self._e.wait_ge(handle, val)


class Prog:
    def __init__(self, layers=(0, 1), chunks=None, debug=()):
        self.layers = tuple(layers)
        self.chunks = list(range(NCHUNK)) if chunks is None else list(chunks)
        self.debug = set(debug)
        self.nc = bass.Bass("TRN2", target_bir_lowering=False)
        self.ctx = ExitStack()
        self.wait_hwm = {}
        self.last_real = {n: None for n in ENGINES}
        self.phase_idx = 0
        self.dbg_tensors = {}
        self._mk_dram()
        self._mk_sbuf()
        self._mk_psum()
        self._mk_sems()
        self.block = self.ctx.enter_context(self.nc.Block())

    def _mk_dram(self):
        nc = self.nc
        def ap(name, shape, dt, kind):
            return nc.dram_tensor(name, shape, dt, kind=kind).ap()
        EI, EO, IN = "ExternalInput", "ExternalOutput", "Internal"
        self.d = d = {}
        d["x"] = ap("x", [T, D], F32, EI)
        d["y"] = ap("y", [T, D], F32, EO)
        d["wq0"] = ap("wq0", [D, 2048], F32, EI)
        d["wk0"] = ap("wk0", [D, 512], F32, EI)
        d["wv0"] = ap("wv0", [D, 512], F32, EI)
        d["wo0"] = ap("wo0", [2048, D], F32, EI)
        d["wg0"] = ap("wg0", [NE, D, F0], F32, EI)
        d["wu0"] = ap("wu0", [NE, D, F0], F32, EI)
        d["wd0"] = ap("wd0", [NE, F0, D], F32, EI)
        d["ga0"] = ap("ga0", [NL, D, RADJ], F32, EI)
        d["gb0"] = ap("gb0", [NL, RADJ, F0], F32, EI)
        d["ua0"] = ap("ua0", [NL, D, RADJ], F32, EI)
        d["ub0"] = ap("ub0", [NL, RADJ, F0], F32, EI)
        d["da0"] = ap("da0", [NL, F0, RADJ], F32, EI)
        d["db0"] = ap("db0", [NL, RADJ, D], F32, EI)
        d["wr0"] = ap("wr0", [D, NE], F32, EI)
        d["temp0"] = ap("temp0", [1, 1], F32, EI)
        d["wqkv1"] = ap("wqkv1", [D, 2176], F32, EI)
        d["wo1"] = ap("wo1", [D, D], F32, EI)
        d["g1q"] = ap("g1q", [I1, D], I32, EI)
        d["g1s"] = ap("g1s", [I1, 1], F32, EI)
        d["u1q"] = ap("u1q", [I1, D], I32, EI)
        d["u1s"] = ap("u1s", [I1, 1], F32, EI)
        d["d1q"] = ap("d1q", [D, I1], I32, EI)
        d["d1s"] = ap("d1s", [D, 1], F32, EI)
        for name, arr in host_constants().items():
            d[name] = ap(name, list(arr.shape), F32, EI)
        d["wg1T"] = ap("wg1T", [I1 // 128, 128, D], BF16, IN)
        d["wg0b"] = ap("wg0b", [NE * 4, 128, 8192], BF16, IN)
        d["wu0b"] = ap("wu0b", [NE * 4, 128, 8192], BF16, IN)
        d["wd0b"] = ap("wd0b", [NE * 4, 128, 8192], BF16, IN)
        d["wu1T"] = ap("wu1T", [I1 // 128, 128, D], BF16, IN)
        d["wd1T"] = ap("wd1T", [D // 128, 128, I1], BF16, IN)

    def row(self, r):
        b = (0, 0, 0, 32, 32, 32, 64, 64, 64)[r]
        col = (r % 3) * TC
        return self.s["rows"][b:b + 1, col:col + TC]

    def dbg(self, name, shape, dt=F32):
        if name not in self.dbg_tensors:
            self.dbg_tensors[name] = self.nc.dram_tensor(
                "dbg_" + name, shape, dt, kind="ExternalOutput").ap()
        return self.dbg_tensors[name]

    def _mk_sbuf(self):
        nc, E = self.nc, self.ctx.enter_context
        def sb(name, shape, dt):
            return E(nc.sbuf_tensor(name, shape, dt))
        self.s = s = {}
        s["identf"] = sb("identf", [128, 128], F32)
        s["identb"] = sb("identb", [128, 128], BF16)
        s["rot0"] = sb("rot0", [128, 128], BF16)
        s["rot1"] = sb("rot1", [128, 128], BF16)
        s["mdiag"] = sb("mdiag", [128, 128], F32)
        s["medge"] = sb("medge", [128, 128], F32)
        s["ones_row"] = sb("ones_row", [1, 128], F32)
        s["ones_col"] = sb("ones_col", [128, 1], BF16)
        s["crows"] = sb("crows", [128, 8], F32)
        s["invtau"] = sb("invtau", [128, 1], F32)
        s["tb"] = sb("tb", [128, 2 * TC], BF16)
        s["xnT"] = sb("xnT", [128, NDK * TC], BF16)
        s["k0s"] = sb("k0s", [128, L0_NKV * T], BF16)
        s["v0s"] = sb("v0s", [128, L0_NKV * T], BF16)
        s["k1s"] = sb("k1s", [128, T], BF16)
        s["v1s"] = sb("v1s", [128, (T // 128) * L1_HD], BF16)
        s["qt"] = sb("qt", [128, TC], BF16)
        s["pst"] = [sb(f"pst{i}", [128, TC], BF16) for i in range(2)]
        s["afm"] = sb("afm", [128, L0_NH * TC], BF16)
        s["accx"] = sb("accx", [128, NDK * TC], F32)
        s["mid"] = sb("mid", [128, 16 * TC], BF16)
        s["ws0"] = sb("ws0", [128, 8192], BF16)
        s["ws1"] = sb("ws1", [128, 8192], BF16)
        s["wl0"] = sb("wl0", [128, 2048], F32)
        s["wl1"] = sb("wl1", [128, 2048], F32)
        s["adsl"] = sb("adsl", [128, 4096], BF16)
        s["xga"] = sb("xga", [64, 4 * TC], BF16)
        s["mda"] = sb("mda", [64, TC], BF16)
        s["lnst"] = sb("lnst", [128, 32], F32)
        s["lnstb"] = sb("lnstb", [128, 32], F32)
        s["rt"] = sb("rt", [128, 4 * 64], F32)
        s["rstat"] = sb("rstat", [128, 48], F32)
        s["rstatb"] = sb("rstatb", [128, 48], F32)
        s["w8f"] = sb("w8f", [8, TC], F32)
        s["w8"] = sb("w8", [128, 4 * 8], F32)
        s["wbc"] = sb("wbc", [128, TC], F32)
        s["rows"] = sb("rows", [128, 3 * TC], F32)
        s["rows8"] = sb("rows8", [8, 128], F32)
        s["scr"] = [sb(f"scr{i}", [128, TC], F32) for i in range(3)]
        s["scrb"] = sb("scrb", [128, TC], BF16)
        s["dmy"] = sb("dmy", [128, 8], F32)

    def _mk_psum(self):
        nc, E = self.nc, self.ctx.enter_context
        self.p = {}
        for name in ["pA", "pB", "pC", "pD", "pE"]:
            self.p[name] = E(nc.psum_tensor(name, [128, 512], F32))
        self.p["pT0"] = E(nc.psum_tensor("pT0", [128, 128], BF16))
        self.p["pT1"] = E(nc.psum_tensor("pT1", [128, 128], BF16))
        self.p["pTf"] = E(nc.psum_tensor("pTf", [128, 128], F32))

    def _mk_sems(self):
        E = self.ctx.enter_context
        self.sems = {}
        for name in ["dW", "dX", "dS", "dO", "dT", "dM",
                     "mT", "mS", "mV", "mG",
                     "bS", "bT", "bC", "bV", "bG",
                     "fV", "fS"]:
            self.sems[name] = Sem(E(self.nc.semaphore(name)))
        self.dL = [Sem(E(self.nc.semaphore(f"dL{i}"))) for i in range(16)]

    def sec(self, engine, fn):
        dec = getattr(self.block, engine)
        def wrapped(raw):
            fn(EW(raw, self, engine))
        dec(wrapped)

    def barrier(self):
        self.phase_idx += 1
        k = self.phase_idx
        dvals = [(self.sems[nm], self.sems[nm].n)
                 for nm in ("dX", "dS", "dW", "dO", "dT", "dM")]
        dvals += [(sem, sem.n) for sem in self.dL]
        def fn_sync(e):
            for sem, val in dvals:
                e.wait_ge(sem.h, val)
            e.nop().then_inc(self.sems["bS"].h, 1)
        self.sec("sync", fn_sync)
        self.sems["bS"].n += 1
        self.last_real["sync"] = None
        bmap = {"tensor": "bT", "scalar": "bC", "vector": "bV", "gpsimd": "bG"}
        for eng in ("tensor", "scalar", "vector", "gpsimd"):
            sem = self.sems[bmap[eng]]
            had_work = self.last_real[eng] is not None
            self.last_real[eng] = None
            def fn(e, sem=sem, eng=eng, had_work=had_work):
                if not had_work:
                    e.nop().then_inc(sem.h, 1)
                    return
                if eng == "tensor":
                    e.wait_ge(self.sems["mV"].h, self.sems["mV"].n)
                    e.wait_ge(self.sems["mS"].h, self.sems["mS"].n)
                    e.wait_ge(self.sems["mT"].h, self.sems["mT"].n)
                    inst = e.matmul(self.p["pE"][0:1, 0:1], self.s["ones_col"][:],
                                    self.s["ones_col"][:], start=True, stop=True)
                elif eng == "scalar":
                    e.wait_ge(self.sems["mS"].h, self.sems["mS"].n)
                    e.wait_ge(self.sems["fS"].h, self.sems["fS"].n)
                    e.wait_ge(self.sems["fV"].h, self.sems["fV"].n)
                    inst = e.activation(self.s["dmy"][:, 0:1], self.s["dmy"][:, 1:2],
                                        AF.Copy)
                elif eng == "vector":
                    e.wait_ge(self.sems["mV"].h, self.sems["mV"].n)
                    e.wait_ge(self.sems["fV"].h, self.sems["fV"].n)
                    inst = e.memset(self.s["dmy"][:, 2:3], 0.0)
                else:
                    e.wait_ge(self.sems["mG"].h, self.sems["mG"].n)
                    inst = e.memset(self.s["dmy"][:, 3:4], 0.0)
                inst.then_inc(sem.h, 1)
            self.sec(eng, fn)
            sem.n += 1
            assert sem.n == k, (eng, sem.n, k)
        for eng in ENGINES:
            def fn(e, k=k):
                for nm in ("bS", "bT", "bC", "bV", "bG"):
                    e.wait_ge(self.sems[nm].h, k)
            self.sec(eng, fn)

    def finalize(self):
        for nm, sem in self.sems.items():
            assert sem.maxw <= sem.n, ("wait beyond final incs", nm, sem.maxw, sem.n)
        self.ctx.close()

    def run(self, in_maps, trace=False):
        self.finalize()
        return run_bass_kernel_spmd(self.nc, in_maps,
                                    core_ids=list(range(len(in_maps))),
                                    trace=trace)

    def sim(self, in_map):
        self.finalize()
        from concourse.bass_interp import CoreSim
        sim = CoreSim(self.nc, require_finite=False, require_nnan=False)
        for name, val in in_map.items():
            view = sim.tensor(name)
            view[:] = val
        sim.simulate()
        outs = {}
        for alloc in self.nc.m.functions[0].allocations:
            import concourse.mybir as _mb
            if isinstance(alloc, _mb.MemoryLocationSet) and alloc.kind == "ExternalOutput":
                nm = alloc.memorylocations[0].name
                outs[nm] = np.array(sim.tensor(nm))
        return outs


def _band0(c):
    out = []
    for kt in range(max(0, 4 * c - 8), 4 * c + 4):
        segs = []
        for tl in range(NTT):
            rel = (4 * c + tl) - kt
            if rel < 0 or rel > 8:
                segs.append("zero")
            elif rel == 0:
                segs.append("diag")
            elif rel == 8:
                segs.append("edge")
            else:
                segs.append("keep")
        out.append((kt, segs))
    return out


def _band1(c):
    out = []
    for kt in range(0, 4 * c + 4):
        segs = []
        for tl in range(NTT):
            rel = (4 * c + tl) - kt
            segs.append("zero" if rel < 0 else ("diag" if rel == 0 else "keep"))
        out.append((kt, segs))
    return out


def ph_setup(P: Prog):
    s, d, p = P.s, P.d, P.p
    dX, dS = P.sems["dX"], P.sems["dS"]
    mV, mT = P.sems["mV"], P.sems["mT"]

    def sync(q):
        for nm_s, nm_d in [("identf", "c_identf"), ("mdiag", "c_mdiag"),
                           ("medge", "c_medge"), ("ones_row", "c_ones_row"),
                           ("crows", "c_rows")]:
            dX.dinc(q.dma_start(out=s[nm_s][:], in_=d[nm_d]))
        dX.dinc(q.dma_start(out=s["scr"][0][:, 0:128], in_=d["c_identf"]))
        dX.dinc(q.dma_start(out=s["scr"][1][:, 0:128], in_=d["c_rot0"]))
        dX.dinc(q.dma_start(out=s["scr"][2][:, 0:128], in_=d["c_rot1"]))
        dX.dinc(q.dma_start(out=s["wl0"][:, 0:1], in_=d["c_ones_colf"]))
        dX.dinc(q.dma_start(out=s["wl1"][0:1, 0:1], in_=d["temp0"]))
        P._t0 = dX.n

    def vector(v):
        P.sems["fV"].inc(v.memset(s["dmy"][:], 0.0))
        v.wait(dX, P._t0)
        v.tensor_copy(s["identb"][:], s["scr"][0][:, 0:128])
        v.tensor_copy(s["rot0"][:], s["scr"][1][:, 0:128])
        v.tensor_copy(s["rot1"][:], s["scr"][2][:, 0:128])
        v.tensor_copy(s["ones_col"][:], s["wl0"][:, 0:1])

    P.sec("sync", sync)
    P.sec("vector", vector)
    P.barrier()

    def tensor2(t):
        mT.inc(t.matmul(p["pTf"][:, 0:1], s["ones_row"][:], s["wl1"][0:1, 0:1],
                        start=True, stop=True))

    def vector2(v):
        v.wait(mT, mT.n)
        mV.inc(v.reciprocal(s["scr"][0][:, 0:1], p["pTf"][:, 0:1]))

    def sync2(q):
        q.wait(mV, mV.n)
        dS.dinc(q.dma_start(out=s["invtau"][:], in_=s["scr"][0][:, 0:1]))

    P.sec("tensor", tensor2)
    P.sec("vector", vector2)
    P.sec("sync", sync2)
    P.barrier()


def ph_ln0(P: Prog, c):
    s, d, p = P.s, P.d, P.p
    xnt = s["mid"]
    dX, dS = P.sems["dX"], P.sems["dS"]
    mV, mS, mT = P.sems["mV"], P.sems["mS"], P.sems["mT"]
    fV = P.sems["fV"]
    wl = [s["wl0"], s["wl1"]]

    v0 = mV.n
    s0 = mS.n
    dL = [P.dL[0], P.dL[4]]
    xdma = {}

    def sync(q):
        for tt in range(NTT):
            if tt >= 2:
                q.wait(mV, v0 + (tt - 1))
                q.wait(mS, s0 + 4 * (tt - 1))
            r0 = c * TC + tt * 128
            xdma[tt] = dL[tt % 2].dinc(q.dma_start(out=wl[tt % 2][:],
                                                   in_=d["x"][r0:r0 + 128, :]))

    def vector(v):
        for tt in range(NTT):
            v.wait(dL[tt % 2], xdma[tt])
            mV.inc(v.reduce_sum(s["lnst"][:, tt:tt + 1], wl[tt % 2][:], axis=AX.X))

    def scalar(sc):
        for tt in range(NTT):
            sc.wait(dL[tt % 2], xdma[tt])
            for j in range(4):
                k = tt * 4 + j
                inst = sc.activation(xnt[:, k * 512:(k + 1) * 512],
                                     wl[tt % 2][:, j * 512:(j + 1) * 512],
                                     AF.Square, accum_out=s["rt"][:, 4 * tt + j:4 * tt + j + 1])
                mS.inc(inst)

    P.sec("sync", sync)
    P.sec("vector", vector)
    P.sec("scalar", scalar)

    v1 = mV.n
    s1 = mS.n

    def vector2(v):
        v.wait(mS, s1)
        v.wait(mV, v0 + 4)
        for tt in range(NTT):
            fV.inc(v.reduce_sum(s["lnst"][:, 4 + tt:5 + tt],
                                s["rt"][:, 4 * tt:4 * tt + 4], axis=AX.X))
        v.wait(fV, fV.n)
        i1 = v.tensor_scalar(s["lnst"][:, 20:24], s["lnst"][:, 0:4], 1.0 / D, None, ALU.mult)
        fV.inc(i1); v.wait(fV, fV.n)
        i2 = v.tensor_scalar(s["lnst"][:, 24:28], s["lnst"][:, 4:8], 1.0 / D, None, ALU.mult)
        fV.inc(i2)
        i3 = v.tensor_tensor(s["lnst"][:, 28:32], s["lnst"][:, 20:24], s["lnst"][:, 20:24], ALU.mult)
        fV.inc(i3); v.wait(fV, fV.n)
        i4 = v.tensor_tensor(s["lnst"][:, 24:28], s["lnst"][:, 24:28], s["lnst"][:, 28:32], ALU.subtract)
        mV.inc(i4)

    def scalar2(sc):
        sc.wait(mV, mV.n)
        inst = sc.activation(s["lnst"][:, 16:20], s["lnst"][:, 24:28], AF.Sqrt,
                             bias=s["crows"][:, 1:2])
        mS.inc(inst)

    def vector3(v):
        v.wait(mS, mS.n)
        i1 = v.reciprocal(s["lnst"][:, 8:12], s["lnst"][:, 16:20])
        fV.inc(i1); v.wait(fV, fV.n)
        i2 = v.tensor_tensor(s["lnst"][:, 12:16], s["lnst"][:, 20:24], s["lnst"][:, 8:12], ALU.mult)
        fV.inc(i2); v.wait(fV, fV.n)
        i3 = v.tensor_scalar(s["lnst"][:, 12:16], s["lnst"][:, 12:16], -1.0, None, ALU.mult)
        mV.inc(i3)

    P.sec("vector", vector2)
    P.sec("scalar", scalar2)
    P.sec("vector", vector3)

    vstat = mV.n

    def sync2(q):
        q.wait(mV, vstat)
        dS.dinc(q.dma_start(out=s["lnstb"][:, 8:16], in_=s["lnst"][:, 8:16]))

    def tensor2(t):
        t.wait(mV, vstat)
        mT.inc(t.transpose(p["pTf"][0:8, :], s["lnst"][:, 16:24], s["identf"][:]))

    def vector4(v):
        v.wait(mT, mT.n)
        mV.inc(v.tensor_copy(s["rows8"][:], p["pTf"][0:8, :]))

    def sync3(q):
        q.wait(mV, mV.n)
        for tt in range(NTT):
            dS.dinc(q.dma_start(out=P.row(0)[:, tt * 128:(tt + 1) * 128],
                                in_=s["rows8"][tt:tt + 1, :]))
            dS.dinc(q.dma_start(out=P.row(1)[:, tt * 128:(tt + 1) * 128],
                                in_=s["rows8"][4 + tt:5 + tt, :]))

    P.sec("sync", sync2)
    P.sec("tensor", tensor2)
    P.sec("vector", vector4)
    P.sec("sync", sync3)
    P.barrier()

    v2 = mV.n
    t2 = mT.n
    xdma2 = {}

    def cast_inc(tt):
        return v2 + tt * 17 + 1

    def evac_inc(tt, dk):
        return v2 + tt * 17 + 1 + dk + 1

    def trans_inc(idx):
        return t2 + idx + 1

    def sync4(q):
        for tt in range(NTT):
            if tt >= 2:
                q.wait(mV, cast_inc(tt - 1))
            r0 = c * TC + tt * 128
            xdma2[tt] = dL[tt % 2].dinc(q.dma_start(out=wl[tt % 2][:],
                                                    in_=d["x"][r0:r0 + 128, :]))

    def vector5(v):
        for tt in range(NTT):
            v.wait(dL[tt % 2], xdma2[tt])
            inst = v.tensor_scalar(xnt[:, 0:D], wl[tt % 2][:],
                                   s["lnstb"][:, 8 + tt:9 + tt],
                                   s["lnstb"][:, 12 + tt:13 + tt],
                                   ALU.mult, ALU.add)
            mV.inc(inst)
            assert mV.n == cast_inc(tt)
            for dk in range(NDK):
                idx = tt * NDK + dk
                v.wait(mT, trans_inc(idx))
                inst = v.tensor_copy(
                    s["xnT"][:, dk * TC + tt * 128: dk * TC + (tt + 1) * 128],
                    p["pT0" if idx % 2 == 0 else "pT1"][:])
                mV.inc(inst)
                assert mV.n == evac_inc(tt, dk)

    def tensor3(t):
        for tt in range(NTT):
            t.wait(mV, cast_inc(tt))
            for dk in range(NDK):
                idx = tt * NDK + dk
                if idx >= 2:
                    ptt, pdk = divmod(idx - 2, NDK)
                    t.wait(mV, evac_inc(ptt, pdk))
                inst = t.transpose(p["pT0" if idx % 2 == 0 else "pT1"][:],
                                   xnt[:, dk * 128:(dk + 1) * 128],
                                   s["identb"][:])
                mT.inc(inst)
                assert mT.n == trans_inc(idx)

    P.sec("sync", sync4)
    P.sec("tensor", tensor3)
    P.sec("vector", vector5)
    P.barrier()


def ph_dump(P: Prog, items):
    dO = P.sems["dO"]

    def sync(q):
        for name, apv, shape in items:
            dO.dinc(q.dma_start(out=P.dbg(name, list(shape), dt=apv.tensor.dtype),
                                in_=apv))
        q.wait(dO, dO.n)

    P.sec("sync", sync)
    P.barrier()


def mk_inmaps(inputs, n_cores=8):
    consts = host_constants()
    base = {
        "wq0": inputs["L0_attn_q_kernel"], "wk0": inputs["L0_attn_k_kernel"],
        "wv0": inputs["L0_attn_v_kernel"], "wo0": inputs["L0_attn_out_kernel"],
        "wg0": inputs["L0_ffn_Wg"], "wu0": inputs["L0_ffn_Wu"], "wd0": inputs["L0_ffn_Wd"],
        "ga0": inputs["L0_ffn_Ga"], "gb0": inputs["L0_ffn_Gb"],
        "ua0": inputs["L0_ffn_Ua"], "ub0": inputs["L0_ffn_Ub"],
        "da0": inputs["L0_ffn_Da"], "db0": inputs["L0_ffn_Db"],
        "wr0": inputs["L0_ffn_Wr"],
        "temp0": np.asarray(inputs["L0_ffn_temperature"], np.float32).reshape(1, 1),
        "wqkv1": inputs["L1_attn_w_qkv"], "wo1": inputs["L1_attn_w_dense"],
        "g1q": np.asarray(inputs["L1_ffn_gate_w"], np.int32),
        "g1s": np.asarray(inputs["L1_ffn_gate_scale"], np.float32).reshape(-1, 1),
        "u1q": np.asarray(inputs["L1_ffn_up_w"], np.int32),
        "u1s": np.asarray(inputs["L1_ffn_up_scale"], np.float32).reshape(-1, 1),
        "d1q": np.asarray(inputs["L1_ffn_down_w"], np.int32),
        "d1s": np.asarray(inputs["L1_ffn_down_scale"], np.float32).reshape(-1, 1),
    }
    base = {k: np.ascontiguousarray(np.asarray(v, dtype=np.int32 if k in
            ("g1q", "u1q", "d1q") else np.float32)) for k, v in base.items()}
    base.update(consts)
    x = np.asarray(inputs["L0_attn_x"], np.float32)
    maps = []
    for b in range(n_cores):
        m = dict(base)
        m["x"] = np.ascontiguousarray(x[b])
        maps.append(m)
    return maps


def ph_prep0(P: Prog, c):
    s, d, p = P.s, P.d, P.p
    dT, dM = P.sems["dT"], P.sems["dM"]
    mV, mS, mT = P.sems["mV"], P.sems["mS"], P.sems["mT"]
    wl = [s["wl0"], s["wl1"]]
    dL = [P.dL[0], P.dL[4]]
    tbc = s["tb"][:, 0:TC]
    tbs = s["tb"][:, TC:2 * TC]

    v0, t0, s0 = mV.n, mT.n, mS.n
    pdma = {}
    tbv = [0]
    wrv = [0]

    def cast(i):
        return v0 + 2 + (2 + i if i <= 5 else (10 if i == 6 else 13))
    qtcopy = lambda m: v0 + 2 + {0: 8, 1: 11, 2: 14, 3: 16}[m]
    krope = lambda m: v0 + 2 + {0: 9, 1: 12, 2: 15, 3: 17}[m]
    vevac = lambda idx: v0 + 2 + 18 + idx
    tbcast = v0 + 2
    v_end = v0 + 35
    _kv_pos = {0: 1, 1: 2, 2: 3, 3: 5, 4: 7, 5: 9, 6: 11, 7: 16}
    kvproj = lambda i: t0 + _kv_pos[i]
    rot = lambda m: t0 + 4 + 2 * m
    tr = lambda idx: t0 + (12 + idx if idx < 4 else 13 + idx)
    kevac = lambda i: s0 + i + 1

    def sync(q):
        dT.dinc(q.dma_start(out=s["scr"][1][:], in_=d["c_cos0"][:, c * TC:(c + 1) * TC]))
        tbv[0] = dT.dinc(q.dma_start(out=s["scr"][2][:], in_=d["c_sin0"][:, c * TC:(c + 1) * TC]))
        wrv[0] = dM.dinc(q.dma_start(
            out=s["scr"][0][:, 0:128].rearrange("p (k e) -> p k e", e=NE),
            in_=d["wr0"].rearrange("(k p) e -> p k e", p=128)))
        for i in range(8):
            m = i % 4
            srcw = d["wk0"] if i < 4 else d["wv0"]
            if i >= 2:
                q.wait(mV, cast(i - 1))
            pdma[i] = dL[i % 2].dinc(q.dma_start(
                out=wl[i % 2][:].rearrange("p (k m) -> p k m", m=128),
                in_=srcw[:, m * 128:(m + 1) * 128].rearrange("(k p) m -> p k m", p=128)))

    def emit_cast(v, i):
        v.wait(dL[i % 2], pdma[i])
        if i >= 4:
            v.wait(mT, kvproj(i - 4))
        mV.inc(v.tensor_copy(s["ws0"][:, (i % 4) * 2048:(i % 4 + 1) * 2048],
                             wl[i % 2][:]))
        assert mV.n == cast(i)

    def emit_qk(v, m):
        tgt = s["scr"][1] if m % 2 == 0 else s["scr"][2]
        v.wait(mS, kevac(m))
        if m >= 1:
            v.wait(mT, rot(m - 1))
            v.wait(mV, krope(m - 1))
        mV.inc(v.tensor_copy(s["qt"][:], tgt[:]))
        assert mV.n == qtcopy(m)
        v.wait(mT, rot(m))
        fV = P.sems["fV"]
        fV.inc(v.tensor_tensor(s["scr"][0][:], tgt[:], tbc, ALU.mult))
        fV.inc(v.tensor_tensor(s["wbc"][:], p["pC"][:], tbs, ALU.mult))
        v.wait(fV, fV.n)
        inst = v.tensor_tensor(s["k0s"][:, m * T + c * TC: m * T + (c + 1) * TC],
                               s["scr"][0][:], s["wbc"][:], ALU.add)
        mV.inc(inst)
        assert mV.n == krope(m)

    def vector_all(v):
        v.wait(dT, tbv[0])
        mV.inc(v.tensor_copy(tbc, s["scr"][1][:]))
        mV.inc(v.tensor_copy(tbs, s["scr"][2][:]))
        v.wait(dM, wrv[0])
        mV.inc(v.tensor_copy(s["adsl"][:, 0:128], s["scr"][0][:, 0:128]))
        for i in range(6):
            emit_cast(v, i)
        emit_qk(v, 0)
        emit_cast(v, 6)
        emit_qk(v, 1)
        emit_cast(v, 7)
        emit_qk(v, 2)
        emit_qk(v, 3)
        for idx in range(16):
            m, j = divmod(idx, NTT)
            v.wait(mT, tr(idx))
            kt = 4 * c + j
            inst = v.tensor_copy(s["v0s"][:, (m * 16 + kt) * 128:(m * 16 + kt + 1) * 128],
                                 p["pT0" if idx % 2 == 0 else "pT1"][:])
            mV.inc(inst)
            assert mV.n == vevac(idx)
        assert mV.n == v_end

    def emit_rot(t, m):
        t.wait(mV, qtcopy(m))
        if m >= 1:
            t.wait(mV, krope(m - 1))
        inst = t.matmul(p["pC"][:], s["rot0"][:], s["qt"][:], start=True, stop=True)
        mT.inc(inst)
        assert mT.n == rot(m)

    def emit_tr(t, idx):
        m, j = divmod(idx, NTT)
        t.wait(mS, kevac(4 + m))
        if idx >= 2:
            t.wait(mV, vevac(idx - 2))
        inst = t.transpose(p["pT0" if idx % 2 == 0 else "pT1"][:],
                           s["scrb"][:, j * 128:(j + 1) * 128],
                           s["identb"][:])
        mT.inc(inst)
        assert mT.n == tr(idx)

    def tensor_kv(t):
        for i in range(8):
            m = i % 4
            t.wait(mV, cast(i))
            if i >= 2:
                t.wait(mS, kevac(i - 2))
            pj = p["pA"] if i % 2 == 0 else p["pB"]
            for dk in range(NDK):
                inst = t.matmul(pj[:], s["ws0"][:, (m * 16 + dk) * 128:(m * 16 + dk + 1) * 128],
                                s["xnT"][:, dk * TC:(dk + 1) * TC],
                                start=(dk == 0), stop=(dk == NDK - 1))
            mT.inc(inst)
            assert mT.n == kvproj(i)
            if 2 <= i <= 5:
                emit_rot(t, i - 2)
            if i == 6:
                for idx in range(4):
                    emit_tr(t, idx)
        for idx in range(4, 16):
            emit_tr(t, idx)

    def scalar_kv(sc):
        for i in range(8):
            m = i % 4
            sc.wait(mT, kvproj(i))
            pj = p["pA"] if i % 2 == 0 else p["pB"]
            if i < 4:
                tgt = s["scr"][1] if m % 2 == 0 else s["scr"][2]
                if i >= 2:
                    sc.wait(mV, krope(i - 2))
                else:
                    sc.wait(mV, tbcast)
                inst = sc.activation(tgt[:], pj[:], AF.Copy)
            else:
                if m >= 1:
                    sc.wait(mT, tr(m * 4 - 1))
                inst = sc.activation(s["scrb"][:], pj[:], AF.Copy)
            mS.inc(inst)
            assert mS.n == kevac(i)

    P.sec("sync", sync)
    P.sec("tensor", tensor_kv)
    P.sec("scalar", scalar_kv)
    P.sec("vector", vector_all)
    P.barrier()


def ph_router0(P: Prog, c):
    s, d, p = P.s, P.d, P.p
    dS = P.sems["dS"]
    mV, mS, mT = P.sems["mV"], P.sems["mS"], P.sems["mT"]
    fV, fS = P.sems["fV"], P.sems["fS"]
    v0, t0, s0, b0 = mV.n, mT.n, mS.n, dS.n

    def R(tt, off, w=8):
        return s["rt"][:, tt * 64 + off: tt * 64 + off + w]

    def tensor_r(t):
        for tt in range(NTT):
            for dk in range(NDK):
                inst = t.matmul(p["pD"][:, tt * 8:(tt + 1) * 8],
                                s["xnT"][:, dk * TC + tt * 128: dk * TC + (tt + 1) * 128],
                                s["adsl"][:, dk * 8:(dk + 1) * 8],
                                start=(dk == 0), stop=(dk == NDK - 1))
            mT.inc(inst)

    def vector_lg(v):
        v.wait(mT, t0 + 4)
        for tt in range(NTT):
            inst = v.tensor_copy(R(tt, 0), p["pD"][:, tt * 8:(tt + 1) * 8])
            mV.inc(inst)

    def scalar_r(sc):
        sc.wait(mV, v0 + 4)
        for tt in range(NTT):
            mS.inc(sc.activation(R(tt, 56), R(tt, 0), AF.Sigmoid,
                                 scale=s["invtau"][:, 0:1]))
        sc.wait(mS, mS.n)
        for tt in range(NTT):
            mS.inc(sc.activation(R(tt, 8), R(tt, 56), AF.Abs, bias=s["crows"][:, 2:3]))
        sc.wait(mS, mS.n)
        for tt in range(NTT):
            mS.inc(sc.activation(R(tt, 24), R(tt, 8), AF.Exp, scale=-10.0))
        for tt in range(NTT):
            mS.inc(sc.activation(R(tt, 32), R(tt, 8), AF.Exp, scale=-(1.0 / 0.3)))

    def vector_r(v):
        v.wait(mS, s0 + 16)
        fV = P.sems["fV"]
        def grp(insts_done):
            fV.inc(insts_done)
            v.wait(fV, fV.n)
        for tt in range(NTT):
            i1 = v.tensor_scalar(R(tt, 16), R(tt, 56), GOLD_L, None, ALU.is_ge)
        grp(i1)
        for tt in range(NTT):
            i1 = v.tensor_scalar(R(tt, 40), R(tt, 56), GOLD_U, None, ALU.is_le)
        grp(i1)
        for tt in range(NTT):
            i1 = v.tensor_tensor(R(tt, 16), R(tt, 16), R(tt, 40), ALU.mult)
        grp(i1)
        for tt in range(NTT):
            i1 = v.tensor_tensor(R(tt, 24), R(tt, 24), R(tt, 16), ALU.mult)
        grp(i1)
        for tt in range(NTT):
            i1 = v.reduce_max(s["rstat"][:, tt:tt + 1], R(tt, 32), axis=AX.X)
        grp(i1)
        inst = v.memset(s["dmy"][:, 5:6], 0.0)
        mV.inc(inst)

    def sync_b1(q):
        q.wait(mV, v0 + 5)
        dS.dinc(q.dma_start(out=s["rstatb"][:, 0:4], in_=s["rstat"][:, 0:4]))

    def vector_r1(v):
        fV = P.sems["fV"]
        def grp(insts_done):
            fV.inc(insts_done)
            v.wait(fV, fV.n)
        v.wait(dS, b0 + 16)
        for tt in range(NTT):
            i1 = v.tensor_scalar(R(tt, 40), R(tt, 32), s["rstatb"][:, tt:tt + 1], None, ALU.is_ge)
        grp(i1)
        for tt in range(NTT):
            i1 = v.tensor_scalar(R(tt, 56), R(tt, 40), -1.0e9, None, ALU.mult)
        grp(i1)
        for tt in range(NTT):
            i1 = v.tensor_tensor(R(tt, 56), R(tt, 56), R(tt, 32), ALU.add)
        grp(i1)
        for tt in range(NTT):
            i1 = v.reduce_max(s["rstat"][:, 4 + tt:5 + tt], R(tt, 56), axis=AX.X)
        grp(i1)
        for tt in range(NTT):
            i1 = v.reduce_sum(s["rstat"][:, 24 + tt:25 + tt], R(tt, 24), axis=AX.X)
        grp(i1)
        i2 = v.tensor_scalar(s["rstat"][:, 8:12], s["rstat"][:, 24:28], 1e-8, None,
                             ALU.max)
        grp(i2)
        inst = v.reciprocal(s["rstat"][:, 28:32], s["rstat"][:, 8:12])
        mV.inc(inst)

    def sync_b2(q):
        q.wait(mV, v0 + 6)
        dS.dinc(q.dma_start(out=s["rstatb"][:, 4:12], in_=s["rstat"][:, 4:12]))
        dS.dinc(q.dma_start(out=s["rstatb"][:, 24:32], in_=s["rstat"][:, 24:32]))

    def vector_r2(v):
        fV = P.sems["fV"]
        def grp(insts_done):
            fV.inc(insts_done)
            v.wait(fV, fV.n)
        v.wait(dS, b0 + 48)
        for tt in range(NTT):
            i1 = v.tensor_scalar(R(tt, 40), R(tt, 32), s["rstatb"][:, 4 + tt:5 + tt],
                                 None, ALU.is_ge)
        grp(i1)
        for tt in range(NTT):
            i1 = v.tensor_tensor(R(tt, 48), R(tt, 32), R(tt, 40), ALU.mult)
        grp(i1)
        for tt in range(NTT):
            i1 = v.reduce_sum(s["rstat"][:, 12 + tt:13 + tt], R(tt, 48), axis=AX.X)
        grp(i1)
        i1 = v.tensor_scalar(s["rstat"][:, 16:20], s["rstatb"][:, 24:28], 1e-8, None, ALU.is_lt)
        i2 = v.tensor_scalar(s["rstat"][:, 20:24], s["rstatb"][:, 24:28], 1e-8, None, ALU.is_ge)
        grp(i2)
        i2 = v.tensor_scalar(s["rstat"][:, 12:16], s["rstat"][:, 12:16], 1e-8, None,
                             ALU.max)
        grp(i2)
        inst = v.reciprocal(s["rstat"][:, 32:36], s["rstat"][:, 12:16])
        mV.inc(inst)

    def sync_b3(q):
        q.wait(mV, v0 + 7)
        dS.dinc(q.dma_start(out=s["rstatb"][:, 12:24], in_=s["rstat"][:, 12:24]))
        dS.dinc(q.dma_start(out=s["rstatb"][:, 32:36], in_=s["rstat"][:, 32:36]))

    def vector_r3(v):
        fV = P.sems["fV"]
        v.wait(dS, b0 + 80)
        for tt in range(NTT):
            i1 = v.tensor_scalar(R(tt, 48), R(tt, 48), s["rstatb"][:, 32 + tt:33 + tt],
                                 s["rstatb"][:, 16 + tt:17 + tt], ALU.mult, ALU.mult)
        fV.inc(i1)
        for tt in range(NTT):
            i1 = v.tensor_scalar(R(tt, 24), R(tt, 24), s["rstatb"][:, 28 + tt:29 + tt],
                                 s["rstatb"][:, 20 + tt:21 + tt], ALU.mult, ALU.mult)
        fV.inc(i1)
        v.wait(fV, fV.n)
        for tt in range(NTT):
            inst = v.tensor_tensor(s["w8"][:, tt * 8:(tt + 1) * 8],
                                   R(tt, 48), R(tt, 24), ALU.add)
            mV.inc(inst)

    def tensor_w8(t):
        for tt in range(NTT):
            t.wait(mV, v0 + 8 + tt)
            if tt >= 1:
                t.wait(mV, v0 + 11 + tt)
            inst = t.transpose(p["pTf"][0:8, :], s["w8"][:, tt * 8:(tt + 1) * 8],
                               s["identf"][:])
            mT.inc(inst)

    def vector_w8(v):
        for tt in range(NTT):
            v.wait(mT, t0 + 4 + tt + 1)
            inst = v.tensor_copy(s["w8f"][0:8, tt * 128:(tt + 1) * 128], p["pTf"][0:8, :])
            mV.inc(inst)

    P.sec("tensor", tensor_r)
    P.sec("vector", vector_lg)
    P.sec("scalar", scalar_r)
    P.sec("vector", vector_r)
    P.sec("sync", sync_b1)
    P.sec("vector", vector_r1)
    P.sec("sync", sync_b2)
    P.sec("vector", vector_r2)
    P.sec("sync", sync_b3)
    P.sec("vector", vector_r3)
    P.sec("tensor", tensor_w8)
    P.sec("vector", vector_w8)
    P.barrier()


def ph_xga0(P: Prog, c):
    s, d, p = P.s, P.d, P.p
    mV, mS, mT = P.sems["mV"], P.sems["mS"], P.sems["mT"]
    v0, t0, s0 = mV.n, mT.n, mS.n
    wl = [s["wl0"], s["wl1"]]
    dL = [P.dL[0], P.dL[4]]
    adma = {}

    def sync(q):
        for i, (e, nm) in enumerate([(0, "ga0"), (0, "ua0"), (1, "ga0"), (1, "ua0")]):
            if i >= 2:
                q.wait(mV, v0 + (i - 1))
            adma[i] = dL[i % 2].dinc(q.dma_start(
                out=wl[i % 2][:, 0:1024].rearrange("p (k r) -> p k r", r=RADJ),
                in_=d[nm][e].rearrange("(k p) r -> p k r", p=128)))

    def vector(v):
        for i in range(4):
            v.wait(dL[i % 2], adma[i])
            if i >= 2:
                v.wait(mT, t0 + (i - 1))
            mV.inc(v.tensor_copy(s["adsl"][:, (i % 2) * 1024:(i % 2 + 1) * 1024],
                                 wl[i % 2][:, 0:1024]))

    def tensor(t):
        for i in range(4):
            t.wait(mV, v0 + i + 1)
            if i >= 2:
                t.wait(mS, s0 + (i - 1))
            pj = p["pA"] if i % 2 == 0 else p["pB"]
            for dk in range(NDK):
                inst = t.matmul(pj[0:64, :],
                                s["adsl"][:, (i % 2) * 1024 + dk * 64:(i % 2) * 1024 + (dk + 1) * 64],
                                s["xnT"][:, dk * TC:(dk + 1) * TC],
                                start=(dk == 0), stop=(dk == NDK - 1))
            mT.inc(inst)

    def scalar(sc):
        for i in range(4):
            sc.wait(mT, t0 + i + 1)
            pj = p["pA"] if i % 2 == 0 else p["pB"]
            e, which = divmod(i, 2)
            inst = sc.activation(s["xga"][0:64, (e * 2 + which) * TC:(e * 2 + which + 1) * TC],
                                 pj[0:64, :], AF.Copy)
            mS.inc(inst)

    P.sec("sync", sync)
    P.sec("vector", vector)
    P.sec("tensor", tensor)
    P.sec("scalar", scalar)
    P.barrier()


def ph_ffn0(P: Prog, c):
    s, d, p = P.s, P.d, P.p
    mV, mS, mT, mG = P.sems["mV"], P.sems["mS"], P.sems["mT"], P.sems["mG"]
    slots = [(s["wl0"], 0), (s["wl0"], 512), (s["wl0"], 1024), (s["wl0"], 1536),
             (s["wl1"], 0), (s["wl1"], 512), (s["wl1"], 1024), (s["wl1"], 1536)]
    nslot = len(slots)

    sync_ops, vec_ops, ten_ops, sca_ops, gps_ops = [], [], [], [], []

    class Ev:
        pass
    ev = Ev()
    ev.v, ev.t, ev.s, ev.g = mV.n, mT.n, mS.n, mG.n

    def S_dma(waits, out_ap, in_ap, slot):
        sem = P.dL[slot]
        sem.n += 16
        myval = sem.n
        def fn(q, out_ap=out_ap, in_ap=in_ap, sem=sem):
            inst = q.dma_start(out=out_ap, in_=in_ap)
            inst.then_inc(sem.h, 16)
            _UPDATED[id(inst)] = inst
        sync_ops.append((list(waits), fn))
        return (slot, myval)

    def V_op(waits, fn):
        ev.v += 1
        def wrap(v, fn=fn):
            mV.inc(fn(v))
        vec_ops.append((list(waits), wrap))
        return ev.v

    def T_group(waits, fn):
        ev.t += 1
        def wrap(t, fn=fn):
            mT.inc(fn(t))
        ten_ops.append((list(waits), wrap))
        return ev.t

    def C_op(waits, fn):
        ev.s += 1
        def wrap(sc, fn=fn):
            mS.inc(fn(sc))
        sca_ops.append((list(waits), wrap))
        return ev.s

    def G_op(waits, fn):
        ev.g += 1
        def wrap(g, fn=fn):
            mG.inc(fn(g))
        gps_ops.append((list(waits), wrap))
        return ev.g

    state = Ev()
    state.dmai = 0
    slot_last = [0] * nslot
    wb_guard = {"ws0": [], "ws1": []}

    def S_bf(waits, out_ap, in_ap, slot):
        sem = P.dL[slot]
        sem.n += 16
        myval = sem.n
        def fn(q, out_ap=out_ap, in_ap=in_ap, sem=sem):
            inst = q.dma_start(out=out_ap, in_=in_ap)
            inst.then_inc(sem.h, 16)
            _UPDATED[id(inst)] = inst
        sync_ops.append((list(waits), fn))
        return (slot, myval)

    def S_wb(cast_toks, dst, src_ap, slot):
        sem = P.dL[slot]
        sem.n += 16
        myval = sem.n
        w = [("v", m) for m in sorted({t for t in cast_toks if isinstance(t, int)})]
        for t in cast_toks:
            if isinstance(t, int):
                continue
        w = [("v", t) for t in cast_toks]
        def fn(q, dst=dst, src_ap=src_ap, sem=sem):
            inst = q.dma_start(out=dst.rearrange("p (a c) -> p a c", c=2048) if False
                               else dst, in_=src_ap)
            inst.then_inc(sem.h, 16)
            _UPDATED[id(inst)] = inst
        sync_ops.append((w, fn))
        return ("dl", (slot, myval))

    def stream_panel(src_ap, slab_ap, extra_cast_waits):
        width = src_ap.shape[-1]
        if width <= 512:
            sidx = state.dmai % nslot
            wlt, off = slots[sidx]
            w = [("v", slot_last[sidx])] if slot_last[sidx] else []
            dn = S_dma(w, wlt[:, off:off + width], src_ap, sidx)
            cn = V_op([("dl", dn)] + extra_cast_waits,
                      lambda v, wlt=wlt, off=off, width=width, slab_ap=slab_ap:
                      v.tensor_copy(slab_ap, wlt[:, off:off + width]))
            slot_last[sidx] = cn
            state.dmai += 1
            return cn
        assert width == 2048
        half = state.dmai % 2
        base = half * 4
        wlt = s["wl0"] if half == 0 else s["wl1"]
        w = [("v", m) for m in {slot_last[base + k] for k in range(4)} if m]
        dn = S_dma(w, wlt[:], src_ap, base)
        cn = V_op([("dl", dn)] + extra_cast_waits,
                  lambda v, wlt=wlt, slab_ap=slab_ap: v.tensor_copy(slab_ap, wlt[:]))
        for k in range(4):
            slot_last[base + k] = cn
        state.dmai += 1
        return cn

    def clobber_all_slots(val):
        for k in range(nslot):
            slot_last[k] = max(slot_last[k], val)

    last_gate_mm = 0
    last_up_mm = 0
    last_down_mm = 0
    bank_rd = {0: [], 1: []}
    wbc_rd = 0
    prev_wbc_mm = 0
    prev_silu = 0
    prev_mul = 0
    prev_pool = 0
    prev_mda_evac = 0

    for e in range(NE):
        prev_down_last = last_down_mm
        dMsem = P.sems["dM"]
        dMsem.n += 16
        dmval = dMsem.n
        gw = [("t", prev_wbc_mm)] if prev_wbc_mm else []
        def g_dma(q, e=e, dMsem=dMsem):
            inst = q.dma_start(out=P.row(2), in_=s["w8f"][e:e + 1, :])
            inst.then_inc(dMsem.h, 16)
            _UPDATED[id(inst)] = inst
        sync_ops.append((gw, g_dma))
        ww = list(bank_rd[1]) + [("dm", dmval)]
        if wbc_rd:
            ww.append(("g", wbc_rd))
        tw = T_group(ww, lambda t: t.matmul(
            p["pE"][0:1, :] if False else p["pE"][:], s["ones_row"][:],
            P.row(2), start=True, stop=True))
        prev_wbc_mm = tw
        wbc_done = C_op([("t", tw)], lambda sc: sc.activation(
            s["wbc"][:], p["pE"][:], AF.Copy))
        bank_rd[1] = [("s", wbc_done)]

        gbcast = ubcast = None
        if e < NL:
            dn = S_dma([("v", ev.v)], s["wl0"][0:64, :], d["gb0"][e], 0)
            gbcast = V_op([("dl", dn), ("t", last_gate_mm)],
                          lambda v: v.tensor_copy(s["adsl"][0:64, 0:2048],
                                                  s["wl0"][0:64, :]))
            dn2 = S_dma([("v", gbcast)], s["wl1"][0:64, :], d["ub0"][e], 4)
            ubcast = V_op([("dl", dn2), ("t", last_up_mm)],
                          lambda v: v.tensor_copy(s["adsl"][0:64, 2048:4096],
                                                  s["wl1"][0:64, :]))
            clobber_all_slots(ubcast)

        for fq in range(4):
            gcasts, ucasts = [], []
            if c == 0:
                for dk in range(NDK):
                    src = d["wg0"][e, dk * 128:(dk + 1) * 128, fq * 512:(fq + 1) * 512]
                    cwl = [("t", last_gate_mm), ("t", last_down_mm)] + wb_guard["ws0"]
                    gcasts.append(stream_panel(src, s["ws0"][:, dk * 512:(dk + 1) * 512], cwl))
                for dk in range(NDK):
                    src = d["wu0"][e, dk * 128:(dk + 1) * 128, fq * 512:(fq + 1) * 512]
                    cwl = [("t", last_up_mm), ("t", last_down_mm)] + wb_guard["ws1"]
                    ucasts.append(stream_panel(src, s["ws1"][:, dk * 512:(dk + 1) * 512], cwl))
                wb_guard["ws0"] = [S_wb(gcasts, d["wg0b"][e * 4 + fq], s["ws0"][:], 13)]
                wb_guard["ws1"] = [S_wb(ucasts, d["wu0b"][e * 4 + fq], s["ws1"][:], 14)]
            else:
                g_ld = S_bf(
                    [("t", last_gate_mm), ("t", last_down_mm)],
                    s["ws0"][:].rearrange("p (a c) -> p a c", c=2048),
                    d["wg0b"][e * 4 + fq].rearrange("p (a c) -> p a c", c=2048), 8)
                u_ld = S_bf(
                    [("t", last_up_mm), ("t", last_down_mm)],
                    s["ws1"][:].rearrange("p (a c) -> p a c", c=2048),
                    d["wu0b"][e * 4 + fq].rearrange("p (a c) -> p a c", c=2048), 12)
                gcasts = [("dl", g_ld)]
                ucasts = [("dl", u_ld)]
            for ftl in range(4):
                ft = fq * 4 + ftl

                def g_mm(t, e=e, ft=ft, ftl=ftl):
                    ad = e < NL
                    for dk in range(NDK):
                        inst = t.matmul(
                            p["pA"][:],
                            s["ws0"][:, dk * 512 + ftl * 128: dk * 512 + ftl * 128 + 128],
                            s["xnT"][:, dk * TC:(dk + 1) * TC],
                            start=(dk == 0), stop=(dk == NDK - 1) and not ad)
                    if ad:
                        inst = t.matmul(p["pA"][:],
                                        s["adsl"][0:64, ft * 128:(ft + 1) * 128],
                                        s["xga"][0:64, (e * 2) * TC:(e * 2 + 1) * TC],
                                        start=False, stop=True)
                    return inst

                gwl = [gcasts[0] if c > 0 else ("v", max(gcasts)), ("s", prev_silu)]
                if e < NL:
                    gwl.append(("v", gbcast))
                tg = T_group(gwl, g_mm)
                last_gate_mm = tg

                def u_mm(t, e=e, ft=ft, ftl=ftl):
                    ad = e < NL
                    for dk in range(NDK):
                        inst = t.matmul(
                            p["pB"][:],
                            s["ws1"][:, dk * 512 + ftl * 128: dk * 512 + ftl * 128 + 128],
                            s["xnT"][:, dk * TC:(dk + 1) * TC],
                            start=(dk == 0), stop=(dk == NDK - 1) and not ad)
                    if ad:
                        inst = t.matmul(p["pB"][:],
                                        s["adsl"][0:64, 2048 + ft * 128:2048 + (ft + 1) * 128],
                                        s["xga"][0:64, (e * 2 + 1) * TC:(e * 2 + 2) * TC],
                                        start=False, stop=True)
                    return inst

                uwl = [ucasts[0] if c > 0 else ("v", max(ucasts)), ("v", prev_mul)]
                if e < NL:
                    uwl.append(("v", ubcast))
                tu = T_group(uwl, u_mm)
                last_up_mm = tu

                cs = C_op([("t", tg), ("v", prev_mul)],
                          lambda sc: sc.activation(s["scr"][0][:], p["pA"][:], AF.Silu))
                prev_silu = cs
                mv_ = V_op([("s", cs), ("t", tu), ("g", prev_pool)],
                           lambda v: v.tensor_tensor(s["scr"][1][:], s["scr"][0][:],
                                                     p["pB"][:], ALU.mult))
                prev_mul = mv_
                pg = G_op([("v", mv_), ("s", wbc_done), ("t", prev_down_last)],
                          lambda g, ft=ft: g.tensor_tensor(
                              s["mid"][:, ft * TC:(ft + 1) * TC], s["scr"][1][:],
                              s["wbc"][:], ALU.mult))
                prev_pool = pg
        wbc_rd = prev_pool

        mda_done = None
        if e < NL:
            dn = S_dma([("v", ev.v)], s["wl0"][:, 0:1024].rearrange(
                "p (k r) -> p k r", r=RADJ),
                d["da0"][e].rearrange("(k p) r -> p k r", p=128), 0)
            dacast = V_op([("dl", dn), ("t", last_gate_mm)],
                          lambda v: v.tensor_copy(s["adsl"][:, 0:1024],
                                                  s["wl0"][:, 0:1024]))
            dn2 = S_dma([("v", dacast)], s["wl1"][0:64, :], d["db0"][e], 4)
            dbcast = V_op([("dl", dn2), ("t", last_up_mm)],
                          lambda v: v.tensor_copy(s["adsl"][0:64, 2048:4096],
                                                  s["wl1"][0:64, :]))
            clobber_all_slots(dbcast)

            def mda_mm(t):
                for fk in range(NDK):
                    inst = t.matmul(p["pD"][0:64, :],
                                    s["adsl"][:, fk * 64:(fk + 1) * 64],
                                    s["mid"][:, fk * TC:(fk + 1) * TC],
                                    start=(fk == 0), stop=(fk == NDK - 1))
                return inst
            mdw = [("v", dacast), ("g", prev_pool)]
            if prev_mda_evac:
                mdw.append(("s", prev_mda_evac))
            tmda = T_group(mdw, mda_mm)
            mda_done = C_op([("t", tmda)], lambda sc: sc.activation(
                s["mda"][0:64, :], p["pD"][0:64, :], AF.Copy))
            prev_mda_evac = mda_done

        for half in range(2):
            dcasts = []
            if c == 0:
                for fkl in range(8):
                    fk = half * 8 + fkl
                    src = d["wd0"][e, fk * 128:(fk + 1) * 128, :]
                    tgt = s["ws0"] if fkl < 4 else s["ws1"]
                    cwl = [("t", last_down_mm), ("t", last_gate_mm), ("t", last_up_mm)]
                    cwl += wb_guard["ws0"] if fkl < 4 else wb_guard["ws1"]
                    dcasts.append(stream_panel(
                        src, tgt[:, (fkl % 4) * 2048:(fkl % 4 + 1) * 2048], cwl))
                wb_guard["ws0"] = [S_wb(dcasts[:4], d["wd0b"][e * 4 + half * 2],
                                        s["ws0"][:], 13)]
                wb_guard["ws1"] = [S_wb(dcasts[4:], d["wd0b"][e * 4 + half * 2 + 1],
                                        s["ws1"][:], 14)]
                dtok = [("v", max(dcasts))]
            else:
                d_ld0 = S_bf([("t", last_down_mm), ("t", last_gate_mm), ("t", last_up_mm)],
                             s["ws0"][:].rearrange("p (a c) -> p a c", c=2048),
                             d["wd0b"][e * 4 + half * 2].rearrange("p (a c) -> p a c", c=2048), 8)
                d_ld1 = S_bf([("t", last_down_mm), ("t", last_up_mm)],
                             s["ws1"][:].rearrange("p (a c) -> p a c", c=2048),
                             d["wd0b"][e * 4 + half * 2 + 1].rearrange("p (a c) -> p a c", c=2048), 12)
                dtok = [("dl", d_ld0), ("dl", d_ld1)]
            for dt in range(NDK):
                bank = dt % 2
                pb = p["pC"] if bank == 0 else p["pE"]

                def d_mm(t, e=e, half=half, dt=dt, pb=pb):
                    ad = (half == 0 and e < NL)
                    for fkl in range(8):
                        fk = half * 8 + fkl
                        slab = s["ws0"] if fkl < 4 else s["ws1"]
                        inst = t.matmul(
                            pb[:],
                            slab[:, (fkl % 4) * 2048 + dt * 128:(fkl % 4) * 2048 + dt * 128 + 128],
                            s["mid"][:, fk * TC:(fk + 1) * TC],
                            start=(fkl == 0), stop=(fkl == 7) and not ad)
                    if ad:
                        inst = t.matmul(pb[:],
                                        s["adsl"][0:64, 2048 + dt * 128:2048 + (dt + 1) * 128],
                                        s["mda"][0:64, :],
                                        start=False, stop=True)
                    return inst

                dwl = list(dtok) + [("g", prev_pool)] + list(bank_rd[bank])
                if half == 0 and e < NL:
                    dwl.append(("s", mda_done))
                td = T_group(dwl, d_mm)
                last_down_mm = td

                def acc_fn(v, dt=dt, pb=pb, do_copy=(e == 0 and half == 0)):
                    if do_copy:
                        return v.tensor_copy(s["accx"][:, dt * TC:(dt + 1) * TC], pb[:])
                    return v.tensor_tensor(s["accx"][:, dt * TC:(dt + 1) * TC],
                                           s["accx"][:, dt * TC:(dt + 1) * TC],
                                           pb[:], ALU.add)
                av = V_op([("t", td)], acc_fn)
                bank_rd[bank] = [("v", av)]

    def run_ops(ops):
        def fn(eng):
            for waits, op in ops:
                seen = set()
                for kind, val in waits:
                    if val == 0 or val == (0, 0) or (kind, val) in seen:
                        continue
                    seen.add((kind, val))
                    if kind == "dl":
                        slot, v16 = val
                        eng.wait(P.dL[slot], v16)
                    else:
                        sem = {"v": mV, "t": mT, "s": mS, "g": mG,
                               "dm": P.sems["dM"]}[kind]
                        eng.wait(sem, val)
                op(eng)
        return fn

    P.sec("sync", run_ops(sync_ops))
    P.sec("tensor", run_ops(ten_ops))
    P.sec("scalar", run_ops(sca_ops))
    P.sec("vector", run_ops(vec_ops))
    P.sec("gpsimd", run_ops(gps_ops))
    assert mV.n == ev.v and mT.n == ev.t and mS.n == ev.s and mG.n == ev.g
    P.barrier()


class Sched:
    def __init__(self, P: Prog):
        self.P = P
        self.ops = {e: [] for e in ENGINES}
        self.sems = P.sems
        self.slots = [(P.s["wl0"], 0), (P.s["wl0"], 512), (P.s["wl0"], 1024),
                      (P.s["wl0"], 1536), (P.s["wl1"], 0), (P.s["wl1"], 512),
                      (P.s["wl1"], 1024), (P.s["wl1"], 1536)]
        self.slot_last = [0] * 8
        self.dmai = 0
        self.v = P.sems["mV"].n
        self.t = P.sems["mT"].n
        self.c = P.sems["mS"].n
        self.g = P.sems["mG"].n
        self.f = P.sems["fV"].n

    def _add(self, eng, waits, emit):
        self.ops[eng].append((list(waits), emit))

    def dma(self, waits, out_ap, in_ap, slot):
        sem = self.P.dL[slot]
        sem.n += 16
        val = sem.n
        def emit(q, out_ap=out_ap, in_ap=in_ap, sem=sem):
            inst = q.dma_start(out=out_ap, in_=in_ap)
            inst.then_inc(sem.h, 16)
            _UPDATED[id(inst)] = inst
        self._add("sync", waits, emit)
        return ("dl", (slot, val))

    def dma_plain(self, waits, out_ap, in_ap, semname="dM"):
        sem = self.sems[semname]
        sem.n += 16
        val = sem.n
        def emit(q, out_ap=out_ap, in_ap=in_ap, sem=sem):
            inst = q.dma_start(out=out_ap, in_=in_ap)
            inst.then_inc(sem.h, 16)
            _UPDATED[id(inst)] = inst
        self._add("sync", waits, emit)
        return (semname, val)

    def V(self, waits, fn, flush=False):
        self.v += 1
        def emit(v, fn=fn):
            self.sems["mV"].inc(fn(v))
        self._add("vector", waits, emit)
        tok = ("v", self.v)
        if flush:
            self._add("vector", [tok], lambda v: None)
        return tok

    def Vf(self, waits, fn):
        self.f += 1
        def emit(v, fn=fn):
            self.sems["fV"].inc(fn(v))
        self._add("vector", waits, emit)
        tok = ("f", self.f)
        self._add("vector", [tok], lambda v: None)
        return tok

    def T(self, waits, fn):
        self.t += 1
        def emit(t, fn=fn):
            self.sems["mT"].inc(fn(t))
        self._add("tensor", waits, emit)
        return ("t", self.t)

    def C(self, waits, fn):
        self.c += 1
        def emit(sc, fn=fn):
            self.sems["mS"].inc(fn(sc))
        self._add("scalar", waits, emit)
        return ("s", self.c)

    def G(self, waits, fn):
        self.g += 1
        def emit(g, fn=fn):
            self.sems["mG"].inc(fn(g))
        self._add("gpsimd", waits, emit)
        return ("g", self.g)

    def stream(self, waits, src_ap, slab_ap, extra_cast_waits=()):
        shape = list(src_ap.shape)
        width = 1
        for x in shape[1:]:
            width *= x
        inner = shape[-1]
        def land_view(wlt, off):
            flat = wlt[:, off:off + width]
            if len(shape) == 3:
                return flat.rearrange("p (k m) -> p k m", m=inner)
            return flat
        if width <= 512:
            sidx = self.dmai % 8
            wlt, off = self.slots[sidx]
            w = list(waits) + ([("v", self.slot_last[sidx])] if self.slot_last[sidx] else [])
            dn = self.dma(w, land_view(wlt, off), src_ap, sidx)
            cn = self.V([dn] + list(extra_cast_waits),
                        lambda v, wlt=wlt, off=off, width=width, slab_ap=slab_ap:
                        v.tensor_copy(slab_ap, wlt[:, off:off + width]))
            self.slot_last[sidx] = cn[1]
            self.dmai += 1
            return cn
        assert width in (1024, 2048)
        half = self.dmai % 2
        base = half * 4
        wlt = self.P.s["wl0"] if half == 0 else self.P.s["wl1"]
        w = list(waits) + [("v", m) for m in
                           sorted({self.slot_last[base + k] for k in range(4)}) if m]
        dn = self.dma(w, land_view(wlt, 0), src_ap, base)
        cn = self.V([dn] + list(extra_cast_waits),
                    lambda v, wlt=wlt, width=width, slab_ap=slab_ap:
                    v.tensor_copy(slab_ap, wlt[:, 0:width]))
        for k in range(4):
            self.slot_last[base + k] = cn[1]
        self.dmai += 1
        return cn

    def finish(self, barrier=True):
        P = self.P
        semmap = {"v": P.sems["mV"], "t": P.sems["mT"], "s": P.sems["mS"],
                  "g": P.sems["mG"], "f": P.sems["fV"], "dM": P.sems["dM"],
                  "dT": P.sems["dT"], "dS": P.sems["dS"], "dO": P.sems["dO"],
                  "dW": P.sems["dW"], "dX": P.sems["dX"]}
        for eng in ENGINES:
            ops = self.ops[eng]
            if not ops:
                continue
            def fn(e, ops=ops):
                for waits, emit in ops:
                    seen = set()
                    for kind, val in waits:
                        if (kind, val) in seen:
                            continue
                        seen.add((kind, val))
                        if kind == "dl":
                            slot, v16 = val
                            e.wait(P.dL[slot], v16)
                        else:
                            if val == 0:
                                continue
                            e.wait(semmap[kind], val)
                    emit(e)
            P.sec(eng, fn)
        assert P.sems["mV"].n == self.v and P.sems["mT"].n == self.t
        assert P.sems["mS"].n == self.c and P.sems["mG"].n == self.g
        assert P.sems["fV"].n == self.f
        if barrier:
            P.barrier()


def _seg_kinds(segs):
    return segs


def ph_attn0(P: Prog, c):
    s, d, p = P.s, P.d, P.p
    S = Sched(P)
    band = _band0(c)
    nkt = len(band)
    tbc = s["tb"][:, 0:TC]
    tbs = s["tb"][:, TC:2 * TC]

    prev_qevac_tok = prev_qrope_tok = prev_afm_tok = prev_rot_tok = None
    prev_tb_tok = None
    prev_wbc_rd = None
    prev_scr0_rd = None
    prev_bcast_evac = None
    qproj_tok = {}
    for h in range(L0_NH):
        kh = h // 4
        cast_w = []
        if h >= 4:
            cast_w.append(qproj_tok[h - 4])
        cn = S.stream([], d["wq0"][:, h * 128:(h + 1) * 128].rearrange(
            "(k p) m -> p k m", p=128),
            s["ws1"][:, (h % 4) * 2048:(h % 4 + 1) * 2048], cast_w)
        qw = [cn]
        if prev_qevac_tok:
            qw.append(prev_qevac_tok)
        def qmm(t, h=h):
            for dk in range(NDK):
                inst = t.matmul(p["pA"][:],
                                s["ws1"][:, (h % 4) * 2048 + dk * 128:(h % 4) * 2048 + (dk + 1) * 128],
                                s["xnT"][:, dk * TC:(dk + 1) * TC],
                                start=(dk == 0), stop=(dk == NDK - 1))
            return inst
        tq = S.T(qw, qmm)
        qproj_tok[h] = tq
        ew = [tq]
        if prev_qrope_tok:
            ew.append(prev_qrope_tok)
        ce = S.C(ew, lambda sc: sc.activation(s["scr"][1][:], p["pA"][:], AF.Copy))
        prev_qevac_tok = ce
        bw = [ce]
        if prev_rot_tok:
            bw.append(prev_rot_tok)
        vb = S.V(bw, lambda v: v.tensor_copy(s["qt"][:], s["scr"][1][:]))
        rw = [vb]
        if prev_qrope_tok:
            rw.append(prev_qrope_tok)
        tr_ = S.T(rw, lambda t: t.matmul(p["pC"][:], s["rot0"][:], s["qt"][:],
                                         start=True, stop=True))
        prev_rot_tok = tr_
        f1w = [tr_]
        if prev_wbc_rd:
            f1w.append(prev_wbc_rd)
        f1 = S.Vf(f1w, lambda v: v.tensor_tensor(s["wbc"][:], s["scr"][1][:], tbc, ALU.mult))
        f2 = S.Vf([f1], lambda v: v.tensor_tensor(s["scr"][1][:], p["pC"][:], tbs, ALU.mult))
        vq = S.V([f2], lambda v: v.tensor_tensor(s["qt"][:], s["wbc"][:], s["scr"][1][:],
                                                 ALU.add))
        prev_qrope_tok = vq
        prev_tanh = {}
        prev_exp = {}
        prev_pv = {}
        mask_toks = {}
        sc_toks = {}
        t1_toks = {}
        tanh_buf = lambda i: s["scr"][0] if i % 2 == 0 else s["wbc"]

        def emit_exp_pv(i):
            kt, segs = band[i]
            xw = [prev_tanh[i]] + mask_toks[i]
            if i >= 2:
                xw.append(prev_pv[i - 2])
            cxp = S.C(xw, lambda sc, i=i: sc.activation(
                s["pst"][i % 2][:], tanh_buf(i)[:], AF.Exp, scale=L0_CAP))
            prev_exp[i] = cxp
            ztoks = []
            for tl, kind in enumerate(segs):
                if kind == "zero":
                    ztoks.append(S.V([cxp], lambda v, tl=tl, i=i: v.memset(
                        s["pst"][i % 2][:, tl * 128:(tl + 1) * 128], 0.0)))
            pw = [cxp] + ztoks
            if i == 0 and prev_afm_tok:
                pw.append(prev_afm_tok)
            t1 = S.T(pw, lambda t, kt=kt, i=i, kh=kh: t.matmul(
                p["pB"][:], s["v0s"][:, (kh * 16 + kt) * 128:(kh * 16 + kt + 1) * 128],
                s["pst"][i % 2][:], start=(i == 0), stop=(i == nkt - 1)))
            t2 = S.T([], lambda t, i=i: t.matmul(
                p["pE"][0:1, :], s["ones_col"][:], s["pst"][i % 2][:],
                start=(i == 0), stop=(i == nkt - 1)))
            prev_pv[i] = t2
            t1_toks[i] = t1

        for i, (kt, segs) in enumerate(band):
            psc = p["pD"] if i % 2 == 0 else p["pA"]
            sw = [vq]
            if i % 2 == 0 and i >= 2:
                sw.append(prev_tanh[i - 2])
            if i == 0 and prev_bcast_evac:
                sw.append(prev_bcast_evac)
            if i % 2 == 1:
                sw.append(ce)
                if i >= 3:
                    sw.append(prev_tanh[i - 2])
            tsc = S.T(sw, lambda t, kt=kt, psc=psc, kh=kh: t.matmul(
                psc[:], s["k0s"][:, kh * T + kt * 128: kh * T + (kt + 1) * 128],
                s["qt"][:], start=True, stop=True))
            sc_toks[i] = tsc
            tw = [tsc]
            if i >= 2:
                tw.append(prev_exp[i - 2])
            elif i == 0 and prev_scr0_rd:
                tw.append(prev_scr0_rd)
            cth = S.C(tw, lambda sc, psc=psc, i=i: sc.activation(
                tanh_buf(i)[:], psc[:], AF.Tanh, scale=1.0 / (L0_CAP * math.sqrt(L0_HD))))
            prev_tanh[i] = cth
            mtoks = []
            for tl, kind in enumerate(segs):
                if kind in ("diag", "edge"):
                    mask = s["mdiag"] if kind == "diag" else s["medge"]
                    mtoks.append(S.V([cth], lambda v, tl=tl, mask=mask, i=i: v.tensor_tensor(
                        tanh_buf(i)[:, tl * 128:(tl + 1) * 128],
                        tanh_buf(i)[:, tl * 128:(tl + 1) * 128], mask[:], ALU.add)))
            mask_toks[i] = mtoks
            if i >= 1:
                emit_exp_pv(i - 1)
        emit_exp_pv(nkt - 1)
        t1 = t1_toks[nkt - 1]
        odd_is = [i for i in range(nkt) if i % 2 == 1]
        even_is = [i for i in range(nkt) if i % 2 == 0]
        prev_wbc_rd = prev_exp[odd_is[-1]] if odd_is else None
        prev_scr0_rd = prev_exp[even_is[-1]] if even_is else None
        vrw = [prev_pv[nkt - 1]]
        if prev_tb_tok:
            vrw.append(prev_tb_tok)
        vr = S.V(vrw, lambda v: v.reciprocal(P.row(2), p["pE"][0:1, :]))
        bw2 = [vr, prev_tanh[nkt - 1] if (nkt - 1) % 2 == 0 else prev_tanh[nkt - 2]]
        tb_ = S.T(bw2, lambda t: t.matmul(p["pD"][:], s["ones_row"][:],
                                          P.row(2), start=True, stop=True))
        prev_tb_tok = tb_
        bev_w = [tb_]
        if prev_afm_tok:
            bev_w.append(prev_afm_tok)
        cbe = S.C(bev_w, lambda sc: sc.activation(s["scr"][2][:], p["pD"][:], AF.Copy))
        prev_bcast_evac = cbe
        va = S.V([cbe, prev_pv[nkt - 1], t1], lambda v, h=h: v.tensor_tensor(
            s["afm"][:, h * TC:(h + 1) * TC], p["pB"][:], s["scr"][2][:], ALU.mult))
        prev_afm_tok = va
    S.finish()


def ph_comb0(P: Prog, c):
    s, d, p = P.s, P.d, P.p
    S = Sched(P)

    t_s = S.T([], lambda t: t.matmul(p["pD"][:], s["ones_row"][:], P.row(0),
                                     start=True, stop=True))
    t_m = S.T([], lambda t: t.matmul(p["pE"][:], s["ones_row"][:], P.row(1),
                                     start=True, stop=True))
    c_s = S.C([t_s], lambda sc: sc.activation(s["scr"][1][:], p["pD"][:], AF.Copy))
    c_m = S.C([t_m], lambda sc: sc.activation(s["scr"][2][:], p["pE"][:], AF.Copy))

    prev_comb = {}
    prev_silu = {}
    prev_t1 = {}
    prev_dense = {}
    casts = {}
    sq_mm = {}
    sq_op = {}
    for dt in range(NDK):
        cw = []
        if dt >= 4:
            cw.append(prev_dense[dt - 4])
        casts[dt] = S.stream([], d["wo0"][:, dt * 128:(dt + 1) * 128].rearrange(
            "(k p) m -> p k m", p=128), s["ws0"][:, (dt % 4) * 2048:(dt % 4 + 1) * 2048],
            cw)
        pb = p["pA"] if dt % 2 == 0 else p["pB"]
        dwl = [casts[dt]]
        if dt >= 2:
            dwl.append(prev_t1[dt - 2])

        def dmm(t, dt=dt, pb=pb):
            for ht in range(L0_NH):
                inst = t.matmul(pb[:],
                                s["ws0"][:, (dt % 4) * 2048 + ht * 128:(dt % 4) * 2048 + ht * 128 + 128],
                                s["afm"][:, ht * TC:(ht + 1) * TC],
                                start=(ht == 0), stop=(ht == L0_NH - 1))
            return inst
        td = S.T(dwl, dmm)
        prev_dense[dt] = td
        sw = [c_s, c_m]
        if dt >= 1:
            sw.append(prev_t1[dt - 1])
        csl = S.C(sw, lambda sc, dt=dt: sc.activation(
            s["scr"][0][:], s["accx"][:, dt * TC:(dt + 1) * TC], AF.Silu))
        prev_silu[dt] = csl
        t1w = [td, csl]
        t1_ = S.Vf(t1w, lambda v, pb=pb: v.tensor_tensor(
            s["wbc"][:], pb[:], s["scr"][0][:], ALU.mult))
        prev_t1[dt] = t1_
        a1 = S.Vf([t1_, csl], lambda v, dt=dt: v.tensor_tensor(
            s["accx"][:, dt * TC:(dt + 1) * TC],
            s["xnT"][:, dt * TC:(dt + 1) * TC], s["scr"][1][:], ALU.mult))
        a2 = S.Vf([a1], lambda v, dt=dt: v.tensor_tensor(
            s["accx"][:, dt * TC:(dt + 1) * TC],
            s["accx"][:, dt * TC:(dt + 1) * TC], s["scr"][2][:], ALU.add))
        a3 = S.V([a2], lambda v, dt=dt: v.tensor_tensor(
            s["accx"][:, dt * TC:(dt + 1) * TC],
            s["accx"][:, dt * TC:(dt + 1) * TC], s["wbc"][:], ALU.add))
        prev_comb[dt] = a3

    for dt in range(NDK):
        sqw = [prev_comb[dt]]
        if dt >= 1:
            sqw.append(sq_mm[dt - 1])
        else:
            sqw.append(prev_t1[NDK - 1])
        csq = S.C(sqw, lambda sc, dt=dt: sc.activation(
            s["scr"][0][:], s["accx"][:, dt * TC:(dt + 1) * TC], AF.Square))
        sq_op[dt] = csq
        m1w = [prev_comb[dt]]
        if dt == 0:
            m1w += [prev_t1[NDK - 1], prev_t1[NDK - 2]]
        t1_ = S.T(m1w, lambda t, dt=dt: t.matmul(
            p["pC"][0:1, :], s["crows"][:, 4:5], s["accx"][:, dt * TC:(dt + 1) * TC],
            start=(dt == 0), stop=(dt == NDK - 1)))
        t2_ = S.T([csq], lambda t, dt=dt: t.matmul(
            p["pE"][0:1, :], s["crows"][:, 4:5], s["scr"][0][:],
            start=(dt == 0), stop=(dt == NDK - 1)))
        sq_mm[dt] = t2_
    sum_mm = t1_
    ssq_mm = t2_

    M0, M1, M2 = P.row(0), P.row(1), P.row(2)
    f1 = S.Vf([sum_mm], lambda v: v.tensor_scalar(M0, p["pC"][0:1, :], 1.0 / D, None, ALU.mult))
    f2 = S.Vf([ssq_mm, f1], lambda v: v.tensor_scalar(M1, p["pE"][0:1, :], 1.0 / D, None, ALU.mult))
    f3 = S.Vf([f2], lambda v: v.tensor_tensor(M2, M0, M0, ALU.mult))
    f4 = S.V([f3], lambda v: v.tensor_tensor(M1, M1, M2, ALU.subtract))
    c_sq = S.C([f4], lambda sc: sc.activation(M2, M1, AF.Sqrt, bias=s["crows"][0:1, 1:2]))
    f5 = S.Vf([c_sq], lambda v: v.reciprocal(M1, M2))
    f6 = S.Vf([f5], lambda v: v.tensor_tensor(M2, M0, M1, ALU.mult))
    f7 = S.V([f6], lambda v: v.tensor_scalar(M2, M2, -1.0, None, ALU.mult))
    fp1 = S.Vf([f7], lambda v: v.tensor_copy(P.row(3), M0))
    fp2 = S.Vf([fp1], lambda v: v.tensor_copy(P.row(4), M1))

    tb1 = S.T([f5, f7, fp2] + [prev_t1[NDK - 1]], lambda t: t.matmul(
        p["pD"][:], s["ones_row"][:], M1, start=True, stop=True))
    tb2 = S.T([f7], lambda t: t.matmul(
        p["pE"][:], s["ones_row"][:], M2, start=True, stop=True))
    cb1w = [tb1, prev_silu[NDK - 1]]
    cb1 = S.C(cb1w, lambda sc: sc.activation(s["scr"][1][:], p["pD"][:], AF.Copy))
    cb2 = S.C([tb2], lambda sc: sc.activation(s["scr"][2][:], p["pE"][:], AF.Copy))

    prev_add = None
    for dt in range(NDK):
        uw = [cb1, cb2, sq_mm[NDK - 1], sum_mm]
        if prev_add:
            uw.append(prev_add)
        u1 = S.Vf(uw, lambda v, dt=dt: v.tensor_tensor(
            s["wbc"][:], s["accx"][:, dt * TC:(dt + 1) * TC],
            s["scr"][1][:], ALU.mult))
        prev_add = S.V([u1], lambda v, dt=dt: v.tensor_tensor(
            s["xnT"][:, dt * TC:(dt + 1) * TC], s["wbc"][:], s["scr"][2][:], ALU.add))
    S.finish()


def ph_prep1(P: Prog, c):
    s, d, p = P.s, P.d, P.p
    S = Sched(P)
    d1 = S.dma_plain([], s["scr"][1][:], d["c_cos1"][:, c * TC:(c + 1) * TC], "dT")
    d2 = S.dma_plain([], s["scr"][2][:], d["c_sin1"][:, c * TC:(c + 1) * TC], "dT")
    tb1 = S.V([d2], lambda v: v.tensor_copy(s["tb"][:, 0:TC], s["scr"][1][:]))
    tb2 = S.V([tb1], lambda v: v.tensor_copy(s["tb"][:, TC:2 * TC], s["scr"][2][:]))
    tbc = s["tb"][0:64, 0:TC]
    tbs = s["tb"][0:64, TC:2 * TC]
    cn = S.stream([], d["wqkv1"][:, 2048:2176].rearrange("(k p) m -> p k m", p=128),
                  s["ws0"][:, 0:2048])
    def kvmm(t):
        for dk in range(NDK):
            inst = t.matmul(p["pA"][:], s["ws0"][:, dk * 128:(dk + 1) * 128],
                            s["xnT"][:, dk * TC:(dk + 1) * TC],
                            start=(dk == 0), stop=(dk == NDK - 1))
        return inst
    tkv = S.T([cn], kvmm)
    ce = S.C([tkv, tb2], lambda sc: sc.activation(s["scr"][1][0:64, :], p["pA"][0:64, :],
                                                  AF.Copy))
    vb = S.V([ce], lambda v: v.tensor_copy(s["qt"][0:64, :], s["scr"][1][0:64, :]))
    tr_ = S.T([vb], lambda t: t.matmul(p["pC"][0:64, :], s["rot1"][0:64, 0:64],
                                       s["qt"][0:64, :], start=True, stop=True))
    f1 = S.Vf([tr_], lambda v: v.tensor_tensor(s["wbc"][0:64, :], s["scr"][1][0:64, :],
                                               tbc, ALU.mult))
    f2 = S.Vf([f1], lambda v: v.tensor_tensor(s["scr"][1][0:64, :], p["pC"][0:64, :],
                                              tbs, ALU.mult))
    k1w = S.V([f2], lambda v: v.tensor_tensor(
        s["k1s"][0:64, c * TC:(c + 1) * TC], s["wbc"][0:64, :], s["scr"][1][0:64, :],
        ALU.add))
    S.V([k1w], lambda v: v.tensor_copy(s["k1s"][64:128, c * TC:(c + 1) * TC],
                                       s["k1s"][0:64, c * TC:(c + 1) * TC]))
    cv = S.C([tkv], lambda sc: sc.activation(s["scrb"][0:64, :], p["pA"][64:128, :],
                                             AF.Copy))
    prev_ev = None
    for j in range(NTT):
        tw = [cv]
        if prev_ev is not None and j >= 2:
            tw.append(evs[j - 2])
        tt_ = S.T(tw, lambda t, j=j: t.transpose(
            p["pT0" if j % 2 == 0 else "pT1"][:, 0:64],
            s["scrb"][0:64, j * 128:(j + 1) * 128], s["identb"][0:64, 0:64]))
        kt = 4 * c + j
        ev = S.V([tt_], lambda v, j=j, kt=kt: v.tensor_copy(
            s["v1s"][:, kt * 64:(kt + 1) * 64],
            p["pT0" if j % 2 == 0 else "pT1"][:, 0:64]))
        if j == 0:
            evs = {}
        evs[j] = ev
        prev_ev = ev
    S.finish()


def ph_attn1(P: Prog, c):
    s, d, p = P.s, P.d, P.p
    S = Sched(P)
    band = _band1(c)
    nkt = len(band)
    tbcF = s["tb"][:, 0:TC]
    tbsF = s["tb"][:, TC:2 * TC]

    prev_qevac = prev_qrope = prev_afm = prev_rot = None
    prev_bc = None
    qproj_tok = {}
    for j in range(L1_NH // 2):
        cast_w = []
        if j >= 4:
            cast_w.append(qproj_tok[j - 4])
        cn = S.stream([], d["wqkv1"][:, j * 128:(j + 1) * 128].rearrange(
            "(k p) m -> p k m", p=128),
            s["ws1"][:, (j % 4) * 2048:(j % 4 + 1) * 2048], cast_w)
        qw = [cn]
        if prev_qevac:
            qw.append(prev_qevac)
        def qmm(t, j=j):
            for dk in range(NDK):
                inst = t.matmul(p["pA"][:],
                                s["ws1"][:, (j % 4) * 2048 + dk * 128:(j % 4) * 2048 + (dk + 1) * 128],
                                s["xnT"][:, dk * TC:(dk + 1) * TC],
                                start=(dk == 0), stop=(dk == NDK - 1))
            return inst
        tq = S.T(qw, qmm)
        qproj_tok[j] = tq
        ew = [tq]
        if prev_qrope:
            ew.append(prev_qrope)
        ce = S.C(ew, lambda sc: sc.activation(s["scr"][1][:], p["pA"][:], AF.Copy))
        prev_qevac = ce
        bw = [ce]
        if prev_rot:
            bw.append(prev_rot)
        vb = S.V(bw, lambda v: v.tensor_copy(s["qt"][:], s["scr"][1][:]))
        rw = [vb]
        if prev_qrope:
            rw.append(prev_qrope)
        tro = S.T(rw, lambda t: t.matmul(p["pC"][:], s["rot1"][:], s["qt"][:],
                                         start=True, stop=True))
        prev_rot = tro
        f1 = S.Vf([tro], lambda v: v.tensor_tensor(s["wbc"][:], s["scr"][1][:], tbcF, ALU.mult))
        f2 = S.Vf([f1], lambda v: v.tensor_tensor(s["scr"][1][:], p["pC"][:], tbsF, ALU.mult))
        vq = S.V([f2], lambda v: v.tensor_tensor(s["qt"][:], s["wbc"][:], s["scr"][1][:],
                                                 ALU.add))
        prev_qrope = vq

        ctx_evac = {}
        recips = {}
        for par in range(2):
            b0 = par * 64
            prev_exp = {}
            prev_pv = {}
            prev_diag = None
            for i, (kt, segs) in enumerate(band):
                psc = p["pD"] if i % 2 == 0 else p["pC"]
                sw = [vq]
                if i >= 2:
                    sw.append(prev_exp[i - 2])
                if i < 2 and par == 1:
                    sw.append(ctx_evac[0])
                if i < 2 and par == 0 and prev_bc:
                    sw.append(prev_bc)
                tsc = S.T(sw, lambda t, kt=kt, psc=psc, b0=b0: t.matmul(
                    psc[:], s["k1s"][b0:b0 + 64, kt * 128:(kt + 1) * 128],
                    s["qt"][b0:b0 + 64, :], start=True, stop=True))
                etoks = []
                for tl, kind in enumerate(segs):
                    sl = slice(tl * 128, (tl + 1) * 128)
                    if kind == "keep":
                        w = [tsc]
                        if i >= 2:
                            w.append(prev_pv[i - 2])
                        etoks.append(S.C(w, lambda sc, psc=psc, sl=sl, i=i:
                                         sc.activation(s["pst"][i % 2][:, sl], psc[:, sl],
                                                       AF.Exp, scale=0.125,
                                                       bias=s["crows"][:, 0:1])))
                    elif kind == "diag":
                        mw = [tsc]
                        if prev_diag:
                            mw.append(prev_diag)
                        vm = S.Vf(mw, lambda v, psc=psc, sl=sl: v.tensor_tensor(
                            s["scr"][0][:, 0:128], psc[:, sl], s["mdiag"][:], ALU.add))
                        w = [vm]
                        if i >= 2:
                            w.append(prev_pv[i - 2])
                        tok = S.C(w, lambda sc, sl=sl, i=i: sc.activation(
                            s["pst"][i % 2][:, sl], s["scr"][0][:, 0:128],
                            AF.Exp, scale=0.125, bias=s["crows"][:, 0:1]))
                        etoks.append(tok)
                        prev_diag = tok
                    else:
                        w = [tsc]
                        if i >= 2:
                            w.append(prev_pv[i - 2])
                        etoks.append(S.V(w, lambda v, sl=sl, i=i: v.memset(
                            s["pst"][i % 2][:, sl], 0.0)))
                prev_exp[i] = etoks[-1] if etoks else tsc
                pw = list(etoks)
                if i == 0:
                    if par == 1:
                        pw.append(ctx_evac[0])
                        pw.append(recips[0])
                    elif prev_afm:
                        pw.append(prev_afm)
                t1 = S.T(pw, lambda t, kt=kt, i=i: t.matmul(
                    p["pB"][0:64, :], s["v1s"][:, kt * 64:(kt + 1) * 64],
                    s["pst"][i % 2][:], start=(i == 0), stop=(i == nkt - 1)))
                t2 = S.T([], lambda t, i=i: t.matmul(
                    p["pE"][0:1, :], s["ones_col"][:], s["pst"][i % 2][:],
                    start=(i == 0), stop=(i == nkt - 1)))
                prev_pv[i] = t2
            rrow = P.row(2) if par == 0 else P.row(0)
            vrw = [prev_pv[nkt - 1]]
            if par == 0 and prev_bc:
                vrw.append(prev_bc)
            vr = S.V(vrw, lambda v, rrow=rrow: v.reciprocal(rrow, p["pE"][0:1, :]))
            recips[par] = vr
            cw = [prev_pv[nkt - 1], t1]
            if par == 0 and prev_afm:
                cw.append(prev_afm)
            cev = S.C(cw, lambda sc, b0=b0: sc.activation(
                s["scrb"][b0:b0 + 64, :], p["pB"][0:64, :], AF.Copy))
            ctx_evac[par] = cev
        tb1_ = S.T([recips[0], recips[1], ctx_evac[1]], lambda t: t.matmul(
            p["pD"][0:64, :], s["ones_row"][0:1, 0:64], P.row(2),
            start=True, stop=True, tile_position=(0, 0)))
        tb2_ = S.T([], lambda t: t.matmul(
            p["pD"][64:128, :], s["ones_row"][0:1, 0:64], P.row(0),
            start=True, stop=True, tile_position=(0, 64)))
        cw = [tb1_, tb2_]
        if prev_afm:
            cw.append(prev_afm)
        cbe = S.C(cw, lambda sc: sc.activation(s["scr"][2][:], p["pD"][:], AF.Copy))
        prev_bc = cbe
        va = S.V([cbe, ctx_evac[1]], lambda v, j=j: v.tensor_tensor(
            s["afm"][:, j * TC:(j + 1) * TC], s["scrb"][:], s["scr"][2][:], ALU.mult))
        prev_afm = va
    S.finish()


def ph_dequant1(P: Prog):
    s, d, p = P.s, P.d, P.p
    S = Sched(P)
    dO = P.sems["dO"]

    def panel_gateup(which, it, prev_toks):
        src = d["g1q" if which == 0 else "u1q"]
        ssc = d["g1s" if which == 0 else "u1s"]
        tgt = d["wg1T" if which == 0 else "wu1T"]
        half = S.dmai_parity = it % 2
        wlt = s["wl0"] if half == 0 else s["wl1"]
        dsc = S.dma_plain([], s["rstat"][:, 40 + which:41 + which],
                          ssc[it * 128:(it + 1) * 128, :], "dT")
        dw = []
        if prev_toks.get(("cast", half)):
            dw.append(prev_toks[("cast", half)])
        dn = S.dma(dw, wlt[:].bitcast(I32), src[it * 128:(it + 1) * 128, :],
                   half * 4)
        cw = [dn, dsc]
        if prev_toks.get(("mmread", half)):
            cw.append(prev_toks[("mmread", half)])
        ct = S.V(cw, lambda v, wlt=wlt, which=which, half=half: v.tensor_scalar(
            s["ws0"][:, half * 4096:half * 4096 + 2048],
            wlt[:].bitcast(I32), 128.0, s["rstat"][:, 40 + which:41 + which],
            ALU.subtract, ALU.mult))
        prev_toks[("cast", half)] = ct
        evs = []
        for k in range(NDK):
            tw = [ct]
            if len(evs) >= 2:
                tw.append(evs[-2])
            elif prev_toks.get(("ev", half)):
                tw.append(prev_toks[("ev", half)])
            ttk = S.T(tw, lambda t, k=k, half=half: t.transpose(
                p["pT0" if k % 2 == 0 else "pT1"][:],
                s["ws0"][:, half * 4096 + k * 128: half * 4096 + (k + 1) * 128],
                s["identb"][:]))
            ew = [ttk]
            if prev_toks.get(("out", half)):
                ew.append(prev_toks[("out", half)])
            ev = S.V(ew, lambda v, k=k, half=half: v.tensor_copy(
                s["ws1"][:, half * 4096 + k * 128: half * 4096 + (k + 1) * 128],
                p["pT0" if k % 2 == 0 else "pT1"][:]))
            evs.append(ev)
        prev_toks[("ev", half)] = evs[-1]
        prev_toks[("mmread", half)] = evs[-1]
        od = S.dma([evs[-1]], tgt[it], s["ws1"][:, half * 4096:half * 4096 + 2048],
                   12 + half)
        prev_toks[("out", half)] = od
        return od

    toks = {}
    for it in range(I1 // 128):
        panel_gateup(0, it, toks)
        panel_gateup(1, it, toks)

    toks2 = {"out": [toks.get(("out", 0)), toks.get(("out", 1))]}
    toks2["out"] = [x for x in toks2["out"] if x]
    for hp in range(NDK):
        dsc = S.dma_plain([], s["rstat"][:, 42:43], d["d1s"][hp * 128:(hp + 1) * 128, :],
                          "dT")
        evs_all = []
        for sub in range(4):
            half = sub % 2
            wlt = s["wl0"] if half == 0 else s["wl1"]
            dw = []
            if toks2.get(("cast", half)):
                dw.append(toks2[("cast", half)])
            dn = S.dma(dw, wlt[:].bitcast(I32),
                       d["d1q"][hp * 128:(hp + 1) * 128, sub * 2048:(sub + 1) * 2048],
                       half * 4)
            cw = [dn, dsc]
            if toks2.get(("tr", half)):
                cw.append(toks2[("tr", half)])
            ct = S.V(cw, lambda v, wlt=wlt, half=half: v.tensor_scalar(
                s["ws0"][:, half * 4096:half * 4096 + 2048],
                wlt[:].bitcast(I32), 128.0, s["rstat"][:, 42:43],
                ALU.subtract, ALU.mult))
            toks2[("cast", half)] = ct
            evs = []
            for k in range(NDK):
                tw = [ct]
                if len(evs) >= 2:
                    tw.append(evs[-2])
                elif evs_all:
                    tw.append(evs_all[-2] if len(evs_all) >= 2 else evs_all[-1])
                ttk = S.T(tw, lambda t, k=k, half=half: t.transpose(
                    p["pT0" if k % 2 == 0 else "pT1"][:],
                    s["ws0"][:, half * 4096 + k * 128: half * 4096 + (k + 1) * 128],
                    s["identb"][:]))
                toks2[("tr", half)] = ttk
                ew = [ttk] + list(toks2.get("out") or [])
                ev = S.V(ew, lambda v, k=k, sub=sub: v.tensor_copy(
                    s["ws1"][:, sub * 2048 + k * 128: sub * 2048 + (k + 1) * 128],
                    p["pT0" if k % 2 == 0 else "pT1"][:]))
                evs.append(ev)
                evs_all.append(ev)
        od = S.dma([evs_all[-1]], d["wd1T"][hp], s["ws1"][:], 14)
        toks2["out"] = [od]
    S.finish()


def ph_ffn1(P: Prog, c):
    s, d, p = P.s, P.d, P.p
    S = Sched(P)
    prev_silu = prev_mul = None
    prev_gmm = prev_umm = prev_dmm = None
    bank_rd = {0: None, 1: None}
    for q1 in range(4):
        mids = {}
        for itg in range(4):
            itbase = q1 * 16 + itg * 4
            gw = []
            if prev_dmm:
                gw.append(prev_dmm)
            if prev_gmm:
                gw.append(prev_gmm)
            gcast = S.dma(gw, s["ws0"][:].rearrange("p (it c) -> p it c", c=2048),
                          d["wg1T"][itbase:itbase + 4].rearrange("it p c -> p it c"), 8)
            ucast = S.dma([prev_umm] if prev_umm else [],
                          s["ws1"][:].rearrange("p (it c) -> p it c", c=2048),
                          d["wu1T"][itbase:itbase + 4].rearrange("it p c -> p it c"), 12)
            for itl in range(4):
                def gmm(t, itl=itl):
                    for hk in range(NDK):
                        inst = t.matmul(p["pA"][:],
                                        s["ws0"][:, itl * 2048 + hk * 128: itl * 2048 + (hk + 1) * 128],
                                        s["xnT"][:, hk * TC:(hk + 1) * TC],
                                        start=(hk == 0), stop=(hk == NDK - 1))
                    return inst
                gwl = [gcast]
                if prev_silu:
                    gwl.append(prev_silu)
                tg = S.T(gwl, gmm)
                prev_gmm = tg
                def umm(t, itl=itl):
                    for hk in range(NDK):
                        inst = t.matmul(p["pB"][:],
                                        s["ws1"][:, itl * 2048 + hk * 128: itl * 2048 + (hk + 1) * 128],
                                        s["xnT"][:, hk * TC:(hk + 1) * TC],
                                        start=(hk == 0), stop=(hk == NDK - 1))
                    return inst
                uwl = [ucast]
                if prev_mul:
                    uwl.append(prev_mul)
                tu = S.T(uwl, umm)
                prev_umm = tu
                sw = [tg]
                if prev_mul:
                    sw.append(prev_mul)
                cs = S.C(sw, lambda sc: sc.activation(s["scr"][0][:], p["pA"][:], AF.Silu))
                prev_silu = cs
                itg_l = itg * 4 + itl
                mw = [cs, tu]
                if prev_dmm:
                    mw.append(prev_dmm)
                mv_ = S.V(mw, lambda v, itg_l=itg_l: v.tensor_tensor(
                    s["mid"][:, itg_l * TC:(itg_l + 1) * TC], s["scr"][0][:],
                    p["pB"][:], ALU.mult))
                prev_mul = mv_
                mids[itg_l] = mv_
        down_slot_rd = {}
        for dt in range(NDK):
            bank = dt % 2
            pb = p["pC"] if bank == 0 else p["pE"]
            dcw = [prev_gmm]
            if (dt % 4) in down_slot_rd:
                dcw.append(down_slot_rd[dt % 4])
            dcast = S.dma(dcw, s["ws0"][:, (dt % 4) * 2048:(dt % 4 + 1) * 2048],
                          d["wd1T"][dt][:, q1 * 2048:(q1 + 1) * 2048], 8 + dt % 4)
            def dmm(t, dt=dt, pb=pb):
                for ik in range(NDK):
                    inst = t.matmul(pb[:],
                                    s["ws0"][:, (dt % 4) * 2048 + ik * 128:(dt % 4) * 2048 + ik * 128 + 128],
                                    s["mid"][:, ik * TC:(ik + 1) * TC],
                                    start=(ik == 0), stop=(ik == NDK - 1))
                return inst
            dwl = [dcast, mids[15]]
            if bank_rd[bank]:
                dwl.append(bank_rd[bank])
            td = S.T(dwl, dmm)
            prev_dmm = td
            down_slot_rd[dt % 4] = td
            av = S.V([td], lambda v, dt=dt, pb=pb, first=(q1 == 0): (
                v.tensor_copy(s["accx"][:, dt * TC:(dt + 1) * TC], pb[:]) if first
                else v.tensor_tensor(s["accx"][:, dt * TC:(dt + 1) * TC],
                                     s["accx"][:, dt * TC:(dt + 1) * TC], pb[:], ALU.add)))
            bank_rd[bank] = av
    S.finish()


def ph_comb1(P: Prog, c):
    s, d, p = P.s, P.d, P.p
    S = Sched(P)
    dO = P.sems["dO"]
    c1 = S.Vf([], lambda v: v.tensor_copy(P.row(0), P.row(4)))
    c2 = S.Vf([c1], lambda v: v.tensor_copy(P.row(1), P.row(3)))
    c3 = S.V([c2], lambda v: v.reciprocal(P.row(2), P.row(0)))
    t_s = S.T([c3], lambda t: t.matmul(p["pD"][:], s["ones_row"][:], P.row(2),
                                       start=True, stop=True))
    t_m = S.T([c2, c3], lambda t: t.matmul(p["pE"][:], s["ones_row"][:], P.row(1),
                                           start=True, stop=True))
    c_s = S.C([t_s], lambda sc: sc.activation(s["scr"][1][:], p["pD"][:], AF.Copy))
    c_m = S.C([t_m], lambda sc: sc.activation(s["scr"][2][:], p["pE"][:], AF.Copy))

    prev_t1 = {}
    prev_dense = {}
    prev_comb = {}
    for dt in range(NDK):
        cw = []
        if dt >= 4:
            cw.append(prev_dense[dt - 4])
        cn = S.stream([], d["wo1"][:, dt * 128:(dt + 1) * 128].rearrange(
            "(k p) m -> p k m", p=128), s["ws1"][:, (dt % 4) * 2048:(dt % 4 + 1) * 2048],
            cw)
        pb = p["pA"] if dt % 2 == 0 else p["pB"]
        dwl = [cn]
        if dt >= 2:
            dwl.append(prev_t1[dt - 2])
        def dmm(t, dt=dt, pb=pb):
            for ht in range(NDK):
                inst = t.matmul(pb[:],
                                s["ws1"][:, (dt % 4) * 2048 + ht * 128:(dt % 4) * 2048 + ht * 128 + 128],
                                s["afm"][:, ht * TC:(ht + 1) * TC],
                                start=(ht == 0), stop=(ht == NDK - 1))
            return inst
        td = S.T(dwl, dmm)
        prev_dense[dt] = td
        sw = [c_s, c_m]
        if dt >= 1:
            sw.append(prev_t1[dt - 1])
        csl = S.C(sw, lambda sc, dt=dt: sc.activation(
            s["scr"][0][:], s["accx"][:, dt * TC:(dt + 1) * TC], AF.Silu))
        t1_ = S.Vf([td, csl], lambda v, pb=pb: v.tensor_tensor(
            s["wbc"][:], pb[:], s["scr"][0][:], ALU.mult))
        prev_t1[dt] = t1_
        a1 = S.Vf([t1_, csl], lambda v, dt=dt: v.tensor_tensor(
            s["accx"][:, dt * TC:(dt + 1) * TC],
            s["xnT"][:, dt * TC:(dt + 1) * TC], s["scr"][1][:], ALU.mult))
        a2 = S.Vf([a1], lambda v, dt=dt: v.tensor_tensor(
            s["accx"][:, dt * TC:(dt + 1) * TC],
            s["accx"][:, dt * TC:(dt + 1) * TC], s["scr"][2][:], ALU.add))
        a3 = S.V([a2], lambda v, dt=dt: v.tensor_tensor(
            s["accx"][:, dt * TC:(dt + 1) * TC],
            s["accx"][:, dt * TC:(dt + 1) * TC], s["wbc"][:], ALU.add))
        prev_comb[dt] = a3

    prev_out_dma = {0: None, 1: None}
    last_evac = None
    for tt in range(NTT):
        half = tt % 2
        wlt = s["wl0"] if half == 0 else s["wl1"]
        evs = []
        for dt in range(NDK):
            tw = [prev_comb[dt]]
            if last_evac is not None:
                tw.append(last_evac)
            ttk = S.T(tw, lambda t, tt=tt, dt=dt: t.transpose(
                p["pTf"][:], s["accx"][:, dt * TC + tt * 128: dt * TC + (tt + 1) * 128],
                s["identf"][:]))
            ew = [ttk]
            if dt == 0 and prev_out_dma[half]:
                ew.append(prev_out_dma[half])
            ev = S.C(ew, lambda sc, dt=dt, wlt=wlt: sc.activation(
                wlt[:, dt * 128:(dt + 1) * 128], p["pTf"][:], AF.Copy))
            evs.append(ev)
            last_evac = ev
        r0 = c * TC + tt * 128
        od = S.dma([evs[-1]], d["y"][r0:r0 + 128, :], wlt[:], half * 4)
        prev_out_dma[half] = od
    S.finish()


def build_full():
    P = Prog()
    ph_setup(P)
    for c in range(NCHUNK):
        ph_ln0(P, c)
        ph_prep0(P, c)
        ph_router0(P, c)
        ph_xga0(P, c)
        ph_ffn0(P, c)
        ph_attn0(P, c)
        ph_comb0(P, c)
        ph_prep1(P, c)
        if c == 0:
            ph_dequant1(P)
        ph_ffn1(P, c)
        ph_attn1(P, c)
        ph_comb1(P, c)
    return P


def kernel(**inputs):
    P = build_full()
    maps = mk_inmaps(inputs, n_cores=8)
    res = P.run(maps)
    return np.stack([np.asarray(r["y"], dtype=np.float32) for r in res.results], axis=0)
